# version 08
# speedup vs baseline: 1.4238x; 1.0679x over previous
import contextlib
import numpy as np
import ml_dtypes
import concourse.bass as bass
import concourse.mybir as mybir
from concourse.bass_utils import run_bass_kernel_spmd

F32 = mybir.dt.float32
BF16 = mybir.dt.bfloat16
AF = mybir.ActivationFunctionType
ALU = mybir.AluOpType

NCORES = 8
SEQ = 2048
E = 1024
NF = 8
TOKOFF = 2
NPAD = TOKOFF + SEQ + 2
LN_EPS = 1e-5
KB = 1024

XT_STRIDE = 8224
R_XT = 0
R_LNS = 65792
R_CON = 99072
R_WAR = 101632
R_BIG = 126208


class Prog:
    ENGINES = ("sync", "tensor", "vector", "scalar", "gpsimd")

    def __init__(self, nc):
        self.nc = nc
        self.streams = {e: [] for e in self.ENGINES}
        self.sems = {}
        self.counts = {}
        self._last_dma = {}
        self._es = contextlib.ExitStack()

    def sem(self, name):
        if name not in self.sems:
            self.sems[name] = self._es.enter_context(self.nc.semaphore(name))
            self.counts[name] = 0
        return self.sems[name]

    def emit(self, engine, fn):
        self.streams[engine].append(fn)

    def wait(self, engine, semname, value=None):
        self._last_dma[engine] = None
        if semname in self.DMA_SEMS:
            for e2 in list(self._last_dma):
                if self._last_dma.get(e2) == semname:
                    self._last_dma[e2] = None
        s = self.sem(semname)
        t = self.counts[semname] if value is None else value
        if t <= 0:
            return
        self.emit(engine, lambda eng, s=s, t=t: eng.wait_ge(s, t))

    DMA_SEMS = ("dma", "wdma", "st", "rt")

    def inc(self, engine, fn, semname, amount=1):
        s = self.sem(semname)
        if semname in self.DMA_SEMS:
            key = (engine, semname)
            if self._last_dma.get(engine) != semname:
                t = self.counts[semname]
                if t > 0:
                    self.emit(engine, lambda eng, s=s, t=t: eng.wait_ge(s, t))
            self._last_dma[engine] = semname
        else:
            self._last_dma[engine] = None
        self.counts[semname] += amount
        self.emit(engine, lambda eng, s=s, a=amount, f=fn: f(eng).then_inc(s, a))

    def run(self, fn, engine):
        self._last_dma[engine] = None
        self.emit(engine, fn)

    def mark(self, semname):
        self.sem(semname)
        return self.counts[semname]

    def barrier(self, engines=("tensor", "vector", "scalar", "gpsimd", "sync")):
        for e in ("pe", "dve", "act", "gp"):
            self.sem(e)
        for eng in engines:
            for s in ("pe", "dve", "act", "gp", "dma", "wdma", "st"):
                self.wait(eng, s)

    def build(self):
        with self.nc.Block() as block:
            for e in self.ENGINES:
                if self.streams[e]:
                    getattr(block, e)(lambda eng, ops=tuple(self.streams[e]): [f(eng) for f in ops])
        return self.nc


class SbufMap:
    BASE = 16512

    def __init__(self, nc, limit=212500):
        self.nc = nc
        self.limit = limit
        self.cursor = 0

    def alloc(self, name, shape, dtype, at=None):
        dsize = 2 if dtype == BF16 else 4
        nbytes = shape[1] * dsize
        if at is None:
            at = (self.cursor + 31) & ~31
            self.cursor = at + nbytes
        assert at + nbytes <= self.limit, f"SBUF overflow at {name}: {at + nbytes}"
        return self.nc.alloc_sbuf_tensor_at(name, list(shape), dtype, offset=self.BASE + at)

    def arena(self, nbytes):
        at = (self.cursor + 31) & ~31
        self.cursor = at + nbytes
        assert self.cursor <= self.limit, f"SBUF overflow (arena): {self.cursor}"
        return at


class Ctx:
    pass


def tokc(start, n):
    return slice(TOKOFF + start, TOKOFF + start + n)


def emit_ln(ctx, src_tiles, dst_tiles, gamma_off=None, beta_off=None,
            resid=False, src_padded=True, src_bf=False, src_offs=None):
    p = ctx.p
    PS = ctx.PS
    so = TOKOFF if src_padded else 0
    n = SEQ
    offs = src_offs if src_offs is not None else [so] * NF
    sq, misc = ctx.ln_sq, ctx.ln_misc
    ones_src = ctx.ones_col_bf if src_bf else ctx.ones_col

    p.barrier(("tensor", "scalar", "vector"))
    for f in range(NF):
        p.wait("scalar", "pe")
        p.inc("scalar", lambda e, f=f: e.activation(
            sq[:, 0:n], src_tiles[f][:, offs[f]:offs[f] + n], AF.Square), "act")
        am = p.mark("act")
        p.wait("tensor", "act", am)
        for cch in range(4):
            cs = slice(cch * 512, cch * 512 + 512)
            scs = slice(offs[f] + cch * 512, offs[f] + cch * 512 + 512)
            p.inc("tensor", lambda e, f=f, cs=cs, scs=scs: e.matmul(
                PS[0:1, cs], ones_src[:], src_tiles[f][:, scs],
                start=(f == 0), stop=(f == NF - 1)), "pe")
            p.inc("tensor", lambda e, f=f, cs=cs: e.matmul(
                PS[0:1, 2048 + cs.start:2048 + cs.stop], ctx.ones_col[:], sq[:, cs],
                start=(f == 0), stop=(f == NF - 1)), "pe")
    p.wait("vector", "pe")
    p.inc("vector", lambda e: e.tensor_scalar(
        misc[0:1, 0:n], PS[0:1, 0:n], 1.0 / E, None, ALU.mult), "dve")
    p.inc("vector", lambda e: e.tensor_scalar(
        misc[0:1, 2048:2048 + n], PS[0:1, 2048:2048 + n], 1.0 / E, None, ALU.mult), "dve")
    p.wait("vector", "dve")
    p.inc("vector", lambda e: e.tensor_tensor(
        misc[0:1, 4096:4096 + n], misc[0:1, 0:n], misc[0:1, 0:n], ALU.mult), "dve")
    p.wait("vector", "dve")
    p.inc("vector", lambda e: e.tensor_tensor(
        misc[0:1, 4096:4096 + n], misc[0:1, 2048:2048 + n], misc[0:1, 4096:4096 + n],
        ALU.subtract), "dve")
    p.wait("scalar", "dve")
    p.inc("scalar", lambda e: e.activation(
        misc[0:1, 2048:2048 + n], misc[0:1, 4096:4096 + n], AF.Sqrt, bias=ctx.eps_s[0:1, 0:1]), "act")
    p.wait("vector", "act")
    p.inc("vector", lambda e: e.reciprocal(
        misc[0:1, 2048:2048 + n], misc[0:1, 2048:2048 + n]), "dve")
    p.wait("vector", "dve")
    p.inc("vector", lambda e: e.tensor_tensor(
        misc[0:1, 4096:4096 + n], misc[0:1, 0:n], misc[0:1, 2048:2048 + n], ALU.mult), "dve")
    p.wait("vector", "dve")
    p.inc("vector", lambda e: e.tensor_scalar(
        misc[0:1, 4096:4096 + n], misc[0:1, 4096:4096 + n], -1.0, None, ALU.mult), "dve")
    p.wait("tensor", "dve")
    for cch in range(4):
        cs = slice(cch * 512, cch * 512 + 512)
        p.inc("tensor", lambda e, cs=cs: e.matmul(
            PS[:, cs], ctx.ones_row[0:1, :], misc[0:1, 2048 + cs.start:2048 + cs.stop],
            start=True, stop=True), "pe")
        p.inc("tensor", lambda e, cs=cs: e.matmul(
            PS[:, 2048 + cs.start:2048 + cs.stop], ctx.ones_row[0:1, :],
            misc[0:1, 4096 + cs.start:4096 + cs.stop], start=True, stop=True), "pe")
    p.wait("vector", "pe")
    for f in range(NF):
        p.wait("vector", "act")
        p.inc("vector", lambda e, f=f: e.tensor_tensor(
            sq[:, 0:n], src_tiles[f][:, offs[f]:offs[f] + n], PS[:, 0:n], ALU.mult), "dve")
        p.wait("vector", "dve")
        if gamma_off is None and not resid:
            p.inc("vector", lambda e, f=f: e.tensor_tensor(
                dst_tiles[f][:, tokc(0, n)], sq[:, 0:n], PS[:, 2048:2048 + n], ALU.add), "dve")
            p.wait("vector", "dve")
        else:
            p.inc("vector", lambda e, f=f: e.tensor_tensor(
                sq[:, 0:n], sq[:, 0:n], PS[:, 2048:2048 + n], ALU.add), "dve")
            p.wait("scalar", "dve")
            if gamma_off is not None:
                p.inc("scalar", lambda e, f=f: e.activation(
                    sq[:, 0:n], sq[:, 0:n], AF.Identity,
                    bias=ctx.cols[:, beta_off + f:beta_off + f + 1],
                    scale=ctx.cols[:, gamma_off + f:gamma_off + f + 1]), "act")
                p.wait("vector", "act")
            if resid:
                p.inc("vector", lambda e, f=f: e.tensor_tensor(
                    dst_tiles[f][:, tokc(0, n)], dst_tiles[f][:, tokc(0, n)],
                    sq[:, 0:n], ALU.add), "dve")
            else:
                p.inc("vector", lambda e, f=f: e.tensor_copy(
                    dst_tiles[f][:, tokc(0, n)], sq[:, 0:n]), "dve")
            p.wait("vector", "dve")
    p.barrier(("tensor", "scalar", "vector"))


def emit_colvec(ctx, dram_ap, n, dst_off, scale=None, dst=None):
    p = ctx.p
    if dst is None:
        dst = ctx.cols
    assert n <= 2048
    nt = (n + 127) // 128
    p.barrier(("sync", "tensor", "scalar"))
    p.inc("sync", lambda e: e.dma_start(ctx.row_tmp[0:1, 0:n], dram_ap), "dma", 16)
    p.wait("tensor", "dma")
    for i in range(nt):
        w = min(128, n - i * 128)
        p.inc("tensor", lambda e, i=i, w=w: e.transpose(
            ctx.PS[0:w, 3584 + i:3584 + i + 1], ctx.row_tmp[0:1, i * 128:i * 128 + w],
            ctx.ident_f32[0:1, 0:1]), "pe")
    p.wait("scalar", "pe")
    if scale is None:
        p.inc("scalar", lambda e: e.copy(
            dst[:, dst_off:dst_off + nt], ctx.PS[:, 3584:3584 + nt]), "act")
    else:
        p.inc("scalar", lambda e: e.activation(
            dst[:, dst_off:dst_off + nt], ctx.PS[:, 3584:3584 + nt], AF.Identity,
            bias=0.0, scale=scale), "act")
    p.wait("tensor", "act")


def emit_bcast_row(ctx, src_row_ap, dst_tile_ap, n, nparts=128):
    p = ctx.p
    p.wait("tensor", "act")
    p.wait("tensor", "dve")
    nch = (n + 511) // 512
    for cch in range(nch):
        w = min(512, n - cch * 512)
        p.inc("tensor", lambda e, cch=cch, w=w: e.matmul(
            ctx.PS[0:nparts, 2048 + cch * 512:2048 + cch * 512 + w],
            ctx.ones_row[0:1, 0:nparts],
            src_row_ap[0:1, cch * 512:cch * 512 + w], start=True, stop=True), "pe")
    p.wait("scalar", "pe")
    p.inc("scalar", lambda e: e.copy(dst_tile_ap, ctx.PS[0:nparts, 2048:2048 + n]), "act")
    p.wait("tensor", "act")
    p.wait("sync", "pe")
    p.wait("gpsimd", "pe")



def stream_weights(p, n, loadfn, consumefn, sembase, depth=2):
    semtot, pem = {}, {}

    def issue(c):
        sem = f"{sembase}{c % depth}"
        fns = loadfn(c)
        if not isinstance(fns, (list, tuple)):
            fns = [fns]
        for fn in fns:
            p.inc("gpsimd", fn, sem, 16)
        semtot[c] = p.mark(sem)

    p.wait("gpsimd", "pe")
    for c in range(min(depth, n)):
        issue(c)
    for c in range(n):
        p.wait("tensor", f"{sembase}{c % depth}", semtot[c])
        consumefn(c)
        pem[c] = p.mark("pe")
        if c + depth < n:
            p.wait("gpsimd", "pe", pem[c])
            issue(c + depth)


def emit_l0_attn(ctx, batch):
    p, m = ctx.p, ctx.m
    PS = ctx.PS
    H = 16
    B0 = ctx.BIG

    xnT = [m.alloc(f"b1_xnT{f}", [128, NPAD], BF16, at=B0 + f * 4128) for f in range(NF)]
    vbuf = [m.alloc(f"b1_v{t}", [128, 16 * 65], BF16, at=R_LNS + t * 2080) for t in range(16)]
    QOFF = B0 + 8 * 4128
    qt = m.alloc("b1_qt", [128, 2048], BF16, at=QOFF)
    kt = m.alloc("b1_kt", [128, 2048], BF16, at=QOFF + 4096)
    expb = m.alloc("b1_exp", [128, 2048], BF16, at=QOFF + 8192)
    rb = m.alloc("b1_rb", [64, 1024], F32, at=QOFF + 8192)
    tmb = m.alloc("b1_tmb", [64, 1024], F32, at=QOFF + 12288)
    rows2 = m.alloc("b1_rows2", [128, 1024], F32, at=QOFF + 16384)
    bv_bcast = m.alloc("b1_bvb", [128, 1024], F32, at=QOFF + 12288)
    OOFF = QOFF + 20480
    ocatT = [m.alloc(f"b1_ocat{g}", [128, 2048], BF16, at=OOFF + g * 4096) for g in range(NF)]
    ybuf = [m.alloc(f"b1_y{f}", [128, 2048], BF16, at=B0 + f * 4128) for f in range(NF)]
    ctx.bv_bcast = bv_bcast

    W0 = ctx.WARENA
    wv_sb = [m.alloc(f"b1_wv{f}", [128, 1024], BF16, at=W0 + f * 2048) for f in range(NF)]
    wq_sb = m.alloc("b1_wq", [128, 1024], BF16, at=W0 + 16 * KB)
    wk_sb = m.alloc("b1_wk", [128, 1024], BF16, at=W0 + 18 * KB)
    wo_sb = [m.alloc(f"b1_wo{t}", [128, 1024], BF16, at=W0 + t * 2048) for t in range(NF)]

    emit_ln(ctx, ctx.xT, xnT)
    if "xn" in ctx.taps and batch == 0:
        _tap_tiles(ctx, xnT, "xn", padded=True)
    if "xnb1" in ctx.taps and batch == 1:
        _tap_tiles(ctx, xnT, "xnb1", padded=True)

    emit_colvec(ctx, ctx.d["L0_attn_bo"].ap().rearrange("(a e) -> a e", a=1), E, 0)
    emit_colvec(ctx, ctx.d["L0_attn_gamma"].ap().rearrange("(a e) -> a e", a=1), E, 8, scale=0.2)
    emit_colvec(ctx, ctx.d["L0_attn_beta"].ap().rearrange("(a e) -> a e", a=1), E, 16, scale=0.2)
    p.inc("sync", lambda e: e.dma_start(ctx.lam_s[0:1, 0:1], ctx.d["lam11"].ap()), "dma", 16)
    p.inc("sync", lambda e: e.dma_start(ctx.bq_col[:, 0:H], ctx.d["bq_t"].ap()), "dma", 16)
    p.inc("sync", lambda e: e.dma_start(ctx.bk_col[:, 0:H], ctx.d["bk_t"].ap()), "dma", 16)
    p.wait("sync", "pe")
    p.inc("sync", lambda e: e.dma_start(ctx.row_tmp[0:1, 0:E],
                                        ctx.d["L0_attn_bv"].ap().rearrange("h d -> (h d)").rearrange("(a e) -> a e", a=1)), "dma", 16)
    p.wait("tensor", "dma")
    p.wait("vector", "dma")
    emit_bcast_row(ctx, ctx.row_tmp, ctx.bv_bcast[:, 0:E], E)

    p.wait("gpsimd", "pe")
    for f in range(NF):
        p.inc("gpsimd", lambda e, f=f: e.dma_start(
            wv_sb[f].ap().rearrange("p (h d) -> p h d", d=64),
            ctx.d["L0_attn_Wv"].ap().rearrange("h (ft pp) d -> ft pp h d", pp=128)[f]),
            "wdma", 16)
    for t in range(15):
        p.run(lambda e, t=t: e.memset(vbuf[t][:], 1.0), "gpsimd")
    p.inc("gpsimd", lambda e: e.memset(vbuf[15][:], 1.0), "gp", 1)
    p.wait("tensor", "wdma")
    p.wait("tensor", "gp")
    p.wait("tensor", "act")
    for t in range(16):
        pso = 2048 + (t % 2) * 1024
        p.wait("tensor", "dve")
        for cch in range(2):
            cs = slice(cch * 512, cch * 512 + 512)
            for f in range(NF):
                p.inc("tensor", lambda e, t=t, cs=cs, f=f, pso=pso: e.matmul(
                    PS[:, pso + cs.start:pso + cs.stop],
                    xnT[f][:, TOKOFF + t * 128:TOKOFF + (t + 1) * 128],
                    wv_sb[f][:, cs], start=(f == 0), stop=(f == NF - 1)), "pe")
        p.wait("vector", "pe")
        p.inc("vector", lambda e, t=t, pso=pso: e.tensor_tensor(
            vbuf[t].ap().rearrange("p (h c) -> p h c", c=65)[:, :, 0:64],
            PS[:, pso:pso + E].rearrange("p (h c) -> p h c", c=64),
            ctx.bv_bcast[:, 0:E].rearrange("p (h c) -> p h c", c=64), ALU.add), "dve")
    p.barrier()

    for h in range(H):
        p.wait("gpsimd", "pe")
        p.inc("gpsimd", lambda e, h=h: e.dma_start(
            wq_sb.ap().rearrange("p (ft c) -> p ft c", c=128),
            ctx.d["L0_attn_Wq"].ap()[h].rearrange("(ft pp) c -> pp ft c", pp=128)),
            "wdma", 16)
        p.inc("gpsimd", lambda e, h=h: e.dma_start(
            wk_sb.ap().rearrange("p (ft c) -> p ft c", c=128),
            ctx.d["L0_attn_Wk"].ap()[h].rearrange("(ft pp) c -> pp ft c", pp=128)),
            "wdma", 16)
        p.wait("tensor", "wdma")
        for cch in range(4):
            cs = slice(cch * 512, cch * 512 + 512)
            p.wait("tensor", "act")
            for f in range(NF):
                p.inc("tensor", lambda e, cs=cs, f=f: e.matmul(
                    PS[:, 2048:2560], wq_sb[:, f * 128:(f + 1) * 128],
                    xnT[f][:, TOKOFF + cs.start:TOKOFF + cs.stop],
                    start=(f == 0), stop=(f == NF - 1)), "pe")
                p.inc("tensor", lambda e, cs=cs, f=f: e.matmul(
                    PS[:, 2560:3072], wk_sb[:, f * 128:(f + 1) * 128],
                    xnT[f][:, TOKOFF + cs.start:TOKOFF + cs.stop],
                    start=(f == 0), stop=(f == NF - 1)), "pe")
            p.wait("scalar", "pe")
            p.inc("scalar", lambda e, h=h, cs=cs: e.activation(
                qt[:, cs], PS[:, 2048:2560], AF.Identity, bias=ctx.bq_col[:, h:h + 1]), "act")
            p.inc("scalar", lambda e, h=h, cs=cs: e.activation(
                kt[:, cs], PS[:, 2560:3072], AF.Identity, bias=ctx.bk_col[:, h:h + 1]), "act")
        p.wait("tensor", "act")

        for qc in range(2):
            qs0 = qc * 1024
            for br in range(2):
                oo = br * 1024
                dp = slice(0, 64) if br == 0 else slice(64, 128)
                expmarks = {}

                def emit_scores(kti, dp=dp, qs0=qs0):
                    so_ = 2048 + (kti % 2) * 1024
                    if kti - 2 in expmarks:
                        p.wait("tensor", "act", expmarks[kti - 2])
                    elif kti < 2:
                        p.wait("tensor", "act", expmarks.get("pre", 0))
                    for sc in range(2):
                        cs = slice(sc * 512, sc * 512 + 512)
                        p.inc("tensor", lambda e, kti=kti, cs=cs, qs0=qs0, so_=so_, dp=dp: e.matmul(
                            PS[:, so_ + cs.start:so_ + cs.stop],
                            kt[dp, kti * 128:(kti + 1) * 128],
                            qt[dp, qs0 + cs.start:qs0 + cs.stop],
                            start=True, stop=True), "pe")

                def emit_exp(kti):
                    so_ = 2048 + (kti % 2) * 1024
                    eo = (kti % 2) * 1024
                    p.wait("scalar", "pe")
                    p.inc("scalar", lambda e, so_=so_, eo=eo: e.activation(
                        expb[:, eo:eo + 1024], PS[:, so_:so_ + 1024], AF.Exp,
                        scale=0.125), "act")
                    expmarks[kti] = p.mark("act")

                def emit_pv(kti, h=h, oo=oo):
                    eo = (kti % 2) * 1024
                    p.wait("tensor", "act")
                    for sc in range(2):
                        cs = slice(sc * 512, sc * 512 + 512)
                        p.inc("tensor", lambda e, h=h, kti=kti, cs=cs, oo=oo, eo=eo: e.matmul(
                            PS[0:65, oo + cs.start:oo + cs.stop],
                            vbuf[kti][:, h * 65:(h + 1) * 65],
                            expb[:, eo + cs.start:eo + cs.stop],
                            start=(kti == 0), stop=(kti == 15)), "pe")

                expmarks["pre"] = p.mark("act")
                emit_scores(0)
                for kti in range(16):
                    emit_exp(kti)
                    if kti < 15:
                        emit_scores(kti + 1)
                    emit_pv(kti)
            p.wait("vector", "pe")
            g, rbase = h // 2, (h % 2) * 64
            for sc in range(2):
                cs = slice(sc * 512, sc * 512 + 512)
                p.wait("scalar", "pe")
                p.inc("scalar", lambda e, cs=cs: e.activation(
                    rows2[0:1, 0:512], PS[64:65, cs], AF.Ln), "act")
                p.inc("scalar", lambda e, cs=cs: e.activation(
                    rows2[0:1, 512:1024], PS[64:65, 1024 + cs.start:1024 + cs.stop],
                    AF.Ln), "act")
                p.wait("scalar", "act")
                p.inc("scalar", lambda e: e.activation(
                    rows2[0:1, 0:512], rows2[0:1, 0:512], AF.Exp, scale=-1.0), "act")
                p.inc("scalar", lambda e: e.activation(
                    rows2[0:1, 512:1024], rows2[0:1, 512:1024], AF.Exp, scale=-1.0), "act")
                p.wait("vector", "act")
                p.inc("vector", lambda e: e.tensor_scalar(
                    rows2[0:1, 512:1024], rows2[0:1, 512:1024], ctx.lam_s[0:1, 0:1], -1.0,
                    ALU.mult, ALU.mult), "dve")
                p.wait("tensor", "dve")
                p.wait("tensor", "act")
                p.inc("tensor", lambda e: e.matmul(
                    PS[0:64, 2048:2560], ctx.ones_row[0:1, 0:64],
                    rows2[0:1, 0:512], start=True, stop=True), "pe")
                p.inc("tensor", lambda e: e.matmul(
                    PS[0:64, 2560:3072], ctx.ones_row[0:1, 0:64],
                    rows2[0:1, 512:1024], start=True, stop=True), "pe")
                p.wait("scalar", "pe")
                p.inc("scalar", lambda e: e.copy(rb[0:64, 0:1024], PS[0:64, 2048:3072]), "act")
                p.wait("vector", "act")
                p.inc("vector", lambda e, cs=cs: e.tensor_tensor(
                    tmb[0:64, 0:512], PS[0:64, cs], rb[0:64, 0:512], ALU.mult), "dve")
                p.wait("vector", "dve")
                p.inc("vector", lambda e, cs=cs: e.tensor_tensor(
                    tmb[0:64, 512:1024], PS[0:64, 1024 + cs.start:1024 + cs.stop],
                    rb[0:64, 512:1024], ALU.mult), "dve")
                p.wait("vector", "dve")
                p.inc("vector", lambda e, cs=cs, g=g, rbase=rbase, qs0=qs0: e.tensor_tensor(
                    ocatT[g][rbase:rbase + 64, qs0 + cs.start:qs0 + cs.stop],
                    tmb[0:64, 0:512], tmb[0:64, 512:1024], ALU.add), "dve")
                p.wait("vector", "dve")
                p.wait("tensor", "dve")

    p.barrier()
    p.wait("gpsimd", "pe")
    p.wait("gpsimd", "dve")
    p.wait("gpsimd", "act")
    for it in range(NF):
        p.inc("gpsimd", lambda e, it=it: e.dma_start(
            wo_sb[it][:], ctx.d["L0_attn_Wo"].ap().rearrange(
                "(it2 pp) e -> it2 pp e", pp=128)[it]), "wdma", 16)
    p.wait("tensor", "wdma")
    for fo in range(NF):
        for cch in range(4):
            cs = slice(cch * 512, cch * 512 + 512)
            pso = (cch % 4) * 512
            p.wait("tensor", "act")
            for it in range(NF):
                p.inc("tensor", lambda e, fo=fo, cs=cs, it=it, pso=pso: e.matmul(
                    PS[:, pso:pso + 512], wo_sb[it][:, fo * 128:(fo + 1) * 128],
                    ocatT[it][:, cs], start=(it == 0), stop=(it == NF - 1)), "pe")
            p.wait("scalar", "pe")
            p.inc("scalar", lambda e, fo=fo, cs=cs, pso=pso: e.activation(
                ybuf[fo][:, cs], PS[:, pso:pso + 512], AF.Identity,
                bias=ctx.cols[:, fo:fo + 1]), "act")
    p.wait("tensor", "act")
    p.wait("vector", "act")
    if "y" in ctx.taps and batch == 0:
        _tap_tiles(ctx, ybuf, "y", padded=False)
    if "yb1" in ctx.taps and batch == 1:
        _tap_tiles(ctx, ybuf, "yb1", padded=False)

    emit_ln(ctx, ybuf, ctx.xT, gamma_off=8, beta_off=16, resid=True,
            src_padded=False, src_bf=True)



def emit_l1_attn(ctx, batch):
    p, m = ctx.p, ctx.m
    PS = ctx.PS
    H = 16
    B0 = ctx.BIG
    NT = 16

    xnT = [m.alloc(f"b3_xnT{f}", [128, NPAD], BF16, at=B0 + f * 4128) for f in range(NF)]
    scb = m.alloc("b3_scb", [128, 2048], F32, at=B0 + 33024)
    pb = m.alloc("b3_pb", [128, 2048], F32, at=B0 + 41216)
    qf = m.alloc("b3_qf", [128, 192], F32, at=B0 + 49408)
    rvb = m.alloc("b3_rvb", [128, 512], F32, at=B0 + 50176)
    ptb = m.alloc("b3_ptb", [128, 256], BF16, at=B0 + 52224)
    sums = m.alloc("b3_sums", [128, 8], F32, at=B0 + 52736)
    OOFF = B0 + 52768
    ocatT = [m.alloc(f"b3_ocat{g}", [128, 2048], BF16, at=OOFF + g * 4096) for g in range(NF)]

    W0 = ctx.WARENA
    wqkv_sb = [m.alloc(f"b3_wq{f}", [128, 192], BF16, at=W0 + f * 384) for f in range(NF)]
    brow = m.alloc("b3_brow", [1, 192], F32, at=W0 + 3072)
    bb = m.alloc("b3_bb", [128, 192], F32, at=W0 + 3872)
    qaugT = m.alloc("b3_qaug", [128, 2048], BF16, at=W0 + 4672)
    kaugT = m.alloc("b3_kaug", [128, 2048], BF16, at=W0 + 8768)
    vh = m.alloc("b3_vh", [128, 16 * 65], BF16, at=W0 + 12864)
    wout_sb = [m.alloc(f"b3_wo{t}", [128, 1024], BF16, at=W0 + t * 2048) for t in range(NF)]

    emit_ln(ctx, ctx.xT, xnT)
    if "xn3" in ctx.taps and batch == 1:
        _tap_tiles(ctx, xnT, "xn3", padded=True)
    emit_colvec(ctx, ctx.d["L1_attn_b_out"].ap().rearrange("(a e) -> a e", a=1), E, 0)

    for h in range(H):
        p.wait("gpsimd", "pe")
        p.wait("gpsimd", "dve")
        for f in range(NF):
            p.inc("gpsimd", lambda e, f=f, h=h: e.dma_start(
                wqkv_sb[f].ap().rearrange("p (c d) -> p c d", d=64),
                ctx.d["L1_attn_w_qkv"].ap().rearrange("(ft pp) c -> ft pp c", pp=128)[f]
                .rearrange("p (c e) -> p c e", e=1024)[:, :, h * 64:(h + 1) * 64]), "wdma", 16)
        p.wait("sync", "pe")
        p.inc("sync", lambda e, h=h: e.dma_start(
            brow.ap().rearrange("a (c d) -> a c d", d=64),
            ctx.d["L1_attn_b_qkv"].ap().rearrange("(a c e) -> a c e", a=1, e=1024)
            [:, :, h * 64:(h + 1) * 64]), "dma", 16)
        p.wait("tensor", "dma")
        p.wait("tensor", "act")
        p.inc("tensor", lambda e: e.matmul(
            PS[:, 512:704], ctx.ones_row[0:1, :], brow[0:1, 0:192],
            start=True, stop=True), "pe")
        p.wait("scalar", "pe")
        p.inc("scalar", lambda e: e.copy(bb[:], PS[:, 512:704]), "act")
        if h == 0:
            p.run(lambda e: e.memset(sums[:, 1:2], 1.0), "gpsimd")
            p.inc("gpsimd", lambda e: e.memset(vh[:], 1.0), "gp", 1)
        p.wait("tensor", "gp")
        p.wait("tensor", "wdma")
        p.wait("tensor", "act")

        for t in range(NT):
            tc = slice(t * 128, (t + 1) * 128)
            p.wait("tensor", "dve")
            p.wait("tensor", "act")
            for f in range(NF):
                p.inc("tensor", lambda e, t=t, f=f: e.matmul(
                    PS[:, 0:192], xnT[f][:, TOKOFF + t * 128:TOKOFF + (t + 1) * 128],
                    wqkv_sb[f][:], start=(f == 0), stop=(f == NF - 1)), "pe")
            p.wait("vector", "pe")
            p.inc("vector", lambda e: e.tensor_tensor(
                qf[:, 0:192], PS[:, 0:192], bb[:, 0:192], ALU.add), "dve")
            p.wait("scalar", "dve")
            p.inc("scalar", lambda e, t=t: e.copy(
                vh[:, t * 65:t * 65 + 64], qf[:, 128:192]), "act")
            p.inc("scalar", lambda e: e.activation(
                ctx.ln_misc[:, 0:64], qf[:, 0:64], AF.Square, accum_out=sums[:, 0:1]), "act")
            p.inc("scalar", lambda e: e.activation(
                ctx.ln_misc[:, 64:128], qf[:, 64:128], AF.Square, accum_out=sums[:, 2:3]), "act")
            p.wait("tensor", "act")
            p.inc("tensor", lambda e: e.transpose(
                PS[0:64, 1024:1152], qf[:, 0:64], ctx.ident_f32[:]), "pe")
            p.inc("tensor", lambda e: e.transpose(
                PS[0:64, 1152:1280], qf[:, 64:128], ctx.ident_f32[:]), "pe")
            p.inc("tensor", lambda e: e.transpose(
                PS[0:2, 1280:1408], sums[:, 0:2], ctx.ident_f32[:]), "pe")
            p.inc("tensor", lambda e: e.transpose(
                PS[0:2, 1408:1536], sums[:, 1:3], ctx.ident_f32[:]), "pe")
            p.wait("scalar", "pe")
            p.inc("scalar", lambda e, tc=tc: e.copy(qaugT[0:64, tc], PS[0:64, 1024:1152]), "act")
            p.inc("scalar", lambda e, tc=tc: e.activation(
                kaugT[0:64, tc], PS[0:64, 1152:1280], AF.Identity, bias=0.0, scale=-2.0), "act")
            p.inc("scalar", lambda e, tc=tc: e.copy(qaugT[64:66, tc], PS[0:2, 1280:1408]), "act")
            p.inc("scalar", lambda e, tc=tc: e.activation(
                kaugT[64:66, tc], PS[0:2, 1408:1536], AF.Identity,
                bias=ctx.epsc[0:2, 0:1]), "act")

        g, rbase = h // 2, (h % 2) * 64
        for tq in range(NT):
            nv = (tq + 1) * 128
            p.wait("tensor", "act")
            p.wait("tensor", "dve")
            nch = (nv + 511) // 512
            for c in range(nch):
                w_ = min(512, nv - c * 512)
                cs = slice(c * 512, c * 512 + w_)
                pd = 2048 + (c % 2) * 1024
                pv_ = 2560 + (c % 2) * 1024
                p.wait("tensor", "act")
                p.wait("tensor", "dve")
                p.inc("tensor", lambda e, tq=tq, cs=cs, pd=pd: e.matmul(
                    PS[:, pd:pd + (cs.stop - cs.start)],
                    qaugT[0:64, tq * 128:(tq + 1) * 128], kaugT[0:64, cs],
                    start=True, stop=True), "pe")
                p.inc("tensor", lambda e, tq=tq, cs=cs, pv_=pv_: e.matmul(
                    PS[:, pv_:pv_ + (cs.stop - cs.start)],
                    qaugT[0:66, tq * 128:(tq + 1) * 128], kaugT[0:66, cs],
                    start=True, stop=True), "pe")
                p.wait("scalar", "pe")
                p.wait("vector", "pe")
                p.inc("scalar", lambda e, cs=cs, pd=pd: e.activation(
                    scb[:, cs], PS[:, pd:pd + (cs.stop - cs.start)], AF.Square,
                    scale=0.5), "act")
                p.inc("vector", lambda e, cs=cs, pv_=pv_: e.reciprocal(
                    rvb[:, 0:cs.stop - cs.start], PS[:, pv_:pv_ + (cs.stop - cs.start)]), "dve")
                p.wait("vector", "act")
                p.wait("vector", "dve")
                p.inc("vector", lambda e, cs=cs: e.tensor_tensor(
                    scb[:, cs], scb[:, cs], rvb[:, 0:cs.stop - cs.start], ALU.mult), "dve")
            p.wait("vector", "dve")
            p.inc("vector", lambda e, tq=tq: e.tensor_tensor(
                scb[:, tq * 128:(tq + 1) * 128], scb[:, tq * 128:(tq + 1) * 128],
                ctx.maskc[:, 0:128], ALU.add), "dve")
            p.wait("vector", "dve")
            p.inc("vector", lambda e, nv=nv: e.reduce_max(
                sums[:, 3:4], scb[:, 0:nv], axis=mybir.AxisListType.X, negate=True), "dve")
            p.wait("scalar", "dve")
            p.inc("scalar", lambda e, nv=nv: e.activation(
                pb[:, 0:nv], scb[:, 0:nv], AF.Exp, bias=sums[:, 3:4],
                accum_out=sums[:, 4:5]), "act")
            p.wait("vector", "act")
            p.inc("vector", lambda e: e.reciprocal(sums[:, 5:6], sums[:, 4:5]), "dve")
            p.wait("vector", "dve")
            p.inc("vector", lambda e, nv=nv: e.tensor_scalar(
                pb[:, 0:nv], pb[:, 0:nv], sums[:, 5:6], None, ALU.mult), "dve")
            p.wait("tensor", "dve")
            trm = {}
            evk = {}

            def emit_tr(tk):
                po_ = 2048 + (tk % 2) * 1024
                if tk - 2 in evk:
                    p.wait("tensor", "act", evk[tk - 2])
                p.inc("tensor", lambda e, tk=tk, po_=po_: e.transpose(
                    PS[0:128, po_:po_ + 128], pb[:, tk * 128:(tk + 1) * 128],
                    ctx.ident_f32[:]), "pe")
                trm[tk] = p.mark("pe")

            def emit_ev(tk):
                po_ = 2048 + (tk % 2) * 1024
                eo_ = (tk % 2) * 128
                p.wait("scalar", "pe", trm[tk])
                p.inc("scalar", lambda e, po_=po_, eo_=eo_: e.copy(
                    ptb[:, eo_:eo_ + 128], PS[0:128, po_:po_ + 128]), "act")
                evk[tk] = p.mark("act")

            def emit_pv2(tk, tq=tq):
                eo_ = (tk % 2) * 128
                p.wait("tensor", "act", evk[tk])
                p.inc("tensor", lambda e, tk=tk, tq=tq, eo_=eo_: e.matmul(
                    PS[0:65, 3584:3712], vh[:, tk * 65:(tk + 1) * 65],
                    ptb[:, eo_:eo_ + 128], start=(tk == 0), stop=(tk == tq)), "pe")

            emit_tr(0)
            emit_ev(0)
            for tk in range(tq + 1):
                if tk + 1 <= tq:
                    emit_tr(tk + 1)
                emit_pv2(tk)
                if tk + 1 <= tq:
                    emit_ev(tk + 1)
            p.wait("scalar", "pe")
            p.inc("scalar", lambda e, g=g, rbase=rbase, tq=tq: e.copy(
                ocatT[g][rbase:rbase + 64, tq * 128:(tq + 1) * 128],
                PS[0:64, 3584:3712]), "act")
            p.wait("tensor", "act")

    if "oc3" in ctx.taps and batch == 1:
        _tap_tiles(ctx, ocatT, "oc3", padded=False)
    p.wait("gpsimd", "pe")
    p.wait("gpsimd", "act")
    p.wait("gpsimd", "dve")
    for it in range(NF):
        p.inc("gpsimd", lambda e, it=it: e.dma_start(
            wout_sb[it][:], ctx.d["L1_attn_w_out"].ap().rearrange(
                "(it2 pp) e -> it2 pp e", pp=128)[it]), "wdma", 16)
    p.wait("tensor", "wdma")
    for fo in range(NF):
        for cch in range(4):
            cs = slice(cch * 512, cch * 512 + 512)
            pso = (cch % 4) * 512
            p.wait("tensor", "dve")
            for it in range(NF):
                p.inc("tensor", lambda e, fo=fo, cs=cs, it=it, pso=pso: e.matmul(
                    PS[:, pso:pso + 512], wout_sb[it][:, fo * 128:(fo + 1) * 128],
                    ocatT[it][:, cs], start=(it == 0), stop=(it == NF - 1)), "pe")
            p.wait("scalar", "pe")
            p.inc("scalar", lambda e, fo=fo, pso=pso: e.activation(
                ctx.ln_sq[:, 0:512], PS[:, pso:pso + 512], AF.Identity,
                bias=ctx.cols[:, fo:fo + 1]), "act")
            p.wait("vector", "act")
            p.inc("vector", lambda e, fo=fo, cs=cs: e.tensor_tensor(
                ctx.xT[fo][:, TOKOFF + cs.start:TOKOFF + cs.stop],
                ctx.xT[fo][:, TOKOFF + cs.start:TOKOFF + cs.stop],
                ctx.ln_sq[:, 0:512], ALU.add), "dve")
            p.wait("scalar", "dve")
    p.barrier(("tensor", "scalar", "vector", "sync", "gpsimd"))



def emit_l1_ffn(ctx, batch):
    p, m = ctx.p, ctx.m
    PS = ctx.PS
    B0 = ctx.BIG
    TCW = 256
    NTC = SEQ // TCW

    xnb = [m.alloc(f"b4_xn{f}", [128, NPAD], F32, at=B0 + f * XT_STRIDE) for f in range(NF)]
    xnn = [m.alloc(f"b4_xnn{f}", [128, NPAD], BF16, at=B0 + f * XT_STRIDE) for f in range(NF)]
    frag = [B0 + f * XT_STRIDE + 4128 for f in range(NF)]
    hbuf = [m.alloc(f"b4_hb{c}", [128, TCW], BF16, at=frag[c // 8] + (c % 8) * 512)
            for c in range(32)]
    acc = [m.alloc(f"b4_acc{fo}", [128, TCW], F32, at=frag[4 + fo // 4] + (fo % 4) * 1024)
           for fo in range(NF)]
    macb = [m.alloc(f"b4_mac{fo}", [128, TCW], BF16, at=frag[6] + fo * 512) for fo in range(NF)]
    abelT = [m.alloc(f"b4_ab{fo}", [128, TCW], BF16, at=frag[7] + fo * 512) for fo in range(NF)]
    TAIL = B0 + 8 * XT_STRIDE
    wsb = m.alloc("b4_wsb", [6, 2048], F32, at=TAIL)
    nonabT = [m.alloc(f"b4_nb{fo}", [128, TCW], BF16, at=TAIL + 8192 + fo * 512)
              for fo in range(NF)]
    gbuf = m.alloc("b4_gb", [128, TCW], F32, at=TAIL + 12288)
    webc = m.alloc("b4_web", [128, TCW], F32, at=TAIL + 13312)
    t1 = m.alloc("b4_t1", [128, TCW], F32, at=TAIL + 14336)
    t2 = m.alloc("b4_t2", [128, TCW], F32, at=TAIL + 15360)
    glog = m.alloc("b4_glog", [128, 16], F32, at=TAIL + 16384)
    grow = m.alloc("b4_grow", [1, 256], F32, at=TAIL + 16448)
    cols2 = m.alloc("b4_cols2", [128, 144], F32, at=TAIL + 17472)
    gcb = m.alloc("b4_gcb", [128, 16], BF16, at=TAIL + 18048)
    W0 = ctx.WARENA
    sel6 = m.alloc("b4_sel6", [6, 768], F32, at=W0 + 8320)
    gbb = m.alloc("b4_gbb", [128, 6], F32, at=W0 + 11424)
    grow6 = m.alloc("b4_grow6", [1, 8], F32, at=W0 + 11456)
    w1c = [m.alloc(f"b4_w1c{i}", [128, 1024], BF16, at=W0 + i * 2048) for i in range(2)]
    w2c = [m.alloc(f"b4_w2c{i}", [128, 1024], BF16, at=W0 + 4096 + i * 2048) for i in range(2)]
    gatewc = m.alloc("b4_gw", [128, 48], BF16, at=W0 + 8192)

    p.barrier(("gpsimd",))
    for f in range(NF - 1):
        p.run(lambda e, f=f: e.memset(xnb[f][:, 0:TOKOFF], 0.0), "gpsimd")
    p.inc("gpsimd", lambda e: e.memset(xnb[NF - 1][:, 0:TOKOFF], 0.0), "gp", 1)
    emit_ln(ctx, ctx.xT, xnb)
    for f in range(NF):
        p.wait("vector", "act")
        p.inc("vector", lambda e, f=f: e.tensor_tensor(
            ctx.xT[f][:, tokc(0, SEQ)], ctx.xT[f][:, tokc(0, SEQ)],
            xnb[f][:, tokc(0, SEQ)], ALU.add), "dve")
    p.barrier(("tensor", "scalar", "vector"))

    emit_colvec(ctx, ctx.d["L1_ffn_ln_g"].ap().rearrange("(a e) -> a e", a=1), E, 0)
    emit_colvec(ctx, ctx.d["L1_ffn_ln_b"].ap().rearrange("(a e) -> a e", a=1), E, 8)
    emit_colvec(ctx, ctx.d["L1_ffn_ab1"].ap()[0:2048].rearrange("(a e) -> a e", a=1), 2048, 16)
    emit_colvec(ctx, ctx.d["L1_ffn_ab1"].ap()[2048:4096].rearrange("(a e) -> a e", a=1), 2048, 32)
    emit_colvec(ctx, ctx.d["L1_ffn_ab2"].ap().rearrange("(a e) -> a e", a=1), E, 48)
    emit_colvec(ctx, ctx.d["L1_ffn_gc_w"].ap().rearrange("e a -> a e"), 2048, 56)
    for ex in range(6):
        emit_colvec(ctx, ctx.d["L1_ffn_eb1"].ap()[ex].rearrange("(a e) -> a e", a=1),
                    2048, ex * 16, dst=cols2)
        emit_colvec(ctx, ctx.d["L1_ffn_eb2"].ap()[ex].rearrange("(a e) -> a e", a=1),
                    1024, 96 + ex * 8, dst=cols2)
    p.wait("vector", "act")
    p.inc("vector", lambda e: e.tensor_copy(gcb[:, 0:16], ctx.cols[:, 56:72]), "dve")
    p.inc("sync", lambda e: e.dma_start(sel6[:], ctx.cd["sel6"].ap()), "dma", 16)
    p.inc("sync", lambda e: e.dma_start(
        grow6[0:1, 0:6], ctx.d["L1_ffn_gate_b"].ap().rearrange("(a e) -> a e", a=1)), "dma", 16)
    p.inc("sync", lambda e: e.dma_start(
        ctx.lam_s[0:1, 0:1], ctx.d["L1_ffn_gc_b"].ap().rearrange("(a e) -> a e", a=1)), "dma", 16)
    p.wait("tensor", "dma")
    emit_bcast_row(ctx, grow6, gbb[:, 0:6], 6)
    p.wait("gpsimd", "pe")
    for f in range(NF):
        p.inc("gpsimd", lambda e, f=f: e.dma_start(
            gatewc[:, f * 6:(f + 1) * 6], ctx.d["L1_ffn_gate_w"].ap()
            [f * 128:(f + 1) * 128, :]), "wdma", 16)

    emit_ln(ctx, xnb, xnn, gamma_off=0, beta_off=8)

    p.wait("tensor", "wdma")
    for t in range(16):
        p.wait("tensor", "act")
        p.wait("tensor", "dve")
        for f in range(NF):
            p.inc("tensor", lambda e, t=t, f=f: e.matmul(
                PS[0:128, 0:6], xnn[f][:, TOKOFF + t * 128:TOKOFF + (t + 1) * 128],
                gatewc[:, f * 6:(f + 1) * 6], start=(f == 0), stop=(f == NF - 1)), "pe")
        p.wait("vector", "pe")
        p.inc("vector", lambda e: e.tensor_tensor(
            glog[:, 0:6], PS[0:128, 0:6], gbb[:, 0:6], ALU.add), "dve")
        p.wait("vector", "dve")
        p.inc("vector", lambda e: e.reduce_max(
            glog[:, 8:9], glog[:, 0:6], axis=mybir.AxisListType.X, negate=True), "dve")
        p.wait("vector", "dve")
        p.inc("vector", lambda e: e.tensor_scalar(
            glog[:, 9:10], glog[:, 8:9], -1.0, None, ALU.mult), "dve")
        p.wait("vector", "dve")
        p.inc("vector", lambda e: e.tensor_scalar(
            glog[:, 10:16][:, 0:6], glog[:, 0:6], glog[:, 9:10], None, ALU.is_ge), "dve")
        p.wait("vector", "dve")
        p.inc("vector", lambda e: e.scalar_tensor_tensor(
            glog[:, 10:16][:, 0:6], glog[:, 10:16][:, 0:6], -1e30, glog[:, 0:6],
            ALU.mult, ALU.add), "dve")
        p.wait("vector", "dve")
        p.inc("vector", lambda e: e.reduce_max(
            glog[:, 7:8], glog[:, 10:16][:, 0:6], axis=mybir.AxisListType.X), "dve")
        p.wait("scalar", "dve")
        p.inc("scalar", lambda e: e.activation(
            glog[:, 10:16], glog[:, 0:6], AF.Exp, bias=glog[:, 8:9]), "act")
        p.wait("vector", "act")
        p.inc("vector", lambda e: e.tensor_tensor(
            glog[:, 6:7], glog[:, 7:8], glog[:, 8:9], ALU.add), "dve")
        p.wait("scalar", "dve")
        p.inc("scalar", lambda e: e.activation(glog[:, 6:7], glog[:, 6:7], AF.Exp), "act")
        p.wait("vector", "act")
        p.inc("vector", lambda e: e.tensor_scalar(
            glog[:, 6:7], glog[:, 6:7], 1.0, None, ALU.add), "dve")
        p.wait("vector", "dve")
        p.inc("vector", lambda e: e.reciprocal(glog[:, 6:7], glog[:, 6:7]), "dve")
        p.wait("vector", "dve")
        p.inc("vector", lambda e: e.tensor_scalar(
            glog[:, 0:6], glog[:, 0:6], glog[:, 7:8], None, ALU.is_ge), "dve")
        p.wait("vector", "dve")
        p.inc("vector", lambda e: e.tensor_tensor(
            glog[:, 0:6], glog[:, 0:6], glog[:, 10:16], ALU.mult), "dve")
        p.wait("vector", "dve")
        p.inc("vector", lambda e: e.tensor_scalar(
            glog[:, 0:6], glog[:, 0:6], glog[:, 6:7], None, ALU.mult), "dve")
        p.wait("tensor", "dve")
        p.inc("tensor", lambda e: e.transpose(
            PS[0:6, 512:640], glog[:, 0:6], ctx.ident_f32[:]), "pe")
        p.wait("scalar", "pe")
        p.inc("scalar", lambda e, t=t: e.copy(
            wsb[0:6, t * 128:(t + 1) * 128], PS[0:6, 512:640]), "act")
        p.wait("tensor", "act")

    for tc in range(NTC):
        TC = slice(tc * TCW, (tc + 1) * TCW)
        xcol = slice(TOKOFF + TC.start, TOKOFF + TC.stop)
        actm = {}

        def ld_w1(hc):
            return lambda e, hc=hc: e.dma_start(
                w1c[hc % 2].ap().rearrange("p (ft c) -> p ft c", c=128),
                ctx.d["aw1s"].ap().rearrange("(ft pp) hh -> pp ft hh", pp=128)
                [:, :, hc * 128:(hc + 1) * 128])

        def use_w1(hc, xcol=xcol):
            wb = w1c[hc % 2]
            pso = 2048 + (hc % 2) * 512
            if hc - 2 in actm:
                p.wait("tensor", "act", actm[hc - 2])
            for f in range(NF):
                p.inc("tensor", lambda e, hc=hc, f=f, wb=wb, pso=pso, xcol=xcol: e.matmul(
                    PS[:, pso:pso + TCW], wb[:, f * 128:(f + 1) * 128],
                    xnn[f][:, xcol], start=(f == 0), stop=(f == NF - 1)), "pe")
            pm = p.mark("pe")
            p.wait("scalar", "pe", pm)
            p.inc("scalar", lambda e, hc=hc, pso=pso: e.activation(
                hbuf[hc][:], PS[:, pso:pso + TCW], AF.Gelu,
                bias=ctx.cols[:, 16 + hc:17 + hc]), "act")
            actm[hc] = p.mark("act")

        stream_weights(p, 32, ld_w1, use_w1, "w4a")
        p.wait("tensor", "act")

        def ld_w2(hc):
            return lambda e, hc=hc: e.dma_start(
                w2c[hc % 2][:], ctx.d["aw2s"].ap()[hc * 128:(hc + 1) * 128, :])

        def use_w2(hc):
            wb = w2c[hc % 2]
            for fo in range(NF):
                p.inc("tensor", lambda e, hc=hc, fo=fo, wb=wb: e.matmul(
                    PS[:, fo * 512:fo * 512 + TCW], wb[:, fo * 128:(fo + 1) * 128],
                    hbuf[hc][:], start=(hc == 0), stop=(hc == 31)), "pe")

        stream_weights(p, 32, ld_w2, use_w2, "w4b")
        p.wait("scalar", "pe")
        for fo in range(NF):
            p.inc("scalar", lambda e, fo=fo: e.activation(
                abelT[fo][:], PS[:, fo * 512:fo * 512 + TCW], AF.Identity,
                bias=ctx.cols[:, 48 + fo:49 + fo]), "act")
        p.wait("tensor", "act")
        p.wait("vector", "act")
        for ex in range(6):
            p.wait("tensor", "dve")
            p.inc("tensor", lambda e, ex=ex, TC=TC: e.matmul(
                PS[:, 0:TCW], sel6[:, ex * 128:(ex + 1) * 128], wsb[0:6, TC],
                start=True, stop=True), "pe")
            p.wait("scalar", "pe")
            p.inc("scalar", lambda e: e.copy(webc[:], PS[:, 0:TCW]), "act")
            p.wait("tensor", "act")
            actm2 = {}

            def ld_e1(hc, ex=ex):
                return lambda e, ex=ex, hc=hc: e.dma_start(
                    w1c[hc % 2].ap().rearrange("p (ft c) -> p ft c", c=128),
                    ctx.d["ew1s"].ap()[ex].rearrange("(ft pp) hh -> pp ft hh", pp=128)
                    [:, :, hc * 128:(hc + 1) * 128])

            def use_e1(hc, ex=ex, xcol=xcol):
                wb = w1c[hc % 2]
                pso = 2048 + (hc % 2) * 512
                if hc - 2 in actm2:
                    p.wait("tensor", "act", actm2[hc - 2])
                for f in range(NF):
                    p.inc("tensor", lambda e, hc=hc, f=f, wb=wb, pso=pso, xcol=xcol: e.matmul(
                        PS[:, pso:pso + TCW], wb[:, f * 128:(f + 1) * 128],
                        xnn[f][:, xcol], start=(f == 0), stop=(f == NF - 1)), "pe")
                pm = p.mark("pe")
                p.wait("scalar", "pe", pm)
                p.inc("scalar", lambda e, ex=ex, hc=hc, pso=pso: e.activation(
                    hbuf[hc][:], PS[:, pso:pso + TCW], AF.Gelu,
                    bias=cols2[:, ex * 16 + hc:ex * 16 + hc + 1]), "act")
                actm2[hc] = p.mark("act")

            stream_weights(p, 16, ld_e1, use_e1, "w4a")
            p.wait("tensor", "act")

            def ld_e2(hc, ex=ex):
                return lambda e, ex=ex, hc=hc: e.dma_start(
                    w2c[hc % 2][:], ctx.d["ew2s"].ap()[ex][hc * 128:(hc + 1) * 128, :])

            def use_e2(hc):
                wb = w2c[hc % 2]
                for fo in range(NF):
                    p.inc("tensor", lambda e, hc=hc, fo=fo, wb=wb: e.matmul(
                        PS[:, fo * 512:fo * 512 + TCW], wb[:, fo * 128:(fo + 1) * 128],
                        hbuf[hc][:], start=(hc == 0), stop=(hc == 15)), "pe")

            stream_weights(p, 16, ld_e2, use_e2, "w4b")
            p.wait("scalar", "pe")
            p.wait("vector", "pe")
            for fo in range(NF):
                p.inc("scalar", lambda e, ex=ex, fo=fo: e.activation(
                    t1[:], PS[:, fo * 512:fo * 512 + TCW], AF.Identity,
                    bias=cols2[:, 96 + ex * 8 + fo:96 + ex * 8 + fo + 1]), "act")
                p.wait("vector", "act")
                p.inc("vector", lambda e: e.tensor_tensor(t2[:], t1[:], webc[:], ALU.mult), "dve")
                p.wait("vector", "dve")
                if ex == 0:
                    p.inc("vector", lambda e, fo=fo: e.tensor_copy(acc[fo][:], t2[:]), "dve")
                else:
                    p.inc("vector", lambda e, fo=fo: e.tensor_tensor(
                        acc[fo][:], acc[fo][:], t2[:], ALU.add), "dve")
                p.wait("scalar", "dve")
        p.wait("scalar", "dve")
        for fo in range(NF):
            p.inc("scalar", lambda e, fo=fo: e.copy(macb[fo][:], acc[fo][:]), "act")
        p.wait("tensor", "act")
        def ld_r(fi):
            return lambda e, fi=fi: e.dma_start(
                w1c[fi % 2][:], ctx.d["rs"].ap()[fi * 128:(fi + 1) * 128, :])

        def use_r(fi):
            wb = w1c[fi % 2]
            for fo in range(NF):
                p.inc("tensor", lambda e, fi=fi, fo=fo, wb=wb: e.matmul(
                    PS[:, fo * 512:fo * 512 + TCW], wb[:, fo * 128:(fo + 1) * 128],
                    macb[fi][:], start=(fi == 0), stop=(fi == NF - 1)), "pe")

        stream_weights(p, NF, ld_r, use_r, "w4a")
        p.wait("scalar", "pe")
        for fo in range(NF):
            p.inc("scalar", lambda e, fo=fo: e.copy(
                nonabT[fo][:], PS[:, fo * 512:fo * 512 + TCW]), "act")
        p.wait("tensor", "act")
        if tc == 0 and batch == 0:
            for nm, tiles in (("ab4", abelT), ("nb4", nonabT)):
                if nm in ctx.taps:
                    p.barrier(("sync", "vector"))
                    for f in range(NF):
                        p.wait("vector", "st")
                        p.inc("vector", lambda e, f=f, tiles=tiles: e.tensor_copy(
                            t1[:], tiles[f][:]), "dve")
                        p.wait("sync", "dve")
                        p.inc("sync", lambda e, f=f, nm=nm: e.dma_start(
                            ctx.tap_drams[nm].ap()[f * 128:(f + 1) * 128, 0:TCW], t1[:]),
                            "st", 16)
                        p.wait("sync", "st")
                    p.wait("gpsimd", "st")
                    p.wait("scalar", "st")
            if "mo4" in ctx.taps:
                p.barrier(("sync",))
                for f in range(NF):
                    p.inc("sync", lambda e, f=f: e.dma_start(
                        ctx.tap_drams["mo4"].ap()[f * 128:(f + 1) * 128, 0:TCW],
                        acc[f][:]), "st", 16)
                p.wait("sync", "st")
        p.wait("tensor", "dve")
        for i, tiles in ((0, abelT), (8, nonabT)):
            for f in range(NF):
                p.inc("tensor", lambda e, i=i, f=f, tiles=tiles: e.matmul(
                    PS[0:1, 3584:3584 + TCW], gcb[:, i + f:i + f + 1], tiles[f][:],
                    start=(i == 0 and f == 0), stop=(i == 8 and f == NF - 1)), "pe")
        p.wait("scalar", "pe")
        p.inc("scalar", lambda e: e.activation(
            grow[0:1, 0:TCW], PS[0:1, 3584:3584 + TCW], AF.Sigmoid,
            bias=ctx.lam_s[0:1, 0:1]), "act")
        p.wait("tensor", "act")
        p.inc("tensor", lambda e: e.matmul(
            PS[:, 3072:3072 + TCW], ctx.ones_row[0:1, :], grow[0:1, 0:TCW],
            start=True, stop=True), "pe")
        p.wait("scalar", "pe")
        p.inc("scalar", lambda e: e.copy(gbuf[:], PS[:, 3072:3072 + TCW]), "act")
        p.wait("vector", "act")
        for f in range(NF):
            p.inc("vector", lambda e, f=f: e.tensor_tensor(
                t1[:], nonabT[f][:], abelT[f][:], ALU.subtract), "dve")
            p.wait("vector", "dve")
            p.inc("vector", lambda e: e.tensor_tensor(t1[:], t1[:], gbuf[:], ALU.mult), "dve")
            p.wait("vector", "dve")
            p.inc("vector", lambda e, f=f: e.tensor_tensor(
                t2[:], abelT[f][:], t1[:], ALU.add), "dve")
            p.wait("vector", "dve")
            p.inc("vector", lambda e, f=f, xcol=xcol: e.tensor_tensor(
                ctx.xT[f][:, xcol], ctx.xT[f][:, xcol], t2[:], ALU.add), "dve")
            p.wait("vector", "dve")
        p.wait("tensor", "dve")
        p.wait("scalar", "dve")
    p.barrier()


def _tap_tiles(ctx, tiles, name, padded=True):
    p = ctx.p
    dram = ctx.tap_drams[name]
    so = TOKOFF if padded else 0
    p.barrier(("sync", "vector"))
    bf_src = tiles[0].dtype == BF16
    for f in range(NF):
        if bf_src:
            p.wait("vector", "st")
            p.inc("vector", lambda e, f=f: e.tensor_copy(
                ctx.ln_sq[:, 0:SEQ], tiles[f][:, so:so + SEQ]), "dve")
            p.wait("sync", "dve")
            p.inc("sync", lambda e, f=f: e.dma_start(
                dram.ap()[f * 128:(f + 1) * 128, :], ctx.ln_sq[:, 0:SEQ]), "st", 16)
            p.wait("sync", "st")
        else:
            p.inc("sync", lambda e, f=f: e.dma_start(
                dram.ap()[f * 128:(f + 1) * 128, :], tiles[f][:, so:so + SEQ]), "st", 16)
    p.wait("sync", "st")
    p.wait("gpsimd", "st")
    p.wait("scalar", "st")


def emit_load_x(ctx, batch):
    p = ctx.p
    p.barrier()
    for f in range(NF):
        p.inc("sync", lambda e, f=f, b=batch: e.dma_start(
            ctx.xT[f][:, tokc(0, SEQ)], ctx.d["xin"].ap()[b, f * 128:(f + 1) * 128, :]),
            "dma", 16)
    p.wait("tensor", "dma")
    p.wait("vector", "dma")
    p.wait("scalar", "dma")


def emit_store_x(ctx, batch):
    p = ctx.p
    p.wait("sync", "dve")
    p.wait("sync", "act")
    p.wait("sync", "pe")
    for f in range(NF):
        p.inc("sync", lambda e, f=f, b=batch: e.dma_start(
            ctx.d["xout"].ap()[b, f * 128:(f + 1) * 128, :], ctx.xT[f][:, tokc(0, SEQ)]),
            "st", 16)
    p.wait("sync", "st")



def _host_consts():
    import ml_dtypes as _md
    c = {}
    c["ident_f32"] = np.eye(128, dtype=np.float32)
    c["ident_bf16"] = np.eye(128, dtype=_md.bfloat16)
    DH, WIN = 64, 512
    half = np.arange(0, DH, 2, dtype=np.float64)
    inv_freq = 1.0 / 10000.0 ** (half / DH)
    t = np.arange(SEQ, dtype=np.float64)
    freqs = t[:, None] * inv_freq[None, :]
    freqs = np.concatenate([freqs, freqs], axis=-1)
    base = (half + 0.4 * DH) / (1.4 * DH)
    power = (t - SEQ // 2) / float(WIN // 2)
    scale = base[None, :] ** power[:, None]
    scale = np.concatenate([scale, scale], axis=-1)
    cos, sin = np.cos(freqs), np.sin(freqs)
    sgn = np.concatenate([-np.ones(32), np.ones(32)])[None, :]
    c["rotA_q"] = (cos * scale).astype(np.float32)
    c["rotB_q"] = (sin * scale * sgn).astype(np.float32)
    c["rotA_k"] = (cos / scale).astype(np.float32)
    c["rotB_k"] = (sin / scale * sgn).astype(np.float32)
    mk = np.zeros((4, 128, 512), dtype=np.float32)
    mkp = np.zeros((4, 128, 512), dtype=np.float32)
    for r in range(4):
        ks = r * 128 + np.arange(128)[:, None]
        qs = np.arange(512)[None, :]
        mk[r] = np.where(ks <= qs, 0.0, -1e30)
        mkp[r] = np.where(ks >= qs, 0.0, -1e30)
    c["locmask"] = mk
    c["locmaskp"] = mkp
    qrow = np.arange(128)[:, None]
    kcol = np.arange(128)[None, :]
    c["maskdiag"] = np.where(kcol <= qrow, 0.0, -1e9).astype(np.float32)
    c["epsc"] = np.array([[0.0], [1e-6]], dtype=np.float32)
    sel = np.zeros((6, 768), dtype=np.float32)
    for e in range(6):
        sel[e, e * 128:(e + 1) * 128] = 1.0
    c["sel6"] = sel
    return c


WEIGHT_SHAPES = {
    "L0_attn_Wq": (16, 1024, 128), "L0_attn_Wk": (16, 1024, 128),
    "L0_attn_Wv": (16, 1024, 64), "L0_attn_Wo": (1024, 1024),
    "L0_attn_bo": (1024,), "L0_attn_gamma": (1024,), "L0_attn_beta": (1024,),
    "L0_attn_bv": (16, 64),
    "L0_ffn_wqkv": (2, 1024, 1536), "L0_ffn_q_scale": (2, 64), "L0_ffn_k_scale": (2, 64),
    "L0_ffn_wo": (2, 512, 1024), "L0_ffn_ln_g": (2, 1024), "L0_ffn_ln_b": (2, 1024),
    "L0_ffn_w1": (2, 1024, 5460), "L0_ffn_b1": (2, 5460),
    "L0_ffn_w2": (2, 2730, 1024), "L0_ffn_b2": (2, 1024),
    "L1_attn_w_qkv": (1024, 3072), "L1_attn_b_qkv": (3072,),
    "L1_attn_w_out": (1024, 1024), "L1_attn_b_out": (1024,),
    "L1_ffn_ln_g": (1024,), "L1_ffn_ln_b": (1024,),
    "L1_ffn_aw1": (1024, 4096), "L1_ffn_ab1": (4096,),
    "L1_ffn_aw2": (4096, 1024), "L1_ffn_ab2": (1024,),
    "L1_ffn_gate_w": (1024, 6), "L1_ffn_gate_b": (6,),
    "L1_ffn_ew1": (6, 1024, 2048), "L1_ffn_eb1": (6, 2048),
    "L1_ffn_ew2": (6, 2048, 1024), "L1_ffn_eb2": (6, 1024),
    "L1_ffn_gens": (4, 1024, 1024), "L1_ffn_gc_w": (2048, 1), "L1_ffn_gc_b": (1,),
    "bq_t": (128, 16), "bk_t": (128, 16), "lam11": (1, 1),
}

BLOCK_EMITTERS = {}




def _emit_expm(ctx):
    p, m = ctx.p, ctx.m
    PS = ctx.PS
    bases = [R_XT, R_XT + 32768, ctx.BIG, ctx.BIG + 32768]
    X = [[m.alloc(f"xm{i}_{f}", [128, 1024], F32, at=bases[i] + f * 4096)
          for f in range(NF)] for i in range(4)]
    stb = m.alloc("xm_st", [128, 1024], BF16, at=ctx.BIG + 65536)

    def mm_product(lhsT_tiles, rhs_tiles, out_tiles, post=None):
        evm = {}
        gx = 0
        for fo in range(NF):
            for ch in range(2):
                cs = slice(ch * 512, ch * 512 + 512)
                pso = (gx % 4) * 512
                if gx - 3 in evm:
                    p.wait("tensor", "act", evm[gx - 3])
                for fi in range(NF):
                    p.inc("tensor", lambda e, fo=fo, cs=cs, fi=fi, pso=pso: e.matmul(
                        PS[:, pso:pso + 512],
                        lhsT_tiles[fi][:, fo * 128:(fo + 1) * 128],
                        rhs_tiles[fi][:, cs], start=(fi == 0), stop=(fi == NF - 1)), "pe")
                if post is None:
                    p.wait("scalar", "pe")
                    p.inc("scalar", lambda e, fo=fo, cs=cs, pso=pso: e.copy(
                        out_tiles[fo][:, cs], PS[:, pso:pso + 512]), "act")
                else:
                    post(fo, cs, pso)
                evm[gx] = p.mark("act")
                gx += 1

    def transpose_to(src_tiles, dst_tiles):
        evm = {}
        gx = 0
        for a in range(NF):
            for b in range(NF):
                po = 3072 + (gx % 2) * 512
                if gx - 2 in evm:
                    p.wait("tensor", "act", evm[gx - 2])
                p.inc("tensor", lambda e, a=a, b=b, po=po: e.transpose(
                    PS[:, po:po + 128], src_tiles[a][:, b * 128:(b + 1) * 128],
                    ctx.ident_f32[:]), "pe")
                p.wait("scalar", "pe")
                p.inc("scalar", lambda e, a=a, b=b, po=po: e.copy(
                    dst_tiles[b][:, a * 128:(a + 1) * 128], PS[:, po:po + 128]), "act")
                evm[gx] = p.mark("act")
                gx += 1

    def load_ident(tiles):
        for f in range(NF - 1):
            p.run(lambda e, f=f: e.memset(tiles[f][:], 0.0), "gpsimd")
        p.inc("gpsimd", lambda e, tiles=tiles: e.memset(tiles[NF - 1][:], 0.0), "gp", 1)
        p.wait("sync", "gp")
        for f in range(NF):
            p.inc("sync", lambda e, f=f, tiles=tiles: e.dma_start(
                tiles[f][:, f * 128:(f + 1) * 128], ctx.cd["identf"].ap()), "dma", 16)

    racc = None
    free = [0, 1, 2, 3]
    for gi in range(4):
        bA = free[0]; bH = free[1]; bT = free[2]
        p.barrier(("sync", "gpsimd", "tensor", "scalar", "vector"))
        for f in range(NF):
            p.inc("sync", lambda e, f=f, gi=gi, bA=bA: e.dma_start(
                X[bA][f][:], ctx.d["L1_ffn_gens"].ap()[gi][f * 128:(f + 1) * 128, :]),
                "dma", 16)
        load_ident(X[bH])
        p.barrier(("tensor", "scalar", "vector", "sync"))
        for j in range(7, 0, -1):
            coef = -0.25 / j

            def post(fo, cs, pso, coef=coef, bT=bT):
                p.wait("scalar", "pe")
                p.inc("scalar", lambda e, fo=fo, cs=cs, pso=pso, coef=coef, bT=bT:
                      e.activation(X[bT][fo][:, cs], PS[:, pso:pso + 512], AF.Identity,
                                   bias=0.0, scale=coef), "act")
                p.wait("vector", "act")
                ds = max(cs.start, fo * 128)
                de = min(cs.stop, (fo + 1) * 128)
                if ds < de:
                    p.inc("vector", lambda e, fo=fo, ds=ds, de=de, bT=bT: e.tensor_tensor(
                        X[bT][fo][:, ds:de], X[bT][fo][:, ds:de],
                        ctx.ident_f32[:, ds - fo * 128:de - fo * 128], ALU.add), "dve")

            mm_product(X[bA], X[bH], None, post=post)
            p.barrier(("tensor", "scalar", "vector"))
            bH, bT = bT, bH
        cur = bH
        scratch = [bA, bT]
        for sq in range(2):
            tb, ob = scratch[0], scratch[1]
            transpose_to(X[cur], X[tb])
            p.barrier(("tensor", "scalar", "vector"))
            mm_product(X[tb], X[cur], X[ob])
            p.barrier(("tensor", "scalar", "vector"))
            scratch = [cur, tb]
            cur = ob
        if racc is None:
            racc = cur
            free = [b for b in range(4) if b != racc]
        else:
            tb = [b for b in range(4) if b not in (racc, cur)][0]
            transpose_to(X[racc], X[tb])
            p.barrier(("tensor", "scalar", "vector"))
            ob = [b for b in range(4) if b not in (racc, cur, tb)][0]
            mm_product(X[tb], X[cur], X[ob])
            p.barrier(("tensor", "scalar", "vector"))
            racc = ob
            free = [b for b in range(4) if b != racc]
    if "rexp" in ctx.taps:
        p.barrier(("sync",))
        for f in range(NF):
            p.inc("sync", lambda e, f=f: e.dma_start(
                ctx.tap_drams["rexp"].ap()[f * 128:(f + 1) * 128, 0:1024], X[racc][f][:]),
                "st", 16)
        p.wait("sync", "st")
    for f in range(NF):
        p.wait("vector", "st")
        p.inc("vector", lambda e, f=f: e.tensor_copy(stb[:], X[racc][f][:]), "dve")
        p.wait("sync", "dve")
        p.inc("sync", lambda e, f=f: e.dma_start(
            ctx.d["rs"].ap()[f * 128:(f + 1) * 128, :], stb[:]), "st", 16)
        p.wait("sync", "st")
    p.barrier()


def _emit_w_prep4(ctx):
    p, m = ctx.p, ctx.m
    st = m.alloc("wp4_st", [128, 4096], BF16, at=ctx.BIG)
    jobs = []
    for f in range(NF):
        jobs.append((ctx.d["L1_ffn_aw1"].ap()[f * 128:(f + 1) * 128, :],
                     ctx.d["aw1s"].ap()[f * 128:(f + 1) * 128, :], 4096))
    for t in range(32):
        jobs.append((ctx.d["L1_ffn_aw2"].ap()[t * 128:(t + 1) * 128, :],
                     ctx.d["aw2s"].ap()[t * 128:(t + 1) * 128, :], 1024))
    for ex in range(6):
        for f in range(NF):
            jobs.append((ctx.d["L1_ffn_ew1"].ap()[ex][f * 128:(f + 1) * 128, :],
                         ctx.d["ew1s"].ap()[ex][f * 128:(f + 1) * 128, :], 2048))
        for t in range(16):
            jobs.append((ctx.d["L1_ffn_ew2"].ap()[ex][t * 128:(t + 1) * 128, :],
                         ctx.d["ew2s"].ap()[ex][t * 128:(t + 1) * 128, :], 1024))
    for src_ap, dst_ap, n in jobs:
        p.wait("gpsimd", "st")
        p.inc("gpsimd", lambda e, s=src_ap, n=n: e.dma_start(st[:, 0:n], s), "wdma", 16)
        p.wait("sync", "wdma")
        p.inc("sync", lambda e, d=dst_ap, n=n: e.dma_start(d, st[:, 0:n]), "st", 16)
    p.wait("gpsimd", "st")
    p.wait("sync", "st")


def _emit_w_prep(ctx):
    p, m = ctx.p, ctx.m
    st1 = m.alloc("wp_st1", [128, 5460], BF16, at=ctx.BIG)
    st2 = m.alloc("wp_st2", [128, 1024], BF16, at=ctx.BIG + 10944)
    for lay in range(2):
        for f in range(NF):
            p.wait("gpsimd", "st")
            p.inc("gpsimd", lambda e, lay=lay, f=f: e.dma_start(
                st1[:], ctx.d["L0_ffn_w1"].ap()[lay][f * 128:(f + 1) * 128, :]), "wdma", 16)
            p.wait("sync", "wdma")
            p.inc("sync", lambda e, lay=lay, f=f: e.dma_start(
                ctx.d["w1s"].ap()[lay][f * 128:(f + 1) * 128, :], st1[:]), "st", 16)
        for c in range(22):
            cw = 128 if c < 21 else 42
            p.wait("gpsimd", "st")
            p.inc("gpsimd", lambda e, lay=lay, c=c, cw=cw: e.dma_start(
                st2[0:cw, :], ctx.d["L0_ffn_w2"].ap()[lay][c * 128:c * 128 + cw, :]), "wdma", 16)
            p.wait("sync", "wdma")
            p.inc("sync", lambda e, lay=lay, c=c, cw=cw: e.dma_start(
                ctx.d["w2s"].ap()[lay][c * 128:c * 128 + cw, :], st2[0:cw, :]), "st", 16)
    p.wait("gpsimd", "st")
    p.wait("sync", "st")


def build_program(blocks=("l0_attn",), taps=(), nb=2):
    nc = bass.Bass("TRN2", target_bir_lowering=False)
    ctx = Ctx()
    ctx.nc = nc
    ctx.p = Prog(nc)
    ctx.taps = set(taps)
    m = SbufMap(nc)
    ctx.m = m

    d = {}
    d["xin"] = nc.dram_tensor("xin", [nb, E, SEQ], F32, kind="ExternalInput")
    d["xout"] = nc.dram_tensor("xout", [nb, E, SEQ], F32, kind="ExternalOutput")
    for name, shape in WEIGHT_SHAPES.items():
        d[name] = nc.dram_tensor(name, list(shape), F32, kind="ExternalInput")
    ctx.d = d
    ctx.tap_drams = {t: nc.dram_tensor(
        "tap_" + t, [128 if t == "rtq" else (512 if t.endswith("t") else E),
                     1024 if t == "rexp" else SEQ], F32,
        kind="ExternalOutput") for t in taps}

    consts = _host_consts()
    identf_d = nc.inline_tensor(consts["ident_f32"], name="identf")
    identb_d = nc.inline_tensor(consts["ident_bf16"], name="identb")
    ctx.cd = {}
    ctx.cd["maskdiag"] = nc.inline_tensor(consts["maskdiag"], name="maskdiag")
    ctx.cd["epsc"] = nc.inline_tensor(consts["epsc"], name="epsc")
    ctx.cd["identf"] = identf_d
    if "l1_ffn" in blocks:
        ctx.cd["sel6"] = nc.inline_tensor(consts["sel6"], name="sel6")
        d["aw1s"] = nc.dram_tensor("aw1s", [1024, 4096], BF16, kind="Internal")
        d["aw2s"] = nc.dram_tensor("aw2s", [4096, 1024], BF16, kind="Internal")
        d["ew1s"] = nc.dram_tensor("ew1s", [6, 1024, 2048], BF16, kind="Internal")
        d["ew2s"] = nc.dram_tensor("ew2s", [6, 2048, 1024], BF16, kind="Internal")
        d["rs"] = nc.dram_tensor("rs", [1024, 1024], BF16, kind="Internal")
    if "l0_ffn" in blocks:
        for k in ("rotA_q", "rotB_q", "rotA_k", "rotB_k", "locmask", "locmaskp"):
            ctx.cd[k] = nc.inline_tensor(consts[k], name=k)
        d["xpark"] = nc.dram_tensor("xpark", [E, SEQ], F32, kind="Internal")
        d["w1s"] = nc.dram_tensor("w1s", [2, 1024, 5460], BF16, kind="Internal")
        d["w2s"] = nc.dram_tensor("w2s", [2, 2730, 1024], BF16, kind="Internal")

    ctx.xT = [m.alloc(f"xT{f}", [128, NPAD], F32, at=R_XT + f * XT_STRIDE) for f in range(NF)]
    co = R_CON
    ctx.ident_f32 = m.alloc("identf", [128, 128], F32, at=co); co += 512
    ctx.ident_bf16 = m.alloc("identb", [128, 128], BF16, at=co); co += 256
    ctx.ones_col = m.alloc("ones_col", [128, 1], F32, at=co); co += 32
    ctx.ones_col_bf = m.alloc("ones_col_bf", [128, 1], BF16, at=co); co += 32
    ctx.ones_row = m.alloc("ones_row", [1, 128], F32, at=co); co += 512
    ctx.cols = m.alloc("cols", [128, 96], F32, at=co); co += 384
    ctx.bq_col = m.alloc("bq_col", [128, 16], F32, at=co); co += 64
    ctx.bk_col = m.alloc("bk_col", [128, 16], F32, at=co); co += 64
    ctx.lam_s = m.alloc("lam_s", [1, 1], F32, at=co); co += 32
    ctx.eps_s = m.alloc("eps_s", [1, 1], F32, at=co); co += 32
    ctx.eps6_s = m.alloc("eps6_s", [1, 1], F32, at=co); co += 32
    ctx.maskc = m.alloc("maskc", [128, 128], F32, at=co); co += 512
    ctx.epsc = m.alloc("epsc", [2, 1], F32, at=co); co += 32
    assert co <= R_WAR
    ctx.WARENA = R_WAR
    ctx.BIG = R_BIG
    ctx.ln_sq = m.alloc("ln_sq", [128, 2048], F32, at=R_LNS)
    ctx.ln_misc = m.alloc("ln_misc", [128, 6144], F32, at=R_LNS + 8192)
    ctx.row_tmp = m.alloc("row_tmp", [1, 2048], F32, at=R_LNS)

    ctx.PS = nc.alloc_psum_tensor("PS", [128, 4096], F32)

    p = ctx.p
    p.inc("sync", lambda e: e.dma_start(ctx.ident_f32[:], identf_d.ap()), "dma", 16)
    p.inc("sync", lambda e: e.dma_start(ctx.ident_bf16[:], identb_d.ap()), "dma", 16)
    p.inc("sync", lambda e: e.dma_start(ctx.maskc[:], ctx.cd["maskdiag"].ap()), "dma", 16)
    p.inc("sync", lambda e: e.dma_start(ctx.epsc[:], ctx.cd["epsc"].ap()), "dma", 16)
    p.run(lambda e: e.memset(ctx.ones_col[:], 1.0), "gpsimd")
    p.run(lambda e: e.memset(ctx.ones_col_bf[:], 1.0), "gpsimd")
    p.run(lambda e: e.memset(ctx.ones_row[:], 1.0), "gpsimd")
    p.run(lambda e: e.memset(ctx.eps_s[:], LN_EPS), "gpsimd")
    p.run(lambda e: e.memset(ctx.eps6_s[:], 1e-6), "gpsimd")
    for f in range(NF):
        p.run(lambda e, f=f: e.memset(ctx.xT[f][:, 0:TOKOFF], 0.0), "gpsimd")
        if f < NF - 1:
            p.run(lambda e, f=f: e.memset(ctx.xT[f][:, TOKOFF + SEQ:NPAD], 0.0), "gpsimd")
    p.inc("gpsimd", lambda e: e.memset(ctx.xT[NF - 1][:, TOKOFF + SEQ:NPAD], 0.0), "gp", 1)
    p.wait("tensor", "gp")
    p.wait("vector", "gp")
    p.wait("scalar", "gp")
    p.wait("sync", "gp")

    if "l0_ffn" in blocks:
        _emit_w_prep(ctx)
    if "l1_ffn" in blocks:
        _emit_w_prep4(ctx)
        _emit_expm(ctx)

    for b in range(nb):
        emit_load_x(ctx, b)
        for bi, blk in enumerate(blocks):
            BLOCK_EMITTERS[blk](ctx, b)
            tname = f"xb{b}p{bi}"
            if tname in ctx.taps:
                _tap_tiles(ctx, ctx.xT, tname, padded=True)
        emit_store_x(ctx, b)

    p.build()
    nc.finalize()
    return nc, ctx


BLOCK_EMITTERS["l0_attn"] = emit_l0_attn
BLOCK_EMITTERS["l1_attn"] = emit_l1_attn
BLOCK_EMITTERS["l1_ffn"] = emit_l1_ffn

def emit_l0_ffn(ctx, batch):
    p, m, nc = ctx.p, ctx.m, ctx.nc
    PS = ctx.PS
    B0 = ctx.BIG
    H, DH = 8, 64
    NT = 16
    NC2 = 22

    zT = [m.alloc(f"b2_z{f}", [128, NPAD], F32, at=B0 + f * XT_STRIDE) for f in range(NF)]
    rott = {k: m.alloc(f"b2_rt{k}", [128, 1024], F32, at=B0 + 8 * XT_STRIDE + i * 4096)
            for i, k in enumerate(("Aq", "Bq", "Ak", "Bk"))}
    vbuf = [m.alloc(f"b2_v{t}", [128, 8 * 65], BF16, at=R_XT + t * 1056) for t in range(NT)]
    qtb = [m.alloc(f"b2_qt{g}", [128, 2048], BF16, at=R_XT + 16896 + g * 4096) for g in range(4)]
    ktb = [m.alloc(f"b2_kt{g}", [128, 2048], BF16, at=R_XT + 33280 + g * 4096) for g in range(4)]
    PRE = R_XT + 49664
    sq32 = m.alloc("b2_sq32", [128, 512], F32, at=PRE)
    qn = m.alloc("b2_qn", [128, 512], F32, at=PRE + 2048)
    t1b = m.alloc("b2_t1", [128, 512], F32, at=PRE + 4096)
    t2b = m.alloc("b2_t2", [128, 512], F32, at=PRE + 6144)
    glb = m.alloc("b2_gl", [128, 512], BF16, at=PRE + 8192)
    sc32 = m.alloc("b2_sc32", [128, 512], F32, at=PRE + 9216)
    rowsb = m.alloc("b2_rows", [128, 512], F32, at=PRE + 11264)
    ssq = m.alloc("b2_ssq", [128, 16], F32, at=PRE + 13312)
    qsb = m.alloc("b2_qsb", [128, 64], F32, at=PRE + 13376)
    ksb = m.alloc("b2_ksb", [128, 64], F32, at=PRE + 13632)
    rbc = m.alloc("b2_rbc", [64, 512], F32, at=PRE + 13888)
    hnT = [m.alloc(f"b2_hn{f}", [128, NPAD], BF16, at=R_XT + f * 4128) for f in range(NF)]
    glub = [m.alloc(f"b2_glu{c}", [128, 512], BF16, at=R_XT + 33280 + c * 1024)
            for c in range(NC2)]
    hbT = [m.alloc(f"b2_hb{f}", [128, 2048], BF16, at=R_LNS + f * 4096) for f in range(NF)]
    oT = [m.alloc(f"b2_oT{g}", [128, 2048], BF16, at=R_LNS + g * 4096) for g in range(4)]
    W0 = ctx.WARENA
    wqkv_sb = [m.alloc(f"b2_wqkv{f}", [128, 1536], BF16, at=W0 + f * 3072) for f in range(NF)]
    masks_sb = m.alloc("b2_masks", [128, 4096], F32, at=W0)
    wol_sb = [m.alloc(f"b2_wo{i}", [128, 1024], BF16, at=W0 + 8192 + i * 2048) for i in range(4)]
    w1a_sb = [m.alloc(f"b2_w1a{i}", [128, 1024], BF16, at=W0 + 16384 + i * 2048) for i in range(2)]
    w1g_sb = [m.alloc(f"b2_w1g{i}", [128, 1024], BF16, at=W0 + 20480 + i * 2048) for i in range(2)]
    w2_sb = [m.alloc(f"b2_w2{i}", [128, 1024], BF16, at=W0 + 16384 + i * 2048) for i in range(2)]

    dpark = ctx.d["xpark"]

    p.barrier(("gpsimd",))
    for f in range(NF):
        p.run(lambda e, f=f: e.memset(zT[f][:, 0:TOKOFF], 0.0), "gpsimd")
        if f < NF - 1:
            p.run(lambda e, f=f: e.memset(zT[f][:, TOKOFF + SEQ:NPAD], 0.0), "gpsimd")
    p.inc("gpsimd", lambda e: e.memset(zT[NF - 1][:, TOKOFF + SEQ:NPAD], 0.0), "gp", 1)
    emit_ln(ctx, ctx.xT, zT)
    p.barrier(("sync",))
    for f in range(NF):
        p.inc("sync", lambda e, f=f: e.dma_start(
            dpark.ap()[f * 128:(f + 1) * 128, :], ctx.xT[f][:, tokc(0, SEQ)]), "st", 16)
    p.wait("gpsimd", "st")

    for lay in range(2):
        p.barrier(("sync", "tensor", "scalar", "vector"))
        p.inc("sync", lambda e, lay=lay: e.dma_start(
            ctx.row_tmp[0:1, 0:64], ctx.d["L0_ffn_q_scale"].ap()[lay].rearrange(
                "(a d) -> a d", a=1)), "dma", 16)
        p.wait("tensor", "dma")
        emit_bcast_row(ctx, ctx.row_tmp, qsb[:, 0:64], 64)
        p.inc("sync", lambda e, lay=lay: e.dma_start(
            ctx.row_tmp[0:1, 0:64], ctx.d["L0_ffn_k_scale"].ap()[lay].rearrange(
                "(a d) -> a d", a=1)), "dma", 16)
        p.wait("tensor", "dma")
        emit_bcast_row(ctx, ctx.row_tmp, ksb[:, 0:64], 64)
        for k, dname, scl in (("Aq", "rotA_q", qsb), ("Bq", "rotB_q", qsb),
                              ("Ak", "rotA_k", ksb), ("Bk", "rotB_k", ksb)):
            p.inc("sync", lambda e, k=k, dname=dname: e.dma_start(
                rott[k].ap().rearrange("p (tt d) -> p tt d", d=64),
                ctx.cd[dname].ap().rearrange("(tt pp) d -> pp tt d", pp=128)), "dma", 16)
        p.wait("vector", "dma")
        p.wait("vector", "act")
        for k, scl in (("Aq", qsb), ("Ak", ksb)):
            for t in range(NT):
                p.inc("vector", lambda e, k=k, t=t, scl=scl: e.tensor_tensor(
                    rott[k][:, t * 64:(t + 1) * 64], rott[k][:, t * 64:(t + 1) * 64],
                    scl[:, 0:64], ALU.mult), "dve")
        for k, scl in (("Bq", qsb), ("Bk", ksb)):
            for t in range(NT):
                p.inc("vector", lambda e, k=k, t=t, scl=scl: e.tensor_tensor(
                    rott[k][:, t * 64:t * 64 + 32], rott[k][:, t * 64:t * 64 + 32],
                    scl[:, 32:64], ALU.mult), "dve")
                p.inc("vector", lambda e, k=k, t=t, scl=scl: e.tensor_tensor(
                    rott[k][:, t * 64 + 32:(t + 1) * 64], rott[k][:, t * 64 + 32:(t + 1) * 64],
                    scl[:, 0:32], ALU.mult), "dve")
        if lay == 0 and batch == 0 and "rtq" in ctx.taps:
            p.barrier(("sync",))
            p.inc("sync", lambda e: e.dma_start(
                ctx.tap_drams["rtq"].ap()[0:128, 0:1024], rott["Aq"][:, 0:1024]), "st", 16)
            p.inc("sync", lambda e: e.dma_start(
                ctx.tap_drams["rtq"].ap()[0:128, 1024:2048], rott["Bq"][:, 0:1024]), "st", 16)
            p.wait("sync", "st")
        p.wait("gpsimd", "pe")
        for f in range(NF):
            p.inc("gpsimd", lambda e, f=f, lay=lay: e.dma_start(
                wqkv_sb[f][:], ctx.d["L0_ffn_wqkv"].ap()[lay].rearrange(
                    "(ft pp) c -> ft pp c", pp=128)[f]), "wdma", 16)
        p.wait("scalar", "dve")
        for f in range(NF):
            soff = TOKOFF - (1 if f >= 4 else 0)
            p.inc("scalar", lambda e, f=f, soff=soff: e.copy(
                hbT[f][:, 0:SEQ], zT[f][:, soff:soff + SEQ]), "act")
        p.wait("gpsimd", "act")
        for t in range(NT - 1):
            p.run(lambda e, t=t: e.memset(vbuf[t][:], 1.0), "gpsimd")
        p.inc("gpsimd", lambda e: e.memset(vbuf[NT - 1][:], 1.0), "gp", 1)
        p.wait("tensor", "gp")
        p.wait("tensor", "act")
        p.wait("tensor", "wdma")

        for t in range(NT):
            p.wait("tensor", "act")
            p.wait("tensor", "dve")
            for cch in range(3):
                cs = slice(cch * 512, cch * 512 + 512)
                for f in range(NF):
                    p.inc("tensor", lambda e, t=t, cs=cs, f=f: e.matmul(
                        PS[:, cs], hbT[f][:, t * 128:(t + 1) * 128],
                        wqkv_sb[f][:, cs], start=(f == 0), stop=(f == NF - 1)), "pe")
            p.wait("scalar", "pe")
            p.wait("vector", "pe")
            p.inc("scalar", lambda e, t=t: e.copy(
                vbuf[t].ap().rearrange("p (h c) -> p h c", c=65)[:, :, 0:64],
                PS[:, 1024:1536].rearrange("p (h c) -> p h c", c=64)), "act")
            for qk in range(2):
                po = qk * 512
                AT, BT = (rott["Aq"], rott["Bq"]) if qk == 0 else (rott["Ak"], rott["Bk"])
                sso = qk * 8
                p.inc("scalar", lambda e, po=po: e.activation(
                    sq32[:, 0:512], PS[:, po:po + 512], AF.Square), "act")
                p.wait("vector", "act")
                p.inc("vector", lambda e, sso=sso: e.reduce_sum(
                    ssq[:, sso:sso + 8],
                    sq32.ap().rearrange("p (h d) -> p h d", d=64), axis=mybir.AxisListType.X),
                    "dve")
                p.wait("scalar", "dve")
                p.inc("scalar", lambda e, sso=sso: e.activation(
                    ssq[:, sso:sso + 8], ssq[:, sso:sso + 8], AF.Sqrt), "act")
                p.wait("vector", "act")
                p.inc("vector", lambda e, sso=sso: e.tensor_scalar(
                    ssq[:, sso:sso + 8], ssq[:, sso:sso + 8], 1e-12, None, ALU.max), "dve")
                p.wait("vector", "dve")
                p.inc("vector", lambda e, sso=sso: e.reciprocal(
                    ssq[:, sso:sso + 8], ssq[:, sso:sso + 8]), "dve")
                p.wait("vector", "dve")
                p.wait("vector", "pe")
                for h in range(H):
                    p.inc("vector", lambda e, h=h, po=po, sso=sso: e.tensor_scalar(
                        qn[:, h * 64:(h + 1) * 64], PS[:, po + h * 64:po + (h + 1) * 64],
                        ssq[:, sso + h:sso + h + 1], None, ALU.mult), "dve")
                p.wait("vector", "dve")
                p.inc("vector", lambda e, t=t, AT=AT: e.tensor_tensor(
                    t1b.ap().rearrange("p (h d) -> p h d", d=64),
                    qn.ap().rearrange("p (h d) -> p h d", d=64),
                    AT[:, t * 64:(t + 1) * 64].rearrange("p (a d) -> p a d", a=1)
                    .broadcast_to([128, 8, 64]), ALU.mult), "dve")
                p.inc("vector", lambda e, t=t, BT=BT: e.tensor_tensor(
                    t2b.ap().rearrange("p (h d) -> p h d", d=64)[:, :, 0:32],
                    qn.ap().rearrange("p (h d) -> p h d", d=64)[:, :, 32:64],
                    BT[:, t * 64:t * 64 + 32].rearrange("p (a d) -> p a d", a=1)
                    .broadcast_to([128, 8, 32]), ALU.mult), "dve")
                p.inc("vector", lambda e, t=t, BT=BT: e.tensor_tensor(
                    t2b.ap().rearrange("p (h d) -> p h d", d=64)[:, :, 32:64],
                    qn.ap().rearrange("p (h d) -> p h d", d=64)[:, :, 0:32],
                    BT[:, t * 64 + 32:(t + 1) * 64].rearrange("p (a d) -> p a d", a=1)
                    .broadcast_to([128, 8, 32]), ALU.mult), "dve")
                p.wait("vector", "dve")
                p.inc("vector", lambda e: e.tensor_tensor(
                    t1b[:, 0:512], t1b[:, 0:512], t2b[:, 0:512], ALU.add), "dve")
                p.wait("tensor", "dve")
                p.wait("tensor", "act")
                for hh in range(H):
                    p.inc("tensor", lambda e, hh=hh: e.transpose(
                        PS[0:64, 1024 + hh * 128:1024 + (hh + 1) * 128],
                        t1b[:, hh * 64:(hh + 1) * 64], ctx.ident_f32[:]), "pe")
                p.wait("scalar", "pe")
                dst = qtb if qk == 0 else ktb
                for hh in range(H):
                    g, rbase = hh // 2, (hh % 2) * 64
                    p.inc("scalar", lambda e, g=g, rbase=rbase, hh=hh, t=t, dst=dst: e.copy(
                        dst[g][rbase:rbase + 64, t * 128:(t + 1) * 128],
                        PS[0:64, 1024 + hh * 128:1024 + (hh + 1) * 128]), "act")
                p.wait("tensor", "act")
                p.wait("vector", "act")

        p.wait("sync", "pe")
        for r in range(4):
            p.inc("sync", lambda e, r=r: e.dma_start(
                masks_sb[:, r * 512:(r + 1) * 512], ctx.cd["locmaskp"].ap()[r]), "dma", 16)
            p.inc("sync", lambda e, r=r: e.dma_start(
                masks_sb[:, 2048 + r * 512:2048 + (r + 1) * 512], ctx.cd["locmask"].ap()[r]),
                "dma", 16)
        p.wait("vector", "dma")
        p.wait("scalar", "dma")
        for h in range(H):
            g, rbase = h // 2, (h % 2) * 64
            for w in range(4):
                akts = list(range(4 * (w - 1), 4 * w)) if w > 0 else []
                akts += list(range(4 * w, 4 * w + 4))
                first = True
                for j, akt in enumerate(akts):
                    cur = (akt >= 4 * w)
                    r = akt - 4 * w if cur else akt - 4 * (w - 1)
                    moff = (2048 + r * 512) if cur else (r * 512)
                    pso = 2048 + (j % 2) * 512
                    p.wait("tensor", "act")
                    p.wait("tensor", "dve")
                    p.inc("tensor", lambda e, g=g, rbase=rbase, akt=akt, w=w, pso=pso: e.matmul(
                        PS[:, pso:pso + 512],
                        ktb[g][rbase:rbase + 64, akt * 128:(akt + 1) * 128],
                        qtb[g][rbase:rbase + 64, w * 512:(w + 1) * 512],
                        start=True, stop=True), "pe")
                    p.wait("vector", "pe")
                    p.inc("vector", lambda e, moff=moff, pso=pso: e.tensor_tensor(
                        sc32[:, 0:512], PS[:, pso:pso + 512],
                        masks_sb[:, moff:moff + 512], ALU.add), "dve")
                    p.wait("scalar", "dve")
                    p.wait("scalar", "pe")
                    p.inc("scalar", lambda e: e.activation(
                        glb[:, 0:512], sc32[:, 0:512], AF.Exp, scale=8.0), "act")
                    p.wait("tensor", "act")
                    p.inc("tensor", lambda e, h=h, akt=akt, first=first, last=(j == len(akts) - 1):
                          e.matmul(PS[0:65, 3072:3584], vbuf[akt][:, h * 65:(h + 1) * 65],
                                   glb[:, 0:512], start=first, stop=last), "pe")
                    first = False
                p.wait("scalar", "pe")
                p.inc("scalar", lambda e: e.activation(
                    rowsb[0:1, 0:512], PS[64:65, 3072:3584], AF.Ln), "act")
                p.wait("scalar", "act")
                p.inc("scalar", lambda e: e.activation(
                    rowsb[0:1, 0:512], rowsb[0:1, 0:512], AF.Exp, scale=-1.0), "act")
                p.wait("tensor", "act")
                p.inc("tensor", lambda e: e.matmul(
                    PS[0:64, 3584:4096], ctx.ones_row[0:1, 0:64], rowsb[0:1, 0:512],
                    start=True, stop=True), "pe")
                p.wait("scalar", "pe")
                p.inc("scalar", lambda e: e.copy(rbc[0:64, 0:512], PS[0:64, 3584:4096]), "act")
                p.wait("vector", "act")
                p.inc("vector", lambda e, g=g, rbase=rbase, w=w: e.tensor_tensor(
                    oT[g][rbase:rbase + 64, w * 512:(w + 1) * 512],
                    PS[0:64, 3072:3584], rbc[0:64, 0:512], ALU.mult), "dve")
                p.wait("tensor", "dve")

        if lay == 0 and batch == 0:
            for nm, tiles in (("q0t", qtb), ("k0t", ktb), ("o0t", oT)):
                if nm in ctx.taps:
                    p.barrier(("sync",))
                    dram = ctx.tap_drams[nm]
                    for g in range(4):
                        for ch in range(4):
                            p.wait("vector", "st")
                            p.inc("vector", lambda e, g=g, ch=ch, tiles=tiles: e.tensor_copy(
                                sc32[:, 0:512], tiles[g][:, ch * 512:(ch + 1) * 512]), "dve")
                            p.wait("sync", "dve")
                            p.inc("sync", lambda e, g=g, ch=ch, dram=dram: e.dma_start(
                                dram.ap()[g * 128:(g + 1) * 128, ch * 512:(ch + 1) * 512],
                                sc32[:, 0:512]), "st", 16)
                            p.wait("sync", "st")
                    p.wait("gpsimd", "st")
        p.wait("gpsimd", "pe")
        p.wait("gpsimd", "act")
        for i in range(4):
            p.inc("gpsimd", lambda e, i=i, lay=lay: e.dma_start(
                wol_sb[i][:], ctx.d["L0_ffn_wo"].ap()[lay].rearrange(
                    "(it pp) o -> it pp o", pp=128)[i]), "wdma", 16)
        p.wait("tensor", "wdma")
        for fo in range(NF):
            for cch in range(4):
                cs = slice(cch * 512, cch * 512 + 512)
                pso = (cch % 4) * 512
                p.wait("tensor", "dve")
                for i in range(4):
                    p.inc("tensor", lambda e, fo=fo, cs=cs, i=i, pso=pso: e.matmul(
                        PS[:, pso:pso + 512], wol_sb[i][:, fo * 128:(fo + 1) * 128],
                        oT[i][:, cs], start=(i == 0), stop=(i == 3)), "pe")
                p.wait("vector", "pe")
                p.inc("vector", lambda e, fo=fo, cs=cs, pso=pso: e.tensor_tensor(
                    zT[fo][:, TOKOFF + cs.start:TOKOFF + cs.stop],
                    zT[fo][:, TOKOFF + cs.start:TOKOFF + cs.stop],
                    PS[:, pso:pso + 512], ALU.add), "dve")

        if f"zo{lay}" in ctx.taps and batch == 0:
            _tap_tiles(ctx, zT, f"zo{lay}", padded=True)
        emit_colvec(ctx, ctx.d["L0_ffn_ln_g"].ap()[lay].rearrange("(a e) -> a e", a=1), E, 0)
        emit_colvec(ctx, ctx.d["L0_ffn_ln_b"].ap()[lay].rearrange("(a e) -> a e", a=1), E, 8)
        emit_colvec(ctx, ctx.d["L0_ffn_b2"].ap()[lay].rearrange("(a e) -> a e", a=1), E, 16)
        emit_colvec(ctx, ctx.d["L0_ffn_b1"].ap()[lay][0:2048].rearrange("(a e) -> a e", a=1),
                    2048, 24)
        emit_colvec(ctx, ctx.d["L0_ffn_b1"].ap()[lay][2048:2730].rearrange("(a e) -> a e", a=1),
                    682, 40)
        emit_colvec(ctx, ctx.d["L0_ffn_b1"].ap()[lay][2730:4778].rearrange("(a e) -> a e", a=1),
                    2048, 46)
        emit_colvec(ctx, ctx.d["L0_ffn_b1"].ap()[lay][4778:5460].rearrange("(a e) -> a e", a=1),
                    682, 62)
        offs = [TOKOFF - (1 if f >= 4 else 0) for f in range(NF)]
        emit_ln(ctx, zT, hnT, gamma_off=0, beta_off=8, src_offs=offs)
        for tcn in range(4):
            TC = slice(tcn * 512, tcn * 512 + 512)
            evm = {}

            def ld_w1(c, lay=lay):
                cw = 128 if c < 21 else 42
                return [lambda e, c=c, cw=cw, lay=lay: e.dma_start(
                    w1a_sb[c % 2].ap().rearrange("p (ft c) -> p ft c", c=128)[:, :, 0:cw],
                    ctx.d["w1s"].ap()[lay].rearrange("(ft pp) hh -> pp ft hh", pp=128)
                    [:, :, c * 128:c * 128 + cw]),
                    lambda e, c=c, cw=cw, lay=lay: e.dma_start(
                    w1g_sb[c % 2].ap().rearrange("p (ft c) -> p ft c", c=128)[:, :, 0:cw],
                    ctx.d["w1s"].ap()[lay].rearrange("(ft pp) hh -> pp ft hh", pp=128)
                    [:, :, 2730 + c * 128:2730 + c * 128 + cw])]

            def use_w1(c, TC=TC):
                cw = 128 if c < 21 else 42
                wa, wg = w1a_sb[c % 2], w1g_sb[c % 2]
                po = (c % 2) * 2048
                if c - 2 in evm:
                    p.wait("tensor", "dve", evm[c - 2][0])
                    p.wait("tensor", "act", evm[c - 2][1])
                for f in range(NF):
                    p.inc("tensor", lambda e, c=c, cw=cw, f=f, TC=TC, wa=wa, po=po: e.matmul(
                        PS[0:cw, po:po + 512], wa[:, f * 128:f * 128 + cw],
                        hnT[f][:, TOKOFF + TC.start:TOKOFF + TC.stop],
                        start=(f == 0), stop=(f == NF - 1)), "pe")
                    p.inc("tensor", lambda e, c=c, cw=cw, f=f, TC=TC, wg=wg, po=po: e.matmul(
                        PS[0:cw, po + 1024:po + 1536], wg[:, f * 128:f * 128 + cw],
                        hnT[f][:, TOKOFF + TC.start:TOKOFF + TC.stop],
                        start=(f == 0), stop=(f == NF - 1)), "pe")
                pm = p.mark("pe")
                p.wait("scalar", "pe", pm)
                if c - 1 in evm:
                    p.wait("scalar", "dve", evm[c - 1][0])
                p.inc("scalar", lambda e, c=c, cw=cw, po=po: e.activation(
                    glb[0:cw, 0:512], PS[0:cw, po + 1024:po + 1536], AF.Gelu,
                    bias=ctx.cols[0:cw, 46 + c:47 + c]), "act")
                am = p.mark("act")
                p.wait("vector", "act", am)
                p.inc("vector", lambda e, c=c, cw=cw, po=po: e.scalar_tensor_tensor(
                    glub[c][0:cw, 0:512], PS[0:cw, po:po + 512], ctx.cols[0:cw, 24 + c:25 + c],
                    glb[0:cw, 0:512], ALU.add, ALU.mult), "dve")
                evm[c] = (p.mark("dve"), am)

            p.wait("tensor", "act")
            p.wait("tensor", "dve")
            stream_weights(p, NC2, ld_w1, use_w1, "w2a")
            p.wait("tensor", "dve")

            def ld_w2(c, lay=lay):
                cw = 128 if c < 21 else 42
                return lambda e, c=c, cw=cw, lay=lay: e.dma_start(
                    w2_sb[c % 2][0:cw, :], ctx.d["w2s"].ap()[lay][c * 128:c * 128 + cw, :])

            def use_w2(c):
                cw = 128 if c < 21 else 42
                wc = w2_sb[c % 2]
                for fo in range(NF):
                    p.inc("tensor", lambda e, c=c, cw=cw, fo=fo, wc=wc: e.matmul(
                        PS[:, fo * 512:(fo + 1) * 512], wc[0:cw, fo * 128:(fo + 1) * 128],
                        glub[c][0:cw, 0:512], start=(c == 0), stop=(c == NC2 - 1)), "pe")

            stream_weights(p, NC2, ld_w2, use_w2, "w2b")
            p.wait("scalar", "pe")
            p.wait("vector", "pe")
            for fo in range(NF):
                p.inc("scalar", lambda e, fo=fo: e.activation(
                    sc32[:, 0:512], PS[:, fo * 512:(fo + 1) * 512], AF.Identity,
                    bias=ctx.cols[:, 16 + fo:17 + fo]), "act")
                p.wait("vector", "act")
                p.inc("vector", lambda e, fo=fo, TC=TC: e.tensor_tensor(
                    zT[fo][:, TOKOFF + TC.start:TOKOFF + TC.stop],
                    zT[fo][:, TOKOFF + TC.start:TOKOFF + TC.stop],
                    sc32[:, 0:512], ALU.add), "dve")
                p.wait("scalar", "dve")

        if f"z{lay}" in ctx.taps and batch == 0:
            p.barrier(("tensor", "scalar", "vector", "sync"))
            _tap_tiles(ctx, zT, f"z{lay}", padded=True)

    p.barrier(("sync",))
    for f in range(NF):
        p.inc("sync", lambda e, f=f: e.dma_start(
            ctx.xT[f][:, tokc(0, SEQ)], dpark.ap()[f * 128:(f + 1) * 128, :]), "dma", 16)
    p.wait("vector", "dma")
    for f in range(NF):
        p.inc("vector", lambda e, f=f: e.tensor_tensor(
            ctx.xT[f][:, tokc(0, SEQ)], ctx.xT[f][:, tokc(0, SEQ)],
            zT[f][:, tokc(0, SEQ)], ALU.add), "dve")
    p.barrier(("tensor", "scalar", "vector", "sync", "gpsimd"))


BLOCK_EMITTERS["l0_ffn"] = emit_l0_ffn



def prep_core_inputs(inputs, core, nb=2, boff=None):
    x = inputs["L0_attn_X"]
    b0 = core * nb if boff is None else boff
    xs = np.ascontiguousarray(
        np.transpose(x[b0:b0 + nb], (0, 2, 1))).astype(np.float32)
    m = {"xin": xs}
    for name, shape in WEIGHT_SHAPES.items():
        if name in inputs:
            m[name] = np.ascontiguousarray(np.asarray(inputs[name], dtype=np.float32))
    m["bq_t"] = np.ascontiguousarray(np.asarray(inputs["L0_attn_bq"], np.float32).T)
    m["bk_t"] = np.ascontiguousarray(np.asarray(inputs["L0_attn_bk"], np.float32).T)
    m["lam11"] = np.asarray(inputs["L0_attn_lam"], np.float32).reshape(1, 1)
    return m


ALL_BLOCKS = ("l0_attn", "l0_ffn", "l1_attn", "l1_ffn")


def kernel(**inputs):
    nc, _ = build_program(blocks=ALL_BLOCKS, taps=(), nb=2)
    in_maps = [prep_core_inputs(inputs, c) for c in range(NCORES)]
    res = run_bass_kernel_spmd(nc, in_maps, core_ids=list(range(NCORES)))
    outs = []
    for c in range(NCORES):
        o = np.asarray(res.results[c]["xout"])
        outs.append(np.transpose(o, (0, 2, 1)))
    return np.concatenate(outs, axis=0).astype(np.float32)


# version 10
# speedup vs baseline: 1.5136x; 1.0631x over previous
import contextlib
import numpy as np
import ml_dtypes
import concourse.bass as bass
import concourse.mybir as mybir
from concourse.bass_utils import run_bass_kernel_spmd

F32 = mybir.dt.float32
BF16 = mybir.dt.bfloat16
AF = mybir.ActivationFunctionType
ALU = mybir.AluOpType

NCORES = 8
SEQ = 2048
E = 1024
NF = 8
TOKOFF = 2
NPAD = TOKOFF + SEQ + 2
LN_EPS = 1e-5
KB = 1024

XT_STRIDE = 8224
R_XT = 0
R_LNS = 65792
R_CON = 99072
R_WAR = 101632
R_BIG = 126208


class Prog:
    ENGINES = ("sync", "tensor", "vector", "scalar", "gpsimd")

    def __init__(self, nc):
        self.nc = nc
        self.streams = {e: [] for e in self.ENGINES}
        self.sems = {}
        self.counts = {}
        self._last_dma = {}
        self._es = contextlib.ExitStack()

    def sem(self, name):
        if name not in self.sems:
            self.sems[name] = self._es.enter_context(self.nc.semaphore(name))
            self.counts[name] = 0
        return self.sems[name]

    def emit(self, engine, fn):
        self.streams[engine].append(fn)

    def wait(self, engine, semname, value=None):
        self._last_dma[engine] = None
        if semname in self.DMA_SEMS:
            for e2 in list(self._last_dma):
                if self._last_dma.get(e2) == semname:
                    self._last_dma[e2] = None
        s = self.sem(semname)
        t = self.counts[semname] if value is None else value
        if t <= 0:
            return
        self.emit(engine, lambda eng, s=s, t=t: eng.wait_ge(s, t))

    DMA_SEMS = ("dma", "wdma", "st", "rt", "wdma2", "st2")

    def inc(self, engine, fn, semname, amount=1):
        s = self.sem(semname)
        if semname in self.DMA_SEMS:
            key = (engine, semname)
            if self._last_dma.get(engine) != semname:
                t = self.counts[semname]
                if t > 0:
                    self.emit(engine, lambda eng, s=s, t=t: eng.wait_ge(s, t))
            self._last_dma[engine] = semname
        else:
            self._last_dma[engine] = None
        self.counts[semname] += amount
        self.emit(engine, lambda eng, s=s, a=amount, f=fn: f(eng).then_inc(s, a))

    def run(self, fn, engine):
        self._last_dma[engine] = None
        self.emit(engine, fn)

    def mark(self, semname):
        self.sem(semname)
        return self.counts[semname]

    def barrier(self, engines=("tensor", "vector", "scalar", "gpsimd", "sync"),
                sems=("pe", "dve", "act", "gp", "dma", "wdma", "st", "wdma2", "st2")):
        for e in ("pe", "dve", "act", "gp"):
            self.sem(e)
        for eng in engines:
            for s in sems:
                self.wait(eng, s)

    def build(self):
        with self.nc.Block() as block:
            for e in self.ENGINES:
                if self.streams[e]:
                    getattr(block, e)(lambda eng, ops=tuple(self.streams[e]): [f(eng) for f in ops])
        return self.nc


class SbufMap:
    BASE = 16512

    def __init__(self, nc, limit=212500):
        self.nc = nc
        self.limit = limit
        self.cursor = 0

    def alloc(self, name, shape, dtype, at=None):
        dsize = 2 if dtype == BF16 else 4
        nbytes = shape[1] * dsize
        if at is None:
            at = (self.cursor + 31) & ~31
            self.cursor = at + nbytes
        assert at + nbytes <= self.limit, f"SBUF overflow at {name}: {at + nbytes}"
        return self.nc.alloc_sbuf_tensor_at(name, list(shape), dtype, offset=self.BASE + at)

    def arena(self, nbytes):
        at = (self.cursor + 31) & ~31
        self.cursor = at + nbytes
        assert self.cursor <= self.limit, f"SBUF overflow (arena): {self.cursor}"
        return at


class Ctx:
    pass


def tokc(start, n):
    return slice(TOKOFF + start, TOKOFF + start + n)


def emit_ln(ctx, src_tiles, dst_tiles, gamma_off=None, beta_off=None,
            resid=False, src_padded=True, src_bf=False, src_offs=None):
    p = ctx.p
    PS = ctx.PS
    so = TOKOFF if src_padded else 0
    n = SEQ
    offs = src_offs if src_offs is not None else [so] * NF
    sq, misc = ctx.ln_sq, ctx.ln_misc
    ones_src = ctx.ones_col_bf if src_bf else ctx.ones_col

    p.barrier(("tensor", "scalar", "vector"))
    for f in range(NF):
        p.wait("scalar", "pe")
        p.inc("scalar", lambda e, f=f: e.activation(
            sq[:, 0:n], src_tiles[f][:, offs[f]:offs[f] + n], AF.Square), "act")
        am = p.mark("act")
        p.wait("tensor", "act", am)
        for cch in range(4):
            cs = slice(cch * 512, cch * 512 + 512)
            scs = slice(offs[f] + cch * 512, offs[f] + cch * 512 + 512)
            p.inc("tensor", lambda e, f=f, cs=cs, scs=scs: e.matmul(
                PS[0:1, cs], ones_src[:], src_tiles[f][:, scs],
                start=(f == 0), stop=(f == NF - 1)), "pe")
            p.inc("tensor", lambda e, f=f, cs=cs: e.matmul(
                PS[0:1, 2048 + cs.start:2048 + cs.stop], ctx.ones_col[:], sq[:, cs],
                start=(f == 0), stop=(f == NF - 1)), "pe")
    p.wait("vector", "pe")
    p.inc("vector", lambda e: e.tensor_scalar(
        misc[0:1, 0:n], PS[0:1, 0:n], 1.0 / E, None, ALU.mult), "dve")
    p.inc("vector", lambda e: e.tensor_scalar(
        misc[0:1, 2048:2048 + n], PS[0:1, 2048:2048 + n], 1.0 / E, None, ALU.mult), "dve")
    p.wait("vector", "dve")
    p.inc("vector", lambda e: e.tensor_tensor(
        misc[0:1, 4096:4096 + n], misc[0:1, 0:n], misc[0:1, 0:n], ALU.mult), "dve")
    p.wait("vector", "dve")
    p.inc("vector", lambda e: e.tensor_tensor(
        misc[0:1, 4096:4096 + n], misc[0:1, 2048:2048 + n], misc[0:1, 4096:4096 + n],
        ALU.subtract), "dve")
    p.wait("scalar", "dve")
    p.inc("scalar", lambda e: e.activation(
        misc[0:1, 2048:2048 + n], misc[0:1, 4096:4096 + n], AF.Sqrt, bias=ctx.eps_s[0:1, 0:1]), "act")
    p.wait("vector", "act")
    p.inc("vector", lambda e: e.reciprocal(
        misc[0:1, 2048:2048 + n], misc[0:1, 2048:2048 + n]), "dve")
    p.wait("vector", "dve")
    p.inc("vector", lambda e: e.tensor_tensor(
        misc[0:1, 4096:4096 + n], misc[0:1, 0:n], misc[0:1, 2048:2048 + n], ALU.mult), "dve")
    p.wait("vector", "dve")
    p.inc("vector", lambda e: e.tensor_scalar(
        misc[0:1, 4096:4096 + n], misc[0:1, 4096:4096 + n], -1.0, None, ALU.mult), "dve")
    p.wait("tensor", "dve")
    for cch in range(4):
        cs = slice(cch * 512, cch * 512 + 512)
        p.inc("tensor", lambda e, cs=cs: e.matmul(
            PS[:, cs], ctx.ones_row[0:1, :], misc[0:1, 2048 + cs.start:2048 + cs.stop],
            start=True, stop=True), "pe")
        p.inc("tensor", lambda e, cs=cs: e.matmul(
            PS[:, 2048 + cs.start:2048 + cs.stop], ctx.ones_row[0:1, :],
            misc[0:1, 4096 + cs.start:4096 + cs.stop], start=True, stop=True), "pe")
    p.wait("vector", "pe")
    for f in range(NF):
        p.wait("vector", "act")
        p.inc("vector", lambda e, f=f: e.tensor_tensor(
            sq[:, 0:n], src_tiles[f][:, offs[f]:offs[f] + n], PS[:, 0:n], ALU.mult), "dve")
        p.wait("vector", "dve")
        if gamma_off is None and not resid:
            p.inc("vector", lambda e, f=f: e.tensor_tensor(
                dst_tiles[f][:, tokc(0, n)], sq[:, 0:n], PS[:, 2048:2048 + n], ALU.add), "dve")
            p.wait("vector", "dve")
        else:
            p.inc("vector", lambda e, f=f: e.tensor_tensor(
                sq[:, 0:n], sq[:, 0:n], PS[:, 2048:2048 + n], ALU.add), "dve")
            p.wait("scalar", "dve")
            if gamma_off is not None:
                p.inc("scalar", lambda e, f=f: e.activation(
                    sq[:, 0:n], sq[:, 0:n], AF.Identity,
                    bias=ctx.cols[:, beta_off + f:beta_off + f + 1],
                    scale=ctx.cols[:, gamma_off + f:gamma_off + f + 1]), "act")
                p.wait("vector", "act")
            if resid:
                p.inc("vector", lambda e, f=f: e.tensor_tensor(
                    dst_tiles[f][:, tokc(0, n)], dst_tiles[f][:, tokc(0, n)],
                    sq[:, 0:n], ALU.add), "dve")
            else:
                p.inc("vector", lambda e, f=f: e.tensor_copy(
                    dst_tiles[f][:, tokc(0, n)], sq[:, 0:n]), "dve")
            p.wait("vector", "dve")
    p.barrier(("tensor", "scalar", "vector"))


def emit_colvec(ctx, dram_ap, n, dst_off, scale=None, dst=None):
    p = ctx.p
    if dst is None:
        dst = ctx.cols
    assert n <= 2048
    nt = (n + 127) // 128
    p.barrier(("sync", "tensor", "scalar"))
    p.inc("sync", lambda e: e.dma_start(ctx.row_tmp[0:1, 0:n], dram_ap), "dma", 16)
    p.wait("tensor", "dma")
    for i in range(nt):
        w = min(128, n - i * 128)
        p.inc("tensor", lambda e, i=i, w=w: e.transpose(
            ctx.PS[0:w, 3584 + i:3584 + i + 1], ctx.row_tmp[0:1, i * 128:i * 128 + w],
            ctx.ident_f32[0:1, 0:1]), "pe")
    p.wait("scalar", "pe")
    if scale is None:
        p.inc("scalar", lambda e: e.copy(
            dst[:, dst_off:dst_off + nt], ctx.PS[:, 3584:3584 + nt]), "act")
    else:
        p.inc("scalar", lambda e: e.activation(
            dst[:, dst_off:dst_off + nt], ctx.PS[:, 3584:3584 + nt], AF.Identity,
            bias=0.0, scale=scale), "act")
    p.wait("tensor", "act")


def emit_bcast_row(ctx, src_row_ap, dst_tile_ap, n, nparts=128):
    p = ctx.p
    p.wait("tensor", "act")
    p.wait("tensor", "dve")
    nch = (n + 511) // 512
    for cch in range(nch):
        w = min(512, n - cch * 512)
        p.inc("tensor", lambda e, cch=cch, w=w: e.matmul(
            ctx.PS[0:nparts, 2048 + cch * 512:2048 + cch * 512 + w],
            ctx.ones_row[0:1, 0:nparts],
            src_row_ap[0:1, cch * 512:cch * 512 + w], start=True, stop=True), "pe")
    p.wait("scalar", "pe")
    p.inc("scalar", lambda e: e.copy(dst_tile_ap, ctx.PS[0:nparts, 2048:2048 + n]), "act")
    p.wait("tensor", "act")
    p.wait("sync", "pe")
    p.wait("gpsimd", "pe")



def stream_weights(p, n, loadfn, consumefn, sembase, depth=2):
    semtot, pem = {}, {}

    def issue(c):
        sem = f"{sembase}{c % depth}"
        fns = loadfn(c)
        if not isinstance(fns, (list, tuple)):
            fns = [fns]
        for fn in fns:
            p.inc("gpsimd", fn, sem, 16)
        semtot[c] = p.mark(sem)

    p.wait("gpsimd", "pe")
    for c in range(min(depth, n)):
        issue(c)
    for c in range(n):
        p.wait("tensor", f"{sembase}{c % depth}", semtot[c])
        consumefn(c)
        pem[c] = p.mark("pe")
        if c + depth < n:
            p.wait("gpsimd", "pe", pem[c])
            issue(c + depth)


def emit_l0_attn(ctx, batch):
    p, m = ctx.p, ctx.m
    PS = ctx.PS
    H = 16
    B0 = ctx.BIG

    xnT = [m.alloc(f"b1_xnT{f}", [128, NPAD], BF16, at=B0 + f * 4128) for f in range(NF)]
    vbuf = [m.alloc(f"b1_v{t}", [128, 16 * 65], BF16, at=R_LNS + t * 2080) for t in range(16)]
    QOFF = B0 + 8 * 4128
    qt = m.alloc("b1_qt", [128, 2048], BF16, at=QOFF)
    kt = m.alloc("b1_kt", [128, 2048], BF16, at=QOFF + 4096)
    expb = m.alloc("b1_exp", [128, 2048], BF16, at=QOFF + 8192)
    rb = m.alloc("b1_rb", [64, 1024], F32, at=QOFF + 8192)
    tmb = m.alloc("b1_tmb", [64, 1024], F32, at=QOFF + 12288)
    rows2 = m.alloc("b1_rows2", [128, 1024], F32, at=QOFF + 16384)
    bv_bcast = m.alloc("b1_bvb", [128, 1024], F32, at=QOFF + 12288)
    OOFF = QOFF + 20480
    ocatT = [m.alloc(f"b1_ocat{g}", [128, 2048], BF16, at=OOFF + g * 4096) for g in range(NF)]
    ybuf = [m.alloc(f"b1_y{f}", [128, 2048], BF16, at=B0 + f * 4128) for f in range(NF)]
    ctx.bv_bcast = bv_bcast

    W0 = ctx.WARENA
    wv_sb = [m.alloc(f"b1_wv{f}", [128, 1024], BF16, at=W0 + f * 2048) for f in range(NF)]
    wq_sb = m.alloc("b1_wq", [128, 1024], BF16, at=W0 + 16 * KB)
    wk_sb = m.alloc("b1_wk", [128, 1024], BF16, at=W0 + 18 * KB)
    wo_sb = [m.alloc(f"b1_wo{t}", [128, 1024], BF16, at=W0 + t * 2048) for t in range(NF)]

    emit_ln(ctx, ctx.xT, xnT)
    if "xn" in ctx.taps and batch == 0:
        _tap_tiles(ctx, xnT, "xn", padded=True)
    if "xnb1" in ctx.taps and batch == 1:
        _tap_tiles(ctx, xnT, "xnb1", padded=True)

    emit_colvec(ctx, ctx.d["L0_attn_bo"].ap().rearrange("(a e) -> a e", a=1), E, 0)
    emit_colvec(ctx, ctx.d["L0_attn_gamma"].ap().rearrange("(a e) -> a e", a=1), E, 8, scale=0.2)
    emit_colvec(ctx, ctx.d["L0_attn_beta"].ap().rearrange("(a e) -> a e", a=1), E, 16, scale=0.2)
    p.inc("sync", lambda e: e.dma_start(ctx.lam_s[0:1, 0:1], ctx.d["lam11"].ap()), "dma", 16)
    p.inc("sync", lambda e: e.dma_start(ctx.bq_col[:, 0:H], ctx.d["bq_t"].ap()), "dma", 16)
    p.inc("sync", lambda e: e.dma_start(ctx.bk_col[:, 0:H], ctx.d["bk_t"].ap()), "dma", 16)
    p.wait("sync", "pe")
    p.inc("sync", lambda e: e.dma_start(ctx.row_tmp[0:1, 0:E],
                                        ctx.d["L0_attn_bv"].ap().rearrange("h d -> (h d)").rearrange("(a e) -> a e", a=1)), "dma", 16)
    p.wait("tensor", "dma")
    p.wait("vector", "dma")
    emit_bcast_row(ctx, ctx.row_tmp, ctx.bv_bcast[:, 0:E], E)

    p.wait("gpsimd", "pe")
    for f in range(NF):
        p.inc("gpsimd", lambda e, f=f: e.dma_start(
            wv_sb[f].ap().rearrange("p (h d) -> p h d", d=64),
            ctx.d["L0_attn_Wv"].ap().rearrange("h (ft pp) d -> ft pp h d", pp=128)[f]),
            "wdma", 16)
    for t in range(15):
        p.run(lambda e, t=t: e.memset(vbuf[t][:], 1.0), "gpsimd")
    p.inc("gpsimd", lambda e: e.memset(vbuf[15][:], 1.0), "gp", 1)
    p.wait("tensor", "wdma")
    p.wait("tensor", "gp")
    p.wait("tensor", "act")
    for t in range(16):
        pso = 2048 + (t % 2) * 1024
        p.wait("tensor", "dve")
        for cch in range(2):
            cs = slice(cch * 512, cch * 512 + 512)
            for f in range(NF):
                p.inc("tensor", lambda e, t=t, cs=cs, f=f, pso=pso: e.matmul(
                    PS[:, pso + cs.start:pso + cs.stop],
                    xnT[f][:, TOKOFF + t * 128:TOKOFF + (t + 1) * 128],
                    wv_sb[f][:, cs], start=(f == 0), stop=(f == NF - 1)), "pe")
        p.wait("vector", "pe")
        p.inc("vector", lambda e, t=t, pso=pso: e.tensor_tensor(
            vbuf[t].ap().rearrange("p (h c) -> p h c", c=65)[:, :, 0:64],
            PS[:, pso:pso + E].rearrange("p (h c) -> p h c", c=64),
            ctx.bv_bcast[:, 0:E].rearrange("p (h c) -> p h c", c=64), ALU.add), "dve")
    p.barrier()

    for h in range(H):
        p.wait("gpsimd", "pe")
        p.inc("gpsimd", lambda e, h=h: e.dma_start(
            wq_sb.ap().rearrange("p (ft c) -> p ft c", c=128),
            ctx.d["L0_attn_Wq"].ap()[h].rearrange("(ft pp) c -> pp ft c", pp=128)),
            "wdma", 16)
        p.inc("gpsimd", lambda e, h=h: e.dma_start(
            wk_sb.ap().rearrange("p (ft c) -> p ft c", c=128),
            ctx.d["L0_attn_Wk"].ap()[h].rearrange("(ft pp) c -> pp ft c", pp=128)),
            "wdma", 16)
        p.wait("tensor", "wdma")
        for cch in range(4):
            cs = slice(cch * 512, cch * 512 + 512)
            p.wait("tensor", "act")
            for f in range(NF):
                p.inc("tensor", lambda e, cs=cs, f=f: e.matmul(
                    PS[:, 2048:2560], wq_sb[:, f * 128:(f + 1) * 128],
                    xnT[f][:, TOKOFF + cs.start:TOKOFF + cs.stop],
                    start=(f == 0), stop=(f == NF - 1)), "pe")
                p.inc("tensor", lambda e, cs=cs, f=f: e.matmul(
                    PS[:, 2560:3072], wk_sb[:, f * 128:(f + 1) * 128],
                    xnT[f][:, TOKOFF + cs.start:TOKOFF + cs.stop],
                    start=(f == 0), stop=(f == NF - 1)), "pe")
            p.wait("scalar", "pe")
            p.inc("scalar", lambda e, h=h, cs=cs: e.activation(
                qt[:, cs], PS[:, 2048:2560], AF.Identity, bias=ctx.bq_col[:, h:h + 1]), "act")
            p.inc("scalar", lambda e, h=h, cs=cs: e.activation(
                kt[:, cs], PS[:, 2560:3072], AF.Identity, bias=ctx.bk_col[:, h:h + 1]), "act")
        p.wait("tensor", "act")

        for qc in range(2):
            qs0 = qc * 1024
            for br in range(2):
                oo = br * 1024
                dp = slice(0, 64) if br == 0 else slice(64, 128)
                expmarks = {}

                def emit_scores(kti, dp=dp, qs0=qs0):
                    so_ = 2048 + (kti % 2) * 1024
                    if kti - 2 in expmarks:
                        p.wait("tensor", "act", expmarks[kti - 2])
                    elif kti < 2:
                        p.wait("tensor", "act", expmarks.get("pre", 0))
                    for sc in range(2):
                        cs = slice(sc * 512, sc * 512 + 512)
                        p.inc("tensor", lambda e, kti=kti, cs=cs, qs0=qs0, so_=so_, dp=dp: e.matmul(
                            PS[:, so_ + cs.start:so_ + cs.stop],
                            kt[dp, kti * 128:(kti + 1) * 128],
                            qt[dp, qs0 + cs.start:qs0 + cs.stop],
                            start=True, stop=True), "pe")

                def emit_exp(kti):
                    so_ = 2048 + (kti % 2) * 1024
                    eo = (kti % 2) * 1024
                    p.wait("scalar", "pe")
                    p.inc("scalar", lambda e, so_=so_, eo=eo: e.activation(
                        expb[:, eo:eo + 1024], PS[:, so_:so_ + 1024], AF.Exp,
                        scale=0.125), "act")
                    expmarks[kti] = p.mark("act")

                def emit_pv(kti, h=h, oo=oo):
                    eo = (kti % 2) * 1024
                    p.wait("tensor", "act")
                    for sc in range(2):
                        cs = slice(sc * 512, sc * 512 + 512)
                        p.inc("tensor", lambda e, h=h, kti=kti, cs=cs, oo=oo, eo=eo: e.matmul(
                            PS[0:65, oo + cs.start:oo + cs.stop],
                            vbuf[kti][:, h * 65:(h + 1) * 65],
                            expb[:, eo + cs.start:eo + cs.stop],
                            start=(kti == 0), stop=(kti == 15)), "pe")

                expmarks["pre"] = p.mark("act")
                emit_scores(0)
                for kti in range(16):
                    emit_exp(kti)
                    if kti < 15:
                        emit_scores(kti + 1)
                    emit_pv(kti)
            p.wait("vector", "pe")
            g, rbase = h // 2, (h % 2) * 64
            for sc in range(2):
                cs = slice(sc * 512, sc * 512 + 512)
                p.wait("scalar", "pe")
                p.inc("scalar", lambda e, cs=cs: e.activation(
                    rows2[0:1, 0:512], PS[64:65, cs], AF.Ln), "act")
                p.inc("scalar", lambda e, cs=cs: e.activation(
                    rows2[0:1, 512:1024], PS[64:65, 1024 + cs.start:1024 + cs.stop],
                    AF.Ln), "act")
                p.wait("scalar", "act")
                p.inc("scalar", lambda e: e.activation(
                    rows2[0:1, 0:512], rows2[0:1, 0:512], AF.Exp, scale=-1.0), "act")
                p.inc("scalar", lambda e: e.activation(
                    rows2[0:1, 512:1024], rows2[0:1, 512:1024], AF.Exp, scale=-1.0), "act")
                p.wait("vector", "act")
                p.inc("vector", lambda e: e.tensor_scalar(
                    rows2[0:1, 512:1024], rows2[0:1, 512:1024], ctx.lam_s[0:1, 0:1], -1.0,
                    ALU.mult, ALU.mult), "dve")
                p.wait("tensor", "dve")
                p.wait("tensor", "act")
                p.inc("tensor", lambda e: e.matmul(
                    PS[0:64, 2048:2560], ctx.ones_row[0:1, 0:64],
                    rows2[0:1, 0:512], start=True, stop=True), "pe")
                p.inc("tensor", lambda e: e.matmul(
                    PS[0:64, 2560:3072], ctx.ones_row[0:1, 0:64],
                    rows2[0:1, 512:1024], start=True, stop=True), "pe")
                p.wait("scalar", "pe")
                p.inc("scalar", lambda e: e.copy(rb[0:64, 0:1024], PS[0:64, 2048:3072]), "act")
                p.wait("vector", "act")
                p.inc("vector", lambda e, cs=cs: e.tensor_tensor(
                    tmb[0:64, 0:512], PS[0:64, cs], rb[0:64, 0:512], ALU.mult), "dve")
                p.wait("vector", "dve")
                p.inc("vector", lambda e, cs=cs: e.tensor_tensor(
                    tmb[0:64, 512:1024], PS[0:64, 1024 + cs.start:1024 + cs.stop],
                    rb[0:64, 512:1024], ALU.mult), "dve")
                p.wait("vector", "dve")
                p.inc("vector", lambda e, cs=cs, g=g, rbase=rbase, qs0=qs0: e.tensor_tensor(
                    ocatT[g][rbase:rbase + 64, qs0 + cs.start:qs0 + cs.stop],
                    tmb[0:64, 0:512], tmb[0:64, 512:1024], ALU.add), "dve")
                p.wait("vector", "dve")
                p.wait("tensor", "dve")

    p.barrier()
    p.wait("gpsimd", "pe")
    p.wait("gpsimd", "dve")
    p.wait("gpsimd", "act")
    for it in range(NF):
        p.inc("gpsimd", lambda e, it=it: e.dma_start(
            wo_sb[it][:], ctx.d["L0_attn_Wo"].ap().rearrange(
                "(it2 pp) e -> it2 pp e", pp=128)[it]), "wdma", 16)
    p.wait("tensor", "wdma")
    for fo in range(NF):
        for cch in range(4):
            cs = slice(cch * 512, cch * 512 + 512)
            pso = (cch % 4) * 512
            p.wait("tensor", "act")
            for it in range(NF):
                p.inc("tensor", lambda e, fo=fo, cs=cs, it=it, pso=pso: e.matmul(
                    PS[:, pso:pso + 512], wo_sb[it][:, fo * 128:(fo + 1) * 128],
                    ocatT[it][:, cs], start=(it == 0), stop=(it == NF - 1)), "pe")
            p.wait("scalar", "pe")
            p.inc("scalar", lambda e, fo=fo, cs=cs, pso=pso: e.activation(
                ybuf[fo][:, cs], PS[:, pso:pso + 512], AF.Identity,
                bias=ctx.cols[:, fo:fo + 1]), "act")
    p.wait("tensor", "act")
    p.wait("vector", "act")
    if "y" in ctx.taps and batch == 0:
        _tap_tiles(ctx, ybuf, "y", padded=False)
    if "yb1" in ctx.taps and batch == 1:
        _tap_tiles(ctx, ybuf, "yb1", padded=False)

    emit_ln(ctx, ybuf, ctx.xT, gamma_off=8, beta_off=16, resid=True,
            src_padded=False, src_bf=True)



def emit_l1_attn(ctx, batch):
    p, m = ctx.p, ctx.m
    PS = ctx.PS
    H = 16
    B0 = ctx.BIG
    NT = 16

    xnT = [m.alloc(f"b3_xnT{f}", [128, NPAD], BF16, at=B0 + f * 4128) for f in range(NF)]
    scb = m.alloc("b3_scb", [128, 2048], F32, at=B0 + 33024)
    pb = m.alloc("b3_pb", [128, 2048], F32, at=B0 + 41216)
    qf = m.alloc("b3_qf", [128, 192], F32, at=B0 + 49408)
    rvb = m.alloc("b3_rvb", [128, 512], F32, at=B0 + 50176)
    ptb = m.alloc("b3_ptb", [128, 256], BF16, at=B0 + 52224)
    sums = m.alloc("b3_sums", [128, 8], F32, at=B0 + 52736)
    OOFF = B0 + 52768
    ocatT = [m.alloc(f"b3_ocat{g}", [128, 2048], BF16, at=OOFF + g * 4096) for g in range(NF)]

    W0 = ctx.WARENA
    wqkv_sb = [m.alloc(f"b3_wq{f}", [128, 192], BF16, at=W0 + f * 384) for f in range(NF)]
    brow = m.alloc("b3_brow", [1, 192], F32, at=W0 + 3072)
    bb = m.alloc("b3_bb", [128, 192], F32, at=W0 + 3872)
    qaugT = m.alloc("b3_qaug", [128, 2048], BF16, at=W0 + 4672)
    kaugT = m.alloc("b3_kaug", [128, 2048], BF16, at=W0 + 8768)
    vh = m.alloc("b3_vh", [128, 16 * 65], BF16, at=W0 + 12864)
    wout_sb = [m.alloc(f"b3_wo{t}", [128, 1024], BF16, at=W0 + t * 2048) for t in range(NF)]

    emit_ln(ctx, ctx.xT, xnT)
    if "xn3" in ctx.taps and batch == 1:
        _tap_tiles(ctx, xnT, "xn3", padded=True)
    emit_colvec(ctx, ctx.d["L1_attn_b_out"].ap().rearrange("(a e) -> a e", a=1), E, 0)

    for h in range(H):
        p.wait("gpsimd", "pe")
        p.wait("gpsimd", "dve")
        for f in range(NF):
            p.inc("gpsimd", lambda e, f=f, h=h: e.dma_start(
                wqkv_sb[f].ap().rearrange("p (c d) -> p c d", d=64),
                ctx.d["L1_attn_w_qkv"].ap().rearrange("(ft pp) c -> ft pp c", pp=128)[f]
                .rearrange("p (c e) -> p c e", e=1024)[:, :, h * 64:(h + 1) * 64]), "wdma", 16)
        p.wait("sync", "pe")
        p.inc("sync", lambda e, h=h: e.dma_start(
            brow.ap().rearrange("a (c d) -> a c d", d=64),
            ctx.d["L1_attn_b_qkv"].ap().rearrange("(a c e) -> a c e", a=1, e=1024)
            [:, :, h * 64:(h + 1) * 64]), "dma", 16)
        p.wait("tensor", "dma")
        p.wait("tensor", "act")
        p.inc("tensor", lambda e: e.matmul(
            PS[:, 512:704], ctx.ones_row[0:1, :], brow[0:1, 0:192],
            start=True, stop=True), "pe")
        p.wait("scalar", "pe")
        p.inc("scalar", lambda e: e.copy(bb[:], PS[:, 512:704]), "act")
        if h == 0:
            p.run(lambda e: e.memset(sums[:, 1:2], 1.0), "gpsimd")
            p.inc("gpsimd", lambda e: e.memset(vh[:], 1.0), "gp", 1)
        p.wait("tensor", "gp")
        p.wait("tensor", "wdma")
        p.wait("tensor", "act")

        for t in range(NT):
            tc = slice(t * 128, (t + 1) * 128)
            p.wait("tensor", "dve")
            p.wait("tensor", "act")
            for f in range(NF):
                p.inc("tensor", lambda e, t=t, f=f: e.matmul(
                    PS[:, 0:192], xnT[f][:, TOKOFF + t * 128:TOKOFF + (t + 1) * 128],
                    wqkv_sb[f][:], start=(f == 0), stop=(f == NF - 1)), "pe")
            p.wait("vector", "pe")
            p.inc("vector", lambda e: e.tensor_tensor(
                qf[:, 0:192], PS[:, 0:192], bb[:, 0:192], ALU.add), "dve")
            p.wait("scalar", "dve")
            p.inc("scalar", lambda e, t=t: e.copy(
                vh[:, t * 65:t * 65 + 64], qf[:, 128:192]), "act")
            p.inc("scalar", lambda e: e.activation(
                ctx.ln_misc[:, 0:64], qf[:, 0:64], AF.Square, accum_out=sums[:, 0:1]), "act")
            p.inc("scalar", lambda e: e.activation(
                ctx.ln_misc[:, 64:128], qf[:, 64:128], AF.Square, accum_out=sums[:, 2:3]), "act")
            p.wait("tensor", "act")
            p.inc("tensor", lambda e: e.transpose(
                PS[0:64, 1024:1152], qf[:, 0:64], ctx.ident_f32[:]), "pe")
            p.inc("tensor", lambda e: e.transpose(
                PS[0:64, 1152:1280], qf[:, 64:128], ctx.ident_f32[:]), "pe")
            p.inc("tensor", lambda e: e.transpose(
                PS[0:2, 1280:1408], sums[:, 0:2], ctx.ident_f32[:]), "pe")
            p.inc("tensor", lambda e: e.transpose(
                PS[0:2, 1408:1536], sums[:, 1:3], ctx.ident_f32[:]), "pe")
            p.wait("scalar", "pe")
            p.inc("scalar", lambda e, tc=tc: e.copy(qaugT[0:64, tc], PS[0:64, 1024:1152]), "act")
            p.inc("scalar", lambda e, tc=tc: e.activation(
                kaugT[0:64, tc], PS[0:64, 1152:1280], AF.Identity, bias=0.0, scale=-2.0), "act")
            p.inc("scalar", lambda e, tc=tc: e.copy(qaugT[64:66, tc], PS[0:2, 1280:1408]), "act")
            p.inc("scalar", lambda e, tc=tc: e.activation(
                kaugT[64:66, tc], PS[0:2, 1408:1536], AF.Identity,
                bias=ctx.epsc[0:2, 0:1]), "act")

        g, rbase = h // 2, (h % 2) * 64
        for tq in range(NT):
            nv = (tq + 1) * 128
            p.wait("tensor", "act")
            p.wait("tensor", "dve")
            nch = (nv + 511) // 512
            chmA = {}
            chmD = {}
            for c in range(nch):
                w_ = min(512, nv - c * 512)
                cs = slice(c * 512, c * 512 + w_)
                pd = 2048 + (c % 2) * 1024
                pv_ = 2560 + (c % 2) * 1024
                sb = slice((c % 2) * 512, (c % 2) * 512 + w_)
                if c - 2 in chmA:
                    p.wait("tensor", "act", chmA[c - 2])
                    p.wait("tensor", "dve", chmD[c - 2])
                p.inc("tensor", lambda e, tq=tq, cs=cs, pd=pd: e.matmul(
                    PS[:, pd:pd + (cs.stop - cs.start)],
                    qaugT[0:64, tq * 128:(tq + 1) * 128], kaugT[0:64, cs],
                    start=True, stop=True), "pe")
                p.inc("tensor", lambda e, tq=tq, cs=cs, pv_=pv_: e.matmul(
                    PS[:, pv_:pv_ + (cs.stop - cs.start)],
                    qaugT[0:66, tq * 128:(tq + 1) * 128], kaugT[0:66, cs],
                    start=True, stop=True), "pe")
                p.wait("scalar", "pe")
                p.inc("scalar", lambda e, cs=cs, pd=pd: e.activation(
                    scb[:, cs], PS[:, pd:pd + (cs.stop - cs.start)], AF.Square,
                    scale=0.5), "act")
                if c - 2 in chmD:
                    p.wait("scalar", "dve", chmD[c - 2])
                p.inc("scalar", lambda e, sb=sb, pv_=pv_: e.activation(
                    pb[:, sb], PS[:, pv_:pv_ + (sb.stop - sb.start)], AF.Square), "act")
                p.inc("scalar", lambda e, sb=sb: e.activation(
                    pb[:, sb], pb[:, sb], AF.Ln), "act")
                p.inc("scalar", lambda e, sb=sb: e.activation(
                    pb[:, sb], pb[:, sb], AF.Exp, scale=-1.0), "act")
                chmA[c] = p.mark("act")
                p.wait("vector", "act")
                p.inc("vector", lambda e, sb=sb, pv_=pv_: e.tensor_tensor(
                    pb[:, sb], pb[:, sb], PS[:, pv_:pv_ + (sb.stop - sb.start)],
                    ALU.mult), "dve")
                p.inc("vector", lambda e, cs=cs, sb=sb: e.tensor_tensor(
                    scb[:, cs], scb[:, cs], pb[:, sb], ALU.mult), "dve")
                chmD[c] = p.mark("dve")
            p.wait("vector", "dve")
            p.inc("vector", lambda e, tq=tq: e.tensor_tensor(
                scb[:, tq * 128:(tq + 1) * 128], scb[:, tq * 128:(tq + 1) * 128],
                ctx.maskc[:, 0:128], ALU.add), "dve")
            p.wait("vector", "dve")
            p.inc("vector", lambda e, nv=nv: e.reduce_max(
                sums[:, 3:4], scb[:, 0:nv], axis=mybir.AxisListType.X, negate=True), "dve")
            p.wait("scalar", "dve")
            p.inc("scalar", lambda e, nv=nv: e.activation(
                pb[:, 0:nv], scb[:, 0:nv], AF.Exp, bias=sums[:, 3:4],
                accum_out=sums[:, 4:5]), "act")
            p.wait("vector", "act")
            p.inc("vector", lambda e: e.reciprocal(sums[:, 5:6], sums[:, 4:5]), "dve")
            p.wait("vector", "dve")
            p.inc("vector", lambda e, nv=nv: e.tensor_scalar(
                pb[:, 0:nv], pb[:, 0:nv], sums[:, 5:6], None, ALU.mult), "dve")
            p.wait("tensor", "dve")
            trm = {}
            evk = {}

            def emit_tr(tk):
                po_ = 2048 + (tk % 2) * 1024
                if tk - 2 in evk:
                    p.wait("tensor", "act", evk[tk - 2])
                p.inc("tensor", lambda e, tk=tk, po_=po_: e.transpose(
                    PS[0:128, po_:po_ + 128], pb[:, tk * 128:(tk + 1) * 128],
                    ctx.ident_f32[:]), "pe")
                trm[tk] = p.mark("pe")

            def emit_ev(tk):
                po_ = 2048 + (tk % 2) * 1024
                eo_ = (tk % 2) * 128
                p.wait("scalar", "pe", trm[tk])
                p.inc("scalar", lambda e, po_=po_, eo_=eo_: e.copy(
                    ptb[:, eo_:eo_ + 128], PS[0:128, po_:po_ + 128]), "act")
                evk[tk] = p.mark("act")

            def emit_pv2(tk, tq=tq):
                eo_ = (tk % 2) * 128
                p.wait("tensor", "act", evk[tk])
                p.inc("tensor", lambda e, tk=tk, tq=tq, eo_=eo_: e.matmul(
                    PS[0:65, 3584:3712], vh[:, tk * 65:(tk + 1) * 65],
                    ptb[:, eo_:eo_ + 128], start=(tk == 0), stop=(tk == tq)), "pe")

            emit_tr(0)
            emit_ev(0)
            for tk in range(tq + 1):
                if tk + 1 <= tq:
                    emit_tr(tk + 1)
                emit_pv2(tk)
                if tk + 1 <= tq:
                    emit_ev(tk + 1)
            p.wait("scalar", "pe")
            p.inc("scalar", lambda e, g=g, rbase=rbase, tq=tq: e.copy(
                ocatT[g][rbase:rbase + 64, tq * 128:(tq + 1) * 128],
                PS[0:64, 3584:3712]), "act")
            p.wait("tensor", "act")

    if "oc3" in ctx.taps and batch == 1:
        _tap_tiles(ctx, ocatT, "oc3", padded=False)
    p.wait("gpsimd", "pe")
    p.wait("gpsimd", "act")
    p.wait("gpsimd", "dve")
    for it in range(NF):
        p.inc("gpsimd", lambda e, it=it: e.dma_start(
            wout_sb[it][:], ctx.d["L1_attn_w_out"].ap().rearrange(
                "(it2 pp) e -> it2 pp e", pp=128)[it]), "wdma", 16)
    p.wait("tensor", "wdma")
    for fo in range(NF):
        for cch in range(4):
            cs = slice(cch * 512, cch * 512 + 512)
            pso = (cch % 4) * 512
            p.wait("tensor", "dve")
            for it in range(NF):
                p.inc("tensor", lambda e, fo=fo, cs=cs, it=it, pso=pso: e.matmul(
                    PS[:, pso:pso + 512], wout_sb[it][:, fo * 128:(fo + 1) * 128],
                    ocatT[it][:, cs], start=(it == 0), stop=(it == NF - 1)), "pe")
            p.wait("scalar", "pe")
            p.inc("scalar", lambda e, fo=fo, pso=pso: e.activation(
                ctx.ln_sq[:, 0:512], PS[:, pso:pso + 512], AF.Identity,
                bias=ctx.cols[:, fo:fo + 1]), "act")
            p.wait("vector", "act")
            p.inc("vector", lambda e, fo=fo, cs=cs: e.tensor_tensor(
                ctx.xT[fo][:, TOKOFF + cs.start:TOKOFF + cs.stop],
                ctx.xT[fo][:, TOKOFF + cs.start:TOKOFF + cs.stop],
                ctx.ln_sq[:, 0:512], ALU.add), "dve")
            p.wait("scalar", "dve")
    p.barrier(("tensor", "scalar", "vector", "sync", "gpsimd"))



def emit_l1_ffn(ctx, batch):
    p, m = ctx.p, ctx.m
    PS = ctx.PS
    B0 = ctx.BIG
    TCW = 256
    NTC = SEQ // TCW

    xnb = [m.alloc(f"b4_xn{f}", [128, NPAD], F32, at=B0 + f * XT_STRIDE) for f in range(NF)]
    xnn = [m.alloc(f"b4_xnn{f}", [128, NPAD], BF16, at=B0 + f * XT_STRIDE) for f in range(NF)]
    frag = [B0 + f * XT_STRIDE + 4128 for f in range(NF)]
    hbuf = [m.alloc(f"b4_hb{c}", [128, TCW], BF16, at=frag[c // 8] + (c % 8) * 512)
            for c in range(32)]
    acc = [m.alloc(f"b4_acc{fo}", [128, TCW], F32, at=frag[4 + fo // 4] + (fo % 4) * 1024)
           for fo in range(NF)]
    macb = [m.alloc(f"b4_mac{fo}", [128, TCW], BF16, at=frag[6] + fo * 512) for fo in range(NF)]
    abelT = [m.alloc(f"b4_ab{fo}", [128, TCW], BF16, at=frag[7] + fo * 512) for fo in range(NF)]
    TAIL = B0 + 8 * XT_STRIDE
    wsb = m.alloc("b4_wsb", [6, 2048], F32, at=TAIL)
    nonabT = [m.alloc(f"b4_nb{fo}", [128, TCW], BF16, at=TAIL + 8192 + fo * 512)
              for fo in range(NF)]
    gbuf = m.alloc("b4_gb", [128, TCW], F32, at=TAIL + 12288)
    webc = m.alloc("b4_web", [128, TCW], F32, at=TAIL + 13312)
    t1 = m.alloc("b4_t1", [128, TCW], F32, at=TAIL + 14336)
    t2 = m.alloc("b4_t2", [128, TCW], F32, at=TAIL + 15360)
    glog = m.alloc("b4_glog", [128, 16], F32, at=TAIL + 16384)
    grow = m.alloc("b4_grow", [1, 256], F32, at=TAIL + 16448)
    cols2 = m.alloc("b4_cols2", [128, 144], F32, at=TAIL + 17472)
    gcb = m.alloc("b4_gcb", [128, 16], BF16, at=TAIL + 18048)
    W0 = ctx.WARENA
    sel6 = m.alloc("b4_sel6", [6, 768], F32, at=W0 + 8320)
    gbb = m.alloc("b4_gbb", [128, 6], F32, at=W0 + 11424)
    grow6 = m.alloc("b4_grow6", [1, 8], F32, at=W0 + 11456)
    w1c = [m.alloc(f"b4_w1c{i}", [128, 1024], BF16, at=W0 + i * 2048) for i in range(2)]
    w2c = [m.alloc(f"b4_w2c{i}", [128, 1024], BF16, at=W0 + 4096 + i * 2048) for i in range(2)]
    gatewc = m.alloc("b4_gw", [128, 48], BF16, at=W0 + 8192)

    p.barrier(("gpsimd",))
    for f in range(NF - 1):
        p.run(lambda e, f=f: e.memset(xnb[f][:, 0:TOKOFF], 0.0), "gpsimd")
    p.inc("gpsimd", lambda e: e.memset(xnb[NF - 1][:, 0:TOKOFF], 0.0), "gp", 1)
    emit_ln(ctx, ctx.xT, xnb)
    for f in range(NF):
        p.wait("vector", "act")
        p.inc("vector", lambda e, f=f: e.tensor_tensor(
            ctx.xT[f][:, tokc(0, SEQ)], ctx.xT[f][:, tokc(0, SEQ)],
            xnb[f][:, tokc(0, SEQ)], ALU.add), "dve")
    p.barrier(("tensor", "scalar", "vector"))

    emit_colvec(ctx, ctx.d["L1_ffn_ln_g"].ap().rearrange("(a e) -> a e", a=1), E, 0)
    emit_colvec(ctx, ctx.d["L1_ffn_ln_b"].ap().rearrange("(a e) -> a e", a=1), E, 8)
    emit_colvec(ctx, ctx.d["L1_ffn_ab1"].ap()[0:2048].rearrange("(a e) -> a e", a=1), 2048, 16)
    emit_colvec(ctx, ctx.d["L1_ffn_ab1"].ap()[2048:4096].rearrange("(a e) -> a e", a=1), 2048, 32)
    emit_colvec(ctx, ctx.d["L1_ffn_ab2"].ap().rearrange("(a e) -> a e", a=1), E, 48)
    emit_colvec(ctx, ctx.d["L1_ffn_gc_w"].ap().rearrange("e a -> a e"), 2048, 56)
    for ex in range(6):
        emit_colvec(ctx, ctx.d["L1_ffn_eb1"].ap()[ex].rearrange("(a e) -> a e", a=1),
                    2048, ex * 16, dst=cols2)
        emit_colvec(ctx, ctx.d["L1_ffn_eb2"].ap()[ex].rearrange("(a e) -> a e", a=1),
                    1024, 96 + ex * 8, dst=cols2)
    p.wait("vector", "act")
    p.inc("vector", lambda e: e.tensor_copy(gcb[:, 0:16], ctx.cols[:, 56:72]), "dve")
    p.inc("sync", lambda e: e.dma_start(sel6[:], ctx.cd["sel6"].ap()), "dma", 16)
    p.inc("sync", lambda e: e.dma_start(
        grow6[0:1, 0:6], ctx.d["L1_ffn_gate_b"].ap().rearrange("(a e) -> a e", a=1)), "dma", 16)
    p.inc("sync", lambda e: e.dma_start(
        ctx.lam_s[0:1, 0:1], ctx.d["L1_ffn_gc_b"].ap().rearrange("(a e) -> a e", a=1)), "dma", 16)
    p.wait("tensor", "dma")
    emit_bcast_row(ctx, grow6, gbb[:, 0:6], 6)
    p.wait("gpsimd", "pe")
    for f in range(NF):
        p.inc("gpsimd", lambda e, f=f: e.dma_start(
            gatewc[:, f * 6:(f + 1) * 6], ctx.d["L1_ffn_gate_w"].ap()
            [f * 128:(f + 1) * 128, :]), "wdma", 16)

    emit_ln(ctx, xnb, xnn, gamma_off=0, beta_off=8)

    p.wait("tensor", "wdma")
    for t in range(16):
        p.wait("tensor", "act")
        p.wait("tensor", "dve")
        for f in range(NF):
            p.inc("tensor", lambda e, t=t, f=f: e.matmul(
                PS[0:128, 0:6], xnn[f][:, TOKOFF + t * 128:TOKOFF + (t + 1) * 128],
                gatewc[:, f * 6:(f + 1) * 6], start=(f == 0), stop=(f == NF - 1)), "pe")
        p.wait("vector", "pe")
        p.inc("vector", lambda e: e.tensor_tensor(
            glog[:, 0:6], PS[0:128, 0:6], gbb[:, 0:6], ALU.add), "dve")
        p.wait("vector", "dve")
        p.inc("vector", lambda e: e.reduce_max(
            glog[:, 8:9], glog[:, 0:6], axis=mybir.AxisListType.X, negate=True), "dve")
        p.wait("vector", "dve")
        p.inc("vector", lambda e: e.tensor_scalar(
            glog[:, 9:10], glog[:, 8:9], -1.0, None, ALU.mult), "dve")
        p.wait("vector", "dve")
        p.inc("vector", lambda e: e.tensor_scalar(
            glog[:, 10:16][:, 0:6], glog[:, 0:6], glog[:, 9:10], None, ALU.is_ge), "dve")
        p.wait("vector", "dve")
        p.inc("vector", lambda e: e.scalar_tensor_tensor(
            glog[:, 10:16][:, 0:6], glog[:, 10:16][:, 0:6], -1e30, glog[:, 0:6],
            ALU.mult, ALU.add), "dve")
        p.wait("vector", "dve")
        p.inc("vector", lambda e: e.reduce_max(
            glog[:, 7:8], glog[:, 10:16][:, 0:6], axis=mybir.AxisListType.X), "dve")
        p.wait("scalar", "dve")
        p.inc("scalar", lambda e: e.activation(
            glog[:, 10:16], glog[:, 0:6], AF.Exp, bias=glog[:, 8:9]), "act")
        p.wait("vector", "act")
        p.inc("vector", lambda e: e.tensor_tensor(
            glog[:, 6:7], glog[:, 7:8], glog[:, 8:9], ALU.add), "dve")
        p.wait("scalar", "dve")
        p.inc("scalar", lambda e: e.activation(glog[:, 6:7], glog[:, 6:7], AF.Exp), "act")
        p.wait("vector", "act")
        p.inc("vector", lambda e: e.tensor_scalar(
            glog[:, 6:7], glog[:, 6:7], 1.0, None, ALU.add), "dve")
        p.wait("vector", "dve")
        p.inc("vector", lambda e: e.reciprocal(glog[:, 6:7], glog[:, 6:7]), "dve")
        p.wait("vector", "dve")
        p.inc("vector", lambda e: e.tensor_scalar(
            glog[:, 0:6], glog[:, 0:6], glog[:, 7:8], None, ALU.is_ge), "dve")
        p.wait("vector", "dve")
        p.inc("vector", lambda e: e.tensor_tensor(
            glog[:, 0:6], glog[:, 0:6], glog[:, 10:16], ALU.mult), "dve")
        p.wait("vector", "dve")
        p.inc("vector", lambda e: e.tensor_scalar(
            glog[:, 0:6], glog[:, 0:6], glog[:, 6:7], None, ALU.mult), "dve")
        p.wait("tensor", "dve")
        p.inc("tensor", lambda e: e.transpose(
            PS[0:6, 512:640], glog[:, 0:6], ctx.ident_f32[:]), "pe")
        p.wait("scalar", "pe")
        p.inc("scalar", lambda e, t=t: e.copy(
            wsb[0:6, t * 128:(t + 1) * 128], PS[0:6, 512:640]), "act")
        p.wait("tensor", "act")

    for tc in range(NTC):
        TC = slice(tc * TCW, (tc + 1) * TCW)
        xcol = slice(TOKOFF + TC.start, TOKOFF + TC.stop)
        actm = {}

        def ld_w1(hc):
            return lambda e, hc=hc: e.dma_start(
                w1c[hc % 2].ap().rearrange("p (ft c) -> p ft c", c=128),
                ctx.d["aw1s"].ap().rearrange("(ft pp) hh -> pp ft hh", pp=128)
                [:, :, hc * 128:(hc + 1) * 128])

        def use_w1(hc, xcol=xcol):
            wb = w1c[hc % 2]
            pso = 2048 + (hc % 2) * 512
            if hc - 2 in actm:
                p.wait("tensor", "act", actm[hc - 2])
            for f in range(NF):
                p.inc("tensor", lambda e, hc=hc, f=f, wb=wb, pso=pso, xcol=xcol: e.matmul(
                    PS[:, pso:pso + TCW], wb[:, f * 128:(f + 1) * 128],
                    xnn[f][:, xcol], start=(f == 0), stop=(f == NF - 1)), "pe")
            pm = p.mark("pe")
            p.wait("scalar", "pe", pm)
            p.inc("scalar", lambda e, hc=hc, pso=pso: e.activation(
                hbuf[hc][:], PS[:, pso:pso + TCW], AF.Gelu,
                bias=ctx.cols[:, 16 + hc:17 + hc]), "act")
            actm[hc] = p.mark("act")

        stream_weights(p, 32, ld_w1, use_w1, "w4a")
        p.wait("tensor", "act")

        def ld_w2(hc):
            return lambda e, hc=hc: e.dma_start(
                w2c[hc % 2][:], ctx.d["aw2s"].ap()[hc * 128:(hc + 1) * 128, :])

        def use_w2(hc):
            wb = w2c[hc % 2]
            for fo in range(NF):
                p.inc("tensor", lambda e, hc=hc, fo=fo, wb=wb: e.matmul(
                    PS[:, fo * 512:fo * 512 + TCW], wb[:, fo * 128:(fo + 1) * 128],
                    hbuf[hc][:], start=(hc == 0), stop=(hc == 31)), "pe")

        stream_weights(p, 32, ld_w2, use_w2, "w4b")
        p.wait("scalar", "pe")
        for fo in range(NF):
            p.inc("scalar", lambda e, fo=fo: e.activation(
                abelT[fo][:], PS[:, fo * 512:fo * 512 + TCW], AF.Identity,
                bias=ctx.cols[:, 48 + fo:49 + fo]), "act")
        p.wait("tensor", "act")
        p.wait("vector", "act")
        for ex in range(6):
            p.wait("tensor", "dve")
            p.inc("tensor", lambda e, ex=ex, TC=TC: e.matmul(
                PS[:, 0:TCW], sel6[:, ex * 128:(ex + 1) * 128], wsb[0:6, TC],
                start=True, stop=True), "pe")
            p.wait("scalar", "pe")
            p.inc("scalar", lambda e: e.copy(webc[:], PS[:, 0:TCW]), "act")
            p.wait("tensor", "act")
            actm2 = {}

            def ld_e1(hc, ex=ex):
                return lambda e, ex=ex, hc=hc: e.dma_start(
                    w1c[hc % 2].ap().rearrange("p (ft c) -> p ft c", c=128),
                    ctx.d["ew1s"].ap()[ex].rearrange("(ft pp) hh -> pp ft hh", pp=128)
                    [:, :, hc * 128:(hc + 1) * 128])

            def use_e1(hc, ex=ex, xcol=xcol):
                wb = w1c[hc % 2]
                pso = 2048 + (hc % 2) * 512
                if hc - 2 in actm2:
                    p.wait("tensor", "act", actm2[hc - 2])
                for f in range(NF):
                    p.inc("tensor", lambda e, hc=hc, f=f, wb=wb, pso=pso, xcol=xcol: e.matmul(
                        PS[:, pso:pso + TCW], wb[:, f * 128:(f + 1) * 128],
                        xnn[f][:, xcol], start=(f == 0), stop=(f == NF - 1)), "pe")
                pm = p.mark("pe")
                p.wait("scalar", "pe", pm)
                p.inc("scalar", lambda e, ex=ex, hc=hc, pso=pso: e.activation(
                    hbuf[hc][:], PS[:, pso:pso + TCW], AF.Gelu,
                    bias=cols2[:, ex * 16 + hc:ex * 16 + hc + 1]), "act")
                actm2[hc] = p.mark("act")

            stream_weights(p, 16, ld_e1, use_e1, "w4a")
            p.wait("tensor", "act")

            def ld_e2(hc, ex=ex):
                return lambda e, ex=ex, hc=hc: e.dma_start(
                    w2c[hc % 2][:], ctx.d["ew2s"].ap()[ex][hc * 128:(hc + 1) * 128, :])

            def use_e2(hc):
                wb = w2c[hc % 2]
                for fo in range(NF):
                    p.inc("tensor", lambda e, hc=hc, fo=fo, wb=wb: e.matmul(
                        PS[:, fo * 512:fo * 512 + TCW], wb[:, fo * 128:(fo + 1) * 128],
                        hbuf[hc][:], start=(hc == 0), stop=(hc == 15)), "pe")

            stream_weights(p, 16, ld_e2, use_e2, "w4b")
            p.wait("scalar", "pe")
            p.wait("vector", "pe")
            for fo in range(NF):
                p.inc("scalar", lambda e, ex=ex, fo=fo: e.activation(
                    t1[:], PS[:, fo * 512:fo * 512 + TCW], AF.Identity,
                    bias=cols2[:, 96 + ex * 8 + fo:96 + ex * 8 + fo + 1]), "act")
                p.wait("vector", "act")
                p.inc("vector", lambda e: e.tensor_tensor(t2[:], t1[:], webc[:], ALU.mult), "dve")
                p.wait("vector", "dve")
                if ex == 0:
                    p.inc("vector", lambda e, fo=fo: e.tensor_copy(acc[fo][:], t2[:]), "dve")
                else:
                    p.inc("vector", lambda e, fo=fo: e.tensor_tensor(
                        acc[fo][:], acc[fo][:], t2[:], ALU.add), "dve")
                p.wait("scalar", "dve")
        p.wait("scalar", "dve")
        for fo in range(NF):
            p.inc("scalar", lambda e, fo=fo: e.copy(macb[fo][:], acc[fo][:]), "act")
        p.wait("tensor", "act")
        def ld_r(fi):
            return lambda e, fi=fi: e.dma_start(
                w1c[fi % 2][:], ctx.d["rs"].ap()[fi * 128:(fi + 1) * 128, :])

        def use_r(fi):
            wb = w1c[fi % 2]
            for fo in range(NF):
                p.inc("tensor", lambda e, fi=fi, fo=fo, wb=wb: e.matmul(
                    PS[:, fo * 512:fo * 512 + TCW], wb[:, fo * 128:(fo + 1) * 128],
                    macb[fi][:], start=(fi == 0), stop=(fi == NF - 1)), "pe")

        stream_weights(p, NF, ld_r, use_r, "w4a")
        p.wait("scalar", "pe")
        for fo in range(NF):
            p.inc("scalar", lambda e, fo=fo: e.copy(
                nonabT[fo][:], PS[:, fo * 512:fo * 512 + TCW]), "act")
        p.wait("tensor", "act")
        if tc == 0 and batch == 0:
            for nm, tiles in (("ab4", abelT), ("nb4", nonabT)):
                if nm in ctx.taps:
                    p.barrier(("sync", "vector"))
                    for f in range(NF):
                        p.wait("vector", "st")
                        p.inc("vector", lambda e, f=f, tiles=tiles: e.tensor_copy(
                            t1[:], tiles[f][:]), "dve")
                        p.wait("sync", "dve")
                        p.inc("sync", lambda e, f=f, nm=nm: e.dma_start(
                            ctx.tap_drams[nm].ap()[f * 128:(f + 1) * 128, 0:TCW], t1[:]),
                            "st", 16)
                        p.wait("sync", "st")
                    p.wait("gpsimd", "st")
                    p.wait("scalar", "st")
            if "mo4" in ctx.taps:
                p.barrier(("sync",))
                for f in range(NF):
                    p.inc("sync", lambda e, f=f: e.dma_start(
                        ctx.tap_drams["mo4"].ap()[f * 128:(f + 1) * 128, 0:TCW],
                        acc[f][:]), "st", 16)
                p.wait("sync", "st")
        p.wait("tensor", "dve")
        for i, tiles in ((0, abelT), (8, nonabT)):
            for f in range(NF):
                p.inc("tensor", lambda e, i=i, f=f, tiles=tiles: e.matmul(
                    PS[0:1, 3584:3584 + TCW], gcb[:, i + f:i + f + 1], tiles[f][:],
                    start=(i == 0 and f == 0), stop=(i == 8 and f == NF - 1)), "pe")
        p.wait("scalar", "pe")
        p.inc("scalar", lambda e: e.activation(
            grow[0:1, 0:TCW], PS[0:1, 3584:3584 + TCW], AF.Sigmoid,
            bias=ctx.lam_s[0:1, 0:1]), "act")
        p.wait("tensor", "act")
        p.inc("tensor", lambda e: e.matmul(
            PS[:, 3072:3072 + TCW], ctx.ones_row[0:1, :], grow[0:1, 0:TCW],
            start=True, stop=True), "pe")
        p.wait("scalar", "pe")
        p.inc("scalar", lambda e: e.copy(gbuf[:], PS[:, 3072:3072 + TCW]), "act")
        p.wait("vector", "act")
        for f in range(NF):
            p.inc("vector", lambda e, f=f: e.tensor_tensor(
                t1[:], nonabT[f][:], abelT[f][:], ALU.subtract), "dve")
            p.wait("vector", "dve")
            p.inc("vector", lambda e: e.tensor_tensor(t1[:], t1[:], gbuf[:], ALU.mult), "dve")
            p.wait("vector", "dve")
            p.inc("vector", lambda e, f=f: e.tensor_tensor(
                t2[:], abelT[f][:], t1[:], ALU.add), "dve")
            p.wait("vector", "dve")
            p.inc("vector", lambda e, f=f, xcol=xcol: e.tensor_tensor(
                ctx.xT[f][:, xcol], ctx.xT[f][:, xcol], t2[:], ALU.add), "dve")
            p.wait("vector", "dve")
        p.wait("tensor", "dve")
        p.wait("scalar", "dve")
    p.barrier()


def _tap_tiles(ctx, tiles, name, padded=True):
    p = ctx.p
    dram = ctx.tap_drams[name]
    so = TOKOFF if padded else 0
    p.barrier(("sync", "vector"))
    bf_src = tiles[0].dtype == BF16
    for f in range(NF):
        if bf_src:
            p.wait("vector", "st")
            p.inc("vector", lambda e, f=f: e.tensor_copy(
                ctx.ln_sq[:, 0:SEQ], tiles[f][:, so:so + SEQ]), "dve")
            p.wait("sync", "dve")
            p.inc("sync", lambda e, f=f: e.dma_start(
                dram.ap()[f * 128:(f + 1) * 128, :], ctx.ln_sq[:, 0:SEQ]), "st", 16)
            p.wait("sync", "st")
        else:
            p.inc("sync", lambda e, f=f: e.dma_start(
                dram.ap()[f * 128:(f + 1) * 128, :], tiles[f][:, so:so + SEQ]), "st", 16)
    p.wait("sync", "st")
    p.wait("gpsimd", "st")
    p.wait("scalar", "st")


def emit_load_x(ctx, batch):
    p = ctx.p
    p.barrier()
    for f in range(NF):
        p.inc("sync", lambda e, f=f, b=batch: e.dma_start(
            ctx.xT[f][:, tokc(0, SEQ)], ctx.d["xin"].ap()[b, f * 128:(f + 1) * 128, :]),
            "dma", 16)
    p.wait("tensor", "dma")
    p.wait("vector", "dma")
    p.wait("scalar", "dma")


def emit_store_x(ctx, batch):
    p = ctx.p
    p.wait("sync", "dve")
    p.wait("sync", "act")
    p.wait("sync", "pe")
    for f in range(NF):
        p.inc("sync", lambda e, f=f, b=batch: e.dma_start(
            ctx.d["xout"].ap()[b, f * 128:(f + 1) * 128, :], ctx.xT[f][:, tokc(0, SEQ)]),
            "st", 16)
    p.wait("sync", "st")



def _host_consts():
    import ml_dtypes as _md
    c = {}
    c["ident_f32"] = np.eye(128, dtype=np.float32)
    c["ident_bf16"] = np.eye(128, dtype=_md.bfloat16)
    DH, WIN = 64, 512
    half = np.arange(0, DH, 2, dtype=np.float64)
    inv_freq = 1.0 / 10000.0 ** (half / DH)
    t = np.arange(SEQ, dtype=np.float64)
    freqs = t[:, None] * inv_freq[None, :]
    freqs = np.concatenate([freqs, freqs], axis=-1)
    base = (half + 0.4 * DH) / (1.4 * DH)
    power = (t - SEQ // 2) / float(WIN // 2)
    scale = base[None, :] ** power[:, None]
    scale = np.concatenate([scale, scale], axis=-1)
    cos, sin = np.cos(freqs), np.sin(freqs)
    sgn = np.concatenate([-np.ones(32), np.ones(32)])[None, :]
    c["rotA_q"] = (cos * scale).astype(np.float32)
    c["rotB_q"] = (sin * scale * sgn).astype(np.float32)
    c["rotA_k"] = (cos / scale).astype(np.float32)
    c["rotB_k"] = (sin / scale * sgn).astype(np.float32)
    mk = np.zeros((4, 128, 512), dtype=np.float32)
    mkp = np.zeros((4, 128, 512), dtype=np.float32)
    for r in range(4):
        ks = r * 128 + np.arange(128)[:, None]
        qs = np.arange(512)[None, :]
        mk[r] = np.where(ks <= qs, 0.0, -1e30)
        mkp[r] = np.where(ks >= qs, 0.0, -1e30)
    c["locmask"] = mk
    c["locmaskp"] = mkp
    qrow = np.arange(128)[:, None]
    kcol = np.arange(128)[None, :]
    c["maskdiag"] = np.where(kcol <= qrow, 0.0, -1e9).astype(np.float32)
    c["epsc"] = np.array([[0.0], [1e-6]], dtype=np.float32)
    sel = np.zeros((6, 768), dtype=np.float32)
    for e in range(6):
        sel[e, e * 128:(e + 1) * 128] = 1.0
    c["sel6"] = sel
    return c


WEIGHT_SHAPES = {
    "L0_attn_Wq": (16, 1024, 128), "L0_attn_Wk": (16, 1024, 128),
    "L0_attn_Wv": (16, 1024, 64), "L0_attn_Wo": (1024, 1024),
    "L0_attn_bo": (1024,), "L0_attn_gamma": (1024,), "L0_attn_beta": (1024,),
    "L0_attn_bv": (16, 64),
    "L0_ffn_wqkv": (2, 1024, 1536), "L0_ffn_q_scale": (2, 64), "L0_ffn_k_scale": (2, 64),
    "L0_ffn_wo": (2, 512, 1024), "L0_ffn_ln_g": (2, 1024), "L0_ffn_ln_b": (2, 1024),
    "L0_ffn_w1": (2, 1024, 5460), "L0_ffn_b1": (2, 5460),
    "L0_ffn_w2": (2, 2730, 1024), "L0_ffn_b2": (2, 1024),
    "L1_attn_w_qkv": (1024, 3072), "L1_attn_b_qkv": (3072,),
    "L1_attn_w_out": (1024, 1024), "L1_attn_b_out": (1024,),
    "L1_ffn_ln_g": (1024,), "L1_ffn_ln_b": (1024,),
    "L1_ffn_aw1": (1024, 4096), "L1_ffn_ab1": (4096,),
    "L1_ffn_aw2": (4096, 1024), "L1_ffn_ab2": (1024,),
    "L1_ffn_gate_w": (1024, 6), "L1_ffn_gate_b": (6,),
    "L1_ffn_ew1": (6, 1024, 2048), "L1_ffn_eb1": (6, 2048),
    "L1_ffn_ew2": (6, 2048, 1024), "L1_ffn_eb2": (6, 1024),
    "L1_ffn_gens": (4, 1024, 1024), "L1_ffn_gc_w": (2048, 1), "L1_ffn_gc_b": (1,),
    "bq_t": (128, 16), "bk_t": (128, 16), "lam11": (1, 1),
}

BLOCK_EMITTERS = {}




def _emit_expm(ctx):
    p, m = ctx.p, ctx.m
    PS = ctx.PS
    bases = [R_XT, R_XT + 32768, ctx.BIG, ctx.BIG + 32768]
    X = [[m.alloc(f"xm{i}_{f}", [128, 1024], F32, at=bases[i] + f * 4096)
          for f in range(NF)] for i in range(4)]
    stb = m.alloc("xm_st", [128, 1024], BF16, at=ctx.BIG + 65536)

    def mm_product(lhsT_tiles, rhs_tiles, out_tiles, post=None):
        evm = {}
        gx = 0
        for fo in range(NF):
            for ch in range(2):
                cs = slice(ch * 512, ch * 512 + 512)
                pso = (gx % 4) * 512
                if gx - 3 in evm:
                    p.wait("tensor", "act", evm[gx - 3])
                for fi in range(NF):
                    p.inc("tensor", lambda e, fo=fo, cs=cs, fi=fi, pso=pso: e.matmul(
                        PS[:, pso:pso + 512],
                        lhsT_tiles[fi][:, fo * 128:(fo + 1) * 128],
                        rhs_tiles[fi][:, cs], start=(fi == 0), stop=(fi == NF - 1)), "pe")
                if post is None:
                    p.wait("scalar", "pe")
                    p.inc("scalar", lambda e, fo=fo, cs=cs, pso=pso: e.copy(
                        out_tiles[fo][:, cs], PS[:, pso:pso + 512]), "act")
                else:
                    post(fo, cs, pso)
                evm[gx] = p.mark("act")
                gx += 1

    def transpose_to(src_tiles, dst_tiles):
        evm = {}
        gx = 0
        for a in range(NF):
            for b in range(NF):
                po = 3072 + (gx % 2) * 512
                if gx - 2 in evm:
                    p.wait("tensor", "act", evm[gx - 2])
                p.inc("tensor", lambda e, a=a, b=b, po=po: e.transpose(
                    PS[:, po:po + 128], src_tiles[a][:, b * 128:(b + 1) * 128],
                    ctx.ident_f32[:]), "pe")
                p.wait("scalar", "pe")
                p.inc("scalar", lambda e, a=a, b=b, po=po: e.copy(
                    dst_tiles[b][:, a * 128:(a + 1) * 128], PS[:, po:po + 128]), "act")
                evm[gx] = p.mark("act")
                gx += 1

    def load_ident(tiles):
        for f in range(NF - 1):
            p.run(lambda e, f=f: e.memset(tiles[f][:], 0.0), "gpsimd")
        p.inc("gpsimd", lambda e, tiles=tiles: e.memset(tiles[NF - 1][:], 0.0), "gp", 1)
        p.wait("sync", "gp")
        for f in range(NF):
            p.inc("sync", lambda e, f=f, tiles=tiles: e.dma_start(
                tiles[f][:, f * 128:(f + 1) * 128], ctx.cd["identf"].ap()), "dma", 16)

    CS = ("pe", "dve", "act", "gp", "dma")
    racc = None
    free = [0, 1, 2, 3]
    for gi in range(4):
        bA = free[0]; bH = free[1]; bT = free[2]
        p.barrier(("sync", "gpsimd", "tensor", "scalar", "vector"))
        for f in range(NF):
            p.inc("sync", lambda e, f=f, gi=gi, bA=bA: e.dma_start(
                X[bA][f][:], ctx.d["L1_ffn_gens"].ap()[gi][f * 128:(f + 1) * 128, :]),
                "dma", 16)
        load_ident(X[bH])
        p.barrier(("tensor", "scalar", "vector", "sync"))
        for j in range(7, 0, -1):
            coef = -0.25 / j

            def post(fo, cs, pso, coef=coef, bT=bT):
                p.wait("scalar", "pe")
                p.inc("scalar", lambda e, fo=fo, cs=cs, pso=pso, coef=coef, bT=bT:
                      e.activation(X[bT][fo][:, cs], PS[:, pso:pso + 512], AF.Identity,
                                   bias=0.0, scale=coef), "act")
                p.wait("vector", "act")
                ds = max(cs.start, fo * 128)
                de = min(cs.stop, (fo + 1) * 128)
                if ds < de:
                    p.inc("vector", lambda e, fo=fo, ds=ds, de=de, bT=bT: e.tensor_tensor(
                        X[bT][fo][:, ds:de], X[bT][fo][:, ds:de],
                        ctx.ident_f32[:, ds - fo * 128:de - fo * 128], ALU.add), "dve")

            mm_product(X[bA], X[bH], None, post=post)
            p.barrier(("tensor", "scalar", "vector"))
            bH, bT = bT, bH
        cur = bH
        scratch = [bA, bT]
        for sq in range(2):
            tb, ob = scratch[0], scratch[1]
            transpose_to(X[cur], X[tb])
            p.barrier(("tensor", "scalar", "vector"))
            mm_product(X[tb], X[cur], X[ob])
            p.barrier(("tensor", "scalar", "vector"))
            scratch = [cur, tb]
            cur = ob
        if racc is None:
            racc = cur
            free = [b for b in range(4) if b != racc]
        else:
            tb = [b for b in range(4) if b not in (racc, cur)][0]
            transpose_to(X[racc], X[tb])
            p.barrier(("tensor", "scalar", "vector"))
            ob = [b for b in range(4) if b not in (racc, cur, tb)][0]
            mm_product(X[tb], X[cur], X[ob])
            p.barrier(("tensor", "scalar", "vector"))
            racc = ob
            free = [b for b in range(4) if b != racc]
    if "rexp" in ctx.taps:
        p.barrier(("sync",))
        for f in range(NF):
            p.inc("sync", lambda e, f=f: e.dma_start(
                ctx.tap_drams["rexp"].ap()[f * 128:(f + 1) * 128, 0:1024], X[racc][f][:]),
                "st", 16)
        p.wait("sync", "st")
    for f in range(NF):
        p.wait("vector", "st")
        p.inc("vector", lambda e, f=f: e.tensor_copy(stb[:], X[racc][f][:]), "dve")
        p.wait("sync", "dve")
        p.inc("sync", lambda e, f=f: e.dma_start(
            ctx.d["rs"].ap()[f * 128:(f + 1) * 128, :], stb[:]), "st", 16)
        p.wait("sync", "st")
    p.barrier()


def _wp_setup(ctx, blocks):
    m = ctx.m
    ctx.wp_bufs = [m.alloc(f"wp_b{i}", [128, 5460], BF16, at=R_LNS + i * 10944)
                   for i in range(2)]
    jobs = []
    if "l0_ffn" in blocks:
        for lay in range(2):
            for f in range(NF):
                jobs.append((ctx.d["L0_ffn_w1"].ap()[lay][f * 128:(f + 1) * 128, :],
                             ctx.d["w1s"].ap()[lay][f * 128:(f + 1) * 128, :], 128, 5460))
            for c in range(22):
                cw = 128 if c < 21 else 42
                jobs.append((ctx.d["L0_ffn_w2"].ap()[lay][c * 128:c * 128 + cw, :],
                             ctx.d["w2s"].ap()[lay][c * 128:c * 128 + cw, :], cw, 1024))
    if "l1_ffn" in blocks:
        for f in range(NF):
            jobs.append((ctx.d["L1_ffn_aw1"].ap()[f * 128:(f + 1) * 128, :],
                         ctx.d["aw1s"].ap()[f * 128:(f + 1) * 128, :], 128, 4096))
        for t in range(32):
            jobs.append((ctx.d["L1_ffn_aw2"].ap()[t * 128:(t + 1) * 128, :],
                         ctx.d["aw2s"].ap()[t * 128:(t + 1) * 128, :], 128, 1024))
        for ex in range(6):
            for f in range(NF):
                jobs.append((ctx.d["L1_ffn_ew1"].ap()[ex][f * 128:(f + 1) * 128, :],
                             ctx.d["ew1s"].ap()[ex][f * 128:(f + 1) * 128, :], 128, 2048))
            for t in range(16):
                jobs.append((ctx.d["L1_ffn_ew2"].ap()[ex][t * 128:(t + 1) * 128, :],
                             ctx.d["ew2s"].ap()[ex][t * 128:(t + 1) * 128, :], 128, 1024))
    ctx.wp_jobs = jobs
    ctx.wp_pos = 0
    ctx.wp_stm = {}


def _wp_emit(ctx, n):
    p = ctx.p
    end = len(ctx.wp_jobs) if n is None else min(ctx.wp_pos + n, len(ctx.wp_jobs))
    for j in range(ctx.wp_pos, end):
        src, dst, rows, cols = ctx.wp_jobs[j]
        buf = ctx.wp_bufs[j % 2]
        pw = ("wdma", "wdma2")[j % 2]
        ps = ("st", "st2")[j % 2]
        if j - 2 >= 0:
            p.wait("gpsimd", ps, ctx.wp_stm[j - 2])
        p.inc("gpsimd", lambda e, s=src, b=buf, r=rows, c=cols: e.dma_start(
            b[0:r, 0:c], s), pw, 16)
        p.wait("sync", pw)
        if j - 2 >= 0:
            p.wait("sync", ps, ctx.wp_stm[j - 2])
        p.inc("sync", lambda e, d=dst, b=buf, r=rows, c=cols: e.dma_start(
            d, b[0:r, 0:c]), ps, 16)
        ctx.wp_stm[j] = p.mark(ps)
    ctx.wp_pos = end


def build_program(blocks=("l0_attn",), taps=(), nb=2):
    nc = bass.Bass("TRN2", target_bir_lowering=False)
    ctx = Ctx()
    ctx.nc = nc
    ctx.p = Prog(nc)
    ctx.taps = set(taps)
    m = SbufMap(nc)
    ctx.m = m

    d = {}
    d["xin"] = nc.dram_tensor("xin", [nb, E, SEQ], F32, kind="ExternalInput")
    d["xout"] = nc.dram_tensor("xout", [nb, E, SEQ], F32, kind="ExternalOutput")
    for name, shape in WEIGHT_SHAPES.items():
        d[name] = nc.dram_tensor(name, list(shape), F32, kind="ExternalInput")
    ctx.d = d
    ctx.tap_drams = {t: nc.dram_tensor(
        "tap_" + t, [128 if t == "rtq" else (512 if t.endswith("t") else E),
                     1024 if t == "rexp" else SEQ], F32,
        kind="ExternalOutput") for t in taps}

    consts = _host_consts()
    identf_d = nc.inline_tensor(consts["ident_f32"], name="identf")
    identb_d = nc.inline_tensor(consts["ident_bf16"], name="identb")
    ctx.cd = {}
    ctx.cd["maskdiag"] = nc.inline_tensor(consts["maskdiag"], name="maskdiag")
    ctx.cd["epsc"] = nc.inline_tensor(consts["epsc"], name="epsc")
    ctx.cd["identf"] = identf_d
    if "l1_ffn" in blocks:
        ctx.cd["sel6"] = nc.inline_tensor(consts["sel6"], name="sel6")
        d["aw1s"] = nc.dram_tensor("aw1s", [1024, 4096], BF16, kind="Internal")
        d["aw2s"] = nc.dram_tensor("aw2s", [4096, 1024], BF16, kind="Internal")
        d["ew1s"] = nc.dram_tensor("ew1s", [6, 1024, 2048], BF16, kind="Internal")
        d["ew2s"] = nc.dram_tensor("ew2s", [6, 2048, 1024], BF16, kind="Internal")
        d["rs"] = nc.dram_tensor("rs", [1024, 1024], BF16, kind="Internal")
    if "l0_ffn" in blocks:
        for k in ("rotA_q", "rotB_q", "rotA_k", "rotB_k", "locmask", "locmaskp"):
            ctx.cd[k] = nc.inline_tensor(consts[k], name=k)
        d["xpark"] = nc.dram_tensor("xpark", [E, SEQ], F32, kind="Internal")
        d["w1s"] = nc.dram_tensor("w1s", [2, 1024, 5460], BF16, kind="Internal")
        d["w2s"] = nc.dram_tensor("w2s", [2, 2730, 1024], BF16, kind="Internal")

    ctx.xT = [m.alloc(f"xT{f}", [128, NPAD], F32, at=R_XT + f * XT_STRIDE) for f in range(NF)]
    co = R_CON
    ctx.ident_f32 = m.alloc("identf", [128, 128], F32, at=co); co += 512
    ctx.ident_bf16 = m.alloc("identb", [128, 128], BF16, at=co); co += 256
    ctx.ones_col = m.alloc("ones_col", [128, 1], F32, at=co); co += 32
    ctx.ones_col_bf = m.alloc("ones_col_bf", [128, 1], BF16, at=co); co += 32
    ctx.ones_row = m.alloc("ones_row", [1, 128], F32, at=co); co += 512
    ctx.cols = m.alloc("cols", [128, 96], F32, at=co); co += 384
    ctx.bq_col = m.alloc("bq_col", [128, 16], F32, at=co); co += 64
    ctx.bk_col = m.alloc("bk_col", [128, 16], F32, at=co); co += 64
    ctx.lam_s = m.alloc("lam_s", [1, 1], F32, at=co); co += 32
    ctx.eps_s = m.alloc("eps_s", [1, 1], F32, at=co); co += 32
    ctx.eps6_s = m.alloc("eps6_s", [1, 1], F32, at=co); co += 32
    ctx.maskc = m.alloc("maskc", [128, 128], F32, at=co); co += 512
    ctx.epsc = m.alloc("epsc", [2, 1], F32, at=co); co += 32
    assert co <= R_WAR
    ctx.WARENA = R_WAR
    ctx.BIG = R_BIG
    ctx.ln_sq = m.alloc("ln_sq", [128, 2048], F32, at=R_LNS)
    ctx.ln_misc = m.alloc("ln_misc", [128, 6144], F32, at=R_LNS + 8192)
    ctx.row_tmp = m.alloc("row_tmp", [1, 2048], F32, at=R_LNS)

    ctx.PS = nc.alloc_psum_tensor("PS", [128, 4096], F32)

    p = ctx.p
    p.inc("sync", lambda e: e.dma_start(ctx.ident_f32[:], identf_d.ap()), "dma", 16)
    p.inc("sync", lambda e: e.dma_start(ctx.ident_bf16[:], identb_d.ap()), "dma", 16)
    p.inc("sync", lambda e: e.dma_start(ctx.maskc[:], ctx.cd["maskdiag"].ap()), "dma", 16)
    p.inc("sync", lambda e: e.dma_start(ctx.epsc[:], ctx.cd["epsc"].ap()), "dma", 16)
    p.run(lambda e: e.memset(ctx.ones_col[:], 1.0), "gpsimd")
    p.run(lambda e: e.memset(ctx.ones_col_bf[:], 1.0), "gpsimd")
    p.run(lambda e: e.memset(ctx.ones_row[:], 1.0), "gpsimd")
    p.run(lambda e: e.memset(ctx.eps_s[:], LN_EPS), "gpsimd")
    p.run(lambda e: e.memset(ctx.eps6_s[:], 1e-6), "gpsimd")
    for f in range(NF):
        p.run(lambda e, f=f: e.memset(ctx.xT[f][:, 0:TOKOFF], 0.0), "gpsimd")
        if f < NF - 1:
            p.run(lambda e, f=f: e.memset(ctx.xT[f][:, TOKOFF + SEQ:NPAD], 0.0), "gpsimd")
    p.inc("gpsimd", lambda e: e.memset(ctx.xT[NF - 1][:, TOKOFF + SEQ:NPAD], 0.0), "gp", 1)
    p.wait("tensor", "gp")
    p.wait("vector", "gp")
    p.wait("scalar", "gp")
    p.wait("sync", "gp")

    _wp_setup(ctx, blocks)
    if ctx.wp_jobs:
        _wp_emit(ctx, None)
        p.barrier()
    if "l1_ffn" in blocks:
        _emit_expm(ctx)

    for b in range(nb):
        emit_load_x(ctx, b)
        for bi, blk in enumerate(blocks):
            BLOCK_EMITTERS[blk](ctx, b)
            tname = f"xb{b}p{bi}"
            if tname in ctx.taps:
                _tap_tiles(ctx, ctx.xT, tname, padded=True)
        emit_store_x(ctx, b)

    p.build()
    nc.finalize()
    return nc, ctx


BLOCK_EMITTERS["l0_attn"] = emit_l0_attn
BLOCK_EMITTERS["l1_attn"] = emit_l1_attn
BLOCK_EMITTERS["l1_ffn"] = emit_l1_ffn

def emit_l0_ffn(ctx, batch):
    p, m, nc = ctx.p, ctx.m, ctx.nc
    PS = ctx.PS
    B0 = ctx.BIG
    H, DH = 8, 64
    NT = 16
    NC2 = 22

    zT = [m.alloc(f"b2_z{f}", [128, NPAD], F32, at=B0 + f * XT_STRIDE) for f in range(NF)]
    rott = {k: m.alloc(f"b2_rt{k}", [128, 1024], F32, at=B0 + 8 * XT_STRIDE + i * 4096)
            for i, k in enumerate(("Aq", "Bq", "Ak", "Bk"))}
    vbuf = [m.alloc(f"b2_v{t}", [128, 8 * 65], BF16, at=R_XT + t * 1056) for t in range(NT)]
    qtb = [m.alloc(f"b2_qt{g}", [128, 2048], BF16, at=R_XT + 16896 + g * 4096) for g in range(4)]
    ktb = [m.alloc(f"b2_kt{g}", [128, 2048], BF16, at=R_XT + 33280 + g * 4096) for g in range(4)]
    PRE = R_XT + 49664
    sq32 = m.alloc("b2_sq32", [128, 512], F32, at=PRE)
    qn = m.alloc("b2_qn", [128, 512], F32, at=PRE + 2048)
    t1b = m.alloc("b2_t1", [128, 512], F32, at=PRE + 4096)
    t2b = m.alloc("b2_t2", [128, 512], F32, at=PRE + 6144)
    glb = m.alloc("b2_gl", [128, 512], BF16, at=PRE + 8192)
    sc32 = m.alloc("b2_sc32", [128, 512], F32, at=PRE + 9216)
    rowsb = m.alloc("b2_rows", [128, 512], F32, at=PRE + 11264)
    ssq = m.alloc("b2_ssq", [128, 16], F32, at=PRE + 13312)
    qsb = m.alloc("b2_qsb", [128, 64], F32, at=PRE + 13376)
    ksb = m.alloc("b2_ksb", [128, 64], F32, at=PRE + 13632)
    rbc = m.alloc("b2_rbc", [64, 512], F32, at=PRE + 13888)
    hnT = [m.alloc(f"b2_hn{f}", [128, NPAD], BF16, at=R_XT + f * 4128) for f in range(NF)]
    glub = [m.alloc(f"b2_glu{c}", [128, 512], BF16, at=R_XT + 33280 + c * 1024)
            for c in range(NC2)]
    hbT = [m.alloc(f"b2_hb{f}", [128, 2048], BF16, at=R_LNS + f * 4096) for f in range(NF)]
    oT = [m.alloc(f"b2_oT{g}", [128, 2048], BF16, at=R_LNS + g * 4096) for g in range(4)]
    W0 = ctx.WARENA
    wqkv_sb = [m.alloc(f"b2_wqkv{f}", [128, 1536], BF16, at=W0 + f * 3072) for f in range(NF)]
    masks_sb = m.alloc("b2_masks", [128, 4096], F32, at=W0)
    wol_sb = [m.alloc(f"b2_wo{i}", [128, 1024], BF16, at=W0 + 8192 + i * 2048) for i in range(4)]
    w1a_sb = [m.alloc(f"b2_w1a{i}", [128, 1024], BF16, at=W0 + 16384 + i * 2048) for i in range(2)]
    w1g_sb = [m.alloc(f"b2_w1g{i}", [128, 1024], BF16, at=W0 + 20480 + i * 2048) for i in range(2)]
    w2_sb = [m.alloc(f"b2_w2{i}", [128, 1024], BF16, at=W0 + 16384 + i * 2048) for i in range(2)]

    dpark = ctx.d["xpark"]

    p.barrier(("gpsimd",))
    for f in range(NF):
        p.run(lambda e, f=f: e.memset(zT[f][:, 0:TOKOFF], 0.0), "gpsimd")
        if f < NF - 1:
            p.run(lambda e, f=f: e.memset(zT[f][:, TOKOFF + SEQ:NPAD], 0.0), "gpsimd")
    p.inc("gpsimd", lambda e: e.memset(zT[NF - 1][:, TOKOFF + SEQ:NPAD], 0.0), "gp", 1)
    emit_ln(ctx, ctx.xT, zT)
    p.barrier(("sync",))
    for f in range(NF):
        p.inc("sync", lambda e, f=f: e.dma_start(
            dpark.ap()[f * 128:(f + 1) * 128, :], ctx.xT[f][:, tokc(0, SEQ)]), "st", 16)
    p.wait("gpsimd", "st")

    for lay in range(2):
        p.barrier(("sync", "tensor", "scalar", "vector"))
        p.inc("sync", lambda e, lay=lay: e.dma_start(
            ctx.row_tmp[0:1, 0:64], ctx.d["L0_ffn_q_scale"].ap()[lay].rearrange(
                "(a d) -> a d", a=1)), "dma", 16)
        p.wait("tensor", "dma")
        emit_bcast_row(ctx, ctx.row_tmp, qsb[:, 0:64], 64)
        p.inc("sync", lambda e, lay=lay: e.dma_start(
            ctx.row_tmp[0:1, 0:64], ctx.d["L0_ffn_k_scale"].ap()[lay].rearrange(
                "(a d) -> a d", a=1)), "dma", 16)
        p.wait("tensor", "dma")
        emit_bcast_row(ctx, ctx.row_tmp, ksb[:, 0:64], 64)
        for k, dname, scl in (("Aq", "rotA_q", qsb), ("Bq", "rotB_q", qsb),
                              ("Ak", "rotA_k", ksb), ("Bk", "rotB_k", ksb)):
            p.inc("sync", lambda e, k=k, dname=dname: e.dma_start(
                rott[k].ap().rearrange("p (tt d) -> p tt d", d=64),
                ctx.cd[dname].ap().rearrange("(tt pp) d -> pp tt d", pp=128)), "dma", 16)
        p.wait("vector", "dma")
        p.wait("vector", "act")
        for k, scl in (("Aq", qsb), ("Ak", ksb)):
            for t in range(NT):
                p.inc("vector", lambda e, k=k, t=t, scl=scl: e.tensor_tensor(
                    rott[k][:, t * 64:(t + 1) * 64], rott[k][:, t * 64:(t + 1) * 64],
                    scl[:, 0:64], ALU.mult), "dve")
        for k, scl in (("Bq", qsb), ("Bk", ksb)):
            for t in range(NT):
                p.inc("vector", lambda e, k=k, t=t, scl=scl: e.tensor_tensor(
                    rott[k][:, t * 64:t * 64 + 32], rott[k][:, t * 64:t * 64 + 32],
                    scl[:, 32:64], ALU.mult), "dve")
                p.inc("vector", lambda e, k=k, t=t, scl=scl: e.tensor_tensor(
                    rott[k][:, t * 64 + 32:(t + 1) * 64], rott[k][:, t * 64 + 32:(t + 1) * 64],
                    scl[:, 0:32], ALU.mult), "dve")
        if lay == 0 and batch == 0 and "rtq" in ctx.taps:
            p.barrier(("sync",))
            p.inc("sync", lambda e: e.dma_start(
                ctx.tap_drams["rtq"].ap()[0:128, 0:1024], rott["Aq"][:, 0:1024]), "st", 16)
            p.inc("sync", lambda e: e.dma_start(
                ctx.tap_drams["rtq"].ap()[0:128, 1024:2048], rott["Bq"][:, 0:1024]), "st", 16)
            p.wait("sync", "st")
        p.wait("gpsimd", "pe")
        for f in range(NF):
            p.inc("gpsimd", lambda e, f=f, lay=lay: e.dma_start(
                wqkv_sb[f][:], ctx.d["L0_ffn_wqkv"].ap()[lay].rearrange(
                    "(ft pp) c -> ft pp c", pp=128)[f]), "wdma", 16)
        p.wait("scalar", "dve")
        for f in range(NF):
            soff = TOKOFF - (1 if f >= 4 else 0)
            p.inc("scalar", lambda e, f=f, soff=soff: e.copy(
                hbT[f][:, 0:SEQ], zT[f][:, soff:soff + SEQ]), "act")
        p.wait("gpsimd", "act")
        for t in range(NT - 1):
            p.run(lambda e, t=t: e.memset(vbuf[t][:], 1.0), "gpsimd")
        p.inc("gpsimd", lambda e: e.memset(vbuf[NT - 1][:], 1.0), "gp", 1)
        p.wait("tensor", "gp")
        p.wait("tensor", "act")
        p.wait("tensor", "wdma")

        for t in range(NT):
            p.wait("tensor", "act")
            p.wait("tensor", "dve")
            for cch in range(3):
                cs = slice(cch * 512, cch * 512 + 512)
                for f in range(NF):
                    p.inc("tensor", lambda e, t=t, cs=cs, f=f: e.matmul(
                        PS[:, cs], hbT[f][:, t * 128:(t + 1) * 128],
                        wqkv_sb[f][:, cs], start=(f == 0), stop=(f == NF - 1)), "pe")
            p.wait("scalar", "pe")
            p.wait("vector", "pe")
            p.inc("scalar", lambda e, t=t: e.copy(
                vbuf[t].ap().rearrange("p (h c) -> p h c", c=65)[:, :, 0:64],
                PS[:, 1024:1536].rearrange("p (h c) -> p h c", c=64)), "act")
            for qk in range(2):
                po = qk * 512
                AT, BT = (rott["Aq"], rott["Bq"]) if qk == 0 else (rott["Ak"], rott["Bk"])
                sso = qk * 8
                p.inc("scalar", lambda e, po=po: e.activation(
                    sq32[:, 0:512], PS[:, po:po + 512], AF.Square), "act")
                p.wait("vector", "act")
                p.inc("vector", lambda e, sso=sso: e.reduce_sum(
                    ssq[:, sso:sso + 8],
                    sq32.ap().rearrange("p (h d) -> p h d", d=64), axis=mybir.AxisListType.X),
                    "dve")
                p.wait("scalar", "dve")
                p.inc("scalar", lambda e, sso=sso: e.activation(
                    ssq[:, sso:sso + 8], ssq[:, sso:sso + 8], AF.Sqrt), "act")
                p.wait("vector", "act")
                p.inc("vector", lambda e, sso=sso: e.tensor_scalar(
                    ssq[:, sso:sso + 8], ssq[:, sso:sso + 8], 1e-12, None, ALU.max), "dve")
                p.wait("vector", "dve")
                p.inc("vector", lambda e, sso=sso: e.reciprocal(
                    ssq[:, sso:sso + 8], ssq[:, sso:sso + 8]), "dve")
                p.wait("vector", "dve")
                p.wait("vector", "pe")
                for h in range(H):
                    p.inc("vector", lambda e, h=h, po=po, sso=sso: e.tensor_scalar(
                        qn[:, h * 64:(h + 1) * 64], PS[:, po + h * 64:po + (h + 1) * 64],
                        ssq[:, sso + h:sso + h + 1], None, ALU.mult), "dve")
                p.wait("vector", "dve")
                p.inc("vector", lambda e, t=t, AT=AT: e.tensor_tensor(
                    t1b.ap().rearrange("p (h d) -> p h d", d=64),
                    qn.ap().rearrange("p (h d) -> p h d", d=64),
                    AT[:, t * 64:(t + 1) * 64].rearrange("p (a d) -> p a d", a=1)
                    .broadcast_to([128, 8, 64]), ALU.mult), "dve")
                p.inc("vector", lambda e, t=t, BT=BT: e.tensor_tensor(
                    t2b.ap().rearrange("p (h d) -> p h d", d=64)[:, :, 0:32],
                    qn.ap().rearrange("p (h d) -> p h d", d=64)[:, :, 32:64],
                    BT[:, t * 64:t * 64 + 32].rearrange("p (a d) -> p a d", a=1)
                    .broadcast_to([128, 8, 32]), ALU.mult), "dve")
                p.inc("vector", lambda e, t=t, BT=BT: e.tensor_tensor(
                    t2b.ap().rearrange("p (h d) -> p h d", d=64)[:, :, 32:64],
                    qn.ap().rearrange("p (h d) -> p h d", d=64)[:, :, 0:32],
                    BT[:, t * 64 + 32:(t + 1) * 64].rearrange("p (a d) -> p a d", a=1)
                    .broadcast_to([128, 8, 32]), ALU.mult), "dve")
                p.wait("vector", "dve")
                p.inc("vector", lambda e: e.tensor_tensor(
                    t1b[:, 0:512], t1b[:, 0:512], t2b[:, 0:512], ALU.add), "dve")
                p.wait("tensor", "dve")
                p.wait("tensor", "act")
                for hh in range(H):
                    p.inc("tensor", lambda e, hh=hh: e.transpose(
                        PS[0:64, 1024 + hh * 128:1024 + (hh + 1) * 128],
                        t1b[:, hh * 64:(hh + 1) * 64], ctx.ident_f32[:]), "pe")
                p.wait("scalar", "pe")
                dst = qtb if qk == 0 else ktb
                for hh in range(H):
                    g, rbase = hh // 2, (hh % 2) * 64
                    p.inc("scalar", lambda e, g=g, rbase=rbase, hh=hh, t=t, dst=dst: e.copy(
                        dst[g][rbase:rbase + 64, t * 128:(t + 1) * 128],
                        PS[0:64, 1024 + hh * 128:1024 + (hh + 1) * 128]), "act")
                p.wait("tensor", "act")
                p.wait("vector", "act")

        p.wait("sync", "pe")
        for r in range(4):
            p.inc("sync", lambda e, r=r: e.dma_start(
                masks_sb[:, r * 512:(r + 1) * 512], ctx.cd["locmaskp"].ap()[r]), "dma", 16)
            p.inc("sync", lambda e, r=r: e.dma_start(
                masks_sb[:, 2048 + r * 512:2048 + (r + 1) * 512], ctx.cd["locmask"].ap()[r]),
                "dma", 16)
        p.wait("vector", "dma")
        p.wait("scalar", "dma")
        for h in range(H):
            g, rbase = h // 2, (h % 2) * 64
            for w in range(4):
                akts = list(range(4 * (w - 1), 4 * w)) if w > 0 else []
                akts += list(range(4 * w, 4 * w + 4))
                first = True
                for j, akt in enumerate(akts):
                    cur = (akt >= 4 * w)
                    r = akt - 4 * w if cur else akt - 4 * (w - 1)
                    moff = (2048 + r * 512) if cur else (r * 512)
                    pso = 2048 + (j % 2) * 512
                    p.wait("tensor", "act")
                    p.wait("tensor", "dve")
                    p.inc("tensor", lambda e, g=g, rbase=rbase, akt=akt, w=w, pso=pso: e.matmul(
                        PS[:, pso:pso + 512],
                        ktb[g][rbase:rbase + 64, akt * 128:(akt + 1) * 128],
                        qtb[g][rbase:rbase + 64, w * 512:(w + 1) * 512],
                        start=True, stop=True), "pe")
                    p.wait("vector", "pe")
                    p.inc("vector", lambda e, moff=moff, pso=pso: e.tensor_tensor(
                        sc32[:, 0:512], PS[:, pso:pso + 512],
                        masks_sb[:, moff:moff + 512], ALU.add), "dve")
                    p.wait("scalar", "dve")
                    p.wait("scalar", "pe")
                    p.inc("scalar", lambda e: e.activation(
                        glb[:, 0:512], sc32[:, 0:512], AF.Exp, scale=8.0), "act")
                    p.wait("tensor", "act")
                    p.inc("tensor", lambda e, h=h, akt=akt, first=first, last=(j == len(akts) - 1):
                          e.matmul(PS[0:65, 3072:3584], vbuf[akt][:, h * 65:(h + 1) * 65],
                                   glb[:, 0:512], start=first, stop=last), "pe")
                    first = False
                p.wait("scalar", "pe")
                p.inc("scalar", lambda e: e.activation(
                    rowsb[0:1, 0:512], PS[64:65, 3072:3584], AF.Ln), "act")
                p.wait("scalar", "act")
                p.inc("scalar", lambda e: e.activation(
                    rowsb[0:1, 0:512], rowsb[0:1, 0:512], AF.Exp, scale=-1.0), "act")
                p.wait("tensor", "act")
                p.inc("tensor", lambda e: e.matmul(
                    PS[0:64, 3584:4096], ctx.ones_row[0:1, 0:64], rowsb[0:1, 0:512],
                    start=True, stop=True), "pe")
                p.wait("scalar", "pe")
                p.inc("scalar", lambda e: e.copy(rbc[0:64, 0:512], PS[0:64, 3584:4096]), "act")
                p.wait("vector", "act")
                p.inc("vector", lambda e, g=g, rbase=rbase, w=w: e.tensor_tensor(
                    oT[g][rbase:rbase + 64, w * 512:(w + 1) * 512],
                    PS[0:64, 3072:3584], rbc[0:64, 0:512], ALU.mult), "dve")
                p.wait("tensor", "dve")

        if lay == 0 and batch == 0:
            for nm, tiles in (("q0t", qtb), ("k0t", ktb), ("o0t", oT)):
                if nm in ctx.taps:
                    p.barrier(("sync",))
                    dram = ctx.tap_drams[nm]
                    for g in range(4):
                        for ch in range(4):
                            p.wait("vector", "st")
                            p.inc("vector", lambda e, g=g, ch=ch, tiles=tiles: e.tensor_copy(
                                sc32[:, 0:512], tiles[g][:, ch * 512:(ch + 1) * 512]), "dve")
                            p.wait("sync", "dve")
                            p.inc("sync", lambda e, g=g, ch=ch, dram=dram: e.dma_start(
                                dram.ap()[g * 128:(g + 1) * 128, ch * 512:(ch + 1) * 512],
                                sc32[:, 0:512]), "st", 16)
                            p.wait("sync", "st")
                    p.wait("gpsimd", "st")
        p.wait("gpsimd", "pe")
        p.wait("gpsimd", "act")
        for i in range(4):
            p.inc("gpsimd", lambda e, i=i, lay=lay: e.dma_start(
                wol_sb[i][:], ctx.d["L0_ffn_wo"].ap()[lay].rearrange(
                    "(it pp) o -> it pp o", pp=128)[i]), "wdma", 16)
        p.wait("tensor", "wdma")
        for fo in range(NF):
            for cch in range(4):
                cs = slice(cch * 512, cch * 512 + 512)
                pso = (cch % 4) * 512
                p.wait("tensor", "dve")
                for i in range(4):
                    p.inc("tensor", lambda e, fo=fo, cs=cs, i=i, pso=pso: e.matmul(
                        PS[:, pso:pso + 512], wol_sb[i][:, fo * 128:(fo + 1) * 128],
                        oT[i][:, cs], start=(i == 0), stop=(i == 3)), "pe")
                p.wait("vector", "pe")
                p.inc("vector", lambda e, fo=fo, cs=cs, pso=pso: e.tensor_tensor(
                    zT[fo][:, TOKOFF + cs.start:TOKOFF + cs.stop],
                    zT[fo][:, TOKOFF + cs.start:TOKOFF + cs.stop],
                    PS[:, pso:pso + 512], ALU.add), "dve")

        if f"zo{lay}" in ctx.taps and batch == 0:
            _tap_tiles(ctx, zT, f"zo{lay}", padded=True)
        emit_colvec(ctx, ctx.d["L0_ffn_ln_g"].ap()[lay].rearrange("(a e) -> a e", a=1), E, 0)
        emit_colvec(ctx, ctx.d["L0_ffn_ln_b"].ap()[lay].rearrange("(a e) -> a e", a=1), E, 8)
        emit_colvec(ctx, ctx.d["L0_ffn_b2"].ap()[lay].rearrange("(a e) -> a e", a=1), E, 16)
        emit_colvec(ctx, ctx.d["L0_ffn_b1"].ap()[lay][0:2048].rearrange("(a e) -> a e", a=1),
                    2048, 24)
        emit_colvec(ctx, ctx.d["L0_ffn_b1"].ap()[lay][2048:2730].rearrange("(a e) -> a e", a=1),
                    682, 40)
        emit_colvec(ctx, ctx.d["L0_ffn_b1"].ap()[lay][2730:4778].rearrange("(a e) -> a e", a=1),
                    2048, 46)
        emit_colvec(ctx, ctx.d["L0_ffn_b1"].ap()[lay][4778:5460].rearrange("(a e) -> a e", a=1),
                    682, 62)
        offs = [TOKOFF - (1 if f >= 4 else 0) for f in range(NF)]
        emit_ln(ctx, zT, hnT, gamma_off=0, beta_off=8, src_offs=offs)
        for tcn in range(4):
            TC = slice(tcn * 512, tcn * 512 + 512)
            evm = {}

            def ld_w1(c, lay=lay):
                cw = 128 if c < 21 else 42
                return [lambda e, c=c, cw=cw, lay=lay: e.dma_start(
                    w1a_sb[c % 2].ap().rearrange("p (ft c) -> p ft c", c=128)[:, :, 0:cw],
                    ctx.d["w1s"].ap()[lay].rearrange("(ft pp) hh -> pp ft hh", pp=128)
                    [:, :, c * 128:c * 128 + cw]),
                    lambda e, c=c, cw=cw, lay=lay: e.dma_start(
                    w1g_sb[c % 2].ap().rearrange("p (ft c) -> p ft c", c=128)[:, :, 0:cw],
                    ctx.d["w1s"].ap()[lay].rearrange("(ft pp) hh -> pp ft hh", pp=128)
                    [:, :, 2730 + c * 128:2730 + c * 128 + cw])]

            def use_w1(c, TC=TC):
                cw = 128 if c < 21 else 42
                wa, wg = w1a_sb[c % 2], w1g_sb[c % 2]
                po = (c % 2) * 2048
                if c - 2 in evm:
                    p.wait("tensor", "dve", evm[c - 2][0])
                    p.wait("tensor", "act", evm[c - 2][1])
                for f in range(NF):
                    p.inc("tensor", lambda e, c=c, cw=cw, f=f, TC=TC, wa=wa, po=po: e.matmul(
                        PS[0:cw, po:po + 512], wa[:, f * 128:f * 128 + cw],
                        hnT[f][:, TOKOFF + TC.start:TOKOFF + TC.stop],
                        start=(f == 0), stop=(f == NF - 1)), "pe")
                    p.inc("tensor", lambda e, c=c, cw=cw, f=f, TC=TC, wg=wg, po=po: e.matmul(
                        PS[0:cw, po + 1024:po + 1536], wg[:, f * 128:f * 128 + cw],
                        hnT[f][:, TOKOFF + TC.start:TOKOFF + TC.stop],
                        start=(f == 0), stop=(f == NF - 1)), "pe")
                pm = p.mark("pe")
                p.wait("scalar", "pe", pm)
                if c - 1 in evm:
                    p.wait("scalar", "dve", evm[c - 1][0])
                p.inc("scalar", lambda e, c=c, cw=cw, po=po: e.activation(
                    glb[0:cw, 0:512], PS[0:cw, po + 1024:po + 1536], AF.Gelu,
                    bias=ctx.cols[0:cw, 46 + c:47 + c]), "act")
                am = p.mark("act")
                p.wait("vector", "act", am)
                p.inc("vector", lambda e, c=c, cw=cw, po=po: e.scalar_tensor_tensor(
                    glub[c][0:cw, 0:512], PS[0:cw, po:po + 512], ctx.cols[0:cw, 24 + c:25 + c],
                    glb[0:cw, 0:512], ALU.add, ALU.mult), "dve")
                evm[c] = (p.mark("dve"), am)

            p.wait("tensor", "act")
            p.wait("tensor", "dve")
            stream_weights(p, NC2, ld_w1, use_w1, "w2a")
            p.wait("tensor", "dve")

            def ld_w2(c, lay=lay):
                cw = 128 if c < 21 else 42
                return lambda e, c=c, cw=cw, lay=lay: e.dma_start(
                    w2_sb[c % 2][0:cw, :], ctx.d["w2s"].ap()[lay][c * 128:c * 128 + cw, :])

            def use_w2(c):
                cw = 128 if c < 21 else 42
                wc = w2_sb[c % 2]
                for fo in range(NF):
                    p.inc("tensor", lambda e, c=c, cw=cw, fo=fo, wc=wc: e.matmul(
                        PS[:, fo * 512:(fo + 1) * 512], wc[0:cw, fo * 128:(fo + 1) * 128],
                        glub[c][0:cw, 0:512], start=(c == 0), stop=(c == NC2 - 1)), "pe")

            stream_weights(p, NC2, ld_w2, use_w2, "w2b")
            p.wait("scalar", "pe")
            p.wait("vector", "pe")
            for fo in range(NF):
                p.inc("scalar", lambda e, fo=fo: e.activation(
                    sc32[:, 0:512], PS[:, fo * 512:(fo + 1) * 512], AF.Identity,
                    bias=ctx.cols[:, 16 + fo:17 + fo]), "act")
                p.wait("vector", "act")
                p.inc("vector", lambda e, fo=fo, TC=TC: e.tensor_tensor(
                    zT[fo][:, TOKOFF + TC.start:TOKOFF + TC.stop],
                    zT[fo][:, TOKOFF + TC.start:TOKOFF + TC.stop],
                    sc32[:, 0:512], ALU.add), "dve")
                p.wait("scalar", "dve")

        if f"z{lay}" in ctx.taps and batch == 0:
            p.barrier(("tensor", "scalar", "vector", "sync"))
            _tap_tiles(ctx, zT, f"z{lay}", padded=True)

    p.barrier(("sync",))
    for f in range(NF):
        p.inc("sync", lambda e, f=f: e.dma_start(
            ctx.xT[f][:, tokc(0, SEQ)], dpark.ap()[f * 128:(f + 1) * 128, :]), "dma", 16)
    p.wait("vector", "dma")
    for f in range(NF):
        p.inc("vector", lambda e, f=f: e.tensor_tensor(
            ctx.xT[f][:, tokc(0, SEQ)], ctx.xT[f][:, tokc(0, SEQ)],
            zT[f][:, tokc(0, SEQ)], ALU.add), "dve")
    p.barrier(("tensor", "scalar", "vector", "sync", "gpsimd"))


BLOCK_EMITTERS["l0_ffn"] = emit_l0_ffn



def prep_core_inputs(inputs, core, nb=2, boff=None):
    x = inputs["L0_attn_X"]
    b0 = core * nb if boff is None else boff
    xs = np.ascontiguousarray(
        np.transpose(x[b0:b0 + nb], (0, 2, 1))).astype(np.float32)
    m = {"xin": xs}
    for name, shape in WEIGHT_SHAPES.items():
        if name in inputs:
            m[name] = np.ascontiguousarray(np.asarray(inputs[name], dtype=np.float32))
    m["bq_t"] = np.ascontiguousarray(np.asarray(inputs["L0_attn_bq"], np.float32).T)
    m["bk_t"] = np.ascontiguousarray(np.asarray(inputs["L0_attn_bk"], np.float32).T)
    m["lam11"] = np.asarray(inputs["L0_attn_lam"], np.float32).reshape(1, 1)
    return m


ALL_BLOCKS = ("l0_attn", "l0_ffn", "l1_attn", "l1_ffn")


def kernel(**inputs):
    nc, _ = build_program(blocks=ALL_BLOCKS, taps=(), nb=2)
    in_maps = [prep_core_inputs(inputs, c) for c in range(NCORES)]
    res = run_bass_kernel_spmd(nc, in_maps, core_ids=list(range(NCORES)))
    outs = []
    for c in range(NCORES):
        o = np.asarray(res.results[c]["xout"])
        outs.append(np.transpose(o, (0, 2, 1)))
    return np.concatenate(outs, axis=0).astype(np.float32)


# version 11
# speedup vs baseline: 1.5818x; 1.0450x over previous
import contextlib
import numpy as np
import ml_dtypes
import concourse.bass as bass
import concourse.mybir as mybir
from concourse.bass_utils import run_bass_kernel_spmd

F32 = mybir.dt.float32
BF16 = mybir.dt.bfloat16
AF = mybir.ActivationFunctionType
ALU = mybir.AluOpType

NCORES = 8
SEQ = 2048
E = 1024
NF = 8
TOKOFF = 2
NPAD = TOKOFF + SEQ + 2
LN_EPS = 1e-5
KB = 1024

XT_STRIDE = 8224
R_XT = 0
R_LNS = 65792
R_CON = 99072
R_WAR = 101632
R_BIG = 126208


class Prog:
    ENGINES = ("sync", "tensor", "vector", "scalar", "gpsimd")

    def __init__(self, nc):
        self.nc = nc
        self.streams = {e: [] for e in self.ENGINES}
        self.sems = {}
        self.counts = {}
        self._last_dma = {}
        self._es = contextlib.ExitStack()

    def sem(self, name):
        if name not in self.sems:
            self.sems[name] = self._es.enter_context(self.nc.semaphore(name))
            self.counts[name] = 0
        return self.sems[name]

    def emit(self, engine, fn):
        self.streams[engine].append(fn)

    def wait(self, engine, semname, value=None):
        self._last_dma[engine] = None
        if semname in self.DMA_SEMS:
            for e2 in list(self._last_dma):
                if self._last_dma.get(e2) == semname:
                    self._last_dma[e2] = None
        s = self.sem(semname)
        t = self.counts[semname] if value is None else value
        if t <= 0:
            return
        self.emit(engine, lambda eng, s=s, t=t: eng.wait_ge(s, t))

    DMA_SEMS = ("dma", "wdma", "st", "rt", "wdma2", "st2")

    def inc(self, engine, fn, semname, amount=1):
        s = self.sem(semname)
        if semname in self.DMA_SEMS:
            key = (engine, semname)
            if self._last_dma.get(engine) != semname:
                t = self.counts[semname]
                if t > 0:
                    self.emit(engine, lambda eng, s=s, t=t: eng.wait_ge(s, t))
            self._last_dma[engine] = semname
        else:
            self._last_dma[engine] = None
        self.counts[semname] += amount
        self.emit(engine, lambda eng, s=s, a=amount, f=fn: f(eng).then_inc(s, a))

    def run(self, fn, engine):
        self._last_dma[engine] = None
        self.emit(engine, fn)

    def mark(self, semname):
        self.sem(semname)
        return self.counts[semname]

    def barrier(self, engines=("tensor", "vector", "scalar", "gpsimd", "sync"),
                sems=("pe", "dve", "act", "gp", "dma", "wdma", "st", "wdma2", "st2")):
        for e in ("pe", "dve", "act", "gp"):
            self.sem(e)
        for eng in engines:
            for s in sems:
                self.wait(eng, s)

    def build(self):
        with self.nc.Block() as block:
            for e in self.ENGINES:
                if self.streams[e]:
                    getattr(block, e)(lambda eng, ops=tuple(self.streams[e]): [f(eng) for f in ops])
        return self.nc


class SbufMap:
    BASE = 16512

    def __init__(self, nc, limit=212500):
        self.nc = nc
        self.limit = limit
        self.cursor = 0

    def alloc(self, name, shape, dtype, at=None):
        dsize = 2 if dtype == BF16 else 4
        nbytes = shape[1] * dsize
        if at is None:
            at = (self.cursor + 31) & ~31
            self.cursor = at + nbytes
        assert at + nbytes <= self.limit, f"SBUF overflow at {name}: {at + nbytes}"
        return self.nc.alloc_sbuf_tensor_at(name, list(shape), dtype, offset=self.BASE + at)

    def arena(self, nbytes):
        at = (self.cursor + 31) & ~31
        self.cursor = at + nbytes
        assert self.cursor <= self.limit, f"SBUF overflow (arena): {self.cursor}"
        return at


class Ctx:
    pass


def tokc(start, n):
    return slice(TOKOFF + start, TOKOFF + start + n)


def emit_ln(ctx, src_tiles, dst_tiles, gamma_off=None, beta_off=None,
            resid=False, src_padded=True, src_bf=False, src_offs=None):
    p = ctx.p
    PS = ctx.PS
    so = TOKOFF if src_padded else 0
    n = SEQ
    offs = src_offs if src_offs is not None else [so] * NF
    sq, misc = ctx.ln_sq, ctx.ln_misc
    ones_src = ctx.ones_col_bf if src_bf else ctx.ones_col

    p.barrier(("tensor", "scalar", "vector"))
    for f in range(NF):
        p.wait("scalar", "pe")
        p.inc("scalar", lambda e, f=f: e.activation(
            sq[:, 0:n], src_tiles[f][:, offs[f]:offs[f] + n], AF.Square), "act")
        am = p.mark("act")
        p.wait("tensor", "act", am)
        for cch in range(4):
            cs = slice(cch * 512, cch * 512 + 512)
            scs = slice(offs[f] + cch * 512, offs[f] + cch * 512 + 512)
            p.inc("tensor", lambda e, f=f, cs=cs, scs=scs: e.matmul(
                PS[0:1, cs], ones_src[:], src_tiles[f][:, scs],
                start=(f == 0), stop=(f == NF - 1)), "pe")
            p.inc("tensor", lambda e, f=f, cs=cs: e.matmul(
                PS[0:1, 2048 + cs.start:2048 + cs.stop], ctx.ones_col[:], sq[:, cs],
                start=(f == 0), stop=(f == NF - 1)), "pe")
    p.wait("vector", "pe")
    p.inc("vector", lambda e: e.tensor_scalar(
        misc[0:1, 0:n], PS[0:1, 0:n], 1.0 / E, None, ALU.mult), "dve")
    p.inc("vector", lambda e: e.tensor_scalar(
        misc[0:1, 2048:2048 + n], PS[0:1, 2048:2048 + n], 1.0 / E, None, ALU.mult), "dve")
    p.wait("vector", "dve")
    p.inc("vector", lambda e: e.tensor_tensor(
        misc[0:1, 4096:4096 + n], misc[0:1, 0:n], misc[0:1, 0:n], ALU.mult), "dve")
    p.wait("vector", "dve")
    p.inc("vector", lambda e: e.tensor_tensor(
        misc[0:1, 4096:4096 + n], misc[0:1, 2048:2048 + n], misc[0:1, 4096:4096 + n],
        ALU.subtract), "dve")
    p.wait("scalar", "dve")
    p.inc("scalar", lambda e: e.activation(
        misc[0:1, 2048:2048 + n], misc[0:1, 4096:4096 + n], AF.Sqrt, bias=ctx.eps_s[0:1, 0:1]), "act")
    p.wait("vector", "act")
    p.inc("vector", lambda e: e.reciprocal(
        misc[0:1, 2048:2048 + n], misc[0:1, 2048:2048 + n]), "dve")
    p.wait("vector", "dve")
    p.inc("vector", lambda e: e.tensor_tensor(
        misc[0:1, 4096:4096 + n], misc[0:1, 0:n], misc[0:1, 2048:2048 + n], ALU.mult), "dve")
    p.wait("vector", "dve")
    p.inc("vector", lambda e: e.tensor_scalar(
        misc[0:1, 4096:4096 + n], misc[0:1, 4096:4096 + n], -1.0, None, ALU.mult), "dve")
    p.wait("tensor", "dve")
    for cch in range(4):
        cs = slice(cch * 512, cch * 512 + 512)
        p.inc("tensor", lambda e, cs=cs: e.matmul(
            PS[:, cs], ctx.ones_row[0:1, :], misc[0:1, 2048 + cs.start:2048 + cs.stop],
            start=True, stop=True), "pe")
        p.inc("tensor", lambda e, cs=cs: e.matmul(
            PS[:, 2048 + cs.start:2048 + cs.stop], ctx.ones_row[0:1, :],
            misc[0:1, 4096 + cs.start:4096 + cs.stop], start=True, stop=True), "pe")
    p.wait("vector", "pe")
    for f in range(NF):
        p.wait("vector", "act")
        p.inc("vector", lambda e, f=f: e.tensor_tensor(
            sq[:, 0:n], src_tiles[f][:, offs[f]:offs[f] + n], PS[:, 0:n], ALU.mult), "dve")
        p.wait("vector", "dve")
        if gamma_off is None and not resid:
            p.inc("vector", lambda e, f=f: e.tensor_tensor(
                dst_tiles[f][:, tokc(0, n)], sq[:, 0:n], PS[:, 2048:2048 + n], ALU.add), "dve")
            p.wait("vector", "dve")
        else:
            p.inc("vector", lambda e, f=f: e.tensor_tensor(
                sq[:, 0:n], sq[:, 0:n], PS[:, 2048:2048 + n], ALU.add), "dve")
            p.wait("scalar", "dve")
            if gamma_off is not None:
                p.inc("scalar", lambda e, f=f: e.activation(
                    sq[:, 0:n], sq[:, 0:n], AF.Identity,
                    bias=ctx.cols[:, beta_off + f:beta_off + f + 1],
                    scale=ctx.cols[:, gamma_off + f:gamma_off + f + 1]), "act")
                p.wait("vector", "act")
            if resid:
                p.inc("vector", lambda e, f=f: e.tensor_tensor(
                    dst_tiles[f][:, tokc(0, n)], dst_tiles[f][:, tokc(0, n)],
                    sq[:, 0:n], ALU.add), "dve")
            else:
                p.inc("vector", lambda e, f=f: e.tensor_copy(
                    dst_tiles[f][:, tokc(0, n)], sq[:, 0:n]), "dve")
            p.wait("vector", "dve")
    p.barrier(("tensor", "scalar", "vector"))


def emit_colvec(ctx, dram_ap, n, dst_off, scale=None, dst=None):
    p = ctx.p
    if dst is None:
        dst = ctx.cols
    assert n <= 2048
    nt = (n + 127) // 128
    p.barrier(("sync", "tensor", "scalar"))
    p.inc("sync", lambda e: e.dma_start(ctx.row_tmp[0:1, 0:n], dram_ap), "dma", 16)
    p.wait("tensor", "dma")
    for i in range(nt):
        w = min(128, n - i * 128)
        p.inc("tensor", lambda e, i=i, w=w: e.transpose(
            ctx.PS[0:w, 3584 + i:3584 + i + 1], ctx.row_tmp[0:1, i * 128:i * 128 + w],
            ctx.ident_f32[0:1, 0:1]), "pe")
    p.wait("scalar", "pe")
    if scale is None:
        p.inc("scalar", lambda e: e.copy(
            dst[:, dst_off:dst_off + nt], ctx.PS[:, 3584:3584 + nt]), "act")
    else:
        p.inc("scalar", lambda e: e.activation(
            dst[:, dst_off:dst_off + nt], ctx.PS[:, 3584:3584 + nt], AF.Identity,
            bias=0.0, scale=scale), "act")
    p.wait("tensor", "act")


def emit_bcast_row(ctx, src_row_ap, dst_tile_ap, n, nparts=128):
    p = ctx.p
    p.wait("tensor", "act")
    p.wait("tensor", "dve")
    nch = (n + 511) // 512
    for cch in range(nch):
        w = min(512, n - cch * 512)
        p.inc("tensor", lambda e, cch=cch, w=w: e.matmul(
            ctx.PS[0:nparts, 2048 + cch * 512:2048 + cch * 512 + w],
            ctx.ones_row[0:1, 0:nparts],
            src_row_ap[0:1, cch * 512:cch * 512 + w], start=True, stop=True), "pe")
    p.wait("scalar", "pe")
    p.inc("scalar", lambda e: e.copy(dst_tile_ap, ctx.PS[0:nparts, 2048:2048 + n]), "act")
    p.wait("tensor", "act")
    p.wait("sync", "pe")
    p.wait("gpsimd", "pe")



def stream_weights(p, n, loadfn, consumefn, sembase, depth=2):
    semtot, pem = {}, {}

    def issue(c):
        sem = f"{sembase}{c % depth}"
        fns = loadfn(c)
        if not isinstance(fns, (list, tuple)):
            fns = [fns]
        for fn in fns:
            p.inc("gpsimd", fn, sem, 16)
        semtot[c] = p.mark(sem)

    p.wait("gpsimd", "pe")
    for c in range(min(depth, n)):
        issue(c)
    for c in range(n):
        p.wait("tensor", f"{sembase}{c % depth}", semtot[c])
        consumefn(c)
        pem[c] = p.mark("pe")
        if c + depth < n:
            p.wait("gpsimd", "pe", pem[c])
            issue(c + depth)


def emit_l0_attn(ctx, batch):
    p, m = ctx.p, ctx.m
    PS = ctx.PS
    H = 16
    B0 = ctx.BIG

    xnT = [m.alloc(f"b1_xnT{f}", [128, NPAD], BF16, at=B0 + f * 4128) for f in range(NF)]
    vbuf = [m.alloc(f"b1_v{t}", [128, 16 * 65], BF16, at=R_LNS + t * 2080) for t in range(16)]
    QOFF = B0 + 8 * 4128
    qt = m.alloc("b1_qt", [128, 2048], BF16, at=QOFF)
    kt = m.alloc("b1_kt", [128, 2048], BF16, at=QOFF + 4096)
    expb = m.alloc("b1_exp", [128, 2048], BF16, at=QOFF + 8192)
    rb = m.alloc("b1_rb", [64, 1024], F32, at=QOFF + 8192)
    tmb = m.alloc("b1_tmb", [64, 1024], F32, at=QOFF + 12288)
    rows2 = m.alloc("b1_rows2", [128, 1024], F32, at=QOFF + 16384)
    bv_bcast = m.alloc("b1_bvb", [128, 1024], F32, at=QOFF + 12288)
    OOFF = QOFF + 20480
    ocatT = [m.alloc(f"b1_ocat{g}", [128, 2048], BF16, at=OOFF + g * 4096) for g in range(NF)]
    ybuf = [m.alloc(f"b1_y{f}", [128, 2048], BF16, at=B0 + f * 4128) for f in range(NF)]
    ctx.bv_bcast = bv_bcast

    W0 = ctx.WARENA
    wv_sb = [m.alloc(f"b1_wv{f}", [128, 1024], BF16, at=W0 + f * 2048) for f in range(NF)]
    wq_sb = m.alloc("b1_wq", [128, 1024], BF16, at=W0 + 16 * KB)
    wk_sb = m.alloc("b1_wk", [128, 1024], BF16, at=W0 + 18 * KB)
    wo_sb = [m.alloc(f"b1_wo{t}", [128, 1024], BF16, at=W0 + t * 2048) for t in range(NF)]

    emit_ln(ctx, ctx.xT, xnT)
    if "xn" in ctx.taps and batch == 0:
        _tap_tiles(ctx, xnT, "xn", padded=True)
    if "xnb1" in ctx.taps and batch == 1:
        _tap_tiles(ctx, xnT, "xnb1", padded=True)

    emit_colvec(ctx, ctx.d["L0_attn_bo"].ap().rearrange("(a e) -> a e", a=1), E, 0)
    emit_colvec(ctx, ctx.d["L0_attn_gamma"].ap().rearrange("(a e) -> a e", a=1), E, 8, scale=0.2)
    emit_colvec(ctx, ctx.d["L0_attn_beta"].ap().rearrange("(a e) -> a e", a=1), E, 16, scale=0.2)
    p.inc("sync", lambda e: e.dma_start(ctx.lam_s[0:1, 0:1], ctx.d["lam11"].ap()), "dma", 16)
    p.inc("sync", lambda e: e.dma_start(ctx.bq_col[:, 0:H], ctx.d["bq_t"].ap()), "dma", 16)
    p.inc("sync", lambda e: e.dma_start(ctx.bk_col[:, 0:H], ctx.d["bk_t"].ap()), "dma", 16)
    p.wait("sync", "pe")
    p.inc("sync", lambda e: e.dma_start(ctx.row_tmp[0:1, 0:E],
                                        ctx.d["L0_attn_bv"].ap().rearrange("h d -> (h d)").rearrange("(a e) -> a e", a=1)), "dma", 16)
    p.wait("tensor", "dma")
    p.wait("vector", "dma")
    emit_bcast_row(ctx, ctx.row_tmp, ctx.bv_bcast[:, 0:E], E)

    p.wait("gpsimd", "pe")
    for f in range(NF):
        p.inc("gpsimd", lambda e, f=f: e.dma_start(
            wv_sb[f].ap().rearrange("p (h d) -> p h d", d=64),
            ctx.d["L0_attn_Wv"].ap().rearrange("h (ft pp) d -> ft pp h d", pp=128)[f]),
            "wdma", 16)
    for t in range(15):
        p.run(lambda e, t=t: e.memset(vbuf[t][:], 1.0), "gpsimd")
    p.inc("gpsimd", lambda e: e.memset(vbuf[15][:], 1.0), "gp", 1)
    p.wait("tensor", "wdma")
    p.wait("tensor", "gp")
    p.wait("tensor", "act")
    for t in range(16):
        pso = 2048 + (t % 2) * 1024
        p.wait("tensor", "dve")
        for cch in range(2):
            cs = slice(cch * 512, cch * 512 + 512)
            for f in range(NF):
                p.inc("tensor", lambda e, t=t, cs=cs, f=f, pso=pso: e.matmul(
                    PS[:, pso + cs.start:pso + cs.stop],
                    xnT[f][:, TOKOFF + t * 128:TOKOFF + (t + 1) * 128],
                    wv_sb[f][:, cs], start=(f == 0), stop=(f == NF - 1)), "pe")
        p.wait("vector", "pe")
        p.inc("vector", lambda e, t=t, pso=pso: e.tensor_tensor(
            vbuf[t].ap().rearrange("p (h c) -> p h c", c=65)[:, :, 0:64],
            PS[:, pso:pso + E].rearrange("p (h c) -> p h c", c=64),
            ctx.bv_bcast[:, 0:E].rearrange("p (h c) -> p h c", c=64), ALU.add), "dve")
    p.barrier()

    for h in range(H):
        p.wait("gpsimd", "pe")
        p.inc("gpsimd", lambda e, h=h: e.dma_start(
            wq_sb.ap().rearrange("p (ft c) -> p ft c", c=128),
            ctx.d["L0_attn_Wq"].ap()[h].rearrange("(ft pp) c -> pp ft c", pp=128)),
            "wdma", 16)
        p.inc("gpsimd", lambda e, h=h: e.dma_start(
            wk_sb.ap().rearrange("p (ft c) -> p ft c", c=128),
            ctx.d["L0_attn_Wk"].ap()[h].rearrange("(ft pp) c -> pp ft c", pp=128)),
            "wdma", 16)
        p.wait("tensor", "wdma")
        for cch in range(4):
            cs = slice(cch * 512, cch * 512 + 512)
            p.wait("tensor", "act")
            for f in range(NF):
                p.inc("tensor", lambda e, cs=cs, f=f: e.matmul(
                    PS[:, 2048:2560], wq_sb[:, f * 128:(f + 1) * 128],
                    xnT[f][:, TOKOFF + cs.start:TOKOFF + cs.stop],
                    start=(f == 0), stop=(f == NF - 1)), "pe")
                p.inc("tensor", lambda e, cs=cs, f=f: e.matmul(
                    PS[:, 2560:3072], wk_sb[:, f * 128:(f + 1) * 128],
                    xnT[f][:, TOKOFF + cs.start:TOKOFF + cs.stop],
                    start=(f == 0), stop=(f == NF - 1)), "pe")
            p.wait("scalar", "pe")
            p.inc("scalar", lambda e, h=h, cs=cs: e.activation(
                qt[:, cs], PS[:, 2048:2560], AF.Identity, bias=ctx.bq_col[:, h:h + 1]), "act")
            p.inc("scalar", lambda e, h=h, cs=cs: e.activation(
                kt[:, cs], PS[:, 2560:3072], AF.Identity, bias=ctx.bk_col[:, h:h + 1]), "act")
        p.wait("tensor", "act")

        for qc in range(2):
            qs0 = qc * 1024
            for br in range(2):
                oo = br * 1024
                dp = slice(0, 64) if br == 0 else slice(64, 128)
                expmarks = {}

                def emit_scores(kti, dp=dp, qs0=qs0):
                    so_ = 2048 + (kti % 2) * 1024
                    if kti - 2 in expmarks:
                        p.wait("tensor", "act", expmarks[kti - 2])
                    elif kti < 2:
                        p.wait("tensor", "act", expmarks.get("pre", 0))
                    for sc in range(2):
                        cs = slice(sc * 512, sc * 512 + 512)
                        p.inc("tensor", lambda e, kti=kti, cs=cs, qs0=qs0, so_=so_, dp=dp: e.matmul(
                            PS[:, so_ + cs.start:so_ + cs.stop],
                            kt[dp, kti * 128:(kti + 1) * 128],
                            qt[dp, qs0 + cs.start:qs0 + cs.stop],
                            start=True, stop=True), "pe")

                def emit_exp(kti):
                    so_ = 2048 + (kti % 2) * 1024
                    eo = (kti % 2) * 1024
                    p.wait("scalar", "pe")
                    p.inc("scalar", lambda e, so_=so_, eo=eo: e.activation(
                        expb[:, eo:eo + 1024], PS[:, so_:so_ + 1024], AF.Exp,
                        scale=0.125), "act")
                    expmarks[kti] = p.mark("act")

                def emit_pv(kti, h=h, oo=oo):
                    eo = (kti % 2) * 1024
                    p.wait("tensor", "act")
                    for sc in range(2):
                        cs = slice(sc * 512, sc * 512 + 512)
                        p.inc("tensor", lambda e, h=h, kti=kti, cs=cs, oo=oo, eo=eo: e.matmul(
                            PS[0:65, oo + cs.start:oo + cs.stop],
                            vbuf[kti][:, h * 65:(h + 1) * 65],
                            expb[:, eo + cs.start:eo + cs.stop],
                            start=(kti == 0), stop=(kti == 15)), "pe")

                expmarks["pre"] = p.mark("act")
                emit_scores(0)
                for kti in range(16):
                    emit_exp(kti)
                    if kti < 15:
                        emit_scores(kti + 1)
                    emit_pv(kti)
            p.wait("vector", "pe")
            g, rbase = h // 2, (h % 2) * 64
            for sc in range(2):
                cs = slice(sc * 512, sc * 512 + 512)
                p.wait("scalar", "pe")
                p.inc("scalar", lambda e, cs=cs: e.activation(
                    rows2[0:1, 0:512], PS[64:65, cs], AF.Ln), "act")
                p.inc("scalar", lambda e, cs=cs: e.activation(
                    rows2[0:1, 512:1024], PS[64:65, 1024 + cs.start:1024 + cs.stop],
                    AF.Ln), "act")
                p.wait("scalar", "act")
                p.inc("scalar", lambda e: e.activation(
                    rows2[0:1, 0:512], rows2[0:1, 0:512], AF.Exp, scale=-1.0), "act")
                p.inc("scalar", lambda e: e.activation(
                    rows2[0:1, 512:1024], rows2[0:1, 512:1024], AF.Exp, scale=-1.0), "act")
                p.wait("vector", "act")
                p.inc("vector", lambda e: e.tensor_scalar(
                    rows2[0:1, 512:1024], rows2[0:1, 512:1024], ctx.lam_s[0:1, 0:1], -1.0,
                    ALU.mult, ALU.mult), "dve")
                p.wait("tensor", "dve")
                p.wait("tensor", "act")
                p.inc("tensor", lambda e: e.matmul(
                    PS[0:64, 2048:2560], ctx.ones_row[0:1, 0:64],
                    rows2[0:1, 0:512], start=True, stop=True), "pe")
                p.inc("tensor", lambda e: e.matmul(
                    PS[0:64, 2560:3072], ctx.ones_row[0:1, 0:64],
                    rows2[0:1, 512:1024], start=True, stop=True), "pe")
                p.wait("scalar", "pe")
                p.inc("scalar", lambda e: e.copy(rb[0:64, 0:1024], PS[0:64, 2048:3072]), "act")
                p.wait("vector", "act")
                p.inc("vector", lambda e, cs=cs: e.tensor_tensor(
                    tmb[0:64, 0:512], PS[0:64, cs], rb[0:64, 0:512], ALU.mult), "dve")
                p.wait("vector", "dve")
                p.inc("vector", lambda e, cs=cs: e.tensor_tensor(
                    tmb[0:64, 512:1024], PS[0:64, 1024 + cs.start:1024 + cs.stop],
                    rb[0:64, 512:1024], ALU.mult), "dve")
                p.wait("vector", "dve")
                p.inc("vector", lambda e, cs=cs, g=g, rbase=rbase, qs0=qs0: e.tensor_tensor(
                    ocatT[g][rbase:rbase + 64, qs0 + cs.start:qs0 + cs.stop],
                    tmb[0:64, 0:512], tmb[0:64, 512:1024], ALU.add), "dve")
                p.wait("vector", "dve")
                p.wait("tensor", "dve")

    p.barrier()
    p.wait("gpsimd", "pe")
    p.wait("gpsimd", "dve")
    p.wait("gpsimd", "act")
    for it in range(NF):
        p.inc("gpsimd", lambda e, it=it: e.dma_start(
            wo_sb[it][:], ctx.d["L0_attn_Wo"].ap().rearrange(
                "(it2 pp) e -> it2 pp e", pp=128)[it]), "wdma", 16)
    p.wait("tensor", "wdma")
    for fo in range(NF):
        for cch in range(4):
            cs = slice(cch * 512, cch * 512 + 512)
            pso = (cch % 4) * 512
            p.wait("tensor", "act")
            for it in range(NF):
                p.inc("tensor", lambda e, fo=fo, cs=cs, it=it, pso=pso: e.matmul(
                    PS[:, pso:pso + 512], wo_sb[it][:, fo * 128:(fo + 1) * 128],
                    ocatT[it][:, cs], start=(it == 0), stop=(it == NF - 1)), "pe")
            p.wait("scalar", "pe")
            p.inc("scalar", lambda e, fo=fo, cs=cs, pso=pso: e.activation(
                ybuf[fo][:, cs], PS[:, pso:pso + 512], AF.Identity,
                bias=ctx.cols[:, fo:fo + 1]), "act")
    p.wait("tensor", "act")
    p.wait("vector", "act")
    if "y" in ctx.taps and batch == 0:
        _tap_tiles(ctx, ybuf, "y", padded=False)
    if "yb1" in ctx.taps and batch == 1:
        _tap_tiles(ctx, ybuf, "yb1", padded=False)

    emit_ln(ctx, ybuf, ctx.xT, gamma_off=8, beta_off=16, resid=True,
            src_padded=False, src_bf=True)



def emit_l1_attn(ctx, batch):
    p, m = ctx.p, ctx.m
    PS = ctx.PS
    H = 16
    B0 = ctx.BIG
    NT = 16

    xnT = [m.alloc(f"b3_xnT{f}", [128, NPAD], BF16, at=B0 + f * 4128) for f in range(NF)]
    scb = m.alloc("b3_scb", [128, 2048], F32, at=B0 + 33024)
    pb = m.alloc("b3_pb", [128, 2048], F32, at=B0 + 41216)
    pb2 = m.alloc("b3_pb2", [128, 2048], F32, at=R_LNS)
    qf = m.alloc("b3_qf", [128, 192], F32, at=B0 + 49408)
    rvb = m.alloc("b3_rvb", [128, 512], F32, at=B0 + 50176)
    ptb = m.alloc("b3_ptb", [128, 256], BF16, at=B0 + 52224)
    sums = m.alloc("b3_sums", [128, 8], F32, at=B0 + 52736)
    OOFF = B0 + 52768
    ocatT = [m.alloc(f"b3_ocat{g}", [128, 2048], BF16, at=OOFF + g * 4096) for g in range(NF)]

    W0 = ctx.WARENA
    wqkv_sb = [m.alloc(f"b3_wq{f}", [128, 192], BF16, at=W0 + f * 384) for f in range(NF)]
    brow = m.alloc("b3_brow", [1, 192], F32, at=W0 + 3072)
    bb = m.alloc("b3_bb", [128, 192], F32, at=W0 + 3872)
    qaugT = m.alloc("b3_qaug", [128, 2048], BF16, at=W0 + 4672)
    kaugT = m.alloc("b3_kaug", [128, 2048], BF16, at=W0 + 8768)
    vh = m.alloc("b3_vh", [128, 16 * 65], BF16, at=W0 + 12864)
    wout_sb = [m.alloc(f"b3_wo{t}", [128, 1024], BF16, at=W0 + t * 2048) for t in range(NF)]

    emit_ln(ctx, ctx.xT, xnT)
    if "xn3" in ctx.taps and batch == 1:
        _tap_tiles(ctx, xnT, "xn3", padded=True)
    emit_colvec(ctx, ctx.d["L1_attn_b_out"].ap().rearrange("(a e) -> a e", a=1), E, 0)

    for h in range(H):
        p.wait("gpsimd", "pe")
        p.wait("gpsimd", "dve")
        for f in range(NF):
            p.inc("gpsimd", lambda e, f=f, h=h: e.dma_start(
                wqkv_sb[f].ap().rearrange("p (c d) -> p c d", d=64),
                ctx.d["L1_attn_w_qkv"].ap().rearrange("(ft pp) c -> ft pp c", pp=128)[f]
                .rearrange("p (c e) -> p c e", e=1024)[:, :, h * 64:(h + 1) * 64]), "wdma", 16)
        p.wait("sync", "pe")
        p.inc("sync", lambda e, h=h: e.dma_start(
            brow.ap().rearrange("a (c d) -> a c d", d=64),
            ctx.d["L1_attn_b_qkv"].ap().rearrange("(a c e) -> a c e", a=1, e=1024)
            [:, :, h * 64:(h + 1) * 64]), "dma", 16)
        p.wait("tensor", "dma")
        p.wait("tensor", "act")
        p.wait("tensor", "dve")
        p.inc("tensor", lambda e: e.matmul(
            PS[:, 512:704], ctx.ones_row[0:1, :], brow[0:1, 0:192],
            start=True, stop=True), "pe")
        p.wait("scalar", "pe")
        p.inc("scalar", lambda e: e.copy(bb[:], PS[:, 512:704]), "act")
        if h == 0:
            p.run(lambda e: e.memset(sums[:, 1:2], 1.0), "gpsimd")
            p.inc("gpsimd", lambda e: e.memset(vh[:], 1.0), "gp", 1)
        p.wait("tensor", "gp")
        p.wait("tensor", "wdma")
        p.wait("tensor", "act")

        for t in range(NT):
            tc = slice(t * 128, (t + 1) * 128)
            p.wait("tensor", "dve")
            p.wait("tensor", "act")
            for f in range(NF):
                p.inc("tensor", lambda e, t=t, f=f: e.matmul(
                    PS[:, 0:192], xnT[f][:, TOKOFF + t * 128:TOKOFF + (t + 1) * 128],
                    wqkv_sb[f][:], start=(f == 0), stop=(f == NF - 1)), "pe")
            p.wait("vector", "pe")
            p.inc("vector", lambda e: e.tensor_tensor(
                qf[:, 0:192], PS[:, 0:192], bb[:, 0:192], ALU.add), "dve")
            p.wait("scalar", "dve")
            p.inc("scalar", lambda e, t=t: e.copy(
                vh[:, t * 65:t * 65 + 64], qf[:, 128:192]), "act")
            p.inc("scalar", lambda e: e.activation(
                ctx.ln_misc[:, 0:64], qf[:, 0:64], AF.Square, accum_out=sums[:, 0:1]), "act")
            p.inc("scalar", lambda e: e.activation(
                ctx.ln_misc[:, 64:128], qf[:, 64:128], AF.Square, accum_out=sums[:, 2:3]), "act")
            p.wait("tensor", "act")
            p.inc("tensor", lambda e: e.transpose(
                PS[0:64, 1024:1152], qf[:, 0:64], ctx.ident_f32[:]), "pe")
            p.inc("tensor", lambda e: e.transpose(
                PS[0:64, 1152:1280], qf[:, 64:128], ctx.ident_f32[:]), "pe")
            p.inc("tensor", lambda e: e.transpose(
                PS[0:2, 1280:1408], sums[:, 0:2], ctx.ident_f32[:]), "pe")
            p.inc("tensor", lambda e: e.transpose(
                PS[0:2, 1408:1536], sums[:, 1:3], ctx.ident_f32[:]), "pe")
            p.wait("scalar", "pe")
            p.inc("scalar", lambda e, tc=tc: e.copy(qaugT[0:64, tc], PS[0:64, 1024:1152]), "act")
            p.inc("scalar", lambda e, tc=tc: e.activation(
                kaugT[0:64, tc], PS[0:64, 1152:1280], AF.Identity, bias=0.0, scale=-2.0), "act")
            p.inc("scalar", lambda e, tc=tc: e.copy(qaugT[64:66, tc], PS[0:2, 1280:1408]), "act")
            p.inc("scalar", lambda e, tc=tc: e.activation(
                kaugT[64:66, tc], PS[0:2, 1408:1536], AF.Identity,
                bias=ctx.epsc[0:2, 0:1]), "act")

        g, rbase = h // 2, (h % 2) * 64
        pbufs = [pb, pb2]
        gck = [0]
        gtr = [0]
        chmA, chmD = {}, {}
        trmG, evkG = {}, {}
        sMax, sNorm, trEnd, oem = {}, {}, {}, {}

        def emit_C(t):
            nv = (t + 1) * 128
            nch = (nv + 511) // 512
            pbx = pbufs[t % 2]
            if t == 0:
                p.wait("tensor", "act")
                p.wait("tensor", "dve")
            first = True
            for c in range(nch):
                k = gck[0]; gck[0] += 1
                w_ = min(512, nv - c * 512)
                cs = slice(c * 512, c * 512 + w_)
                pd = (k % 2) * 1024
                pv_ = 512 + (k % 2) * 1024
                sb = slice((c % 2) * 512, (c % 2) * 512 + w_)
                sb1 = slice(1024 + (c % 2) * 512, 1024 + (c % 2) * 512 + w_)
                if k - 2 in chmA:
                    p.wait("tensor", "act", chmA[k - 2])
                    p.wait("tensor", "dve", chmD[k - 2])
                p.inc("tensor", lambda e, t=t, cs=cs, pd=pd: e.matmul(
                    PS[:, pd:pd + (cs.stop - cs.start)],
                    qaugT[0:64, t * 128:(t + 1) * 128], kaugT[0:64, cs],
                    start=True, stop=True), "pe")
                p.inc("tensor", lambda e, t=t, cs=cs, pv_=pv_: e.matmul(
                    PS[:, pv_:pv_ + (cs.stop - cs.start)],
                    qaugT[0:66, t * 128:(t + 1) * 128], kaugT[0:66, cs],
                    start=True, stop=True), "pe")
                p.wait("scalar", "pe")
                if first:
                    if t - 1 in sMax:
                        p.wait("scalar", "dve", sMax[t - 1])
                        p.wait("scalar", "act")
                    if t - 2 in trEnd:
                        p.wait("scalar", "pe", trEnd[t - 2])
                    first = False
                p.inc("scalar", lambda e, cs=cs, pd=pd: e.activation(
                    scb[:, cs], PS[:, pd:pd + (cs.stop - cs.start)], AF.Square,
                    scale=0.5), "act")
                if k - 2 in chmD:
                    p.wait("scalar", "dve", chmD[k - 2])
                p.inc("scalar", lambda e, sb=sb, pv_=pv_, pbx=pbx: e.activation(
                    pbx[:, sb], PS[:, pv_:pv_ + (sb.stop - sb.start)], AF.Square), "act")
                p.wait("scalar", "act")
                p.inc("scalar", lambda e, sb=sb, sb1=sb1, pbx=pbx: e.activation(
                    pbx[:, sb1], pbx[:, sb], AF.Ln), "act")
                p.wait("scalar", "act")
                p.inc("scalar", lambda e, sb=sb, sb1=sb1, pbx=pbx: e.activation(
                    pbx[:, sb], pbx[:, sb1], AF.Exp, scale=-1.0), "act")
                chmA[k] = p.mark("act")
                p.wait("vector", "act")
                p.inc("vector", lambda e, sb=sb, pv_=pv_, pbx=pbx: e.tensor_tensor(
                    pbx[:, sb], pbx[:, sb], PS[:, pv_:pv_ + (sb.stop - sb.start)],
                    ALU.mult), "dve")
                p.wait("vector", "dve")
                p.inc("vector", lambda e, cs=cs, sb=sb, pbx=pbx: e.tensor_tensor(
                    scb[:, cs], scb[:, cs], pbx[:, sb], ALU.mult), "dve")
                chmD[k] = p.mark("dve")

        def emit_S(t):
            nv = (t + 1) * 128
            pbx = pbufs[t % 2]
            p.wait("vector", "dve")
            p.inc("vector", lambda e, t=t: e.tensor_tensor(
                scb[:, t * 128:(t + 1) * 128], scb[:, t * 128:(t + 1) * 128],
                ctx.maskc[:, 0:128], ALU.add), "dve")
            p.wait("vector", "dve")
            p.inc("vector", lambda e, nv=nv: e.reduce_max(
                sums[:, 3:4], scb[:, 0:nv], axis=mybir.AxisListType.X, negate=True), "dve")
            sMax[t] = p.mark("dve")
            p.wait("scalar", "dve")
            p.inc("scalar", lambda e, nv=nv, pbx=pbx: e.activation(
                pbx[:, 0:nv], scb[:, 0:nv], AF.Exp, bias=sums[:, 3:4],
                accum_out=sums[:, 4:5]), "act")
            p.wait("vector", "act")
            p.inc("vector", lambda e: e.reciprocal(sums[:, 5:6], sums[:, 4:5]), "dve")
            p.wait("vector", "dve")
            p.inc("vector", lambda e, nv=nv, pbx=pbx: e.tensor_scalar(
                pbx[:, 0:nv], pbx[:, 0:nv], sums[:, 5:6], None, ALU.mult), "dve")
            sNorm[t] = p.mark("dve")

        def emit_P(t):
            pbx = pbufs[t % 2]

            def emit_tr(tk):
                gx = gtr[0] + tk
                po_ = 2048 + (gx % 2) * 1024
                if gx - 2 in evkG:
                    p.wait("tensor", "act", evkG[gx - 2])
                if tk == 0:
                    p.wait("tensor", "dve", sNorm[t])
                p.inc("tensor", lambda e, tk=tk, po_=po_, pbx=pbx: e.transpose(
                    PS[0:128, po_:po_ + 128], pbx[:, tk * 128:(tk + 1) * 128],
                    ctx.ident_f32[:]), "pe")
                trmG[gx] = p.mark("pe")

            def emit_ev(tk):
                gx = gtr[0] + tk
                po_ = 2048 + (gx % 2) * 1024
                eo_ = (gx % 2) * 128
                p.wait("scalar", "pe", trmG[gx])
                p.inc("scalar", lambda e, po_=po_, eo_=eo_: e.copy(
                    ptb[:, eo_:eo_ + 128], PS[0:128, po_:po_ + 128]), "act")
                evkG[gx] = p.mark("act")

            def emit_pv2(tk):
                gx = gtr[0] + tk
                eo_ = (gx % 2) * 128
                p.wait("tensor", "act", evkG[gx])
                if tk == 0 and t - 1 in oem:
                    p.wait("tensor", "act", oem[t - 1])
                p.inc("tensor", lambda e, tk=tk, t=t, eo_=eo_: e.matmul(
                    PS[0:65, 3584:3712], vh[:, tk * 65:(tk + 1) * 65],
                    ptb[:, eo_:eo_ + 128], start=(tk == 0), stop=(tk == t)), "pe")

            emit_tr(0)
            emit_ev(0)
            for tk in range(t + 1):
                if tk + 1 <= t:
                    emit_tr(tk + 1)
                emit_pv2(tk)
                if tk + 1 <= t:
                    emit_ev(tk + 1)
            gtr[0] += t + 1
            p.wait("scalar", "pe")
            p.inc("scalar", lambda e, g=g, rbase=rbase, t=t: e.copy(
                ocatT[g][rbase:rbase + 64, t * 128:(t + 1) * 128],
                PS[0:64, 3584:3712]), "act")
            oem[t] = p.mark("act")
            trEnd[t] = p.mark("pe")

        emit_C(0)
        emit_S(0)
        for t in range(NT):
            if t + 1 < NT:
                emit_C(t + 1)
            emit_P(t)
            if t + 1 < NT:
                emit_S(t + 1)
        p.wait("tensor", "act")

    if "oc3" in ctx.taps and batch == 1:
        _tap_tiles(ctx, ocatT, "oc3", padded=False)
    p.wait("gpsimd", "pe")
    p.wait("gpsimd", "act")
    p.wait("gpsimd", "dve")
    for it in range(NF):
        p.inc("gpsimd", lambda e, it=it: e.dma_start(
            wout_sb[it][:], ctx.d["L1_attn_w_out"].ap().rearrange(
                "(it2 pp) e -> it2 pp e", pp=128)[it]), "wdma", 16)
    p.wait("tensor", "wdma")
    for fo in range(NF):
        for cch in range(4):
            cs = slice(cch * 512, cch * 512 + 512)
            pso = (cch % 4) * 512
            p.wait("tensor", "dve")
            for it in range(NF):
                p.inc("tensor", lambda e, fo=fo, cs=cs, it=it, pso=pso: e.matmul(
                    PS[:, pso:pso + 512], wout_sb[it][:, fo * 128:(fo + 1) * 128],
                    ocatT[it][:, cs], start=(it == 0), stop=(it == NF - 1)), "pe")
            p.wait("scalar", "pe")
            p.inc("scalar", lambda e, fo=fo, pso=pso: e.activation(
                ctx.ln_sq[:, 0:512], PS[:, pso:pso + 512], AF.Identity,
                bias=ctx.cols[:, fo:fo + 1]), "act")
            p.wait("vector", "act")
            p.inc("vector", lambda e, fo=fo, cs=cs: e.tensor_tensor(
                ctx.xT[fo][:, TOKOFF + cs.start:TOKOFF + cs.stop],
                ctx.xT[fo][:, TOKOFF + cs.start:TOKOFF + cs.stop],
                ctx.ln_sq[:, 0:512], ALU.add), "dve")
            p.wait("scalar", "dve")
    p.barrier(("tensor", "scalar", "vector", "sync", "gpsimd"))



def emit_l1_ffn(ctx, batch):
    p, m = ctx.p, ctx.m
    PS = ctx.PS
    B0 = ctx.BIG
    TCW = 256
    NTC = SEQ // TCW

    xnb = [m.alloc(f"b4_xn{f}", [128, NPAD], F32, at=B0 + f * XT_STRIDE) for f in range(NF)]
    xnn = [m.alloc(f"b4_xnn{f}", [128, NPAD], BF16, at=B0 + f * XT_STRIDE) for f in range(NF)]
    frag = [B0 + f * XT_STRIDE + 4128 for f in range(NF)]
    hbuf = [m.alloc(f"b4_hb{c}", [128, TCW], BF16, at=frag[c // 8] + (c % 8) * 512)
            for c in range(32)]
    acc = [m.alloc(f"b4_acc{fo}", [128, TCW], F32, at=frag[4 + fo // 4] + (fo % 4) * 1024)
           for fo in range(NF)]
    macb = [m.alloc(f"b4_mac{fo}", [128, TCW], BF16, at=frag[6] + fo * 512) for fo in range(NF)]
    abelT = [m.alloc(f"b4_ab{fo}", [128, TCW], BF16, at=frag[7] + fo * 512) for fo in range(NF)]
    TAIL = B0 + 8 * XT_STRIDE
    wsb = m.alloc("b4_wsb", [6, 2048], F32, at=TAIL)
    nonabT = [m.alloc(f"b4_nb{fo}", [128, TCW], BF16, at=TAIL + 8192 + fo * 512)
              for fo in range(NF)]
    gbuf = m.alloc("b4_gb", [128, TCW], F32, at=TAIL + 12288)
    webc = m.alloc("b4_web", [128, TCW], F32, at=TAIL + 13312)
    t1 = m.alloc("b4_t1", [128, TCW], F32, at=TAIL + 14336)
    t2 = m.alloc("b4_t2", [128, TCW], F32, at=TAIL + 15360)
    glog = m.alloc("b4_glog", [128, 16], F32, at=TAIL + 16384)
    grow = m.alloc("b4_grow", [1, 256], F32, at=TAIL + 16448)
    cols2 = m.alloc("b4_cols2", [128, 144], F32, at=TAIL + 17472)
    gcb = m.alloc("b4_gcb", [128, 16], BF16, at=TAIL + 18048)
    W0 = ctx.WARENA
    sel6 = m.alloc("b4_sel6", [6, 768], F32, at=W0 + 8320)
    gbb = m.alloc("b4_gbb", [128, 6], F32, at=W0 + 11424)
    grow6 = m.alloc("b4_grow6", [1, 8], F32, at=W0 + 11456)
    w1c = [m.alloc(f"b4_w1c{i}", [128, 1024], BF16, at=W0 + i * 2048) for i in range(2)]
    w2c = [m.alloc(f"b4_w2c{i}", [128, 1024], BF16, at=W0 + 4096 + i * 2048) for i in range(2)]
    gatewc = m.alloc("b4_gw", [128, 48], BF16, at=W0 + 8192)

    p.barrier(("gpsimd",))
    for f in range(NF - 1):
        p.run(lambda e, f=f: e.memset(xnb[f][:, 0:TOKOFF], 0.0), "gpsimd")
    p.inc("gpsimd", lambda e: e.memset(xnb[NF - 1][:, 0:TOKOFF], 0.0), "gp", 1)
    emit_ln(ctx, ctx.xT, xnb)
    for f in range(NF):
        p.wait("vector", "act")
        p.inc("vector", lambda e, f=f: e.tensor_tensor(
            ctx.xT[f][:, tokc(0, SEQ)], ctx.xT[f][:, tokc(0, SEQ)],
            xnb[f][:, tokc(0, SEQ)], ALU.add), "dve")
    p.barrier(("tensor", "scalar", "vector"))

    emit_colvec(ctx, ctx.d["L1_ffn_ln_g"].ap().rearrange("(a e) -> a e", a=1), E, 0)
    emit_colvec(ctx, ctx.d["L1_ffn_ln_b"].ap().rearrange("(a e) -> a e", a=1), E, 8)
    emit_colvec(ctx, ctx.d["L1_ffn_ab1"].ap()[0:2048].rearrange("(a e) -> a e", a=1), 2048, 16)
    emit_colvec(ctx, ctx.d["L1_ffn_ab1"].ap()[2048:4096].rearrange("(a e) -> a e", a=1), 2048, 32)
    emit_colvec(ctx, ctx.d["L1_ffn_ab2"].ap().rearrange("(a e) -> a e", a=1), E, 48)
    emit_colvec(ctx, ctx.d["L1_ffn_gc_w"].ap().rearrange("e a -> a e"), 2048, 56)
    for ex in range(6):
        emit_colvec(ctx, ctx.d["L1_ffn_eb1"].ap()[ex].rearrange("(a e) -> a e", a=1),
                    2048, ex * 16, dst=cols2)
        emit_colvec(ctx, ctx.d["L1_ffn_eb2"].ap()[ex].rearrange("(a e) -> a e", a=1),
                    1024, 96 + ex * 8, dst=cols2)
    p.wait("vector", "act")
    p.inc("vector", lambda e: e.tensor_copy(gcb[:, 0:16], ctx.cols[:, 56:72]), "dve")
    p.inc("sync", lambda e: e.dma_start(sel6[:], ctx.cd["sel6"].ap()), "dma", 16)
    p.inc("sync", lambda e: e.dma_start(
        grow6[0:1, 0:6], ctx.d["L1_ffn_gate_b"].ap().rearrange("(a e) -> a e", a=1)), "dma", 16)
    p.inc("sync", lambda e: e.dma_start(
        ctx.lam_s[0:1, 0:1], ctx.d["L1_ffn_gc_b"].ap().rearrange("(a e) -> a e", a=1)), "dma", 16)
    p.wait("tensor", "dma")
    emit_bcast_row(ctx, grow6, gbb[:, 0:6], 6)
    p.wait("gpsimd", "pe")
    for f in range(NF):
        p.inc("gpsimd", lambda e, f=f: e.dma_start(
            gatewc[:, f * 6:(f + 1) * 6], ctx.d["L1_ffn_gate_w"].ap()
            [f * 128:(f + 1) * 128, :]), "wdma", 16)

    emit_ln(ctx, xnb, xnn, gamma_off=0, beta_off=8)

    p.wait("tensor", "wdma")
    for t in range(16):
        p.wait("tensor", "act")
        p.wait("tensor", "dve")
        for f in range(NF):
            p.inc("tensor", lambda e, t=t, f=f: e.matmul(
                PS[0:128, 0:6], xnn[f][:, TOKOFF + t * 128:TOKOFF + (t + 1) * 128],
                gatewc[:, f * 6:(f + 1) * 6], start=(f == 0), stop=(f == NF - 1)), "pe")
        p.wait("vector", "pe")
        p.inc("vector", lambda e: e.tensor_tensor(
            glog[:, 0:6], PS[0:128, 0:6], gbb[:, 0:6], ALU.add), "dve")
        p.wait("vector", "dve")
        p.inc("vector", lambda e: e.reduce_max(
            glog[:, 8:9], glog[:, 0:6], axis=mybir.AxisListType.X, negate=True), "dve")
        p.wait("vector", "dve")
        p.inc("vector", lambda e: e.tensor_scalar(
            glog[:, 9:10], glog[:, 8:9], -1.0, None, ALU.mult), "dve")
        p.wait("vector", "dve")
        p.inc("vector", lambda e: e.tensor_scalar(
            glog[:, 10:16][:, 0:6], glog[:, 0:6], glog[:, 9:10], None, ALU.is_ge), "dve")
        p.wait("vector", "dve")
        p.inc("vector", lambda e: e.scalar_tensor_tensor(
            glog[:, 10:16][:, 0:6], glog[:, 10:16][:, 0:6], -1e30, glog[:, 0:6],
            ALU.mult, ALU.add), "dve")
        p.wait("vector", "dve")
        p.inc("vector", lambda e: e.reduce_max(
            glog[:, 7:8], glog[:, 10:16][:, 0:6], axis=mybir.AxisListType.X), "dve")
        p.wait("scalar", "dve")
        p.inc("scalar", lambda e: e.activation(
            glog[:, 10:16], glog[:, 0:6], AF.Exp, bias=glog[:, 8:9]), "act")
        p.wait("vector", "act")
        p.inc("vector", lambda e: e.tensor_tensor(
            glog[:, 6:7], glog[:, 7:8], glog[:, 8:9], ALU.add), "dve")
        p.wait("scalar", "dve")
        p.inc("scalar", lambda e: e.activation(glog[:, 6:7], glog[:, 6:7], AF.Exp), "act")
        p.wait("vector", "act")
        p.inc("vector", lambda e: e.tensor_scalar(
            glog[:, 6:7], glog[:, 6:7], 1.0, None, ALU.add), "dve")
        p.wait("vector", "dve")
        p.inc("vector", lambda e: e.reciprocal(glog[:, 6:7], glog[:, 6:7]), "dve")
        p.wait("vector", "dve")
        p.inc("vector", lambda e: e.tensor_scalar(
            glog[:, 0:6], glog[:, 0:6], glog[:, 7:8], None, ALU.is_ge), "dve")
        p.wait("vector", "dve")
        p.inc("vector", lambda e: e.tensor_tensor(
            glog[:, 0:6], glog[:, 0:6], glog[:, 10:16], ALU.mult), "dve")
        p.wait("vector", "dve")
        p.inc("vector", lambda e: e.tensor_scalar(
            glog[:, 0:6], glog[:, 0:6], glog[:, 6:7], None, ALU.mult), "dve")
        p.wait("tensor", "dve")
        p.inc("tensor", lambda e: e.transpose(
            PS[0:6, 512:640], glog[:, 0:6], ctx.ident_f32[:]), "pe")
        p.wait("scalar", "pe")
        p.inc("scalar", lambda e, t=t: e.copy(
            wsb[0:6, t * 128:(t + 1) * 128], PS[0:6, 512:640]), "act")
        p.wait("tensor", "act")

    for tc in range(NTC):
        TC = slice(tc * TCW, (tc + 1) * TCW)
        xcol = slice(TOKOFF + TC.start, TOKOFF + TC.stop)
        actm = {}

        def ld_w1(hc):
            return lambda e, hc=hc: e.dma_start(
                w1c[hc % 2].ap().rearrange("p (ft c) -> p ft c", c=128),
                ctx.d["aw1s"].ap().rearrange("(ft pp) hh -> pp ft hh", pp=128)
                [:, :, hc * 128:(hc + 1) * 128])

        def use_w1(hc, xcol=xcol):
            wb = w1c[hc % 2]
            pso = 2048 + (hc % 2) * 512
            if hc - 2 in actm:
                p.wait("tensor", "act", actm[hc - 2])
            for f in range(NF):
                p.inc("tensor", lambda e, hc=hc, f=f, wb=wb, pso=pso, xcol=xcol: e.matmul(
                    PS[:, pso:pso + TCW], wb[:, f * 128:(f + 1) * 128],
                    xnn[f][:, xcol], start=(f == 0), stop=(f == NF - 1)), "pe")
            pm = p.mark("pe")
            p.wait("scalar", "pe", pm)
            p.inc("scalar", lambda e, hc=hc, pso=pso: e.activation(
                hbuf[hc][:], PS[:, pso:pso + TCW], AF.Gelu,
                bias=ctx.cols[:, 16 + hc:17 + hc]), "act")
            actm[hc] = p.mark("act")

        stream_weights(p, 32, ld_w1, use_w1, "w4a")
        p.wait("tensor", "act")

        def ld_w2(hc):
            return lambda e, hc=hc: e.dma_start(
                w2c[hc % 2][:], ctx.d["aw2s"].ap()[hc * 128:(hc + 1) * 128, :])

        def use_w2(hc):
            wb = w2c[hc % 2]
            for fo in range(NF):
                p.inc("tensor", lambda e, hc=hc, fo=fo, wb=wb: e.matmul(
                    PS[:, fo * 512:fo * 512 + TCW], wb[:, fo * 128:(fo + 1) * 128],
                    hbuf[hc][:], start=(hc == 0), stop=(hc == 31)), "pe")

        stream_weights(p, 32, ld_w2, use_w2, "w4b")
        p.wait("scalar", "pe")
        for fo in range(NF):
            p.inc("scalar", lambda e, fo=fo: e.activation(
                abelT[fo][:], PS[:, fo * 512:fo * 512 + TCW], AF.Identity,
                bias=ctx.cols[:, 48 + fo:49 + fo]), "act")
        p.wait("tensor", "act")
        p.wait("vector", "act")
        for ex in range(6):
            p.wait("tensor", "dve")
            p.inc("tensor", lambda e, ex=ex, TC=TC: e.matmul(
                PS[:, 0:TCW], sel6[:, ex * 128:(ex + 1) * 128], wsb[0:6, TC],
                start=True, stop=True), "pe")
            p.wait("scalar", "pe")
            p.inc("scalar", lambda e: e.copy(webc[:], PS[:, 0:TCW]), "act")
            p.wait("tensor", "act")
            actm2 = {}

            def ld_e1(hc, ex=ex):
                return lambda e, ex=ex, hc=hc: e.dma_start(
                    w1c[hc % 2].ap().rearrange("p (ft c) -> p ft c", c=128),
                    ctx.d["ew1s"].ap()[ex].rearrange("(ft pp) hh -> pp ft hh", pp=128)
                    [:, :, hc * 128:(hc + 1) * 128])

            def use_e1(hc, ex=ex, xcol=xcol):
                wb = w1c[hc % 2]
                pso = 2048 + (hc % 2) * 512
                if hc - 2 in actm2:
                    p.wait("tensor", "act", actm2[hc - 2])
                for f in range(NF):
                    p.inc("tensor", lambda e, hc=hc, f=f, wb=wb, pso=pso, xcol=xcol: e.matmul(
                        PS[:, pso:pso + TCW], wb[:, f * 128:(f + 1) * 128],
                        xnn[f][:, xcol], start=(f == 0), stop=(f == NF - 1)), "pe")
                pm = p.mark("pe")
                p.wait("scalar", "pe", pm)
                p.inc("scalar", lambda e, ex=ex, hc=hc, pso=pso: e.activation(
                    hbuf[hc][:], PS[:, pso:pso + TCW], AF.Gelu,
                    bias=cols2[:, ex * 16 + hc:ex * 16 + hc + 1]), "act")
                actm2[hc] = p.mark("act")

            stream_weights(p, 16, ld_e1, use_e1, "w4a")
            p.wait("tensor", "act")

            def ld_e2(hc, ex=ex):
                return lambda e, ex=ex, hc=hc: e.dma_start(
                    w2c[hc % 2][:], ctx.d["ew2s"].ap()[ex][hc * 128:(hc + 1) * 128, :])

            def use_e2(hc):
                wb = w2c[hc % 2]
                for fo in range(NF):
                    p.inc("tensor", lambda e, hc=hc, fo=fo, wb=wb: e.matmul(
                        PS[:, fo * 512:fo * 512 + TCW], wb[:, fo * 128:(fo + 1) * 128],
                        hbuf[hc][:], start=(hc == 0), stop=(hc == 15)), "pe")

            stream_weights(p, 16, ld_e2, use_e2, "w4b")
            p.wait("scalar", "pe")
            p.wait("vector", "pe")
            for fo in range(NF):
                p.inc("scalar", lambda e, ex=ex, fo=fo: e.activation(
                    t1[:], PS[:, fo * 512:fo * 512 + TCW], AF.Identity,
                    bias=cols2[:, 96 + ex * 8 + fo:96 + ex * 8 + fo + 1]), "act")
                p.wait("vector", "act")
                p.inc("vector", lambda e: e.tensor_tensor(t2[:], t1[:], webc[:], ALU.mult), "dve")
                p.wait("vector", "dve")
                if ex == 0:
                    p.inc("vector", lambda e, fo=fo: e.tensor_copy(acc[fo][:], t2[:]), "dve")
                else:
                    p.inc("vector", lambda e, fo=fo: e.tensor_tensor(
                        acc[fo][:], acc[fo][:], t2[:], ALU.add), "dve")
                p.wait("scalar", "dve")
        p.wait("scalar", "dve")
        for fo in range(NF):
            p.inc("scalar", lambda e, fo=fo: e.copy(macb[fo][:], acc[fo][:]), "act")
        p.wait("tensor", "act")
        def ld_r(fi):
            return lambda e, fi=fi: e.dma_start(
                w1c[fi % 2][:], ctx.d["rs"].ap()[fi * 128:(fi + 1) * 128, :])

        def use_r(fi):
            wb = w1c[fi % 2]
            for fo in range(NF):
                p.inc("tensor", lambda e, fi=fi, fo=fo, wb=wb: e.matmul(
                    PS[:, fo * 512:fo * 512 + TCW], wb[:, fo * 128:(fo + 1) * 128],
                    macb[fi][:], start=(fi == 0), stop=(fi == NF - 1)), "pe")

        stream_weights(p, NF, ld_r, use_r, "w4a")
        p.wait("scalar", "pe")
        for fo in range(NF):
            p.inc("scalar", lambda e, fo=fo: e.copy(
                nonabT[fo][:], PS[:, fo * 512:fo * 512 + TCW]), "act")
        p.wait("tensor", "act")
        if tc == 0 and batch == 0:
            for nm, tiles in (("ab4", abelT), ("nb4", nonabT)):
                if nm in ctx.taps:
                    p.barrier(("sync", "vector"))
                    for f in range(NF):
                        p.wait("vector", "st")
                        p.inc("vector", lambda e, f=f, tiles=tiles: e.tensor_copy(
                            t1[:], tiles[f][:]), "dve")
                        p.wait("sync", "dve")
                        p.inc("sync", lambda e, f=f, nm=nm: e.dma_start(
                            ctx.tap_drams[nm].ap()[f * 128:(f + 1) * 128, 0:TCW], t1[:]),
                            "st", 16)
                        p.wait("sync", "st")
                    p.wait("gpsimd", "st")
                    p.wait("scalar", "st")
            if "mo4" in ctx.taps:
                p.barrier(("sync",))
                for f in range(NF):
                    p.inc("sync", lambda e, f=f: e.dma_start(
                        ctx.tap_drams["mo4"].ap()[f * 128:(f + 1) * 128, 0:TCW],
                        acc[f][:]), "st", 16)
                p.wait("sync", "st")
        p.wait("tensor", "dve")
        for i, tiles in ((0, abelT), (8, nonabT)):
            for f in range(NF):
                p.inc("tensor", lambda e, i=i, f=f, tiles=tiles: e.matmul(
                    PS[0:1, 3584:3584 + TCW], gcb[:, i + f:i + f + 1], tiles[f][:],
                    start=(i == 0 and f == 0), stop=(i == 8 and f == NF - 1)), "pe")
        p.wait("scalar", "pe")
        p.inc("scalar", lambda e: e.activation(
            grow[0:1, 0:TCW], PS[0:1, 3584:3584 + TCW], AF.Sigmoid,
            bias=ctx.lam_s[0:1, 0:1]), "act")
        p.wait("tensor", "act")
        p.inc("tensor", lambda e: e.matmul(
            PS[:, 3072:3072 + TCW], ctx.ones_row[0:1, :], grow[0:1, 0:TCW],
            start=True, stop=True), "pe")
        p.wait("scalar", "pe")
        p.inc("scalar", lambda e: e.copy(gbuf[:], PS[:, 3072:3072 + TCW]), "act")
        p.wait("vector", "act")
        for f in range(NF):
            p.inc("vector", lambda e, f=f: e.tensor_tensor(
                t1[:], nonabT[f][:], abelT[f][:], ALU.subtract), "dve")
            p.wait("vector", "dve")
            p.inc("vector", lambda e: e.tensor_tensor(t1[:], t1[:], gbuf[:], ALU.mult), "dve")
            p.wait("vector", "dve")
            p.inc("vector", lambda e, f=f: e.tensor_tensor(
                t2[:], abelT[f][:], t1[:], ALU.add), "dve")
            p.wait("vector", "dve")
            p.inc("vector", lambda e, f=f, xcol=xcol: e.tensor_tensor(
                ctx.xT[f][:, xcol], ctx.xT[f][:, xcol], t2[:], ALU.add), "dve")
            p.wait("vector", "dve")
        p.wait("tensor", "dve")
        p.wait("scalar", "dve")
    p.barrier()


def _tap_tiles(ctx, tiles, name, padded=True):
    p = ctx.p
    dram = ctx.tap_drams[name]
    so = TOKOFF if padded else 0
    p.barrier(("sync", "vector"))
    bf_src = tiles[0].dtype == BF16
    for f in range(NF):
        if bf_src:
            p.wait("vector", "st")
            p.inc("vector", lambda e, f=f: e.tensor_copy(
                ctx.ln_sq[:, 0:SEQ], tiles[f][:, so:so + SEQ]), "dve")
            p.wait("sync", "dve")
            p.inc("sync", lambda e, f=f: e.dma_start(
                dram.ap()[f * 128:(f + 1) * 128, :], ctx.ln_sq[:, 0:SEQ]), "st", 16)
            p.wait("sync", "st")
        else:
            p.inc("sync", lambda e, f=f: e.dma_start(
                dram.ap()[f * 128:(f + 1) * 128, :], tiles[f][:, so:so + SEQ]), "st", 16)
    p.wait("sync", "st")
    p.wait("gpsimd", "st")
    p.wait("scalar", "st")


def emit_load_x(ctx, batch):
    p = ctx.p
    p.barrier()
    for f in range(NF):
        p.inc("sync", lambda e, f=f, b=batch: e.dma_start(
            ctx.xT[f][:, tokc(0, SEQ)], ctx.d["xin"].ap()[b, f * 128:(f + 1) * 128, :]),
            "dma", 16)
    p.wait("tensor", "dma")
    p.wait("vector", "dma")
    p.wait("scalar", "dma")


def emit_store_x(ctx, batch):
    p = ctx.p
    p.wait("sync", "dve")
    p.wait("sync", "act")
    p.wait("sync", "pe")
    for f in range(NF):
        p.inc("sync", lambda e, f=f, b=batch: e.dma_start(
            ctx.d["xout"].ap()[b, f * 128:(f + 1) * 128, :], ctx.xT[f][:, tokc(0, SEQ)]),
            "st", 16)
    p.wait("sync", "st")



def _host_consts():
    import ml_dtypes as _md
    c = {}
    c["ident_f32"] = np.eye(128, dtype=np.float32)
    c["ident_bf16"] = np.eye(128, dtype=_md.bfloat16)
    DH, WIN = 64, 512
    half = np.arange(0, DH, 2, dtype=np.float64)
    inv_freq = 1.0 / 10000.0 ** (half / DH)
    t = np.arange(SEQ, dtype=np.float64)
    freqs = t[:, None] * inv_freq[None, :]
    freqs = np.concatenate([freqs, freqs], axis=-1)
    base = (half + 0.4 * DH) / (1.4 * DH)
    power = (t - SEQ // 2) / float(WIN // 2)
    scale = base[None, :] ** power[:, None]
    scale = np.concatenate([scale, scale], axis=-1)
    cos, sin = np.cos(freqs), np.sin(freqs)
    sgn = np.concatenate([-np.ones(32), np.ones(32)])[None, :]
    c["rotA_q"] = (cos * scale).astype(np.float32)
    c["rotB_q"] = (sin * scale * sgn).astype(np.float32)
    c["rotA_k"] = (cos / scale).astype(np.float32)
    c["rotB_k"] = (sin / scale * sgn).astype(np.float32)
    mk = np.zeros((4, 128, 512), dtype=np.float32)
    mkp = np.zeros((4, 128, 512), dtype=np.float32)
    for r in range(4):
        ks = r * 128 + np.arange(128)[:, None]
        qs = np.arange(512)[None, :]
        mk[r] = np.where(ks <= qs, 0.0, -1e30)
        mkp[r] = np.where(ks >= qs, 0.0, -1e30)
    c["locmask"] = mk
    c["locmaskp"] = mkp
    qrow = np.arange(128)[:, None]
    kcol = np.arange(128)[None, :]
    c["maskdiag"] = np.where(kcol <= qrow, 0.0, -1e9).astype(np.float32)
    c["epsc"] = np.array([[0.0], [1e-6]], dtype=np.float32)
    sel = np.zeros((6, 768), dtype=np.float32)
    for e in range(6):
        sel[e, e * 128:(e + 1) * 128] = 1.0
    c["sel6"] = sel
    return c


WEIGHT_SHAPES = {
    "L0_attn_Wq": (16, 1024, 128), "L0_attn_Wk": (16, 1024, 128),
    "L0_attn_Wv": (16, 1024, 64), "L0_attn_Wo": (1024, 1024),
    "L0_attn_bo": (1024,), "L0_attn_gamma": (1024,), "L0_attn_beta": (1024,),
    "L0_attn_bv": (16, 64),
    "L0_ffn_wqkv": (2, 1024, 1536), "L0_ffn_q_scale": (2, 64), "L0_ffn_k_scale": (2, 64),
    "L0_ffn_wo": (2, 512, 1024), "L0_ffn_ln_g": (2, 1024), "L0_ffn_ln_b": (2, 1024),
    "L0_ffn_w1": (2, 1024, 5460), "L0_ffn_b1": (2, 5460),
    "L0_ffn_w2": (2, 2730, 1024), "L0_ffn_b2": (2, 1024),
    "L1_attn_w_qkv": (1024, 3072), "L1_attn_b_qkv": (3072,),
    "L1_attn_w_out": (1024, 1024), "L1_attn_b_out": (1024,),
    "L1_ffn_ln_g": (1024,), "L1_ffn_ln_b": (1024,),
    "L1_ffn_aw1": (1024, 4096), "L1_ffn_ab1": (4096,),
    "L1_ffn_aw2": (4096, 1024), "L1_ffn_ab2": (1024,),
    "L1_ffn_gate_w": (1024, 6), "L1_ffn_gate_b": (6,),
    "L1_ffn_ew1": (6, 1024, 2048), "L1_ffn_eb1": (6, 2048),
    "L1_ffn_ew2": (6, 2048, 1024), "L1_ffn_eb2": (6, 1024),
    "L1_ffn_gens": (4, 1024, 1024), "L1_ffn_gc_w": (2048, 1), "L1_ffn_gc_b": (1,),
    "bq_t": (128, 16), "bk_t": (128, 16), "lam11": (1, 1),
}

BLOCK_EMITTERS = {}




def _emit_expm(ctx):
    p, m = ctx.p, ctx.m
    PS = ctx.PS
    bases = [R_XT, R_XT + 32768, ctx.BIG, ctx.BIG + 32768]
    X = [[m.alloc(f"xm{i}_{f}", [128, 1024], F32, at=bases[i] + f * 4096)
          for f in range(NF)] for i in range(4)]
    stb = m.alloc("xm_st", [128, 1024], BF16, at=ctx.BIG + 65536)

    def mm_product(lhsT_tiles, rhs_tiles, out_tiles, post=None):
        evm = {}
        gx = 0
        for fo in range(NF):
            for ch in range(2):
                cs = slice(ch * 512, ch * 512 + 512)
                pso = (gx % 4) * 512
                if gx - 3 in evm:
                    p.wait("tensor", "act", evm[gx - 3])
                for fi in range(NF):
                    p.inc("tensor", lambda e, fo=fo, cs=cs, fi=fi, pso=pso: e.matmul(
                        PS[:, pso:pso + 512],
                        lhsT_tiles[fi][:, fo * 128:(fo + 1) * 128],
                        rhs_tiles[fi][:, cs], start=(fi == 0), stop=(fi == NF - 1)), "pe")
                if post is None:
                    p.wait("scalar", "pe")
                    p.inc("scalar", lambda e, fo=fo, cs=cs, pso=pso: e.copy(
                        out_tiles[fo][:, cs], PS[:, pso:pso + 512]), "act")
                else:
                    post(fo, cs, pso)
                evm[gx] = p.mark("act")
                gx += 1

    def transpose_to(src_tiles, dst_tiles):
        evm = {}
        gx = 0
        for a in range(NF):
            for b in range(NF):
                po = 3072 + (gx % 2) * 512
                if gx - 2 in evm:
                    p.wait("tensor", "act", evm[gx - 2])
                p.inc("tensor", lambda e, a=a, b=b, po=po: e.transpose(
                    PS[:, po:po + 128], src_tiles[a][:, b * 128:(b + 1) * 128],
                    ctx.ident_f32[:]), "pe")
                p.wait("scalar", "pe")
                p.inc("scalar", lambda e, a=a, b=b, po=po: e.copy(
                    dst_tiles[b][:, a * 128:(a + 1) * 128], PS[:, po:po + 128]), "act")
                evm[gx] = p.mark("act")
                gx += 1

    def load_ident(tiles):
        for f in range(NF - 1):
            p.run(lambda e, f=f: e.memset(tiles[f][:], 0.0), "gpsimd")
        p.inc("gpsimd", lambda e, tiles=tiles: e.memset(tiles[NF - 1][:], 0.0), "gp", 1)
        p.wait("sync", "gp")
        for f in range(NF):
            p.inc("sync", lambda e, f=f, tiles=tiles: e.dma_start(
                tiles[f][:, f * 128:(f + 1) * 128], ctx.cd["identf"].ap()), "dma", 16)

    CS = ("pe", "dve", "act", "gp", "dma")
    racc = None
    free = [0, 1, 2, 3]
    for gi in range(4):
        bA = free[0]; bH = free[1]; bT = free[2]
        p.barrier(("sync", "gpsimd", "tensor", "scalar", "vector"))
        for f in range(NF):
            p.inc("sync", lambda e, f=f, gi=gi, bA=bA: e.dma_start(
                X[bA][f][:], ctx.d["L1_ffn_gens"].ap()[gi][f * 128:(f + 1) * 128, :]),
                "dma", 16)
        load_ident(X[bH])
        p.barrier(("tensor", "scalar", "vector", "sync"))
        for j in range(7, 0, -1):
            coef = -0.25 / j

            def post(fo, cs, pso, coef=coef, bT=bT):
                p.wait("scalar", "pe")
                p.inc("scalar", lambda e, fo=fo, cs=cs, pso=pso, coef=coef, bT=bT:
                      e.activation(X[bT][fo][:, cs], PS[:, pso:pso + 512], AF.Identity,
                                   bias=0.0, scale=coef), "act")
                p.wait("vector", "act")
                ds = max(cs.start, fo * 128)
                de = min(cs.stop, (fo + 1) * 128)
                if ds < de:
                    p.inc("vector", lambda e, fo=fo, ds=ds, de=de, bT=bT: e.tensor_tensor(
                        X[bT][fo][:, ds:de], X[bT][fo][:, ds:de],
                        ctx.ident_f32[:, ds - fo * 128:de - fo * 128], ALU.add), "dve")

            mm_product(X[bA], X[bH], None, post=post)
            p.barrier(("tensor", "scalar", "vector"))
            bH, bT = bT, bH
        cur = bH
        scratch = [bA, bT]
        for sq in range(2):
            tb, ob = scratch[0], scratch[1]
            transpose_to(X[cur], X[tb])
            p.barrier(("tensor", "scalar", "vector"))
            mm_product(X[tb], X[cur], X[ob])
            p.barrier(("tensor", "scalar", "vector"))
            scratch = [cur, tb]
            cur = ob
        if racc is None:
            racc = cur
            free = [b for b in range(4) if b != racc]
        else:
            tb = [b for b in range(4) if b not in (racc, cur)][0]
            transpose_to(X[racc], X[tb])
            p.barrier(("tensor", "scalar", "vector"))
            ob = [b for b in range(4) if b not in (racc, cur, tb)][0]
            mm_product(X[tb], X[cur], X[ob])
            p.barrier(("tensor", "scalar", "vector"))
            racc = ob
            free = [b for b in range(4) if b != racc]
    if "rexp" in ctx.taps:
        p.barrier(("sync",))
        for f in range(NF):
            p.inc("sync", lambda e, f=f: e.dma_start(
                ctx.tap_drams["rexp"].ap()[f * 128:(f + 1) * 128, 0:1024], X[racc][f][:]),
                "st", 16)
        p.wait("sync", "st")
    for f in range(NF):
        p.wait("vector", "st")
        p.inc("vector", lambda e, f=f: e.tensor_copy(stb[:], X[racc][f][:]), "dve")
        p.wait("sync", "dve")
        p.inc("sync", lambda e, f=f: e.dma_start(
            ctx.d["rs"].ap()[f * 128:(f + 1) * 128, :], stb[:]), "st", 16)
        p.wait("sync", "st")
    p.barrier()


def _wp_setup(ctx, blocks):
    m = ctx.m
    ctx.wp_bufs = [m.alloc(f"wp_b{i}", [128, 5460], BF16, at=R_LNS + i * 10944)
                   for i in range(2)]
    jobs = []
    if "l0_ffn" in blocks:
        for lay in range(2):
            for f in range(NF):
                jobs.append((ctx.d["L0_ffn_w1"].ap()[lay][f * 128:(f + 1) * 128, :],
                             ctx.d["w1s"].ap()[lay][f * 128:(f + 1) * 128, :], 128, 5460))
            for c in range(22):
                cw = 128 if c < 21 else 42
                jobs.append((ctx.d["L0_ffn_w2"].ap()[lay][c * 128:c * 128 + cw, :],
                             ctx.d["w2s"].ap()[lay][c * 128:c * 128 + cw, :], cw, 1024))
    if "l1_ffn" in blocks:
        for f in range(NF):
            jobs.append((ctx.d["L1_ffn_aw1"].ap()[f * 128:(f + 1) * 128, :],
                         ctx.d["aw1s"].ap()[f * 128:(f + 1) * 128, :], 128, 4096))
        for t in range(32):
            jobs.append((ctx.d["L1_ffn_aw2"].ap()[t * 128:(t + 1) * 128, :],
                         ctx.d["aw2s"].ap()[t * 128:(t + 1) * 128, :], 128, 1024))
        for ex in range(6):
            for f in range(NF):
                jobs.append((ctx.d["L1_ffn_ew1"].ap()[ex][f * 128:(f + 1) * 128, :],
                             ctx.d["ew1s"].ap()[ex][f * 128:(f + 1) * 128, :], 128, 2048))
            for t in range(16):
                jobs.append((ctx.d["L1_ffn_ew2"].ap()[ex][t * 128:(t + 1) * 128, :],
                             ctx.d["ew2s"].ap()[ex][t * 128:(t + 1) * 128, :], 128, 1024))
    ctx.wp_jobs = jobs
    ctx.wp_pos = 0
    ctx.wp_stm = {}


def _wp_emit(ctx, n):
    p = ctx.p
    end = len(ctx.wp_jobs) if n is None else min(ctx.wp_pos + n, len(ctx.wp_jobs))
    for j in range(ctx.wp_pos, end):
        src, dst, rows, cols = ctx.wp_jobs[j]
        buf = ctx.wp_bufs[j % 2]
        pw = ("wdma", "wdma2")[j % 2]
        ps = ("st", "st2")[j % 2]
        if j - 2 >= 0:
            p.wait("gpsimd", ps, ctx.wp_stm[j - 2])
        p.inc("gpsimd", lambda e, s=src, b=buf, r=rows, c=cols: e.dma_start(
            b[0:r, 0:c], s), pw, 16)
        p.wait("sync", pw)
        if j - 2 >= 0:
            p.wait("sync", ps, ctx.wp_stm[j - 2])
        p.inc("sync", lambda e, d=dst, b=buf, r=rows, c=cols: e.dma_start(
            d, b[0:r, 0:c]), ps, 16)
        ctx.wp_stm[j] = p.mark(ps)
    ctx.wp_pos = end


def build_program(blocks=("l0_attn",), taps=(), nb=2):
    nc = bass.Bass("TRN2", target_bir_lowering=False)
    ctx = Ctx()
    ctx.nc = nc
    ctx.p = Prog(nc)
    ctx.taps = set(taps)
    m = SbufMap(nc)
    ctx.m = m

    d = {}
    d["xin"] = nc.dram_tensor("xin", [nb, E, SEQ], F32, kind="ExternalInput")
    d["xout"] = nc.dram_tensor("xout", [nb, E, SEQ], F32, kind="ExternalOutput")
    for name, shape in WEIGHT_SHAPES.items():
        d[name] = nc.dram_tensor(name, list(shape), F32, kind="ExternalInput")
    ctx.d = d
    ctx.tap_drams = {t: nc.dram_tensor(
        "tap_" + t, [128 if t == "rtq" else (512 if t.endswith("t") else E),
                     1024 if t == "rexp" else SEQ], F32,
        kind="ExternalOutput") for t in taps}

    consts = _host_consts()
    identf_d = nc.inline_tensor(consts["ident_f32"], name="identf")
    identb_d = nc.inline_tensor(consts["ident_bf16"], name="identb")
    ctx.cd = {}
    ctx.cd["maskdiag"] = nc.inline_tensor(consts["maskdiag"], name="maskdiag")
    ctx.cd["epsc"] = nc.inline_tensor(consts["epsc"], name="epsc")
    ctx.cd["identf"] = identf_d
    if "l1_ffn" in blocks:
        ctx.cd["sel6"] = nc.inline_tensor(consts["sel6"], name="sel6")
        d["aw1s"] = nc.dram_tensor("aw1s", [1024, 4096], BF16, kind="Internal")
        d["aw2s"] = nc.dram_tensor("aw2s", [4096, 1024], BF16, kind="Internal")
        d["ew1s"] = nc.dram_tensor("ew1s", [6, 1024, 2048], BF16, kind="Internal")
        d["ew2s"] = nc.dram_tensor("ew2s", [6, 2048, 1024], BF16, kind="Internal")
        d["rs"] = nc.dram_tensor("rs", [1024, 1024], BF16, kind="Internal")
    if "l0_ffn" in blocks:
        for k in ("rotA_q", "rotB_q", "rotA_k", "rotB_k", "locmask", "locmaskp"):
            ctx.cd[k] = nc.inline_tensor(consts[k], name=k)
        d["xpark"] = nc.dram_tensor("xpark", [E, SEQ], F32, kind="Internal")
        d["w1s"] = nc.dram_tensor("w1s", [2, 1024, 5460], BF16, kind="Internal")
        d["w2s"] = nc.dram_tensor("w2s", [2, 2730, 1024], BF16, kind="Internal")

    ctx.xT = [m.alloc(f"xT{f}", [128, NPAD], F32, at=R_XT + f * XT_STRIDE) for f in range(NF)]
    co = R_CON
    ctx.ident_f32 = m.alloc("identf", [128, 128], F32, at=co); co += 512
    ctx.ident_bf16 = m.alloc("identb", [128, 128], BF16, at=co); co += 256
    ctx.ones_col = m.alloc("ones_col", [128, 1], F32, at=co); co += 32
    ctx.ones_col_bf = m.alloc("ones_col_bf", [128, 1], BF16, at=co); co += 32
    ctx.ones_row = m.alloc("ones_row", [1, 128], F32, at=co); co += 512
    ctx.cols = m.alloc("cols", [128, 96], F32, at=co); co += 384
    ctx.bq_col = m.alloc("bq_col", [128, 16], F32, at=co); co += 64
    ctx.bk_col = m.alloc("bk_col", [128, 16], F32, at=co); co += 64
    ctx.lam_s = m.alloc("lam_s", [1, 1], F32, at=co); co += 32
    ctx.eps_s = m.alloc("eps_s", [1, 1], F32, at=co); co += 32
    ctx.eps6_s = m.alloc("eps6_s", [1, 1], F32, at=co); co += 32
    ctx.maskc = m.alloc("maskc", [128, 128], F32, at=co); co += 512
    ctx.epsc = m.alloc("epsc", [2, 1], F32, at=co); co += 32
    assert co <= R_WAR
    ctx.WARENA = R_WAR
    ctx.BIG = R_BIG
    ctx.ln_sq = m.alloc("ln_sq", [128, 2048], F32, at=R_LNS)
    ctx.ln_misc = m.alloc("ln_misc", [128, 6144], F32, at=R_LNS + 8192)
    ctx.row_tmp = m.alloc("row_tmp", [1, 2048], F32, at=R_LNS)

    ctx.PS = nc.alloc_psum_tensor("PS", [128, 4096], F32)

    p = ctx.p
    p.inc("sync", lambda e: e.dma_start(ctx.ident_f32[:], identf_d.ap()), "dma", 16)
    p.inc("sync", lambda e: e.dma_start(ctx.ident_bf16[:], identb_d.ap()), "dma", 16)
    p.inc("sync", lambda e: e.dma_start(ctx.maskc[:], ctx.cd["maskdiag"].ap()), "dma", 16)
    p.inc("sync", lambda e: e.dma_start(ctx.epsc[:], ctx.cd["epsc"].ap()), "dma", 16)
    p.run(lambda e: e.memset(ctx.ones_col[:], 1.0), "gpsimd")
    p.run(lambda e: e.memset(ctx.ones_col_bf[:], 1.0), "gpsimd")
    p.run(lambda e: e.memset(ctx.ones_row[:], 1.0), "gpsimd")
    p.run(lambda e: e.memset(ctx.eps_s[:], LN_EPS), "gpsimd")
    p.run(lambda e: e.memset(ctx.eps6_s[:], 1e-6), "gpsimd")
    for f in range(NF):
        p.run(lambda e, f=f: e.memset(ctx.xT[f][:, 0:TOKOFF], 0.0), "gpsimd")
        if f < NF - 1:
            p.run(lambda e, f=f: e.memset(ctx.xT[f][:, TOKOFF + SEQ:NPAD], 0.0), "gpsimd")
    p.inc("gpsimd", lambda e: e.memset(ctx.xT[NF - 1][:, TOKOFF + SEQ:NPAD], 0.0), "gp", 1)
    p.wait("tensor", "gp")
    p.wait("vector", "gp")
    p.wait("scalar", "gp")
    p.wait("sync", "gp")

    _wp_setup(ctx, blocks)
    if ctx.wp_jobs:
        _wp_emit(ctx, None)
        p.barrier()
    if "l1_ffn" in blocks:
        _emit_expm(ctx)

    for b in range(nb):
        emit_load_x(ctx, b)
        for bi, blk in enumerate(blocks):
            BLOCK_EMITTERS[blk](ctx, b)
            tname = f"xb{b}p{bi}"
            if tname in ctx.taps:
                _tap_tiles(ctx, ctx.xT, tname, padded=True)
        emit_store_x(ctx, b)

    p.build()
    nc.finalize()
    return nc, ctx


BLOCK_EMITTERS["l0_attn"] = emit_l0_attn
BLOCK_EMITTERS["l1_attn"] = emit_l1_attn
BLOCK_EMITTERS["l1_ffn"] = emit_l1_ffn

def emit_l0_ffn(ctx, batch):
    p, m, nc = ctx.p, ctx.m, ctx.nc
    PS = ctx.PS
    B0 = ctx.BIG
    H, DH = 8, 64
    NT = 16
    NC2 = 22

    zT = [m.alloc(f"b2_z{f}", [128, NPAD], F32, at=B0 + f * XT_STRIDE) for f in range(NF)]
    rott = {k: m.alloc(f"b2_rt{k}", [128, 1024], F32, at=B0 + 8 * XT_STRIDE + i * 4096)
            for i, k in enumerate(("Aq", "Bq", "Ak", "Bk"))}
    vbuf = [m.alloc(f"b2_v{t}", [128, 8 * 65], BF16, at=R_XT + t * 1056) for t in range(NT)]
    qtb = [m.alloc(f"b2_qt{g}", [128, 2048], BF16, at=R_XT + 16896 + g * 4096) for g in range(4)]
    ktb = [m.alloc(f"b2_kt{g}", [128, 2048], BF16, at=R_XT + 33280 + g * 4096) for g in range(4)]
    PRE = R_XT + 49664
    sq32 = m.alloc("b2_sq32", [128, 512], F32, at=PRE)
    qn = m.alloc("b2_qn", [128, 512], F32, at=PRE + 2048)
    t1b = m.alloc("b2_t1", [128, 512], F32, at=PRE + 4096)
    t2b = m.alloc("b2_t2", [128, 512], F32, at=PRE + 6144)
    glb = m.alloc("b2_gl", [128, 512], BF16, at=PRE + 8192)
    sc32 = m.alloc("b2_sc32", [128, 512], F32, at=PRE + 9216)
    rowsb = m.alloc("b2_rows", [128, 512], F32, at=PRE + 11264)
    ssq = m.alloc("b2_ssq", [128, 16], F32, at=PRE + 13312)
    qsb = m.alloc("b2_qsb", [128, 64], F32, at=PRE + 13376)
    ksb = m.alloc("b2_ksb", [128, 64], F32, at=PRE + 13632)
    rbc = m.alloc("b2_rbc", [64, 512], F32, at=PRE + 13888)
    hnT = [m.alloc(f"b2_hn{f}", [128, NPAD], BF16, at=R_XT + f * 4128) for f in range(NF)]
    glub = [m.alloc(f"b2_glu{c}", [128, 512], BF16, at=R_XT + 33280 + c * 1024)
            for c in range(NC2)]
    hbT = [m.alloc(f"b2_hb{f}", [128, 2048], BF16, at=R_LNS + f * 4096) for f in range(NF)]
    oT = [m.alloc(f"b2_oT{g}", [128, 2048], BF16, at=R_LNS + g * 4096) for g in range(4)]
    W0 = ctx.WARENA
    wqkv_sb = [m.alloc(f"b2_wqkv{f}", [128, 1536], BF16, at=W0 + f * 3072) for f in range(NF)]
    masks_sb = m.alloc("b2_masks", [128, 4096], F32, at=W0)
    wol_sb = [m.alloc(f"b2_wo{i}", [128, 1024], BF16, at=W0 + 8192 + i * 2048) for i in range(4)]
    w1a_sb = [m.alloc(f"b2_w1a{i}", [128, 1024], BF16, at=W0 + 16384 + i * 2048) for i in range(2)]
    w1g_sb = [m.alloc(f"b2_w1g{i}", [128, 1024], BF16, at=W0 + 20480 + i * 2048) for i in range(2)]
    w2_sb = [m.alloc(f"b2_w2{i}", [128, 1024], BF16, at=W0 + 16384 + i * 2048) for i in range(2)]

    dpark = ctx.d["xpark"]

    p.barrier(("gpsimd",))
    for f in range(NF):
        p.run(lambda e, f=f: e.memset(zT[f][:, 0:TOKOFF], 0.0), "gpsimd")
        if f < NF - 1:
            p.run(lambda e, f=f: e.memset(zT[f][:, TOKOFF + SEQ:NPAD], 0.0), "gpsimd")
    p.inc("gpsimd", lambda e: e.memset(zT[NF - 1][:, TOKOFF + SEQ:NPAD], 0.0), "gp", 1)
    emit_ln(ctx, ctx.xT, zT)
    p.barrier(("sync",))
    for f in range(NF):
        p.inc("sync", lambda e, f=f: e.dma_start(
            dpark.ap()[f * 128:(f + 1) * 128, :], ctx.xT[f][:, tokc(0, SEQ)]), "st", 16)
    p.wait("gpsimd", "st")

    for lay in range(2):
        p.barrier(("sync", "tensor", "scalar", "vector"))
        p.inc("sync", lambda e, lay=lay: e.dma_start(
            ctx.row_tmp[0:1, 0:64], ctx.d["L0_ffn_q_scale"].ap()[lay].rearrange(
                "(a d) -> a d", a=1)), "dma", 16)
        p.wait("tensor", "dma")
        emit_bcast_row(ctx, ctx.row_tmp, qsb[:, 0:64], 64)
        p.inc("sync", lambda e, lay=lay: e.dma_start(
            ctx.row_tmp[0:1, 0:64], ctx.d["L0_ffn_k_scale"].ap()[lay].rearrange(
                "(a d) -> a d", a=1)), "dma", 16)
        p.wait("tensor", "dma")
        emit_bcast_row(ctx, ctx.row_tmp, ksb[:, 0:64], 64)
        for k, dname, scl in (("Aq", "rotA_q", qsb), ("Bq", "rotB_q", qsb),
                              ("Ak", "rotA_k", ksb), ("Bk", "rotB_k", ksb)):
            p.inc("sync", lambda e, k=k, dname=dname: e.dma_start(
                rott[k].ap().rearrange("p (tt d) -> p tt d", d=64),
                ctx.cd[dname].ap().rearrange("(tt pp) d -> pp tt d", pp=128)), "dma", 16)
        p.wait("vector", "dma")
        p.wait("vector", "act")
        for k, scl in (("Aq", qsb), ("Ak", ksb)):
            for t in range(NT):
                p.inc("vector", lambda e, k=k, t=t, scl=scl: e.tensor_tensor(
                    rott[k][:, t * 64:(t + 1) * 64], rott[k][:, t * 64:(t + 1) * 64],
                    scl[:, 0:64], ALU.mult), "dve")
        for k, scl in (("Bq", qsb), ("Bk", ksb)):
            for t in range(NT):
                p.inc("vector", lambda e, k=k, t=t, scl=scl: e.tensor_tensor(
                    rott[k][:, t * 64:t * 64 + 32], rott[k][:, t * 64:t * 64 + 32],
                    scl[:, 32:64], ALU.mult), "dve")
                p.inc("vector", lambda e, k=k, t=t, scl=scl: e.tensor_tensor(
                    rott[k][:, t * 64 + 32:(t + 1) * 64], rott[k][:, t * 64 + 32:(t + 1) * 64],
                    scl[:, 0:32], ALU.mult), "dve")
        if lay == 0 and batch == 0 and "rtq" in ctx.taps:
            p.barrier(("sync",))
            p.inc("sync", lambda e: e.dma_start(
                ctx.tap_drams["rtq"].ap()[0:128, 0:1024], rott["Aq"][:, 0:1024]), "st", 16)
            p.inc("sync", lambda e: e.dma_start(
                ctx.tap_drams["rtq"].ap()[0:128, 1024:2048], rott["Bq"][:, 0:1024]), "st", 16)
            p.wait("sync", "st")
        p.wait("gpsimd", "pe")
        for f in range(NF):
            p.inc("gpsimd", lambda e, f=f, lay=lay: e.dma_start(
                wqkv_sb[f][:], ctx.d["L0_ffn_wqkv"].ap()[lay].rearrange(
                    "(ft pp) c -> ft pp c", pp=128)[f]), "wdma", 16)
        p.wait("scalar", "dve")
        for f in range(NF):
            soff = TOKOFF - (1 if f >= 4 else 0)
            p.inc("scalar", lambda e, f=f, soff=soff: e.copy(
                hbT[f][:, 0:SEQ], zT[f][:, soff:soff + SEQ]), "act")
        p.wait("gpsimd", "act")
        for t in range(NT - 1):
            p.run(lambda e, t=t: e.memset(vbuf[t][:], 1.0), "gpsimd")
        p.inc("gpsimd", lambda e: e.memset(vbuf[NT - 1][:], 1.0), "gp", 1)
        p.wait("tensor", "gp")
        p.wait("tensor", "act")
        p.wait("tensor", "wdma")

        for t in range(NT):
            p.wait("tensor", "act")
            p.wait("tensor", "dve")
            for cch in range(3):
                cs = slice(cch * 512, cch * 512 + 512)
                for f in range(NF):
                    p.inc("tensor", lambda e, t=t, cs=cs, f=f: e.matmul(
                        PS[:, cs], hbT[f][:, t * 128:(t + 1) * 128],
                        wqkv_sb[f][:, cs], start=(f == 0), stop=(f == NF - 1)), "pe")
            p.wait("scalar", "pe")
            p.wait("vector", "pe")
            p.inc("scalar", lambda e, t=t: e.copy(
                vbuf[t].ap().rearrange("p (h c) -> p h c", c=65)[:, :, 0:64],
                PS[:, 1024:1536].rearrange("p (h c) -> p h c", c=64)), "act")
            for qk in range(2):
                po = qk * 512
                AT, BT = (rott["Aq"], rott["Bq"]) if qk == 0 else (rott["Ak"], rott["Bk"])
                sso = qk * 8
                p.inc("scalar", lambda e, po=po: e.activation(
                    sq32[:, 0:512], PS[:, po:po + 512], AF.Square), "act")
                p.wait("vector", "act")
                p.inc("vector", lambda e, sso=sso: e.reduce_sum(
                    ssq[:, sso:sso + 8],
                    sq32.ap().rearrange("p (h d) -> p h d", d=64), axis=mybir.AxisListType.X),
                    "dve")
                p.wait("scalar", "dve")
                p.inc("scalar", lambda e, sso=sso: e.activation(
                    ssq[:, sso:sso + 8], ssq[:, sso:sso + 8], AF.Sqrt), "act")
                p.wait("vector", "act")
                p.inc("vector", lambda e, sso=sso: e.tensor_scalar(
                    ssq[:, sso:sso + 8], ssq[:, sso:sso + 8], 1e-12, None, ALU.max), "dve")
                p.wait("vector", "dve")
                p.inc("vector", lambda e, sso=sso: e.reciprocal(
                    ssq[:, sso:sso + 8], ssq[:, sso:sso + 8]), "dve")
                p.wait("vector", "dve")
                p.wait("vector", "pe")
                for h in range(H):
                    p.inc("vector", lambda e, h=h, po=po, sso=sso: e.tensor_scalar(
                        qn[:, h * 64:(h + 1) * 64], PS[:, po + h * 64:po + (h + 1) * 64],
                        ssq[:, sso + h:sso + h + 1], None, ALU.mult), "dve")
                p.wait("vector", "dve")
                p.inc("vector", lambda e, t=t, AT=AT: e.tensor_tensor(
                    t1b.ap().rearrange("p (h d) -> p h d", d=64),
                    qn.ap().rearrange("p (h d) -> p h d", d=64),
                    AT[:, t * 64:(t + 1) * 64].rearrange("p (a d) -> p a d", a=1)
                    .broadcast_to([128, 8, 64]), ALU.mult), "dve")
                p.inc("vector", lambda e, t=t, BT=BT: e.tensor_tensor(
                    t2b.ap().rearrange("p (h d) -> p h d", d=64)[:, :, 0:32],
                    qn.ap().rearrange("p (h d) -> p h d", d=64)[:, :, 32:64],
                    BT[:, t * 64:t * 64 + 32].rearrange("p (a d) -> p a d", a=1)
                    .broadcast_to([128, 8, 32]), ALU.mult), "dve")
                p.inc("vector", lambda e, t=t, BT=BT: e.tensor_tensor(
                    t2b.ap().rearrange("p (h d) -> p h d", d=64)[:, :, 32:64],
                    qn.ap().rearrange("p (h d) -> p h d", d=64)[:, :, 0:32],
                    BT[:, t * 64 + 32:(t + 1) * 64].rearrange("p (a d) -> p a d", a=1)
                    .broadcast_to([128, 8, 32]), ALU.mult), "dve")
                p.wait("vector", "dve")
                p.inc("vector", lambda e: e.tensor_tensor(
                    t1b[:, 0:512], t1b[:, 0:512], t2b[:, 0:512], ALU.add), "dve")
                p.wait("tensor", "dve")
                p.wait("tensor", "act")
                for hh in range(H):
                    p.inc("tensor", lambda e, hh=hh: e.transpose(
                        PS[0:64, 1024 + hh * 128:1024 + (hh + 1) * 128],
                        t1b[:, hh * 64:(hh + 1) * 64], ctx.ident_f32[:]), "pe")
                p.wait("scalar", "pe")
                dst = qtb if qk == 0 else ktb
                for hh in range(H):
                    g, rbase = hh // 2, (hh % 2) * 64
                    p.inc("scalar", lambda e, g=g, rbase=rbase, hh=hh, t=t, dst=dst: e.copy(
                        dst[g][rbase:rbase + 64, t * 128:(t + 1) * 128],
                        PS[0:64, 1024 + hh * 128:1024 + (hh + 1) * 128]), "act")
                p.wait("tensor", "act")
                p.wait("vector", "act")

        p.wait("sync", "pe")
        for r in range(4):
            p.inc("sync", lambda e, r=r: e.dma_start(
                masks_sb[:, r * 512:(r + 1) * 512], ctx.cd["locmaskp"].ap()[r]), "dma", 16)
            p.inc("sync", lambda e, r=r: e.dma_start(
                masks_sb[:, 2048 + r * 512:2048 + (r + 1) * 512], ctx.cd["locmask"].ap()[r]),
                "dma", 16)
        p.wait("vector", "dma")
        p.wait("scalar", "dma")
        for h in range(H):
            g, rbase = h // 2, (h % 2) * 64
            for w in range(4):
                akts = list(range(4 * (w - 1), 4 * w)) if w > 0 else []
                akts += list(range(4 * w, 4 * w + 4))
                first = True
                for j, akt in enumerate(akts):
                    cur = (akt >= 4 * w)
                    r = akt - 4 * w if cur else akt - 4 * (w - 1)
                    moff = (2048 + r * 512) if cur else (r * 512)
                    pso = 2048 + (j % 2) * 512
                    p.wait("tensor", "act")
                    p.wait("tensor", "dve")
                    p.inc("tensor", lambda e, g=g, rbase=rbase, akt=akt, w=w, pso=pso: e.matmul(
                        PS[:, pso:pso + 512],
                        ktb[g][rbase:rbase + 64, akt * 128:(akt + 1) * 128],
                        qtb[g][rbase:rbase + 64, w * 512:(w + 1) * 512],
                        start=True, stop=True), "pe")
                    p.wait("vector", "pe")
                    p.inc("vector", lambda e, moff=moff, pso=pso: e.tensor_tensor(
                        sc32[:, 0:512], PS[:, pso:pso + 512],
                        masks_sb[:, moff:moff + 512], ALU.add), "dve")
                    p.wait("scalar", "dve")
                    p.wait("scalar", "pe")
                    p.inc("scalar", lambda e: e.activation(
                        glb[:, 0:512], sc32[:, 0:512], AF.Exp, scale=8.0), "act")
                    p.wait("tensor", "act")
                    p.inc("tensor", lambda e, h=h, akt=akt, first=first, last=(j == len(akts) - 1):
                          e.matmul(PS[0:65, 3072:3584], vbuf[akt][:, h * 65:(h + 1) * 65],
                                   glb[:, 0:512], start=first, stop=last), "pe")
                    first = False
                p.wait("scalar", "pe")
                p.inc("scalar", lambda e: e.activation(
                    rowsb[0:1, 0:512], PS[64:65, 3072:3584], AF.Ln), "act")
                p.wait("scalar", "act")
                p.inc("scalar", lambda e: e.activation(
                    rowsb[0:1, 0:512], rowsb[0:1, 0:512], AF.Exp, scale=-1.0), "act")
                p.wait("tensor", "act")
                p.inc("tensor", lambda e: e.matmul(
                    PS[0:64, 3584:4096], ctx.ones_row[0:1, 0:64], rowsb[0:1, 0:512],
                    start=True, stop=True), "pe")
                p.wait("scalar", "pe")
                p.inc("scalar", lambda e: e.copy(rbc[0:64, 0:512], PS[0:64, 3584:4096]), "act")
                p.wait("vector", "act")
                p.inc("vector", lambda e, g=g, rbase=rbase, w=w: e.tensor_tensor(
                    oT[g][rbase:rbase + 64, w * 512:(w + 1) * 512],
                    PS[0:64, 3072:3584], rbc[0:64, 0:512], ALU.mult), "dve")
                p.wait("tensor", "dve")

        if lay == 0 and batch == 0:
            for nm, tiles in (("q0t", qtb), ("k0t", ktb), ("o0t", oT)):
                if nm in ctx.taps:
                    p.barrier(("sync",))
                    dram = ctx.tap_drams[nm]
                    for g in range(4):
                        for ch in range(4):
                            p.wait("vector", "st")
                            p.inc("vector", lambda e, g=g, ch=ch, tiles=tiles: e.tensor_copy(
                                sc32[:, 0:512], tiles[g][:, ch * 512:(ch + 1) * 512]), "dve")
                            p.wait("sync", "dve")
                            p.inc("sync", lambda e, g=g, ch=ch, dram=dram: e.dma_start(
                                dram.ap()[g * 128:(g + 1) * 128, ch * 512:(ch + 1) * 512],
                                sc32[:, 0:512]), "st", 16)
                            p.wait("sync", "st")
                    p.wait("gpsimd", "st")
        p.wait("gpsimd", "pe")
        p.wait("gpsimd", "act")
        for i in range(4):
            p.inc("gpsimd", lambda e, i=i, lay=lay: e.dma_start(
                wol_sb[i][:], ctx.d["L0_ffn_wo"].ap()[lay].rearrange(
                    "(it pp) o -> it pp o", pp=128)[i]), "wdma", 16)
        p.wait("tensor", "wdma")
        for fo in range(NF):
            for cch in range(4):
                cs = slice(cch * 512, cch * 512 + 512)
                pso = (cch % 4) * 512
                p.wait("tensor", "dve")
                for i in range(4):
                    p.inc("tensor", lambda e, fo=fo, cs=cs, i=i, pso=pso: e.matmul(
                        PS[:, pso:pso + 512], wol_sb[i][:, fo * 128:(fo + 1) * 128],
                        oT[i][:, cs], start=(i == 0), stop=(i == 3)), "pe")
                p.wait("vector", "pe")
                p.inc("vector", lambda e, fo=fo, cs=cs, pso=pso: e.tensor_tensor(
                    zT[fo][:, TOKOFF + cs.start:TOKOFF + cs.stop],
                    zT[fo][:, TOKOFF + cs.start:TOKOFF + cs.stop],
                    PS[:, pso:pso + 512], ALU.add), "dve")

        if f"zo{lay}" in ctx.taps and batch == 0:
            _tap_tiles(ctx, zT, f"zo{lay}", padded=True)
        emit_colvec(ctx, ctx.d["L0_ffn_ln_g"].ap()[lay].rearrange("(a e) -> a e", a=1), E, 0)
        emit_colvec(ctx, ctx.d["L0_ffn_ln_b"].ap()[lay].rearrange("(a e) -> a e", a=1), E, 8)
        emit_colvec(ctx, ctx.d["L0_ffn_b2"].ap()[lay].rearrange("(a e) -> a e", a=1), E, 16)
        emit_colvec(ctx, ctx.d["L0_ffn_b1"].ap()[lay][0:2048].rearrange("(a e) -> a e", a=1),
                    2048, 24)
        emit_colvec(ctx, ctx.d["L0_ffn_b1"].ap()[lay][2048:2730].rearrange("(a e) -> a e", a=1),
                    682, 40)
        emit_colvec(ctx, ctx.d["L0_ffn_b1"].ap()[lay][2730:4778].rearrange("(a e) -> a e", a=1),
                    2048, 46)
        emit_colvec(ctx, ctx.d["L0_ffn_b1"].ap()[lay][4778:5460].rearrange("(a e) -> a e", a=1),
                    682, 62)
        offs = [TOKOFF - (1 if f >= 4 else 0) for f in range(NF)]
        emit_ln(ctx, zT, hnT, gamma_off=0, beta_off=8, src_offs=offs)
        for tcn in range(4):
            TC = slice(tcn * 512, tcn * 512 + 512)
            evm = {}

            def ld_w1(c, lay=lay):
                cw = 128 if c < 21 else 42
                return [lambda e, c=c, cw=cw, lay=lay: e.dma_start(
                    w1a_sb[c % 2].ap().rearrange("p (ft c) -> p ft c", c=128)[:, :, 0:cw],
                    ctx.d["w1s"].ap()[lay].rearrange("(ft pp) hh -> pp ft hh", pp=128)
                    [:, :, c * 128:c * 128 + cw]),
                    lambda e, c=c, cw=cw, lay=lay: e.dma_start(
                    w1g_sb[c % 2].ap().rearrange("p (ft c) -> p ft c", c=128)[:, :, 0:cw],
                    ctx.d["w1s"].ap()[lay].rearrange("(ft pp) hh -> pp ft hh", pp=128)
                    [:, :, 2730 + c * 128:2730 + c * 128 + cw])]

            def use_w1(c, TC=TC):
                cw = 128 if c < 21 else 42
                wa, wg = w1a_sb[c % 2], w1g_sb[c % 2]
                po = (c % 2) * 2048
                if c - 2 in evm:
                    p.wait("tensor", "dve", evm[c - 2][0])
                    p.wait("tensor", "act", evm[c - 2][1])
                for f in range(NF):
                    p.inc("tensor", lambda e, c=c, cw=cw, f=f, TC=TC, wa=wa, po=po: e.matmul(
                        PS[0:cw, po:po + 512], wa[:, f * 128:f * 128 + cw],
                        hnT[f][:, TOKOFF + TC.start:TOKOFF + TC.stop],
                        start=(f == 0), stop=(f == NF - 1)), "pe")
                    p.inc("tensor", lambda e, c=c, cw=cw, f=f, TC=TC, wg=wg, po=po: e.matmul(
                        PS[0:cw, po + 1024:po + 1536], wg[:, f * 128:f * 128 + cw],
                        hnT[f][:, TOKOFF + TC.start:TOKOFF + TC.stop],
                        start=(f == 0), stop=(f == NF - 1)), "pe")
                pm = p.mark("pe")
                p.wait("scalar", "pe", pm)
                if c - 1 in evm:
                    p.wait("scalar", "dve", evm[c - 1][0])
                p.inc("scalar", lambda e, c=c, cw=cw, po=po: e.activation(
                    glb[0:cw, 0:512], PS[0:cw, po + 1024:po + 1536], AF.Gelu,
                    bias=ctx.cols[0:cw, 46 + c:47 + c]), "act")
                am = p.mark("act")
                p.wait("vector", "act", am)
                p.inc("vector", lambda e, c=c, cw=cw, po=po: e.scalar_tensor_tensor(
                    glub[c][0:cw, 0:512], PS[0:cw, po:po + 512], ctx.cols[0:cw, 24 + c:25 + c],
                    glb[0:cw, 0:512], ALU.add, ALU.mult), "dve")
                evm[c] = (p.mark("dve"), am)

            p.wait("tensor", "act")
            p.wait("tensor", "dve")
            stream_weights(p, NC2, ld_w1, use_w1, "w2a")
            p.wait("tensor", "dve")

            def ld_w2(c, lay=lay):
                cw = 128 if c < 21 else 42
                return lambda e, c=c, cw=cw, lay=lay: e.dma_start(
                    w2_sb[c % 2][0:cw, :], ctx.d["w2s"].ap()[lay][c * 128:c * 128 + cw, :])

            def use_w2(c):
                cw = 128 if c < 21 else 42
                wc = w2_sb[c % 2]
                for fo in range(NF):
                    p.inc("tensor", lambda e, c=c, cw=cw, fo=fo, wc=wc: e.matmul(
                        PS[:, fo * 512:(fo + 1) * 512], wc[0:cw, fo * 128:(fo + 1) * 128],
                        glub[c][0:cw, 0:512], start=(c == 0), stop=(c == NC2 - 1)), "pe")

            stream_weights(p, NC2, ld_w2, use_w2, "w2b")
            p.wait("scalar", "pe")
            p.wait("vector", "pe")
            for fo in range(NF):
                p.inc("scalar", lambda e, fo=fo: e.activation(
                    sc32[:, 0:512], PS[:, fo * 512:(fo + 1) * 512], AF.Identity,
                    bias=ctx.cols[:, 16 + fo:17 + fo]), "act")
                p.wait("vector", "act")
                p.inc("vector", lambda e, fo=fo, TC=TC: e.tensor_tensor(
                    zT[fo][:, TOKOFF + TC.start:TOKOFF + TC.stop],
                    zT[fo][:, TOKOFF + TC.start:TOKOFF + TC.stop],
                    sc32[:, 0:512], ALU.add), "dve")
                p.wait("scalar", "dve")

        if f"z{lay}" in ctx.taps and batch == 0:
            p.barrier(("tensor", "scalar", "vector", "sync"))
            _tap_tiles(ctx, zT, f"z{lay}", padded=True)

    p.barrier(("sync",))
    for f in range(NF):
        p.inc("sync", lambda e, f=f: e.dma_start(
            ctx.xT[f][:, tokc(0, SEQ)], dpark.ap()[f * 128:(f + 1) * 128, :]), "dma", 16)
    p.wait("vector", "dma")
    for f in range(NF):
        p.inc("vector", lambda e, f=f: e.tensor_tensor(
            ctx.xT[f][:, tokc(0, SEQ)], ctx.xT[f][:, tokc(0, SEQ)],
            zT[f][:, tokc(0, SEQ)], ALU.add), "dve")
    p.barrier(("tensor", "scalar", "vector", "sync", "gpsimd"))


BLOCK_EMITTERS["l0_ffn"] = emit_l0_ffn



def prep_core_inputs(inputs, core, nb=2, boff=None):
    x = inputs["L0_attn_X"]
    b0 = core * nb if boff is None else boff
    xs = np.ascontiguousarray(
        np.transpose(x[b0:b0 + nb], (0, 2, 1))).astype(np.float32)
    m = {"xin": xs}
    for name, shape in WEIGHT_SHAPES.items():
        if name in inputs:
            m[name] = np.ascontiguousarray(np.asarray(inputs[name], dtype=np.float32))
    m["bq_t"] = np.ascontiguousarray(np.asarray(inputs["L0_attn_bq"], np.float32).T)
    m["bk_t"] = np.ascontiguousarray(np.asarray(inputs["L0_attn_bk"], np.float32).T)
    m["lam11"] = np.asarray(inputs["L0_attn_lam"], np.float32).reshape(1, 1)
    return m


ALL_BLOCKS = ("l0_attn", "l0_ffn", "l1_attn", "l1_ffn")


def kernel(**inputs):
    nc, _ = build_program(blocks=ALL_BLOCKS, taps=(), nb=2)
    in_maps = [prep_core_inputs(inputs, c) for c in range(NCORES)]
    res = run_bass_kernel_spmd(nc, in_maps, core_ids=list(range(NCORES)))
    outs = []
    for c in range(NCORES):
        o = np.asarray(res.results[c]["xout"])
        outs.append(np.transpose(o, (0, 2, 1)))
    return np.concatenate(outs, axis=0).astype(np.float32)


# version 12 (final)
# speedup vs baseline: 1.6003x; 1.0117x over previous
import contextlib
import numpy as np
import ml_dtypes
import concourse.bass as bass
import concourse.mybir as mybir
from concourse.bass_utils import run_bass_kernel_spmd

F32 = mybir.dt.float32
BF16 = mybir.dt.bfloat16
AF = mybir.ActivationFunctionType
ALU = mybir.AluOpType

NCORES = 8
SEQ = 2048
E = 1024
NF = 8
TOKOFF = 2
NPAD = TOKOFF + SEQ + 2
LN_EPS = 1e-5
KB = 1024

XT_STRIDE = 8224
R_XT = 0
R_LNS = 65792
R_CON = 99072
R_WAR = 101632
R_BIG = 126208


class Prog:
    ENGINES = ("sync", "tensor", "vector", "scalar", "gpsimd")

    def __init__(self, nc):
        self.nc = nc
        self.streams = {e: [] for e in self.ENGINES}
        self.sems = {}
        self.counts = {}
        self._last_dma = {}
        self._es = contextlib.ExitStack()

    def sem(self, name):
        if name not in self.sems:
            self.sems[name] = self._es.enter_context(self.nc.semaphore(name))
            self.counts[name] = 0
        return self.sems[name]

    def emit(self, engine, fn):
        self.streams[engine].append(fn)

    def wait(self, engine, semname, value=None):
        self._last_dma[engine] = None
        if semname in self.DMA_SEMS:
            for e2 in list(self._last_dma):
                if self._last_dma.get(e2) == semname:
                    self._last_dma[e2] = None
        s = self.sem(semname)
        t = self.counts[semname] if value is None else value
        if t <= 0:
            return
        self.emit(engine, lambda eng, s=s, t=t: eng.wait_ge(s, t))

    DMA_SEMS = ("dma", "wdma", "st", "rt", "wdma2", "st2")

    def inc(self, engine, fn, semname, amount=1):
        s = self.sem(semname)
        if semname in self.DMA_SEMS:
            key = (engine, semname)
            if self._last_dma.get(engine) != semname:
                t = self.counts[semname]
                if t > 0:
                    self.emit(engine, lambda eng, s=s, t=t: eng.wait_ge(s, t))
            self._last_dma[engine] = semname
        else:
            self._last_dma[engine] = None
        self.counts[semname] += amount
        self.emit(engine, lambda eng, s=s, a=amount, f=fn: f(eng).then_inc(s, a))

    def run(self, fn, engine):
        self._last_dma[engine] = None
        self.emit(engine, fn)

    def mark(self, semname):
        self.sem(semname)
        return self.counts[semname]

    def barrier(self, engines=("tensor", "vector", "scalar", "gpsimd", "sync"),
                sems=("pe", "dve", "act", "gp", "dma", "wdma", "st", "wdma2", "st2")):
        for e in ("pe", "dve", "act", "gp"):
            self.sem(e)
        for eng in engines:
            for s in sems:
                self.wait(eng, s)

    def build(self):
        with self.nc.Block() as block:
            for e in self.ENGINES:
                if self.streams[e]:
                    getattr(block, e)(lambda eng, ops=tuple(self.streams[e]): [f(eng) for f in ops])
        return self.nc


class SbufMap:
    BASE = 16512

    def __init__(self, nc, limit=212500):
        self.nc = nc
        self.limit = limit
        self.cursor = 0

    def alloc(self, name, shape, dtype, at=None):
        dsize = 2 if dtype == BF16 else 4
        nbytes = shape[1] * dsize
        if at is None:
            at = (self.cursor + 31) & ~31
            self.cursor = at + nbytes
        assert at + nbytes <= self.limit, f"SBUF overflow at {name}: {at + nbytes}"
        return self.nc.alloc_sbuf_tensor_at(name, list(shape), dtype, offset=self.BASE + at)

    def arena(self, nbytes):
        at = (self.cursor + 31) & ~31
        self.cursor = at + nbytes
        assert self.cursor <= self.limit, f"SBUF overflow (arena): {self.cursor}"
        return at


class Ctx:
    pass


def tokc(start, n):
    return slice(TOKOFF + start, TOKOFF + start + n)


def emit_ln(ctx, src_tiles, dst_tiles, gamma_off=None, beta_off=None,
            resid=False, src_padded=True, src_bf=False, src_offs=None):
    p = ctx.p
    PS = ctx.PS
    so = TOKOFF if src_padded else 0
    n = SEQ
    offs = src_offs if src_offs is not None else [so] * NF
    sq, misc = ctx.ln_sq, ctx.ln_misc
    ones_src = ctx.ones_col_bf if src_bf else ctx.ones_col

    p.barrier(("tensor", "scalar", "vector"))
    for f in range(NF):
        p.wait("scalar", "pe")
        p.inc("scalar", lambda e, f=f: e.activation(
            sq[:, 0:n], src_tiles[f][:, offs[f]:offs[f] + n], AF.Square), "act")
        am = p.mark("act")
        p.wait("tensor", "act", am)
        for cch in range(4):
            cs = slice(cch * 512, cch * 512 + 512)
            scs = slice(offs[f] + cch * 512, offs[f] + cch * 512 + 512)
            p.inc("tensor", lambda e, f=f, cs=cs, scs=scs: e.matmul(
                PS[0:1, cs], ones_src[:], src_tiles[f][:, scs],
                start=(f == 0), stop=(f == NF - 1)), "pe")
            p.inc("tensor", lambda e, f=f, cs=cs: e.matmul(
                PS[0:1, 2048 + cs.start:2048 + cs.stop], ctx.ones_col[:], sq[:, cs],
                start=(f == 0), stop=(f == NF - 1)), "pe")
    p.wait("vector", "pe")
    p.inc("vector", lambda e: e.tensor_scalar(
        misc[0:1, 0:n], PS[0:1, 0:n], 1.0 / E, None, ALU.mult), "dve")
    p.inc("vector", lambda e: e.tensor_scalar(
        misc[0:1, 2048:2048 + n], PS[0:1, 2048:2048 + n], 1.0 / E, None, ALU.mult), "dve")
    p.wait("vector", "dve")
    p.inc("vector", lambda e: e.tensor_tensor(
        misc[0:1, 4096:4096 + n], misc[0:1, 0:n], misc[0:1, 0:n], ALU.mult), "dve")
    p.wait("vector", "dve")
    p.inc("vector", lambda e: e.tensor_tensor(
        misc[0:1, 4096:4096 + n], misc[0:1, 2048:2048 + n], misc[0:1, 4096:4096 + n],
        ALU.subtract), "dve")
    p.wait("scalar", "dve")
    p.inc("scalar", lambda e: e.activation(
        misc[0:1, 2048:2048 + n], misc[0:1, 4096:4096 + n], AF.Sqrt, bias=ctx.eps_s[0:1, 0:1]), "act")
    p.wait("vector", "act")
    p.inc("vector", lambda e: e.reciprocal(
        misc[0:1, 2048:2048 + n], misc[0:1, 2048:2048 + n]), "dve")
    p.wait("vector", "dve")
    p.inc("vector", lambda e: e.tensor_tensor(
        misc[0:1, 4096:4096 + n], misc[0:1, 0:n], misc[0:1, 2048:2048 + n], ALU.mult), "dve")
    p.wait("vector", "dve")
    p.inc("vector", lambda e: e.tensor_scalar(
        misc[0:1, 4096:4096 + n], misc[0:1, 4096:4096 + n], -1.0, None, ALU.mult), "dve")
    p.wait("tensor", "dve")
    for cch in range(4):
        cs = slice(cch * 512, cch * 512 + 512)
        p.inc("tensor", lambda e, cs=cs: e.matmul(
            PS[:, cs], ctx.ones_row[0:1, :], misc[0:1, 2048 + cs.start:2048 + cs.stop],
            start=True, stop=True), "pe")
        p.inc("tensor", lambda e, cs=cs: e.matmul(
            PS[:, 2048 + cs.start:2048 + cs.stop], ctx.ones_row[0:1, :],
            misc[0:1, 4096 + cs.start:4096 + cs.stop], start=True, stop=True), "pe")
    p.wait("vector", "pe")
    for f in range(NF):
        p.wait("vector", "act")
        p.inc("vector", lambda e, f=f: e.tensor_tensor(
            sq[:, 0:n], src_tiles[f][:, offs[f]:offs[f] + n], PS[:, 0:n], ALU.mult), "dve")
        p.wait("vector", "dve")
        if gamma_off is None and not resid:
            p.inc("vector", lambda e, f=f: e.tensor_tensor(
                dst_tiles[f][:, tokc(0, n)], sq[:, 0:n], PS[:, 2048:2048 + n], ALU.add), "dve")
            p.wait("vector", "dve")
        else:
            p.inc("vector", lambda e, f=f: e.tensor_tensor(
                sq[:, 0:n], sq[:, 0:n], PS[:, 2048:2048 + n], ALU.add), "dve")
            p.wait("scalar", "dve")
            if gamma_off is not None:
                p.inc("scalar", lambda e, f=f: e.activation(
                    sq[:, 0:n], sq[:, 0:n], AF.Identity,
                    bias=ctx.cols[:, beta_off + f:beta_off + f + 1],
                    scale=ctx.cols[:, gamma_off + f:gamma_off + f + 1]), "act")
                p.wait("vector", "act")
            if resid:
                p.inc("vector", lambda e, f=f: e.tensor_tensor(
                    dst_tiles[f][:, tokc(0, n)], dst_tiles[f][:, tokc(0, n)],
                    sq[:, 0:n], ALU.add), "dve")
            else:
                p.inc("vector", lambda e, f=f: e.tensor_copy(
                    dst_tiles[f][:, tokc(0, n)], sq[:, 0:n]), "dve")
            p.wait("vector", "dve")
    p.barrier(("tensor", "scalar", "vector"))


def emit_colvec(ctx, dram_ap, n, dst_off, scale=None, dst=None):
    p = ctx.p
    if dst is None:
        dst = ctx.cols
    assert n <= 2048
    nt = (n + 127) // 128
    p.barrier(("sync", "tensor", "scalar"))
    p.inc("sync", lambda e: e.dma_start(ctx.row_tmp[0:1, 0:n], dram_ap), "dma", 16)
    p.wait("tensor", "dma")
    for i in range(nt):
        w = min(128, n - i * 128)
        p.inc("tensor", lambda e, i=i, w=w: e.transpose(
            ctx.PS[0:w, 3584 + i:3584 + i + 1], ctx.row_tmp[0:1, i * 128:i * 128 + w],
            ctx.ident_f32[0:1, 0:1]), "pe")
    p.wait("scalar", "pe")
    if scale is None:
        p.inc("scalar", lambda e: e.copy(
            dst[:, dst_off:dst_off + nt], ctx.PS[:, 3584:3584 + nt]), "act")
    else:
        p.inc("scalar", lambda e: e.activation(
            dst[:, dst_off:dst_off + nt], ctx.PS[:, 3584:3584 + nt], AF.Identity,
            bias=0.0, scale=scale), "act")
    p.wait("tensor", "act")


def emit_bcast_row(ctx, src_row_ap, dst_tile_ap, n, nparts=128):
    p = ctx.p
    p.wait("tensor", "act")
    p.wait("tensor", "dve")
    nch = (n + 511) // 512
    for cch in range(nch):
        w = min(512, n - cch * 512)
        p.inc("tensor", lambda e, cch=cch, w=w: e.matmul(
            ctx.PS[0:nparts, 2048 + cch * 512:2048 + cch * 512 + w],
            ctx.ones_row[0:1, 0:nparts],
            src_row_ap[0:1, cch * 512:cch * 512 + w], start=True, stop=True), "pe")
    p.wait("scalar", "pe")
    p.inc("scalar", lambda e: e.copy(dst_tile_ap, ctx.PS[0:nparts, 2048:2048 + n]), "act")
    p.wait("tensor", "act")
    p.wait("sync", "pe")
    p.wait("gpsimd", "pe")



def stream_weights(p, n, loadfn, consumefn, sembase, depth=2):
    semtot, pem = {}, {}

    def issue(c):
        sem = f"{sembase}{c % depth}"
        fns = loadfn(c)
        if not isinstance(fns, (list, tuple)):
            fns = [fns]
        for fn in fns:
            p.inc("gpsimd", fn, sem, 16)
        semtot[c] = p.mark(sem)

    p.wait("gpsimd", "pe")
    for c in range(min(depth, n)):
        issue(c)
    for c in range(n):
        p.wait("tensor", f"{sembase}{c % depth}", semtot[c])
        consumefn(c)
        pem[c] = p.mark("pe")
        if c + depth < n:
            p.wait("gpsimd", "pe", pem[c])
            issue(c + depth)


def emit_l0_attn(ctx, batch):
    p, m = ctx.p, ctx.m
    PS = ctx.PS
    H = 16
    B0 = ctx.BIG

    xnT = [m.alloc(f"b1_xnT{f}", [128, NPAD], BF16, at=B0 + f * 4128) for f in range(NF)]
    vbuf = [m.alloc(f"b1_v{t}", [128, 16 * 65], BF16, at=R_LNS + t * 2080) for t in range(16)]
    QOFF = B0 + 8 * 4128
    qt = m.alloc("b1_qt", [128, 2048], BF16, at=QOFF)
    kt = m.alloc("b1_kt", [128, 2048], BF16, at=QOFF + 4096)
    expb = m.alloc("b1_exp", [128, 2048], BF16, at=QOFF + 8192)
    rb = m.alloc("b1_rb", [64, 1024], F32, at=QOFF + 8192)
    tmb = m.alloc("b1_tmb", [64, 1024], F32, at=QOFF + 12288)
    rows2 = m.alloc("b1_rows2", [128, 1024], F32, at=QOFF + 16384)
    bv_bcast = m.alloc("b1_bvb", [128, 1024], F32, at=QOFF + 12288)
    OOFF = QOFF + 20480
    ocatT = [m.alloc(f"b1_ocat{g}", [128, 2048], BF16, at=OOFF + g * 4096) for g in range(NF)]
    ybuf = [m.alloc(f"b1_y{f}", [128, 2048], BF16, at=B0 + f * 4128) for f in range(NF)]
    ctx.bv_bcast = bv_bcast

    W0 = ctx.WARENA
    wv_sb = [m.alloc(f"b1_wv{f}", [128, 1024], BF16, at=W0 + f * 2048) for f in range(NF)]
    wq_sb = m.alloc("b1_wq", [128, 1024], BF16, at=W0 + 16 * KB)
    wk_sb = m.alloc("b1_wk", [128, 1024], BF16, at=W0 + 18 * KB)
    wo_sb = [m.alloc(f"b1_wo{t}", [128, 1024], BF16, at=W0 + t * 2048) for t in range(NF)]

    emit_ln(ctx, ctx.xT, xnT)
    if "xn" in ctx.taps and batch == 0:
        _tap_tiles(ctx, xnT, "xn", padded=True)
    if "xnb1" in ctx.taps and batch == 1:
        _tap_tiles(ctx, xnT, "xnb1", padded=True)

    emit_colvec(ctx, ctx.d["L0_attn_bo"].ap().rearrange("(a e) -> a e", a=1), E, 0)
    emit_colvec(ctx, ctx.d["L0_attn_gamma"].ap().rearrange("(a e) -> a e", a=1), E, 8, scale=0.2)
    emit_colvec(ctx, ctx.d["L0_attn_beta"].ap().rearrange("(a e) -> a e", a=1), E, 16, scale=0.2)
    p.inc("sync", lambda e: e.dma_start(ctx.lam_s[0:1, 0:1], ctx.d["lam11"].ap()), "dma", 16)
    p.inc("sync", lambda e: e.dma_start(ctx.bq_col[:, 0:H], ctx.d["bq_t"].ap()), "dma", 16)
    p.inc("sync", lambda e: e.dma_start(ctx.bk_col[:, 0:H], ctx.d["bk_t"].ap()), "dma", 16)
    p.wait("sync", "pe")
    p.inc("sync", lambda e: e.dma_start(ctx.row_tmp[0:1, 0:E],
                                        ctx.d["L0_attn_bv"].ap().rearrange("h d -> (h d)").rearrange("(a e) -> a e", a=1)), "dma", 16)
    p.wait("tensor", "dma")
    p.wait("vector", "dma")
    emit_bcast_row(ctx, ctx.row_tmp, ctx.bv_bcast[:, 0:E], E)

    p.wait("gpsimd", "pe")
    for f in range(NF):
        p.inc("gpsimd", lambda e, f=f: e.dma_start(
            wv_sb[f].ap().rearrange("p (h d) -> p h d", d=64),
            ctx.d["L0_attn_Wv"].ap().rearrange("h (ft pp) d -> ft pp h d", pp=128)[f]),
            "wdma", 16)
    for t in range(15):
        p.run(lambda e, t=t: e.memset(vbuf[t][:], 1.0), "gpsimd")
    p.inc("gpsimd", lambda e: e.memset(vbuf[15][:], 1.0), "gp", 1)
    p.wait("tensor", "wdma")
    p.wait("tensor", "gp")
    p.wait("tensor", "act")
    for t in range(16):
        pso = 2048 + (t % 2) * 1024
        p.wait("tensor", "dve")
        for cch in range(2):
            cs = slice(cch * 512, cch * 512 + 512)
            for f in range(NF):
                p.inc("tensor", lambda e, t=t, cs=cs, f=f, pso=pso: e.matmul(
                    PS[:, pso + cs.start:pso + cs.stop],
                    xnT[f][:, TOKOFF + t * 128:TOKOFF + (t + 1) * 128],
                    wv_sb[f][:, cs], start=(f == 0), stop=(f == NF - 1)), "pe")
        p.wait("vector", "pe")
        p.inc("vector", lambda e, t=t, pso=pso: e.tensor_tensor(
            vbuf[t].ap().rearrange("p (h c) -> p h c", c=65)[:, :, 0:64],
            PS[:, pso:pso + E].rearrange("p (h c) -> p h c", c=64),
            ctx.bv_bcast[:, 0:E].rearrange("p (h c) -> p h c", c=64), ALU.add), "dve")
    p.barrier()

    for h in range(H):
        p.wait("gpsimd", "pe")
        p.inc("gpsimd", lambda e, h=h: e.dma_start(
            wq_sb.ap().rearrange("p (ft c) -> p ft c", c=128),
            ctx.d["L0_attn_Wq"].ap()[h].rearrange("(ft pp) c -> pp ft c", pp=128)),
            "wdma", 16)
        p.inc("gpsimd", lambda e, h=h: e.dma_start(
            wk_sb.ap().rearrange("p (ft c) -> p ft c", c=128),
            ctx.d["L0_attn_Wk"].ap()[h].rearrange("(ft pp) c -> pp ft c", pp=128)),
            "wdma", 16)
        p.wait("tensor", "wdma")
        for cch in range(4):
            cs = slice(cch * 512, cch * 512 + 512)
            p.wait("tensor", "act")
            for f in range(NF):
                p.inc("tensor", lambda e, cs=cs, f=f: e.matmul(
                    PS[:, 2048:2560], wq_sb[:, f * 128:(f + 1) * 128],
                    xnT[f][:, TOKOFF + cs.start:TOKOFF + cs.stop],
                    start=(f == 0), stop=(f == NF - 1)), "pe")
                p.inc("tensor", lambda e, cs=cs, f=f: e.matmul(
                    PS[:, 2560:3072], wk_sb[:, f * 128:(f + 1) * 128],
                    xnT[f][:, TOKOFF + cs.start:TOKOFF + cs.stop],
                    start=(f == 0), stop=(f == NF - 1)), "pe")
            p.wait("scalar", "pe")
            p.inc("scalar", lambda e, h=h, cs=cs: e.activation(
                qt[:, cs], PS[:, 2048:2560], AF.Identity, bias=ctx.bq_col[:, h:h + 1]), "act")
            p.inc("scalar", lambda e, h=h, cs=cs: e.activation(
                kt[:, cs], PS[:, 2560:3072], AF.Identity, bias=ctx.bk_col[:, h:h + 1]), "act")
        p.wait("tensor", "act")

        for qc in range(2):
            qs0 = qc * 1024
            for br in range(2):
                oo = br * 1024
                dp = slice(0, 64) if br == 0 else slice(64, 128)
                expmarks = {}

                def emit_scores(kti, dp=dp, qs0=qs0):
                    so_ = 2048 + (kti % 2) * 1024
                    if kti - 2 in expmarks:
                        p.wait("tensor", "act", expmarks[kti - 2])
                    elif kti < 2:
                        p.wait("tensor", "act", expmarks.get("pre", 0))
                    for sc in range(2):
                        cs = slice(sc * 512, sc * 512 + 512)
                        p.inc("tensor", lambda e, kti=kti, cs=cs, qs0=qs0, so_=so_, dp=dp: e.matmul(
                            PS[:, so_ + cs.start:so_ + cs.stop],
                            kt[dp, kti * 128:(kti + 1) * 128],
                            qt[dp, qs0 + cs.start:qs0 + cs.stop],
                            start=True, stop=True), "pe")

                def emit_exp(kti):
                    so_ = 2048 + (kti % 2) * 1024
                    eo = (kti % 2) * 1024
                    p.wait("scalar", "pe")
                    p.inc("scalar", lambda e, so_=so_, eo=eo: e.activation(
                        expb[:, eo:eo + 1024], PS[:, so_:so_ + 1024], AF.Exp,
                        scale=0.125), "act")
                    expmarks[kti] = p.mark("act")

                def emit_pv(kti, h=h, oo=oo):
                    eo = (kti % 2) * 1024
                    p.wait("tensor", "act")
                    for sc in range(2):
                        cs = slice(sc * 512, sc * 512 + 512)
                        p.inc("tensor", lambda e, h=h, kti=kti, cs=cs, oo=oo, eo=eo: e.matmul(
                            PS[0:65, oo + cs.start:oo + cs.stop],
                            vbuf[kti][:, h * 65:(h + 1) * 65],
                            expb[:, eo + cs.start:eo + cs.stop],
                            start=(kti == 0), stop=(kti == 15)), "pe")

                expmarks["pre"] = p.mark("act")
                emit_scores(0)
                for kti in range(16):
                    emit_exp(kti)
                    if kti < 15:
                        emit_scores(kti + 1)
                    emit_pv(kti)
            p.wait("vector", "pe")
            g, rbase = h // 2, (h % 2) * 64
            for sc in range(2):
                cs = slice(sc * 512, sc * 512 + 512)
                p.wait("scalar", "pe")
                p.inc("scalar", lambda e, cs=cs: e.activation(
                    rows2[0:1, 0:512], PS[64:65, cs], AF.Ln), "act")
                p.inc("scalar", lambda e, cs=cs: e.activation(
                    rows2[0:1, 512:1024], PS[64:65, 1024 + cs.start:1024 + cs.stop],
                    AF.Ln), "act")
                p.wait("scalar", "act")
                p.inc("scalar", lambda e: e.activation(
                    rows2[0:1, 0:512], rows2[0:1, 0:512], AF.Exp, scale=-1.0), "act")
                p.inc("scalar", lambda e: e.activation(
                    rows2[0:1, 512:1024], rows2[0:1, 512:1024], AF.Exp, scale=-1.0), "act")
                p.wait("vector", "act")
                p.inc("vector", lambda e: e.tensor_scalar(
                    rows2[0:1, 512:1024], rows2[0:1, 512:1024], ctx.lam_s[0:1, 0:1], -1.0,
                    ALU.mult, ALU.mult), "dve")
                p.wait("tensor", "dve")
                p.wait("tensor", "act")
                p.inc("tensor", lambda e: e.matmul(
                    PS[0:64, 2048:2560], ctx.ones_row[0:1, 0:64],
                    rows2[0:1, 0:512], start=True, stop=True), "pe")
                p.inc("tensor", lambda e: e.matmul(
                    PS[0:64, 2560:3072], ctx.ones_row[0:1, 0:64],
                    rows2[0:1, 512:1024], start=True, stop=True), "pe")
                p.wait("scalar", "pe")
                p.inc("scalar", lambda e: e.copy(rb[0:64, 0:1024], PS[0:64, 2048:3072]), "act")
                p.wait("vector", "act")
                p.inc("vector", lambda e, cs=cs: e.tensor_tensor(
                    tmb[0:64, 0:512], PS[0:64, cs], rb[0:64, 0:512], ALU.mult), "dve")
                p.wait("vector", "dve")
                p.inc("vector", lambda e, cs=cs: e.tensor_tensor(
                    tmb[0:64, 512:1024], PS[0:64, 1024 + cs.start:1024 + cs.stop],
                    rb[0:64, 512:1024], ALU.mult), "dve")
                p.wait("vector", "dve")
                p.inc("vector", lambda e, cs=cs, g=g, rbase=rbase, qs0=qs0: e.tensor_tensor(
                    ocatT[g][rbase:rbase + 64, qs0 + cs.start:qs0 + cs.stop],
                    tmb[0:64, 0:512], tmb[0:64, 512:1024], ALU.add), "dve")
                p.wait("vector", "dve")
                p.wait("tensor", "dve")

    p.barrier()
    p.wait("gpsimd", "pe")
    p.wait("gpsimd", "dve")
    p.wait("gpsimd", "act")
    for it in range(NF):
        p.inc("gpsimd", lambda e, it=it: e.dma_start(
            wo_sb[it][:], ctx.d["L0_attn_Wo"].ap().rearrange(
                "(it2 pp) e -> it2 pp e", pp=128)[it]), "wdma", 16)
    p.wait("tensor", "wdma")
    for fo in range(NF):
        for cch in range(4):
            cs = slice(cch * 512, cch * 512 + 512)
            pso = (cch % 4) * 512
            p.wait("tensor", "act")
            for it in range(NF):
                p.inc("tensor", lambda e, fo=fo, cs=cs, it=it, pso=pso: e.matmul(
                    PS[:, pso:pso + 512], wo_sb[it][:, fo * 128:(fo + 1) * 128],
                    ocatT[it][:, cs], start=(it == 0), stop=(it == NF - 1)), "pe")
            p.wait("scalar", "pe")
            p.inc("scalar", lambda e, fo=fo, cs=cs, pso=pso: e.activation(
                ybuf[fo][:, cs], PS[:, pso:pso + 512], AF.Identity,
                bias=ctx.cols[:, fo:fo + 1]), "act")
    p.wait("tensor", "act")
    p.wait("vector", "act")
    if "y" in ctx.taps and batch == 0:
        _tap_tiles(ctx, ybuf, "y", padded=False)
    if "yb1" in ctx.taps and batch == 1:
        _tap_tiles(ctx, ybuf, "yb1", padded=False)

    emit_ln(ctx, ybuf, ctx.xT, gamma_off=8, beta_off=16, resid=True,
            src_padded=False, src_bf=True)



def emit_l1_attn(ctx, batch):
    p, m = ctx.p, ctx.m
    PS = ctx.PS
    H = 16
    B0 = ctx.BIG
    NT = 16

    xnT = [m.alloc(f"b3_xnT{f}", [128, NPAD], BF16, at=B0 + f * 4128) for f in range(NF)]
    scb = m.alloc("b3_scb", [128, 2048], F32, at=B0 + 33024)
    pb = m.alloc("b3_pb", [128, 2048], F32, at=B0 + 41216)
    pb2 = m.alloc("b3_pb2", [128, 2048], F32, at=R_LNS)
    qf = m.alloc("b3_qf", [128, 192], F32, at=B0 + 49408)
    rvb = m.alloc("b3_rvb", [128, 512], F32, at=B0 + 50176)
    ptb = m.alloc("b3_ptb", [128, 256], BF16, at=B0 + 52224)
    sums = m.alloc("b3_sums", [128, 8], F32, at=B0 + 52736)
    OOFF = B0 + 52768
    ocatT = [m.alloc(f"b3_ocat{g}", [128, 2048], BF16, at=OOFF + g * 4096) for g in range(NF)]

    W0 = ctx.WARENA
    wqkv_sb = [m.alloc(f"b3_wq{f}", [128, 192], BF16, at=W0 + f * 384) for f in range(NF)]
    brow = m.alloc("b3_brow", [1, 192], F32, at=W0 + 3072)
    bb = m.alloc("b3_bb", [128, 192], F32, at=W0 + 3872)
    qaugT = m.alloc("b3_qaug", [128, 2048], BF16, at=W0 + 4672)
    kaugT = m.alloc("b3_kaug", [128, 2048], BF16, at=W0 + 8768)
    vh = m.alloc("b3_vh", [128, 16 * 65], BF16, at=W0 + 12864)
    wout_sb = [m.alloc(f"b3_wo{t}", [128, 1024], BF16, at=W0 + t * 2048) for t in range(NF)]

    emit_ln(ctx, ctx.xT, xnT)
    if "xn3" in ctx.taps and batch == 1:
        _tap_tiles(ctx, xnT, "xn3", padded=True)
    emit_colvec(ctx, ctx.d["L1_attn_b_out"].ap().rearrange("(a e) -> a e", a=1), E, 0)

    for h in range(H):
        p.wait("gpsimd", "pe")
        p.wait("gpsimd", "dve")
        for f in range(NF):
            p.inc("gpsimd", lambda e, f=f, h=h: e.dma_start(
                wqkv_sb[f].ap().rearrange("p (c d) -> p c d", d=64),
                ctx.d["L1_attn_w_qkv"].ap().rearrange("(ft pp) c -> ft pp c", pp=128)[f]
                .rearrange("p (c e) -> p c e", e=1024)[:, :, h * 64:(h + 1) * 64]), "wdma", 16)
        p.wait("sync", "pe")
        p.inc("sync", lambda e, h=h: e.dma_start(
            brow.ap().rearrange("a (c d) -> a c d", d=64),
            ctx.d["L1_attn_b_qkv"].ap().rearrange("(a c e) -> a c e", a=1, e=1024)
            [:, :, h * 64:(h + 1) * 64]), "dma", 16)
        p.wait("tensor", "dma")
        p.wait("tensor", "act")
        p.wait("tensor", "dve")
        p.inc("tensor", lambda e: e.matmul(
            PS[:, 512:704], ctx.ones_row[0:1, :], brow[0:1, 0:192],
            start=True, stop=True), "pe")
        p.wait("scalar", "pe")
        p.inc("scalar", lambda e: e.copy(bb[:], PS[:, 512:704]), "act")
        if h == 0:
            p.run(lambda e: e.memset(sums[:, 1:2], 1.0), "gpsimd")
            p.inc("gpsimd", lambda e: e.memset(vh[:], 1.0), "gp", 1)
        p.wait("tensor", "gp")
        p.wait("tensor", "wdma")
        p.wait("tensor", "act")

        for t in range(NT):
            tc = slice(t * 128, (t + 1) * 128)
            p.wait("tensor", "dve")
            p.wait("tensor", "act")
            for f in range(NF):
                p.inc("tensor", lambda e, t=t, f=f: e.matmul(
                    PS[:, 0:192], xnT[f][:, TOKOFF + t * 128:TOKOFF + (t + 1) * 128],
                    wqkv_sb[f][:], start=(f == 0), stop=(f == NF - 1)), "pe")
            p.wait("vector", "pe")
            p.inc("vector", lambda e: e.tensor_tensor(
                qf[:, 0:192], PS[:, 0:192], bb[:, 0:192], ALU.add), "dve")
            p.wait("scalar", "dve")
            p.inc("scalar", lambda e, t=t: e.copy(
                vh[:, t * 65:t * 65 + 64], qf[:, 128:192]), "act")
            p.inc("scalar", lambda e: e.activation(
                ctx.ln_misc[:, 0:64], qf[:, 0:64], AF.Square, accum_out=sums[:, 0:1]), "act")
            p.inc("scalar", lambda e: e.activation(
                ctx.ln_misc[:, 64:128], qf[:, 64:128], AF.Square, accum_out=sums[:, 2:3]), "act")
            p.wait("tensor", "act")
            p.inc("tensor", lambda e: e.transpose(
                PS[0:64, 1024:1152], qf[:, 0:64], ctx.ident_f32[:]), "pe")
            p.inc("tensor", lambda e: e.transpose(
                PS[0:64, 1152:1280], qf[:, 64:128], ctx.ident_f32[:]), "pe")
            p.inc("tensor", lambda e: e.transpose(
                PS[0:2, 1280:1408], sums[:, 0:2], ctx.ident_f32[:]), "pe")
            p.inc("tensor", lambda e: e.transpose(
                PS[0:2, 1408:1536], sums[:, 1:3], ctx.ident_f32[:]), "pe")
            p.wait("scalar", "pe")
            p.inc("scalar", lambda e, tc=tc: e.copy(qaugT[0:64, tc], PS[0:64, 1024:1152]), "act")
            p.inc("scalar", lambda e, tc=tc: e.activation(
                kaugT[0:64, tc], PS[0:64, 1152:1280], AF.Identity, bias=0.0, scale=-2.0), "act")
            p.inc("scalar", lambda e, tc=tc: e.copy(qaugT[64:66, tc], PS[0:2, 1280:1408]), "act")
            p.inc("scalar", lambda e, tc=tc: e.activation(
                kaugT[64:66, tc], PS[0:2, 1408:1536], AF.Identity,
                bias=ctx.epsc[0:2, 0:1]), "act")

        g, rbase = h // 2, (h % 2) * 64
        pbufs = [pb, pb2]
        gck = [0]
        gtr = [0]
        chmA, chmD = {}, {}
        trmG, evkG = {}, {}
        sMax, sNorm, trEnd, oem = {}, {}, {}, {}

        def emit_C(t):
            nv = (t + 1) * 128
            nch = (nv + 511) // 512
            pbx = pbufs[t % 2]
            if t == 0:
                p.wait("tensor", "act")
                p.wait("tensor", "dve")
            first = True
            for c in range(nch):
                k = gck[0]; gck[0] += 1
                w_ = min(512, nv - c * 512)
                cs = slice(c * 512, c * 512 + w_)
                pd = (k % 2) * 1024
                pv_ = 512 + (k % 2) * 1024
                sb = slice((c % 2) * 512, (c % 2) * 512 + w_)
                sb1 = slice(1024 + (c % 2) * 512, 1024 + (c % 2) * 512 + w_)
                if k - 2 in chmA:
                    p.wait("tensor", "act", chmA[k - 2])
                    p.wait("tensor", "dve", chmD[k - 2])
                p.inc("tensor", lambda e, t=t, cs=cs, pd=pd: e.matmul(
                    PS[:, pd:pd + (cs.stop - cs.start)],
                    qaugT[0:64, t * 128:(t + 1) * 128], kaugT[0:64, cs],
                    start=True, stop=True), "pe")
                p.inc("tensor", lambda e, t=t, cs=cs, pv_=pv_: e.matmul(
                    PS[:, pv_:pv_ + (cs.stop - cs.start)],
                    qaugT[0:66, t * 128:(t + 1) * 128], kaugT[0:66, cs],
                    start=True, stop=True), "pe")
                p.wait("scalar", "pe")
                if first:
                    if t - 1 in sMax:
                        p.wait("scalar", "dve", sMax[t - 1])
                        p.wait("scalar", "act")
                    if t - 2 in trEnd:
                        p.wait("scalar", "pe", trEnd[t - 2])
                    first = False
                p.inc("scalar", lambda e, cs=cs, pd=pd: e.activation(
                    scb[:, cs], PS[:, pd:pd + (cs.stop - cs.start)], AF.Square,
                    scale=0.5), "act")
                if k - 2 in chmD:
                    p.wait("scalar", "dve", chmD[k - 2])
                p.inc("scalar", lambda e, sb=sb, pv_=pv_, pbx=pbx: e.activation(
                    pbx[:, sb], PS[:, pv_:pv_ + (sb.stop - sb.start)], AF.Square), "act")
                p.wait("scalar", "act")
                p.inc("scalar", lambda e, sb=sb, sb1=sb1, pbx=pbx: e.activation(
                    pbx[:, sb1], pbx[:, sb], AF.Ln), "act")
                p.wait("scalar", "act")
                p.inc("scalar", lambda e, sb=sb, sb1=sb1, pbx=pbx: e.activation(
                    pbx[:, sb], pbx[:, sb1], AF.Exp, scale=-1.0), "act")
                chmA[k] = p.mark("act")
                p.wait("vector", "act")
                p.inc("vector", lambda e, sb=sb, pv_=pv_, pbx=pbx: e.tensor_tensor(
                    pbx[:, sb], pbx[:, sb], PS[:, pv_:pv_ + (sb.stop - sb.start)],
                    ALU.mult), "dve")
                p.wait("vector", "dve")
                p.inc("vector", lambda e, cs=cs, sb=sb, pbx=pbx: e.tensor_tensor(
                    scb[:, cs], scb[:, cs], pbx[:, sb], ALU.mult), "dve")
                chmD[k] = p.mark("dve")

        def emit_S(t):
            nv = (t + 1) * 128
            pbx = pbufs[t % 2]
            p.wait("vector", "dve")
            p.inc("vector", lambda e, t=t: e.tensor_tensor(
                scb[:, t * 128:(t + 1) * 128], scb[:, t * 128:(t + 1) * 128],
                ctx.maskc[:, 0:128], ALU.add), "dve")
            p.wait("vector", "dve")
            p.inc("vector", lambda e, nv=nv: e.reduce_max(
                sums[:, 3:4], scb[:, 0:nv], axis=mybir.AxisListType.X, negate=True), "dve")
            sMax[t] = p.mark("dve")
            p.wait("scalar", "dve")
            p.inc("scalar", lambda e, nv=nv, pbx=pbx: e.activation(
                pbx[:, 0:nv], scb[:, 0:nv], AF.Exp, bias=sums[:, 3:4],
                accum_out=sums[:, 4:5]), "act")
            p.wait("vector", "act")
            p.inc("vector", lambda e: e.reciprocal(sums[:, 5:6], sums[:, 4:5]), "dve")
            p.wait("vector", "dve")
            p.inc("vector", lambda e, nv=nv, pbx=pbx: e.tensor_scalar(
                pbx[:, 0:nv], pbx[:, 0:nv], sums[:, 5:6], None, ALU.mult), "dve")
            sNorm[t] = p.mark("dve")

        def emit_P(t):
            pbx = pbufs[t % 2]

            def emit_tr(tk):
                gx = gtr[0] + tk
                po_ = 2048 + (gx % 2) * 1024
                if gx - 2 in evkG:
                    p.wait("tensor", "act", evkG[gx - 2])
                if tk == 0:
                    p.wait("tensor", "dve", sNorm[t])
                p.inc("tensor", lambda e, tk=tk, po_=po_, pbx=pbx: e.transpose(
                    PS[0:128, po_:po_ + 128], pbx[:, tk * 128:(tk + 1) * 128],
                    ctx.ident_f32[:]), "pe")
                trmG[gx] = p.mark("pe")

            def emit_ev(tk):
                gx = gtr[0] + tk
                po_ = 2048 + (gx % 2) * 1024
                eo_ = (gx % 2) * 128
                p.wait("scalar", "pe", trmG[gx])
                p.inc("scalar", lambda e, po_=po_, eo_=eo_: e.copy(
                    ptb[:, eo_:eo_ + 128], PS[0:128, po_:po_ + 128]), "act")
                evkG[gx] = p.mark("act")

            def emit_pv2(tk):
                gx = gtr[0] + tk
                eo_ = (gx % 2) * 128
                p.wait("tensor", "act", evkG[gx])
                if tk == 0 and t - 1 in oem:
                    p.wait("tensor", "act", oem[t - 1])
                p.inc("tensor", lambda e, tk=tk, t=t, eo_=eo_: e.matmul(
                    PS[0:65, 3584:3712], vh[:, tk * 65:(tk + 1) * 65],
                    ptb[:, eo_:eo_ + 128], start=(tk == 0), stop=(tk == t)), "pe")

            emit_tr(0)
            emit_ev(0)
            for tk in range(t + 1):
                if tk + 1 <= t:
                    emit_tr(tk + 1)
                emit_pv2(tk)
                if tk + 1 <= t:
                    emit_ev(tk + 1)
            gtr[0] += t + 1
            p.wait("scalar", "pe")
            p.inc("scalar", lambda e, g=g, rbase=rbase, t=t: e.copy(
                ocatT[g][rbase:rbase + 64, t * 128:(t + 1) * 128],
                PS[0:64, 3584:3712]), "act")
            oem[t] = p.mark("act")
            trEnd[t] = p.mark("pe")

        emit_C(0)
        emit_S(0)
        for t in range(NT):
            if t + 1 < NT:
                emit_C(t + 1)
            emit_P(t)
            if t + 1 < NT:
                emit_S(t + 1)
        p.wait("tensor", "act")

    if "oc3" in ctx.taps and batch == 1:
        _tap_tiles(ctx, ocatT, "oc3", padded=False)
    p.wait("gpsimd", "pe")
    p.wait("gpsimd", "act")
    p.wait("gpsimd", "dve")
    for it in range(NF):
        p.inc("gpsimd", lambda e, it=it: e.dma_start(
            wout_sb[it][:], ctx.d["L1_attn_w_out"].ap().rearrange(
                "(it2 pp) e -> it2 pp e", pp=128)[it]), "wdma", 16)
    p.wait("tensor", "wdma")
    p.wait("tensor", "dve")
    yev, ydv = {}, {}
    gx = 0
    for fo in range(NF):
        for cch in range(4):
            cs = slice(cch * 512, cch * 512 + 512)
            pso = (gx % 4) * 512
            st_ = slice((gx % 4) * 512, (gx % 4) * 512 + 512)
            if gx - 3 in yev:
                p.wait("tensor", "act", yev[gx - 3])
            for it in range(NF):
                p.inc("tensor", lambda e, fo=fo, cs=cs, it=it, pso=pso: e.matmul(
                    PS[:, pso:pso + 512], wout_sb[it][:, fo * 128:(fo + 1) * 128],
                    ocatT[it][:, cs], start=(it == 0), stop=(it == NF - 1)), "pe")
            p.wait("scalar", "pe")
            if gx - 4 in ydv:
                p.wait("scalar", "dve", ydv[gx - 4])
            p.inc("scalar", lambda e, fo=fo, pso=pso, st_=st_: e.activation(
                ctx.ln_sq[:, st_], PS[:, pso:pso + 512], AF.Identity,
                bias=ctx.cols[:, fo:fo + 1]), "act")
            yev[gx] = p.mark("act")
            p.wait("vector", "act")
            p.inc("vector", lambda e, fo=fo, cs=cs, st_=st_: e.tensor_tensor(
                ctx.xT[fo][:, TOKOFF + cs.start:TOKOFF + cs.stop],
                ctx.xT[fo][:, TOKOFF + cs.start:TOKOFF + cs.stop],
                ctx.ln_sq[:, st_], ALU.add), "dve")
            ydv[gx] = p.mark("dve")
            gx += 1
    p.barrier(("tensor", "scalar", "vector", "sync", "gpsimd"))



def emit_l1_ffn(ctx, batch):
    p, m = ctx.p, ctx.m
    PS = ctx.PS
    B0 = ctx.BIG
    TCW = 256
    NTC = SEQ // TCW

    xnb = [m.alloc(f"b4_xn{f}", [128, NPAD], F32, at=B0 + f * XT_STRIDE) for f in range(NF)]
    xnn = [m.alloc(f"b4_xnn{f}", [128, NPAD], BF16, at=B0 + f * XT_STRIDE) for f in range(NF)]
    frag = [B0 + f * XT_STRIDE + 4128 for f in range(NF)]
    hbuf = [m.alloc(f"b4_hb{c}", [128, TCW], BF16, at=frag[c // 8] + (c % 8) * 512)
            for c in range(32)]
    acc = [m.alloc(f"b4_acc{fo}", [128, TCW], F32, at=frag[4 + fo // 4] + (fo % 4) * 1024)
           for fo in range(NF)]
    macb = [m.alloc(f"b4_mac{fo}", [128, TCW], BF16, at=frag[6] + fo * 512) for fo in range(NF)]
    abelT = [m.alloc(f"b4_ab{fo}", [128, TCW], BF16, at=frag[7] + fo * 512) for fo in range(NF)]
    TAIL = B0 + 8 * XT_STRIDE
    wsb = m.alloc("b4_wsb", [6, 2048], F32, at=TAIL)
    nonabT = [m.alloc(f"b4_nb{fo}", [128, TCW], BF16, at=TAIL + 8192 + fo * 512)
              for fo in range(NF)]
    gbuf = m.alloc("b4_gb", [128, TCW], F32, at=TAIL + 12288)
    webc = m.alloc("b4_web", [128, TCW], F32, at=TAIL + 13312)
    t1 = m.alloc("b4_t1", [128, TCW], F32, at=TAIL + 14336)
    t2 = m.alloc("b4_t2", [128, TCW], F32, at=TAIL + 15360)
    glog = m.alloc("b4_glog", [128, 16], F32, at=TAIL + 16384)
    grow = m.alloc("b4_grow", [1, 256], F32, at=TAIL + 16448)
    cols2 = m.alloc("b4_cols2", [128, 144], F32, at=TAIL + 17472)
    gcb = m.alloc("b4_gcb", [128, 16], BF16, at=TAIL + 18048)
    W0 = ctx.WARENA
    sel6 = m.alloc("b4_sel6", [6, 768], F32, at=W0 + 8320)
    gbb = m.alloc("b4_gbb", [128, 6], F32, at=W0 + 11424)
    grow6 = m.alloc("b4_grow6", [1, 8], F32, at=W0 + 11456)
    w1c = [m.alloc(f"b4_w1c{i}", [128, 1024], BF16, at=W0 + i * 2048) for i in range(2)]
    w2c = [m.alloc(f"b4_w2c{i}", [128, 1024], BF16, at=W0 + 4096 + i * 2048) for i in range(2)]
    gatewc = m.alloc("b4_gw", [128, 48], BF16, at=W0 + 8192)

    p.barrier(("gpsimd",))
    for f in range(NF - 1):
        p.run(lambda e, f=f: e.memset(xnb[f][:, 0:TOKOFF], 0.0), "gpsimd")
    p.inc("gpsimd", lambda e: e.memset(xnb[NF - 1][:, 0:TOKOFF], 0.0), "gp", 1)
    emit_ln(ctx, ctx.xT, xnb)
    for f in range(NF):
        p.wait("vector", "act")
        p.inc("vector", lambda e, f=f: e.tensor_tensor(
            ctx.xT[f][:, tokc(0, SEQ)], ctx.xT[f][:, tokc(0, SEQ)],
            xnb[f][:, tokc(0, SEQ)], ALU.add), "dve")
    p.barrier(("tensor", "scalar", "vector"))

    emit_colvec(ctx, ctx.d["L1_ffn_ln_g"].ap().rearrange("(a e) -> a e", a=1), E, 0)
    emit_colvec(ctx, ctx.d["L1_ffn_ln_b"].ap().rearrange("(a e) -> a e", a=1), E, 8)
    emit_colvec(ctx, ctx.d["L1_ffn_ab1"].ap()[0:2048].rearrange("(a e) -> a e", a=1), 2048, 16)
    emit_colvec(ctx, ctx.d["L1_ffn_ab1"].ap()[2048:4096].rearrange("(a e) -> a e", a=1), 2048, 32)
    emit_colvec(ctx, ctx.d["L1_ffn_ab2"].ap().rearrange("(a e) -> a e", a=1), E, 48)
    emit_colvec(ctx, ctx.d["L1_ffn_gc_w"].ap().rearrange("e a -> a e"), 2048, 56)
    for ex in range(6):
        emit_colvec(ctx, ctx.d["L1_ffn_eb1"].ap()[ex].rearrange("(a e) -> a e", a=1),
                    2048, ex * 16, dst=cols2)
        emit_colvec(ctx, ctx.d["L1_ffn_eb2"].ap()[ex].rearrange("(a e) -> a e", a=1),
                    1024, 96 + ex * 8, dst=cols2)
    p.wait("vector", "act")
    p.inc("vector", lambda e: e.tensor_copy(gcb[:, 0:16], ctx.cols[:, 56:72]), "dve")
    p.inc("sync", lambda e: e.dma_start(sel6[:], ctx.cd["sel6"].ap()), "dma", 16)
    p.inc("sync", lambda e: e.dma_start(
        grow6[0:1, 0:6], ctx.d["L1_ffn_gate_b"].ap().rearrange("(a e) -> a e", a=1)), "dma", 16)
    p.inc("sync", lambda e: e.dma_start(
        ctx.lam_s[0:1, 0:1], ctx.d["L1_ffn_gc_b"].ap().rearrange("(a e) -> a e", a=1)), "dma", 16)
    p.wait("tensor", "dma")
    emit_bcast_row(ctx, grow6, gbb[:, 0:6], 6)
    p.wait("gpsimd", "pe")
    for f in range(NF):
        p.inc("gpsimd", lambda e, f=f: e.dma_start(
            gatewc[:, f * 6:(f + 1) * 6], ctx.d["L1_ffn_gate_w"].ap()
            [f * 128:(f + 1) * 128, :]), "wdma", 16)

    emit_ln(ctx, xnb, xnn, gamma_off=0, beta_off=8)

    p.wait("tensor", "wdma")
    for t in range(16):
        p.wait("tensor", "act")
        p.wait("tensor", "dve")
        for f in range(NF):
            p.inc("tensor", lambda e, t=t, f=f: e.matmul(
                PS[0:128, 0:6], xnn[f][:, TOKOFF + t * 128:TOKOFF + (t + 1) * 128],
                gatewc[:, f * 6:(f + 1) * 6], start=(f == 0), stop=(f == NF - 1)), "pe")
        p.wait("vector", "pe")
        p.inc("vector", lambda e: e.tensor_tensor(
            glog[:, 0:6], PS[0:128, 0:6], gbb[:, 0:6], ALU.add), "dve")
        p.wait("vector", "dve")
        p.inc("vector", lambda e: e.reduce_max(
            glog[:, 8:9], glog[:, 0:6], axis=mybir.AxisListType.X, negate=True), "dve")
        p.wait("vector", "dve")
        p.inc("vector", lambda e: e.tensor_scalar(
            glog[:, 9:10], glog[:, 8:9], -1.0, None, ALU.mult), "dve")
        p.wait("vector", "dve")
        p.inc("vector", lambda e: e.tensor_scalar(
            glog[:, 10:16][:, 0:6], glog[:, 0:6], glog[:, 9:10], None, ALU.is_ge), "dve")
        p.wait("vector", "dve")
        p.inc("vector", lambda e: e.scalar_tensor_tensor(
            glog[:, 10:16][:, 0:6], glog[:, 10:16][:, 0:6], -1e30, glog[:, 0:6],
            ALU.mult, ALU.add), "dve")
        p.wait("vector", "dve")
        p.inc("vector", lambda e: e.reduce_max(
            glog[:, 7:8], glog[:, 10:16][:, 0:6], axis=mybir.AxisListType.X), "dve")
        p.wait("scalar", "dve")
        p.inc("scalar", lambda e: e.activation(
            glog[:, 10:16], glog[:, 0:6], AF.Exp, bias=glog[:, 8:9]), "act")
        p.wait("vector", "act")
        p.inc("vector", lambda e: e.tensor_tensor(
            glog[:, 6:7], glog[:, 7:8], glog[:, 8:9], ALU.add), "dve")
        p.wait("scalar", "dve")
        p.inc("scalar", lambda e: e.activation(glog[:, 6:7], glog[:, 6:7], AF.Exp), "act")
        p.wait("vector", "act")
        p.inc("vector", lambda e: e.tensor_scalar(
            glog[:, 6:7], glog[:, 6:7], 1.0, None, ALU.add), "dve")
        p.wait("vector", "dve")
        p.inc("vector", lambda e: e.reciprocal(glog[:, 6:7], glog[:, 6:7]), "dve")
        p.wait("vector", "dve")
        p.inc("vector", lambda e: e.tensor_scalar(
            glog[:, 0:6], glog[:, 0:6], glog[:, 7:8], None, ALU.is_ge), "dve")
        p.wait("vector", "dve")
        p.inc("vector", lambda e: e.tensor_tensor(
            glog[:, 0:6], glog[:, 0:6], glog[:, 10:16], ALU.mult), "dve")
        p.wait("vector", "dve")
        p.inc("vector", lambda e: e.tensor_scalar(
            glog[:, 0:6], glog[:, 0:6], glog[:, 6:7], None, ALU.mult), "dve")
        p.wait("tensor", "dve")
        p.inc("tensor", lambda e: e.transpose(
            PS[0:6, 512:640], glog[:, 0:6], ctx.ident_f32[:]), "pe")
        p.wait("scalar", "pe")
        p.inc("scalar", lambda e, t=t: e.copy(
            wsb[0:6, t * 128:(t + 1) * 128], PS[0:6, 512:640]), "act")
        p.wait("tensor", "act")

    for tc in range(NTC):
        TC = slice(tc * TCW, (tc + 1) * TCW)
        xcol = slice(TOKOFF + TC.start, TOKOFF + TC.stop)
        actm = {}

        def ld_w1(hc):
            return lambda e, hc=hc: e.dma_start(
                w1c[hc % 2].ap().rearrange("p (ft c) -> p ft c", c=128),
                ctx.d["aw1s"].ap().rearrange("(ft pp) hh -> pp ft hh", pp=128)
                [:, :, hc * 128:(hc + 1) * 128])

        def use_w1(hc, xcol=xcol):
            wb = w1c[hc % 2]
            pso = 2048 + (hc % 2) * 512
            if hc - 2 in actm:
                p.wait("tensor", "act", actm[hc - 2])
            for f in range(NF):
                p.inc("tensor", lambda e, hc=hc, f=f, wb=wb, pso=pso, xcol=xcol: e.matmul(
                    PS[:, pso:pso + TCW], wb[:, f * 128:(f + 1) * 128],
                    xnn[f][:, xcol], start=(f == 0), stop=(f == NF - 1)), "pe")
            pm = p.mark("pe")
            p.wait("scalar", "pe", pm)
            p.inc("scalar", lambda e, hc=hc, pso=pso: e.activation(
                hbuf[hc][:], PS[:, pso:pso + TCW], AF.Gelu,
                bias=ctx.cols[:, 16 + hc:17 + hc]), "act")
            actm[hc] = p.mark("act")

        stream_weights(p, 32, ld_w1, use_w1, "w4a")
        p.wait("tensor", "act")

        def ld_w2(hc):
            return lambda e, hc=hc: e.dma_start(
                w2c[hc % 2][:], ctx.d["aw2s"].ap()[hc * 128:(hc + 1) * 128, :])

        def use_w2(hc):
            wb = w2c[hc % 2]
            for fo in range(NF):
                p.inc("tensor", lambda e, hc=hc, fo=fo, wb=wb: e.matmul(
                    PS[:, fo * 512:fo * 512 + TCW], wb[:, fo * 128:(fo + 1) * 128],
                    hbuf[hc][:], start=(hc == 0), stop=(hc == 31)), "pe")

        stream_weights(p, 32, ld_w2, use_w2, "w4b")
        p.wait("scalar", "pe")
        for fo in range(NF):
            p.inc("scalar", lambda e, fo=fo: e.activation(
                abelT[fo][:], PS[:, fo * 512:fo * 512 + TCW], AF.Identity,
                bias=ctx.cols[:, 48 + fo:49 + fo]), "act")
        p.wait("tensor", "act")
        p.wait("vector", "act")
        for ex in range(6):
            p.wait("tensor", "dve")
            p.inc("tensor", lambda e, ex=ex, TC=TC: e.matmul(
                PS[:, 0:TCW], sel6[:, ex * 128:(ex + 1) * 128], wsb[0:6, TC],
                start=True, stop=True), "pe")
            p.wait("scalar", "pe")
            p.inc("scalar", lambda e: e.copy(webc[:], PS[:, 0:TCW]), "act")
            p.wait("tensor", "act")
            actm2 = {}

            def ld_e1(hc, ex=ex):
                return lambda e, ex=ex, hc=hc: e.dma_start(
                    w1c[hc % 2].ap().rearrange("p (ft c) -> p ft c", c=128),
                    ctx.d["ew1s"].ap()[ex].rearrange("(ft pp) hh -> pp ft hh", pp=128)
                    [:, :, hc * 128:(hc + 1) * 128])

            def use_e1(hc, ex=ex, xcol=xcol):
                wb = w1c[hc % 2]
                pso = 2048 + (hc % 2) * 512
                if hc - 2 in actm2:
                    p.wait("tensor", "act", actm2[hc - 2])
                for f in range(NF):
                    p.inc("tensor", lambda e, hc=hc, f=f, wb=wb, pso=pso, xcol=xcol: e.matmul(
                        PS[:, pso:pso + TCW], wb[:, f * 128:(f + 1) * 128],
                        xnn[f][:, xcol], start=(f == 0), stop=(f == NF - 1)), "pe")
                pm = p.mark("pe")
                p.wait("scalar", "pe", pm)
                p.inc("scalar", lambda e, ex=ex, hc=hc, pso=pso: e.activation(
                    hbuf[hc][:], PS[:, pso:pso + TCW], AF.Gelu,
                    bias=cols2[:, ex * 16 + hc:ex * 16 + hc + 1]), "act")
                actm2[hc] = p.mark("act")

            stream_weights(p, 16, ld_e1, use_e1, "w4a")
            p.wait("tensor", "act")

            def ld_e2(hc, ex=ex):
                return lambda e, ex=ex, hc=hc: e.dma_start(
                    w2c[hc % 2][:], ctx.d["ew2s"].ap()[ex][hc * 128:(hc + 1) * 128, :])

            def use_e2(hc):
                wb = w2c[hc % 2]
                for fo in range(NF):
                    p.inc("tensor", lambda e, hc=hc, fo=fo, wb=wb: e.matmul(
                        PS[:, fo * 512:fo * 512 + TCW], wb[:, fo * 128:(fo + 1) * 128],
                        hbuf[hc][:], start=(hc == 0), stop=(hc == 15)), "pe")

            stream_weights(p, 16, ld_e2, use_e2, "w4b")
            p.wait("scalar", "pe")
            p.wait("vector", "pe")
            for fo in range(NF):
                p.inc("scalar", lambda e, ex=ex, fo=fo: e.activation(
                    t1[:], PS[:, fo * 512:fo * 512 + TCW], AF.Identity,
                    bias=cols2[:, 96 + ex * 8 + fo:96 + ex * 8 + fo + 1]), "act")
                p.wait("vector", "act")
                p.inc("vector", lambda e: e.tensor_tensor(t2[:], t1[:], webc[:], ALU.mult), "dve")
                p.wait("vector", "dve")
                if ex == 0:
                    p.inc("vector", lambda e, fo=fo: e.tensor_copy(acc[fo][:], t2[:]), "dve")
                else:
                    p.inc("vector", lambda e, fo=fo: e.tensor_tensor(
                        acc[fo][:], acc[fo][:], t2[:], ALU.add), "dve")
                p.wait("scalar", "dve")
        p.wait("scalar", "dve")
        for fo in range(NF):
            p.inc("scalar", lambda e, fo=fo: e.copy(macb[fo][:], acc[fo][:]), "act")
        p.wait("tensor", "act")
        def ld_r(fi):
            return lambda e, fi=fi: e.dma_start(
                w1c[fi % 2][:], ctx.d["rs"].ap()[fi * 128:(fi + 1) * 128, :])

        def use_r(fi):
            wb = w1c[fi % 2]
            for fo in range(NF):
                p.inc("tensor", lambda e, fi=fi, fo=fo, wb=wb: e.matmul(
                    PS[:, fo * 512:fo * 512 + TCW], wb[:, fo * 128:(fo + 1) * 128],
                    macb[fi][:], start=(fi == 0), stop=(fi == NF - 1)), "pe")

        stream_weights(p, NF, ld_r, use_r, "w4a")
        p.wait("scalar", "pe")
        for fo in range(NF):
            p.inc("scalar", lambda e, fo=fo: e.copy(
                nonabT[fo][:], PS[:, fo * 512:fo * 512 + TCW]), "act")
        p.wait("tensor", "act")
        if tc == 0 and batch == 0:
            for nm, tiles in (("ab4", abelT), ("nb4", nonabT)):
                if nm in ctx.taps:
                    p.barrier(("sync", "vector"))
                    for f in range(NF):
                        p.wait("vector", "st")
                        p.inc("vector", lambda e, f=f, tiles=tiles: e.tensor_copy(
                            t1[:], tiles[f][:]), "dve")
                        p.wait("sync", "dve")
                        p.inc("sync", lambda e, f=f, nm=nm: e.dma_start(
                            ctx.tap_drams[nm].ap()[f * 128:(f + 1) * 128, 0:TCW], t1[:]),
                            "st", 16)
                        p.wait("sync", "st")
                    p.wait("gpsimd", "st")
                    p.wait("scalar", "st")
            if "mo4" in ctx.taps:
                p.barrier(("sync",))
                for f in range(NF):
                    p.inc("sync", lambda e, f=f: e.dma_start(
                        ctx.tap_drams["mo4"].ap()[f * 128:(f + 1) * 128, 0:TCW],
                        acc[f][:]), "st", 16)
                p.wait("sync", "st")
        p.wait("tensor", "dve")
        for i, tiles in ((0, abelT), (8, nonabT)):
            for f in range(NF):
                p.inc("tensor", lambda e, i=i, f=f, tiles=tiles: e.matmul(
                    PS[0:1, 3584:3584 + TCW], gcb[:, i + f:i + f + 1], tiles[f][:],
                    start=(i == 0 and f == 0), stop=(i == 8 and f == NF - 1)), "pe")
        p.wait("scalar", "pe")
        p.inc("scalar", lambda e: e.activation(
            grow[0:1, 0:TCW], PS[0:1, 3584:3584 + TCW], AF.Sigmoid,
            bias=ctx.lam_s[0:1, 0:1]), "act")
        p.wait("tensor", "act")
        p.inc("tensor", lambda e: e.matmul(
            PS[:, 3072:3072 + TCW], ctx.ones_row[0:1, :], grow[0:1, 0:TCW],
            start=True, stop=True), "pe")
        p.wait("scalar", "pe")
        p.inc("scalar", lambda e: e.copy(gbuf[:], PS[:, 3072:3072 + TCW]), "act")
        p.wait("vector", "act")
        for f in range(NF):
            p.inc("vector", lambda e, f=f: e.tensor_tensor(
                t1[:], nonabT[f][:], abelT[f][:], ALU.subtract), "dve")
            p.wait("vector", "dve")
            p.inc("vector", lambda e: e.tensor_tensor(t1[:], t1[:], gbuf[:], ALU.mult), "dve")
            p.wait("vector", "dve")
            p.inc("vector", lambda e, f=f: e.tensor_tensor(
                t2[:], abelT[f][:], t1[:], ALU.add), "dve")
            p.wait("vector", "dve")
            p.inc("vector", lambda e, f=f, xcol=xcol: e.tensor_tensor(
                ctx.xT[f][:, xcol], ctx.xT[f][:, xcol], t2[:], ALU.add), "dve")
            p.wait("vector", "dve")
        p.wait("tensor", "dve")
        p.wait("scalar", "dve")
    p.barrier()


def _tap_tiles(ctx, tiles, name, padded=True):
    p = ctx.p
    dram = ctx.tap_drams[name]
    so = TOKOFF if padded else 0
    p.barrier(("sync", "vector"))
    bf_src = tiles[0].dtype == BF16
    for f in range(NF):
        if bf_src:
            p.wait("vector", "st")
            p.inc("vector", lambda e, f=f: e.tensor_copy(
                ctx.ln_sq[:, 0:SEQ], tiles[f][:, so:so + SEQ]), "dve")
            p.wait("sync", "dve")
            p.inc("sync", lambda e, f=f: e.dma_start(
                dram.ap()[f * 128:(f + 1) * 128, :], ctx.ln_sq[:, 0:SEQ]), "st", 16)
            p.wait("sync", "st")
        else:
            p.inc("sync", lambda e, f=f: e.dma_start(
                dram.ap()[f * 128:(f + 1) * 128, :], tiles[f][:, so:so + SEQ]), "st", 16)
    p.wait("sync", "st")
    p.wait("gpsimd", "st")
    p.wait("scalar", "st")


def emit_load_x(ctx, batch):
    p = ctx.p
    p.barrier()
    for f in range(NF):
        p.inc("sync", lambda e, f=f, b=batch: e.dma_start(
            ctx.xT[f][:, tokc(0, SEQ)], ctx.d["xin"].ap()[b, f * 128:(f + 1) * 128, :]),
            "dma", 16)
    p.wait("tensor", "dma")
    p.wait("vector", "dma")
    p.wait("scalar", "dma")


def emit_store_x(ctx, batch):
    p = ctx.p
    p.wait("sync", "dve")
    p.wait("sync", "act")
    p.wait("sync", "pe")
    for f in range(NF):
        p.inc("sync", lambda e, f=f, b=batch: e.dma_start(
            ctx.d["xout"].ap()[b, f * 128:(f + 1) * 128, :], ctx.xT[f][:, tokc(0, SEQ)]),
            "st", 16)
    p.wait("sync", "st")



def _host_consts():
    import ml_dtypes as _md
    c = {}
    c["ident_f32"] = np.eye(128, dtype=np.float32)
    c["ident_bf16"] = np.eye(128, dtype=_md.bfloat16)
    DH, WIN = 64, 512
    half = np.arange(0, DH, 2, dtype=np.float64)
    inv_freq = 1.0 / 10000.0 ** (half / DH)
    t = np.arange(SEQ, dtype=np.float64)
    freqs = t[:, None] * inv_freq[None, :]
    freqs = np.concatenate([freqs, freqs], axis=-1)
    base = (half + 0.4 * DH) / (1.4 * DH)
    power = (t - SEQ // 2) / float(WIN // 2)
    scale = base[None, :] ** power[:, None]
    scale = np.concatenate([scale, scale], axis=-1)
    cos, sin = np.cos(freqs), np.sin(freqs)
    sgn = np.concatenate([-np.ones(32), np.ones(32)])[None, :]
    c["rotA_q"] = (cos * scale).astype(np.float32)
    c["rotB_q"] = (sin * scale * sgn).astype(np.float32)
    c["rotA_k"] = (cos / scale).astype(np.float32)
    c["rotB_k"] = (sin / scale * sgn).astype(np.float32)
    mk = np.zeros((4, 128, 512), dtype=np.float32)
    mkp = np.zeros((4, 128, 512), dtype=np.float32)
    for r in range(4):
        ks = r * 128 + np.arange(128)[:, None]
        qs = np.arange(512)[None, :]
        mk[r] = np.where(ks <= qs, 0.0, -1e30)
        mkp[r] = np.where(ks >= qs, 0.0, -1e30)
    c["locmask"] = mk
    c["locmaskp"] = mkp
    qrow = np.arange(128)[:, None]
    kcol = np.arange(128)[None, :]
    c["maskdiag"] = np.where(kcol <= qrow, 0.0, -1e9).astype(np.float32)
    c["epsc"] = np.array([[0.0], [1e-6]], dtype=np.float32)
    sel = np.zeros((6, 768), dtype=np.float32)
    for e in range(6):
        sel[e, e * 128:(e + 1) * 128] = 1.0
    c["sel6"] = sel
    return c


WEIGHT_SHAPES = {
    "L0_attn_Wq": (16, 1024, 128), "L0_attn_Wk": (16, 1024, 128),
    "L0_attn_Wv": (16, 1024, 64), "L0_attn_Wo": (1024, 1024),
    "L0_attn_bo": (1024,), "L0_attn_gamma": (1024,), "L0_attn_beta": (1024,),
    "L0_attn_bv": (16, 64),
    "L0_ffn_wqkv": (2, 1024, 1536), "L0_ffn_q_scale": (2, 64), "L0_ffn_k_scale": (2, 64),
    "L0_ffn_wo": (2, 512, 1024), "L0_ffn_ln_g": (2, 1024), "L0_ffn_ln_b": (2, 1024),
    "L0_ffn_w1": (2, 1024, 5460), "L0_ffn_b1": (2, 5460),
    "L0_ffn_w2": (2, 2730, 1024), "L0_ffn_b2": (2, 1024),
    "L1_attn_w_qkv": (1024, 3072), "L1_attn_b_qkv": (3072,),
    "L1_attn_w_out": (1024, 1024), "L1_attn_b_out": (1024,),
    "L1_ffn_ln_g": (1024,), "L1_ffn_ln_b": (1024,),
    "L1_ffn_aw1": (1024, 4096), "L1_ffn_ab1": (4096,),
    "L1_ffn_aw2": (4096, 1024), "L1_ffn_ab2": (1024,),
    "L1_ffn_gate_w": (1024, 6), "L1_ffn_gate_b": (6,),
    "L1_ffn_ew1": (6, 1024, 2048), "L1_ffn_eb1": (6, 2048),
    "L1_ffn_ew2": (6, 2048, 1024), "L1_ffn_eb2": (6, 1024),
    "L1_ffn_gens": (4, 1024, 1024), "L1_ffn_gc_w": (2048, 1), "L1_ffn_gc_b": (1,),
    "bq_t": (128, 16), "bk_t": (128, 16), "lam11": (1, 1),
}

BLOCK_EMITTERS = {}




def _emit_expm(ctx):
    p, m = ctx.p, ctx.m
    PS = ctx.PS
    bases = [R_XT, R_XT + 32768, ctx.BIG, ctx.BIG + 32768]
    X = [[m.alloc(f"xm{i}_{f}", [128, 1024], F32, at=bases[i] + f * 4096)
          for f in range(NF)] for i in range(4)]
    stb = m.alloc("xm_st", [128, 1024], BF16, at=ctx.BIG + 65536)

    def mm_product(lhsT_tiles, rhs_tiles, out_tiles, post=None):
        evm = {}
        gx = 0
        for fo in range(NF):
            for ch in range(2):
                cs = slice(ch * 512, ch * 512 + 512)
                pso = (gx % 4) * 512
                if gx - 3 in evm:
                    p.wait("tensor", "act", evm[gx - 3])
                for fi in range(NF):
                    p.inc("tensor", lambda e, fo=fo, cs=cs, fi=fi, pso=pso: e.matmul(
                        PS[:, pso:pso + 512],
                        lhsT_tiles[fi][:, fo * 128:(fo + 1) * 128],
                        rhs_tiles[fi][:, cs], start=(fi == 0), stop=(fi == NF - 1)), "pe")
                if post is None:
                    p.wait("scalar", "pe")
                    p.inc("scalar", lambda e, fo=fo, cs=cs, pso=pso: e.copy(
                        out_tiles[fo][:, cs], PS[:, pso:pso + 512]), "act")
                else:
                    post(fo, cs, pso)
                evm[gx] = p.mark("act")
                gx += 1

    def transpose_to(src_tiles, dst_tiles):
        evm = {}
        gx = 0
        for a in range(NF):
            for b in range(NF):
                po = 3072 + (gx % 2) * 512
                if gx - 2 in evm:
                    p.wait("tensor", "act", evm[gx - 2])
                p.inc("tensor", lambda e, a=a, b=b, po=po: e.transpose(
                    PS[:, po:po + 128], src_tiles[a][:, b * 128:(b + 1) * 128],
                    ctx.ident_f32[:]), "pe")
                p.wait("scalar", "pe")
                p.inc("scalar", lambda e, a=a, b=b, po=po: e.copy(
                    dst_tiles[b][:, a * 128:(a + 1) * 128], PS[:, po:po + 128]), "act")
                evm[gx] = p.mark("act")
                gx += 1

    def load_ident(tiles):
        for f in range(NF - 1):
            p.run(lambda e, f=f: e.memset(tiles[f][:], 0.0), "gpsimd")
        p.inc("gpsimd", lambda e, tiles=tiles: e.memset(tiles[NF - 1][:], 0.0), "gp", 1)
        p.wait("sync", "gp")
        for f in range(NF):
            p.inc("sync", lambda e, f=f, tiles=tiles: e.dma_start(
                tiles[f][:, f * 128:(f + 1) * 128], ctx.cd["identf"].ap()), "dma", 16)

    CS = ("pe", "dve", "act", "gp", "dma")
    racc = None
    free = [0, 1, 2, 3]
    for gi in range(4):
        bA = free[0]; bH = free[1]; bT = free[2]
        p.barrier(("sync", "gpsimd", "tensor", "scalar", "vector"))
        for f in range(NF):
            p.inc("sync", lambda e, f=f, gi=gi, bA=bA: e.dma_start(
                X[bA][f][:], ctx.d["L1_ffn_gens"].ap()[gi][f * 128:(f + 1) * 128, :]),
                "dma", 16)
        load_ident(X[bH])
        p.barrier(("tensor", "scalar", "vector", "sync"))
        for j in range(6, 0, -1):
            coef = -0.25 / j

            def post(fo, cs, pso, coef=coef, bT=bT):
                p.wait("scalar", "pe")
                p.inc("scalar", lambda e, fo=fo, cs=cs, pso=pso, coef=coef, bT=bT:
                      e.activation(X[bT][fo][:, cs], PS[:, pso:pso + 512], AF.Identity,
                                   bias=0.0, scale=coef), "act")
                p.wait("vector", "act")
                ds = max(cs.start, fo * 128)
                de = min(cs.stop, (fo + 1) * 128)
                if ds < de:
                    p.inc("vector", lambda e, fo=fo, ds=ds, de=de, bT=bT: e.tensor_tensor(
                        X[bT][fo][:, ds:de], X[bT][fo][:, ds:de],
                        ctx.ident_f32[:, ds - fo * 128:de - fo * 128], ALU.add), "dve")

            mm_product(X[bA], X[bH], None, post=post)
            p.barrier(("tensor", "scalar", "vector"))
            bH, bT = bT, bH
        cur = bH
        scratch = [bA, bT]
        for sq in range(2):
            tb, ob = scratch[0], scratch[1]
            transpose_to(X[cur], X[tb])
            p.barrier(("tensor", "scalar", "vector"))
            mm_product(X[tb], X[cur], X[ob])
            p.barrier(("tensor", "scalar", "vector"))
            scratch = [cur, tb]
            cur = ob
        if racc is None:
            racc = cur
            free = [b for b in range(4) if b != racc]
        else:
            tb = [b for b in range(4) if b not in (racc, cur)][0]
            transpose_to(X[racc], X[tb])
            p.barrier(("tensor", "scalar", "vector"))
            ob = [b for b in range(4) if b not in (racc, cur, tb)][0]
            mm_product(X[tb], X[cur], X[ob])
            p.barrier(("tensor", "scalar", "vector"))
            racc = ob
            free = [b for b in range(4) if b != racc]
    if "rexp" in ctx.taps:
        p.barrier(("sync",))
        for f in range(NF):
            p.inc("sync", lambda e, f=f: e.dma_start(
                ctx.tap_drams["rexp"].ap()[f * 128:(f + 1) * 128, 0:1024], X[racc][f][:]),
                "st", 16)
        p.wait("sync", "st")
    for f in range(NF):
        p.wait("vector", "st")
        p.inc("vector", lambda e, f=f: e.tensor_copy(stb[:], X[racc][f][:]), "dve")
        p.wait("sync", "dve")
        p.inc("sync", lambda e, f=f: e.dma_start(
            ctx.d["rs"].ap()[f * 128:(f + 1) * 128, :], stb[:]), "st", 16)
        p.wait("sync", "st")
    p.barrier()


def _wp_setup(ctx, blocks):
    m = ctx.m
    ctx.wp_bufs = [m.alloc(f"wp_b{i}", [128, 5460], BF16, at=R_LNS + i * 10944)
                   for i in range(2)]
    jobs = []
    if "l0_ffn" in blocks:
        for lay in range(2):
            for f in range(NF):
                jobs.append((ctx.d["L0_ffn_w1"].ap()[lay][f * 128:(f + 1) * 128, :],
                             ctx.d["w1s"].ap()[lay][f * 128:(f + 1) * 128, :], 128, 5460))
            for c in range(22):
                cw = 128 if c < 21 else 42
                jobs.append((ctx.d["L0_ffn_w2"].ap()[lay][c * 128:c * 128 + cw, :],
                             ctx.d["w2s"].ap()[lay][c * 128:c * 128 + cw, :], cw, 1024))
    if "l1_ffn" in blocks:
        for f in range(NF):
            jobs.append((ctx.d["L1_ffn_aw1"].ap()[f * 128:(f + 1) * 128, :],
                         ctx.d["aw1s"].ap()[f * 128:(f + 1) * 128, :], 128, 4096))
        for t in range(32):
            jobs.append((ctx.d["L1_ffn_aw2"].ap()[t * 128:(t + 1) * 128, :],
                         ctx.d["aw2s"].ap()[t * 128:(t + 1) * 128, :], 128, 1024))
        for ex in range(6):
            for f in range(NF):
                jobs.append((ctx.d["L1_ffn_ew1"].ap()[ex][f * 128:(f + 1) * 128, :],
                             ctx.d["ew1s"].ap()[ex][f * 128:(f + 1) * 128, :], 128, 2048))
            for t in range(16):
                jobs.append((ctx.d["L1_ffn_ew2"].ap()[ex][t * 128:(t + 1) * 128, :],
                             ctx.d["ew2s"].ap()[ex][t * 128:(t + 1) * 128, :], 128, 1024))
    ctx.wp_jobs = jobs
    ctx.wp_pos = 0
    ctx.wp_stm = {}


def _wp_emit(ctx, n):
    p = ctx.p
    end = len(ctx.wp_jobs) if n is None else min(ctx.wp_pos + n, len(ctx.wp_jobs))
    for j in range(ctx.wp_pos, end):
        src, dst, rows, cols = ctx.wp_jobs[j]
        buf = ctx.wp_bufs[j % 2]
        pw = ("wdma", "wdma2")[j % 2]
        ps = ("st", "st2")[j % 2]
        if j - 2 >= 0:
            p.wait("gpsimd", ps, ctx.wp_stm[j - 2])
        p.inc("gpsimd", lambda e, s=src, b=buf, r=rows, c=cols: e.dma_start(
            b[0:r, 0:c], s), pw, 16)
        p.wait("sync", pw)
        if j - 2 >= 0:
            p.wait("sync", ps, ctx.wp_stm[j - 2])
        p.inc("sync", lambda e, d=dst, b=buf, r=rows, c=cols: e.dma_start(
            d, b[0:r, 0:c]), ps, 16)
        ctx.wp_stm[j] = p.mark(ps)
    ctx.wp_pos = end


def build_program(blocks=("l0_attn",), taps=(), nb=2):
    nc = bass.Bass("TRN2", target_bir_lowering=False)
    ctx = Ctx()
    ctx.nc = nc
    ctx.p = Prog(nc)
    ctx.taps = set(taps)
    m = SbufMap(nc)
    ctx.m = m

    d = {}
    d["xin"] = nc.dram_tensor("xin", [nb, E, SEQ], F32, kind="ExternalInput")
    d["xout"] = nc.dram_tensor("xout", [nb, E, SEQ], F32, kind="ExternalOutput")
    for name, shape in WEIGHT_SHAPES.items():
        d[name] = nc.dram_tensor(name, list(shape), F32, kind="ExternalInput")
    ctx.d = d
    ctx.tap_drams = {t: nc.dram_tensor(
        "tap_" + t, [128 if t == "rtq" else (512 if t.endswith("t") else E),
                     1024 if t == "rexp" else SEQ], F32,
        kind="ExternalOutput") for t in taps}

    consts = _host_consts()
    identf_d = nc.inline_tensor(consts["ident_f32"], name="identf")
    identb_d = nc.inline_tensor(consts["ident_bf16"], name="identb")
    ctx.cd = {}
    ctx.cd["maskdiag"] = nc.inline_tensor(consts["maskdiag"], name="maskdiag")
    ctx.cd["epsc"] = nc.inline_tensor(consts["epsc"], name="epsc")
    ctx.cd["identf"] = identf_d
    if "l1_ffn" in blocks:
        ctx.cd["sel6"] = nc.inline_tensor(consts["sel6"], name="sel6")
        d["aw1s"] = nc.dram_tensor("aw1s", [1024, 4096], BF16, kind="Internal")
        d["aw2s"] = nc.dram_tensor("aw2s", [4096, 1024], BF16, kind="Internal")
        d["ew1s"] = nc.dram_tensor("ew1s", [6, 1024, 2048], BF16, kind="Internal")
        d["ew2s"] = nc.dram_tensor("ew2s", [6, 2048, 1024], BF16, kind="Internal")
        d["rs"] = nc.dram_tensor("rs", [1024, 1024], BF16, kind="Internal")
    if "l0_ffn" in blocks:
        for k in ("rotA_q", "rotB_q", "rotA_k", "rotB_k", "locmask", "locmaskp"):
            ctx.cd[k] = nc.inline_tensor(consts[k], name=k)
        d["xpark"] = nc.dram_tensor("xpark", [E, SEQ], F32, kind="Internal")
        d["w1s"] = nc.dram_tensor("w1s", [2, 1024, 5460], BF16, kind="Internal")
        d["w2s"] = nc.dram_tensor("w2s", [2, 2730, 1024], BF16, kind="Internal")

    ctx.xT = [m.alloc(f"xT{f}", [128, NPAD], F32, at=R_XT + f * XT_STRIDE) for f in range(NF)]
    co = R_CON
    ctx.ident_f32 = m.alloc("identf", [128, 128], F32, at=co); co += 512
    ctx.ident_bf16 = m.alloc("identb", [128, 128], BF16, at=co); co += 256
    ctx.ones_col = m.alloc("ones_col", [128, 1], F32, at=co); co += 32
    ctx.ones_col_bf = m.alloc("ones_col_bf", [128, 1], BF16, at=co); co += 32
    ctx.ones_row = m.alloc("ones_row", [1, 128], F32, at=co); co += 512
    ctx.cols = m.alloc("cols", [128, 96], F32, at=co); co += 384
    ctx.bq_col = m.alloc("bq_col", [128, 16], F32, at=co); co += 64
    ctx.bk_col = m.alloc("bk_col", [128, 16], F32, at=co); co += 64
    ctx.lam_s = m.alloc("lam_s", [1, 1], F32, at=co); co += 32
    ctx.eps_s = m.alloc("eps_s", [1, 1], F32, at=co); co += 32
    ctx.eps6_s = m.alloc("eps6_s", [1, 1], F32, at=co); co += 32
    ctx.maskc = m.alloc("maskc", [128, 128], F32, at=co); co += 512
    ctx.epsc = m.alloc("epsc", [2, 1], F32, at=co); co += 32
    assert co <= R_WAR
    ctx.WARENA = R_WAR
    ctx.BIG = R_BIG
    ctx.ln_sq = m.alloc("ln_sq", [128, 2048], F32, at=R_LNS)
    ctx.ln_misc = m.alloc("ln_misc", [128, 6144], F32, at=R_LNS + 8192)
    ctx.row_tmp = m.alloc("row_tmp", [1, 2048], F32, at=R_LNS)

    ctx.PS = nc.alloc_psum_tensor("PS", [128, 4096], F32)

    p = ctx.p
    p.inc("sync", lambda e: e.dma_start(ctx.ident_f32[:], identf_d.ap()), "dma", 16)
    p.inc("sync", lambda e: e.dma_start(ctx.ident_bf16[:], identb_d.ap()), "dma", 16)
    p.inc("sync", lambda e: e.dma_start(ctx.maskc[:], ctx.cd["maskdiag"].ap()), "dma", 16)
    p.inc("sync", lambda e: e.dma_start(ctx.epsc[:], ctx.cd["epsc"].ap()), "dma", 16)
    p.run(lambda e: e.memset(ctx.ones_col[:], 1.0), "gpsimd")
    p.run(lambda e: e.memset(ctx.ones_col_bf[:], 1.0), "gpsimd")
    p.run(lambda e: e.memset(ctx.ones_row[:], 1.0), "gpsimd")
    p.run(lambda e: e.memset(ctx.eps_s[:], LN_EPS), "gpsimd")
    p.run(lambda e: e.memset(ctx.eps6_s[:], 1e-6), "gpsimd")
    for f in range(NF):
        p.run(lambda e, f=f: e.memset(ctx.xT[f][:, 0:TOKOFF], 0.0), "gpsimd")
        if f < NF - 1:
            p.run(lambda e, f=f: e.memset(ctx.xT[f][:, TOKOFF + SEQ:NPAD], 0.0), "gpsimd")
    p.inc("gpsimd", lambda e: e.memset(ctx.xT[NF - 1][:, TOKOFF + SEQ:NPAD], 0.0), "gp", 1)
    p.wait("tensor", "gp")
    p.wait("vector", "gp")
    p.wait("scalar", "gp")
    p.wait("sync", "gp")

    _wp_setup(ctx, blocks)
    if ctx.wp_jobs:
        _wp_emit(ctx, None)
        p.barrier()
    if "l1_ffn" in blocks:
        _emit_expm(ctx)

    for b in range(nb):
        emit_load_x(ctx, b)
        for bi, blk in enumerate(blocks):
            BLOCK_EMITTERS[blk](ctx, b)
            tname = f"xb{b}p{bi}"
            if tname in ctx.taps:
                _tap_tiles(ctx, ctx.xT, tname, padded=True)
        emit_store_x(ctx, b)

    p.build()
    nc.finalize()
    return nc, ctx


BLOCK_EMITTERS["l0_attn"] = emit_l0_attn
BLOCK_EMITTERS["l1_attn"] = emit_l1_attn
BLOCK_EMITTERS["l1_ffn"] = emit_l1_ffn

def emit_l0_ffn(ctx, batch):
    p, m, nc = ctx.p, ctx.m, ctx.nc
    PS = ctx.PS
    B0 = ctx.BIG
    H, DH = 8, 64
    NT = 16
    NC2 = 22

    zT = [m.alloc(f"b2_z{f}", [128, NPAD], F32, at=B0 + f * XT_STRIDE) for f in range(NF)]
    rott = {k: m.alloc(f"b2_rt{k}", [128, 1024], F32, at=B0 + 8 * XT_STRIDE + i * 4096)
            for i, k in enumerate(("Aq", "Bq", "Ak", "Bk"))}
    vbuf = [m.alloc(f"b2_v{t}", [128, 8 * 65], BF16, at=R_XT + t * 1056) for t in range(NT)]
    qtb = [m.alloc(f"b2_qt{g}", [128, 2048], BF16, at=R_XT + 16896 + g * 4096) for g in range(4)]
    ktb = [m.alloc(f"b2_kt{g}", [128, 2048], BF16, at=R_XT + 33280 + g * 4096) for g in range(4)]
    PRE = R_XT + 49664
    sq32 = m.alloc("b2_sq32", [128, 512], F32, at=PRE)
    qn = m.alloc("b2_qn", [128, 512], F32, at=PRE + 2048)
    t1b = m.alloc("b2_t1", [128, 512], F32, at=PRE + 4096)
    t2b = m.alloc("b2_t2", [128, 512], F32, at=PRE + 6144)
    glb = m.alloc("b2_gl", [128, 512], BF16, at=PRE + 8192)
    sc32 = m.alloc("b2_sc32", [128, 512], F32, at=PRE + 9216)
    rowsb = m.alloc("b2_rows", [128, 512], F32, at=PRE + 11264)
    ssq = m.alloc("b2_ssq", [128, 16], F32, at=PRE + 13312)
    qsb = m.alloc("b2_qsb", [128, 64], F32, at=PRE + 13376)
    ksb = m.alloc("b2_ksb", [128, 64], F32, at=PRE + 13632)
    rbc = m.alloc("b2_rbc", [64, 512], F32, at=PRE + 13888)
    hnT = [m.alloc(f"b2_hn{f}", [128, NPAD], BF16, at=R_XT + f * 4128) for f in range(NF)]
    glub = [m.alloc(f"b2_glu{c}", [128, 512], BF16, at=R_XT + 33280 + c * 1024)
            for c in range(NC2)]
    hbT = [m.alloc(f"b2_hb{f}", [128, 2048], BF16, at=R_LNS + f * 4096) for f in range(NF)]
    oT = [m.alloc(f"b2_oT{g}", [128, 2048], BF16, at=R_LNS + g * 4096) for g in range(4)]
    W0 = ctx.WARENA
    wqkv_sb = [m.alloc(f"b2_wqkv{f}", [128, 1536], BF16, at=W0 + f * 3072) for f in range(NF)]
    masks_sb = m.alloc("b2_masks", [128, 4096], F32, at=W0)
    wol_sb = [m.alloc(f"b2_wo{i}", [128, 1024], BF16, at=W0 + 8192 + i * 2048) for i in range(4)]
    w1a_sb = [m.alloc(f"b2_w1a{i}", [128, 1024], BF16, at=W0 + 16384 + i * 2048) for i in range(2)]
    w1g_sb = [m.alloc(f"b2_w1g{i}", [128, 1024], BF16, at=W0 + 20480 + i * 2048) for i in range(2)]
    w2_sb = [m.alloc(f"b2_w2{i}", [128, 1024], BF16, at=W0 + 16384 + i * 2048) for i in range(2)]

    dpark = ctx.d["xpark"]

    p.barrier(("gpsimd",))
    for f in range(NF):
        p.run(lambda e, f=f: e.memset(zT[f][:, 0:TOKOFF], 0.0), "gpsimd")
        if f < NF - 1:
            p.run(lambda e, f=f: e.memset(zT[f][:, TOKOFF + SEQ:NPAD], 0.0), "gpsimd")
    p.inc("gpsimd", lambda e: e.memset(zT[NF - 1][:, TOKOFF + SEQ:NPAD], 0.0), "gp", 1)
    emit_ln(ctx, ctx.xT, zT)
    p.barrier(("sync",))
    for f in range(NF):
        p.inc("sync", lambda e, f=f: e.dma_start(
            dpark.ap()[f * 128:(f + 1) * 128, :], ctx.xT[f][:, tokc(0, SEQ)]), "st", 16)
    p.wait("gpsimd", "st")

    for lay in range(2):
        p.barrier(("sync", "tensor", "scalar", "vector"))
        p.inc("sync", lambda e, lay=lay: e.dma_start(
            ctx.row_tmp[0:1, 0:64], ctx.d["L0_ffn_q_scale"].ap()[lay].rearrange(
                "(a d) -> a d", a=1)), "dma", 16)
        p.wait("tensor", "dma")
        emit_bcast_row(ctx, ctx.row_tmp, qsb[:, 0:64], 64)
        p.inc("sync", lambda e, lay=lay: e.dma_start(
            ctx.row_tmp[0:1, 0:64], ctx.d["L0_ffn_k_scale"].ap()[lay].rearrange(
                "(a d) -> a d", a=1)), "dma", 16)
        p.wait("tensor", "dma")
        emit_bcast_row(ctx, ctx.row_tmp, ksb[:, 0:64], 64)
        for k, dname, scl in (("Aq", "rotA_q", qsb), ("Bq", "rotB_q", qsb),
                              ("Ak", "rotA_k", ksb), ("Bk", "rotB_k", ksb)):
            p.inc("sync", lambda e, k=k, dname=dname: e.dma_start(
                rott[k].ap().rearrange("p (tt d) -> p tt d", d=64),
                ctx.cd[dname].ap().rearrange("(tt pp) d -> pp tt d", pp=128)), "dma", 16)
        p.wait("vector", "dma")
        p.wait("vector", "act")
        for k, scl in (("Aq", qsb), ("Ak", ksb)):
            for t in range(NT):
                p.inc("vector", lambda e, k=k, t=t, scl=scl: e.tensor_tensor(
                    rott[k][:, t * 64:(t + 1) * 64], rott[k][:, t * 64:(t + 1) * 64],
                    scl[:, 0:64], ALU.mult), "dve")
        for k, scl in (("Bq", qsb), ("Bk", ksb)):
            for t in range(NT):
                p.inc("vector", lambda e, k=k, t=t, scl=scl: e.tensor_tensor(
                    rott[k][:, t * 64:t * 64 + 32], rott[k][:, t * 64:t * 64 + 32],
                    scl[:, 32:64], ALU.mult), "dve")
                p.inc("vector", lambda e, k=k, t=t, scl=scl: e.tensor_tensor(
                    rott[k][:, t * 64 + 32:(t + 1) * 64], rott[k][:, t * 64 + 32:(t + 1) * 64],
                    scl[:, 0:32], ALU.mult), "dve")
        if lay == 0 and batch == 0 and "rtq" in ctx.taps:
            p.barrier(("sync",))
            p.inc("sync", lambda e: e.dma_start(
                ctx.tap_drams["rtq"].ap()[0:128, 0:1024], rott["Aq"][:, 0:1024]), "st", 16)
            p.inc("sync", lambda e: e.dma_start(
                ctx.tap_drams["rtq"].ap()[0:128, 1024:2048], rott["Bq"][:, 0:1024]), "st", 16)
            p.wait("sync", "st")
        p.wait("gpsimd", "pe")
        for f in range(NF):
            p.inc("gpsimd", lambda e, f=f, lay=lay: e.dma_start(
                wqkv_sb[f][:], ctx.d["L0_ffn_wqkv"].ap()[lay].rearrange(
                    "(ft pp) c -> ft pp c", pp=128)[f]), "wdma", 16)
        p.wait("scalar", "dve")
        for f in range(NF):
            soff = TOKOFF - (1 if f >= 4 else 0)
            p.inc("scalar", lambda e, f=f, soff=soff: e.copy(
                hbT[f][:, 0:SEQ], zT[f][:, soff:soff + SEQ]), "act")
        p.wait("gpsimd", "act")
        for t in range(NT - 1):
            p.run(lambda e, t=t: e.memset(vbuf[t][:], 1.0), "gpsimd")
        p.inc("gpsimd", lambda e: e.memset(vbuf[NT - 1][:], 1.0), "gp", 1)
        p.wait("tensor", "gp")
        p.wait("tensor", "act")
        p.wait("tensor", "wdma")

        for t in range(NT):
            p.wait("tensor", "act")
            p.wait("tensor", "dve")
            for cch in range(3):
                cs = slice(cch * 512, cch * 512 + 512)
                for f in range(NF):
                    p.inc("tensor", lambda e, t=t, cs=cs, f=f: e.matmul(
                        PS[:, cs], hbT[f][:, t * 128:(t + 1) * 128],
                        wqkv_sb[f][:, cs], start=(f == 0), stop=(f == NF - 1)), "pe")
            p.wait("scalar", "pe")
            p.wait("vector", "pe")
            p.inc("scalar", lambda e, t=t: e.copy(
                vbuf[t].ap().rearrange("p (h c) -> p h c", c=65)[:, :, 0:64],
                PS[:, 1024:1536].rearrange("p (h c) -> p h c", c=64)), "act")
            for qk in range(2):
                po = qk * 512
                AT, BT = (rott["Aq"], rott["Bq"]) if qk == 0 else (rott["Ak"], rott["Bk"])
                sso = qk * 8
                p.inc("scalar", lambda e, po=po: e.activation(
                    sq32[:, 0:512], PS[:, po:po + 512], AF.Square), "act")
                p.wait("vector", "act")
                p.inc("vector", lambda e, sso=sso: e.reduce_sum(
                    ssq[:, sso:sso + 8],
                    sq32.ap().rearrange("p (h d) -> p h d", d=64), axis=mybir.AxisListType.X),
                    "dve")
                p.wait("scalar", "dve")
                p.inc("scalar", lambda e, sso=sso: e.activation(
                    ssq[:, sso:sso + 8], ssq[:, sso:sso + 8], AF.Sqrt), "act")
                p.wait("vector", "act")
                p.inc("vector", lambda e, sso=sso: e.tensor_scalar(
                    ssq[:, sso:sso + 8], ssq[:, sso:sso + 8], 1e-12, None, ALU.max), "dve")
                p.wait("vector", "dve")
                p.inc("vector", lambda e, sso=sso: e.reciprocal(
                    ssq[:, sso:sso + 8], ssq[:, sso:sso + 8]), "dve")
                p.wait("vector", "dve")
                p.wait("vector", "pe")
                for h in range(H):
                    p.inc("vector", lambda e, h=h, po=po, sso=sso: e.tensor_scalar(
                        qn[:, h * 64:(h + 1) * 64], PS[:, po + h * 64:po + (h + 1) * 64],
                        ssq[:, sso + h:sso + h + 1], None, ALU.mult), "dve")
                p.wait("vector", "dve")
                p.inc("vector", lambda e, t=t, AT=AT: e.tensor_tensor(
                    t1b.ap().rearrange("p (h d) -> p h d", d=64),
                    qn.ap().rearrange("p (h d) -> p h d", d=64),
                    AT[:, t * 64:(t + 1) * 64].rearrange("p (a d) -> p a d", a=1)
                    .broadcast_to([128, 8, 64]), ALU.mult), "dve")
                p.inc("vector", lambda e, t=t, BT=BT: e.tensor_tensor(
                    t2b.ap().rearrange("p (h d) -> p h d", d=64)[:, :, 0:32],
                    qn.ap().rearrange("p (h d) -> p h d", d=64)[:, :, 32:64],
                    BT[:, t * 64:t * 64 + 32].rearrange("p (a d) -> p a d", a=1)
                    .broadcast_to([128, 8, 32]), ALU.mult), "dve")
                p.inc("vector", lambda e, t=t, BT=BT: e.tensor_tensor(
                    t2b.ap().rearrange("p (h d) -> p h d", d=64)[:, :, 32:64],
                    qn.ap().rearrange("p (h d) -> p h d", d=64)[:, :, 0:32],
                    BT[:, t * 64 + 32:(t + 1) * 64].rearrange("p (a d) -> p a d", a=1)
                    .broadcast_to([128, 8, 32]), ALU.mult), "dve")
                p.wait("vector", "dve")
                p.inc("vector", lambda e: e.tensor_tensor(
                    t1b[:, 0:512], t1b[:, 0:512], t2b[:, 0:512], ALU.add), "dve")
                p.wait("tensor", "dve")
                p.wait("tensor", "act")
                for hh in range(H):
                    p.inc("tensor", lambda e, hh=hh: e.transpose(
                        PS[0:64, 1024 + hh * 128:1024 + (hh + 1) * 128],
                        t1b[:, hh * 64:(hh + 1) * 64], ctx.ident_f32[:]), "pe")
                p.wait("scalar", "pe")
                dst = qtb if qk == 0 else ktb
                for hh in range(H):
                    g, rbase = hh // 2, (hh % 2) * 64
                    p.inc("scalar", lambda e, g=g, rbase=rbase, hh=hh, t=t, dst=dst: e.copy(
                        dst[g][rbase:rbase + 64, t * 128:(t + 1) * 128],
                        PS[0:64, 1024 + hh * 128:1024 + (hh + 1) * 128]), "act")
                p.wait("tensor", "act")
                p.wait("vector", "act")

        p.wait("sync", "pe")
        for r in range(4):
            p.inc("sync", lambda e, r=r: e.dma_start(
                masks_sb[:, r * 512:(r + 1) * 512], ctx.cd["locmaskp"].ap()[r]), "dma", 16)
            p.inc("sync", lambda e, r=r: e.dma_start(
                masks_sb[:, 2048 + r * 512:2048 + (r + 1) * 512], ctx.cd["locmask"].ap()[r]),
                "dma", 16)
        p.wait("vector", "dma")
        p.wait("scalar", "dma")
        for h in range(H):
            g, rbase = h // 2, (h % 2) * 64
            for w in range(4):
                akts = list(range(4 * (w - 1), 4 * w)) if w > 0 else []
                akts += list(range(4 * w, 4 * w + 4))
                first = True
                for j, akt in enumerate(akts):
                    cur = (akt >= 4 * w)
                    r = akt - 4 * w if cur else akt - 4 * (w - 1)
                    moff = (2048 + r * 512) if cur else (r * 512)
                    pso = 2048 + (j % 2) * 512
                    p.wait("tensor", "act")
                    p.wait("tensor", "dve")
                    p.inc("tensor", lambda e, g=g, rbase=rbase, akt=akt, w=w, pso=pso: e.matmul(
                        PS[:, pso:pso + 512],
                        ktb[g][rbase:rbase + 64, akt * 128:(akt + 1) * 128],
                        qtb[g][rbase:rbase + 64, w * 512:(w + 1) * 512],
                        start=True, stop=True), "pe")
                    p.wait("vector", "pe")
                    p.inc("vector", lambda e, moff=moff, pso=pso: e.tensor_tensor(
                        sc32[:, 0:512], PS[:, pso:pso + 512],
                        masks_sb[:, moff:moff + 512], ALU.add), "dve")
                    p.wait("scalar", "dve")
                    p.wait("scalar", "pe")
                    p.inc("scalar", lambda e: e.activation(
                        glb[:, 0:512], sc32[:, 0:512], AF.Exp, scale=8.0), "act")
                    p.wait("tensor", "act")
                    p.inc("tensor", lambda e, h=h, akt=akt, first=first, last=(j == len(akts) - 1):
                          e.matmul(PS[0:65, 3072:3584], vbuf[akt][:, h * 65:(h + 1) * 65],
                                   glb[:, 0:512], start=first, stop=last), "pe")
                    first = False
                p.wait("scalar", "pe")
                p.inc("scalar", lambda e: e.activation(
                    rowsb[0:1, 0:512], PS[64:65, 3072:3584], AF.Ln), "act")
                p.wait("scalar", "act")
                p.inc("scalar", lambda e: e.activation(
                    rowsb[0:1, 0:512], rowsb[0:1, 0:512], AF.Exp, scale=-1.0), "act")
                p.wait("tensor", "act")
                p.inc("tensor", lambda e: e.matmul(
                    PS[0:64, 3584:4096], ctx.ones_row[0:1, 0:64], rowsb[0:1, 0:512],
                    start=True, stop=True), "pe")
                p.wait("scalar", "pe")
                p.inc("scalar", lambda e: e.copy(rbc[0:64, 0:512], PS[0:64, 3584:4096]), "act")
                p.wait("vector", "act")
                p.inc("vector", lambda e, g=g, rbase=rbase, w=w: e.tensor_tensor(
                    oT[g][rbase:rbase + 64, w * 512:(w + 1) * 512],
                    PS[0:64, 3072:3584], rbc[0:64, 0:512], ALU.mult), "dve")
                p.wait("tensor", "dve")

        if lay == 0 and batch == 0:
            for nm, tiles in (("q0t", qtb), ("k0t", ktb), ("o0t", oT)):
                if nm in ctx.taps:
                    p.barrier(("sync",))
                    dram = ctx.tap_drams[nm]
                    for g in range(4):
                        for ch in range(4):
                            p.wait("vector", "st")
                            p.inc("vector", lambda e, g=g, ch=ch, tiles=tiles: e.tensor_copy(
                                sc32[:, 0:512], tiles[g][:, ch * 512:(ch + 1) * 512]), "dve")
                            p.wait("sync", "dve")
                            p.inc("sync", lambda e, g=g, ch=ch, dram=dram: e.dma_start(
                                dram.ap()[g * 128:(g + 1) * 128, ch * 512:(ch + 1) * 512],
                                sc32[:, 0:512]), "st", 16)
                            p.wait("sync", "st")
                    p.wait("gpsimd", "st")
        p.wait("gpsimd", "pe")
        p.wait("gpsimd", "act")
        for i in range(4):
            p.inc("gpsimd", lambda e, i=i, lay=lay: e.dma_start(
                wol_sb[i][:], ctx.d["L0_ffn_wo"].ap()[lay].rearrange(
                    "(it pp) o -> it pp o", pp=128)[i]), "wdma", 16)
        p.wait("tensor", "wdma")
        for fo in range(NF):
            for cch in range(4):
                cs = slice(cch * 512, cch * 512 + 512)
                pso = (cch % 4) * 512
                p.wait("tensor", "dve")
                for i in range(4):
                    p.inc("tensor", lambda e, fo=fo, cs=cs, i=i, pso=pso: e.matmul(
                        PS[:, pso:pso + 512], wol_sb[i][:, fo * 128:(fo + 1) * 128],
                        oT[i][:, cs], start=(i == 0), stop=(i == 3)), "pe")
                p.wait("vector", "pe")
                p.inc("vector", lambda e, fo=fo, cs=cs, pso=pso: e.tensor_tensor(
                    zT[fo][:, TOKOFF + cs.start:TOKOFF + cs.stop],
                    zT[fo][:, TOKOFF + cs.start:TOKOFF + cs.stop],
                    PS[:, pso:pso + 512], ALU.add), "dve")

        if f"zo{lay}" in ctx.taps and batch == 0:
            _tap_tiles(ctx, zT, f"zo{lay}", padded=True)
        emit_colvec(ctx, ctx.d["L0_ffn_ln_g"].ap()[lay].rearrange("(a e) -> a e", a=1), E, 0)
        emit_colvec(ctx, ctx.d["L0_ffn_ln_b"].ap()[lay].rearrange("(a e) -> a e", a=1), E, 8)
        emit_colvec(ctx, ctx.d["L0_ffn_b2"].ap()[lay].rearrange("(a e) -> a e", a=1), E, 16)
        emit_colvec(ctx, ctx.d["L0_ffn_b1"].ap()[lay][0:2048].rearrange("(a e) -> a e", a=1),
                    2048, 24)
        emit_colvec(ctx, ctx.d["L0_ffn_b1"].ap()[lay][2048:2730].rearrange("(a e) -> a e", a=1),
                    682, 40)
        emit_colvec(ctx, ctx.d["L0_ffn_b1"].ap()[lay][2730:4778].rearrange("(a e) -> a e", a=1),
                    2048, 46)
        emit_colvec(ctx, ctx.d["L0_ffn_b1"].ap()[lay][4778:5460].rearrange("(a e) -> a e", a=1),
                    682, 62)
        offs = [TOKOFF - (1 if f >= 4 else 0) for f in range(NF)]
        emit_ln(ctx, zT, hnT, gamma_off=0, beta_off=8, src_offs=offs)
        for tcn in range(4):
            TC = slice(tcn * 512, tcn * 512 + 512)
            evm = {}

            def ld_w1(c, lay=lay):
                cw = 128 if c < 21 else 42
                return [lambda e, c=c, cw=cw, lay=lay: e.dma_start(
                    w1a_sb[c % 2].ap().rearrange("p (ft c) -> p ft c", c=128)[:, :, 0:cw],
                    ctx.d["w1s"].ap()[lay].rearrange("(ft pp) hh -> pp ft hh", pp=128)
                    [:, :, c * 128:c * 128 + cw]),
                    lambda e, c=c, cw=cw, lay=lay: e.dma_start(
                    w1g_sb[c % 2].ap().rearrange("p (ft c) -> p ft c", c=128)[:, :, 0:cw],
                    ctx.d["w1s"].ap()[lay].rearrange("(ft pp) hh -> pp ft hh", pp=128)
                    [:, :, 2730 + c * 128:2730 + c * 128 + cw])]

            def use_w1(c, TC=TC):
                cw = 128 if c < 21 else 42
                wa, wg = w1a_sb[c % 2], w1g_sb[c % 2]
                po = (c % 2) * 2048
                if c - 2 in evm:
                    p.wait("tensor", "dve", evm[c - 2][0])
                    p.wait("tensor", "act", evm[c - 2][1])
                for f in range(NF):
                    p.inc("tensor", lambda e, c=c, cw=cw, f=f, TC=TC, wa=wa, po=po: e.matmul(
                        PS[0:cw, po:po + 512], wa[:, f * 128:f * 128 + cw],
                        hnT[f][:, TOKOFF + TC.start:TOKOFF + TC.stop],
                        start=(f == 0), stop=(f == NF - 1)), "pe")
                    p.inc("tensor", lambda e, c=c, cw=cw, f=f, TC=TC, wg=wg, po=po: e.matmul(
                        PS[0:cw, po + 1024:po + 1536], wg[:, f * 128:f * 128 + cw],
                        hnT[f][:, TOKOFF + TC.start:TOKOFF + TC.stop],
                        start=(f == 0), stop=(f == NF - 1)), "pe")
                pm = p.mark("pe")
                p.wait("scalar", "pe", pm)
                if c - 1 in evm:
                    p.wait("scalar", "dve", evm[c - 1][0])
                p.inc("scalar", lambda e, c=c, cw=cw, po=po: e.activation(
                    glb[0:cw, 0:512], PS[0:cw, po + 1024:po + 1536], AF.Gelu,
                    bias=ctx.cols[0:cw, 46 + c:47 + c]), "act")
                am = p.mark("act")
                p.wait("vector", "act", am)
                p.inc("vector", lambda e, c=c, cw=cw, po=po: e.scalar_tensor_tensor(
                    glub[c][0:cw, 0:512], PS[0:cw, po:po + 512], ctx.cols[0:cw, 24 + c:25 + c],
                    glb[0:cw, 0:512], ALU.add, ALU.mult), "dve")
                evm[c] = (p.mark("dve"), am)

            p.wait("tensor", "act")
            p.wait("tensor", "dve")
            stream_weights(p, NC2, ld_w1, use_w1, "w2a")
            p.wait("tensor", "dve")

            def ld_w2(c, lay=lay):
                cw = 128 if c < 21 else 42
                return lambda e, c=c, cw=cw, lay=lay: e.dma_start(
                    w2_sb[c % 2][0:cw, :], ctx.d["w2s"].ap()[lay][c * 128:c * 128 + cw, :])

            def use_w2(c):
                cw = 128 if c < 21 else 42
                wc = w2_sb[c % 2]
                for fo in range(NF):
                    p.inc("tensor", lambda e, c=c, cw=cw, fo=fo, wc=wc: e.matmul(
                        PS[:, fo * 512:(fo + 1) * 512], wc[0:cw, fo * 128:(fo + 1) * 128],
                        glub[c][0:cw, 0:512], start=(c == 0), stop=(c == NC2 - 1)), "pe")

            stream_weights(p, NC2, ld_w2, use_w2, "w2b")
            p.wait("scalar", "pe")
            p.wait("vector", "pe")
            for fo in range(NF):
                p.inc("scalar", lambda e, fo=fo: e.activation(
                    sc32[:, 0:512], PS[:, fo * 512:(fo + 1) * 512], AF.Identity,
                    bias=ctx.cols[:, 16 + fo:17 + fo]), "act")
                p.wait("vector", "act")
                p.inc("vector", lambda e, fo=fo, TC=TC: e.tensor_tensor(
                    zT[fo][:, TOKOFF + TC.start:TOKOFF + TC.stop],
                    zT[fo][:, TOKOFF + TC.start:TOKOFF + TC.stop],
                    sc32[:, 0:512], ALU.add), "dve")
                p.wait("scalar", "dve")

        if f"z{lay}" in ctx.taps and batch == 0:
            p.barrier(("tensor", "scalar", "vector", "sync"))
            _tap_tiles(ctx, zT, f"z{lay}", padded=True)

    p.barrier(("sync",))
    for f in range(NF):
        p.inc("sync", lambda e, f=f: e.dma_start(
            ctx.xT[f][:, tokc(0, SEQ)], dpark.ap()[f * 128:(f + 1) * 128, :]), "dma", 16)
    p.wait("vector", "dma")
    for f in range(NF):
        p.inc("vector", lambda e, f=f: e.tensor_tensor(
            ctx.xT[f][:, tokc(0, SEQ)], ctx.xT[f][:, tokc(0, SEQ)],
            zT[f][:, tokc(0, SEQ)], ALU.add), "dve")
    p.barrier(("tensor", "scalar", "vector", "sync", "gpsimd"))


BLOCK_EMITTERS["l0_ffn"] = emit_l0_ffn



def prep_core_inputs(inputs, core, nb=2, boff=None):
    x = inputs["L0_attn_X"]
    b0 = core * nb if boff is None else boff
    xs = np.ascontiguousarray(
        np.transpose(x[b0:b0 + nb], (0, 2, 1))).astype(np.float32)
    m = {"xin": xs}
    for name, shape in WEIGHT_SHAPES.items():
        if name in inputs:
            m[name] = np.ascontiguousarray(np.asarray(inputs[name], dtype=np.float32))
    m["bq_t"] = np.ascontiguousarray(np.asarray(inputs["L0_attn_bq"], np.float32).T)
    m["bk_t"] = np.ascontiguousarray(np.asarray(inputs["L0_attn_bk"], np.float32).T)
    m["lam11"] = np.asarray(inputs["L0_attn_lam"], np.float32).reshape(1, 1)
    return m


ALL_BLOCKS = ("l0_attn", "l0_ffn", "l1_attn", "l1_ffn")


def kernel(**inputs):
    nc, _ = build_program(blocks=ALL_BLOCKS, taps=(), nb=2)
    in_maps = [prep_core_inputs(inputs, c) for c in range(NCORES)]
    res = run_bass_kernel_spmd(nc, in_maps, core_ids=list(range(NCORES)))
    outs = []
    for c in range(NCORES):
        o = np.asarray(res.results[c]["xout"])
        outs.append(np.transpose(o, (0, 2, 1)))
    return np.concatenate(outs, axis=0).astype(np.float32)
